# version 01
# baseline (speedup 1.0000x reference)
import contextlib

import numpy as np
import ml_dtypes

import concourse.bass as bass
import concourse.mybir as mybir
from concourse.bass_utils import run_bass_kernel_spmd

F32 = mybir.dt.float32
BF16 = mybir.dt.bfloat16
I32 = mybir.dt.int32
AF = mybir.ActivationFunctionType
ALU = mybir.AluOpType
AX = mybir.AxisListType

S = 2048
DM = 1024
FF = 4096
NT = S // 128
ND = DM // 128
NF = FF // 128
EPS_LN = 1e-5
EPS_RMS = 1e-6
EPS_Q = 1e-8
LN3 = 1.0986122886681098
INV255 = float(np.float32(1.0) / np.float32(255.0))
INV127 = float(np.float32(1.0) / np.float32(127.0))
SCALE_SM = float(np.float32(LN3) / np.float32(32.0))
INVD = 1.0 / DM


class Ctr:
    def __init__(self, es, nc, name):
        self.h = es.enter_context(nc.semaphore(name))
        self.n = 0

    def inc(self, instr, k=1):
        instr.then_inc(self.h, k)
        self.n += k
        return instr

    def dma(self, instr):
        return self.inc(instr, 16)

    def wait(self, eng, at=None):
        eng.wait_ge(self.h, self.n if at is None else at)


class Prog:

    def __init__(self, es, nc):
        self.nc = nc
        self.es = es
        self.block = es.enter_context(nc.Block())
        self.ctrs = []

    def ctr(self, name):
        c = Ctr(self.es, self.nc, name)
        self.ctrs.append(c)
        return c

    def wait_all(self, eng):
        for c in self.ctrs:
            if c.n > 0:
                c.wait(eng)

    def barrier(self, engines=("sync", "scalar", "vector", "tensor", "gpsimd")):
        b = self.block
        if "sync" in engines:
            @b.sync
            def _(sync):
                self.wait_all(sync)
        if "scalar" in engines:
            @b.scalar
            def _(scalar):
                self.wait_all(scalar)
        if "vector" in engines:
            @b.vector
            def _(vector):
                self.wait_all(vector)
        if "tensor" in engines:
            @b.tensor
            def _(tensor):
                self.wait_all(tensor)
        if "gpsimd" in engines:
            @b.gpsimd
            def _(gpsimd):
                self.wait_all(gpsimd)


def ser(eng, ctr, emit):
    ctr.wait(eng)
    return ctr.inc(emit())


def _round_to_bf16(vector, cv, scratch_f32, out_bf, lo=None, hi=None):
    if lo is not None:
        ser(vector, cv, lambda: vector.tensor_scalar(
            scratch_f32[:], scratch_f32[:], float(lo), float(hi), ALU.max, ALU.min))
    ser(vector, cv, lambda: vector.tensor_copy(scratch_f32.bitcast(I32)[:], scratch_f32[:]))
    ser(vector, cv, lambda: vector.tensor_copy(out_bf, scratch_f32.bitcast(I32)[:]))


def _ln_rstd(block, ca, cv, var_ap, out_ap, eps_ap):
    @block.scalar
    def _(scalar):
        cv.wait(scalar)
        ser(scalar, ca, lambda: scalar.activation(var_ap, var_ap, AF.Ln, bias=eps_ap))
        ser(scalar, ca, lambda: scalar.activation(out_ap, var_ap, AF.Exp, scale=-0.5))


def _fold_max(p, ident, pbank, rmax, rowm, gmx, cp, cv):
    block = p.block

    @block.tensor
    def _(tensor):
        cv.wait(tensor)
        cp.inc(tensor.transpose(pbank[0:1, 0:128], rmax[:], ident[:]))

    @block.vector
    def _(vector):
        cp.wait(vector)
        ser(vector, cv, lambda: vector.tensor_copy(rowm[:], pbank[0:1, 0:128]))
        ser(vector, cv, lambda: vector.reduce_max(gmx[:], rowm[:], axis=AX.X))


def _quant_scalars(p, sgm, srow, onesf, invs_b, s_b, pbank, cv, cp):
    block = p.block

    @block.vector
    def _(vector):
        ser(vector, cv, lambda: vector.reduce_max(srow[:, 0:1], sgm[:], axis=AX.X))
        ser(vector, cv, lambda: vector.tensor_scalar(srow[:, 0:1], srow[:, 0:1], EPS_Q, None, ALU.max))
        ser(vector, cv, lambda: vector.reciprocal(srow[:, 1:2], srow[:, 0:1]))
        ser(vector, cv, lambda: vector.tensor_scalar(srow[:, 1:2], srow[:, 1:2], 255.0, None, ALU.mult))
        ser(vector, cv, lambda: vector.tensor_scalar(srow[:, 2:3], srow[:, 0:1], INV255, None, ALU.mult))

    @block.tensor
    def _(tensor):
        cv.wait(tensor)
        tensor.matmul(pbank[:, 0:1], onesf[:], srow[:, 1:2], start=True, stop=True)
        cp.inc(tensor.matmul(pbank[:, 1:2], onesf[:], srow[:, 2:3], start=True, stop=True))

    @block.vector
    def _(vector):
        cp.wait(vector)
        ser(vector, cv, lambda: vector.tensor_copy(invs_b[:], pbank[:, 0:1]))
        ser(vector, cv, lambda: vector.tensor_copy(s_b[:], pbank[:, 1:2]))



def build_stats(nc):
    x = nc.dram_tensor("x", [S, DM], F32, kind="ExternalInput").ap()
    ident_d = nc.dram_tensor("ident", [128, 128], F32, kind="ExternalInput").ap()
    st_o = nc.dram_tensor("st", [128, 2 * NT], F32, kind="ExternalOutput").ap()
    mx_o = nc.dram_tensor("mx", [1, 1], F32, kind="ExternalOutput").ap()

    es = contextlib.ExitStack()
    with es:
        sx = es.enter_context(nc.sbuf_tensor("sx_s", [128, NT * DM], F32))
        ident = es.enter_context(nc.sbuf_tensor("identS", [128, 128], F32))
        sq = es.enter_context(nc.sbuf_tensor("sq", [128, DM], BF16))
        sums = es.enter_context(nc.sbuf_tensor("sums", [128, NT], F32))
        sqs = es.enter_context(nc.sbuf_tensor("sqs", [128, NT], F32))
        mxs = es.enter_context(nc.sbuf_tensor("mxs", [128, NT], F32))
        stt = es.enter_context(nc.sbuf_tensor("stt", [128, 2 * NT], F32))
        tmp = es.enter_context(nc.sbuf_tensor("tmp", [128, NT], F32))
        tmax = es.enter_context(nc.sbuf_tensor("tmax", [128, NT], F32))
        rmax = es.enter_context(nc.sbuf_tensor("rmax", [128, 1], F32))
        rowm = es.enter_context(nc.sbuf_tensor("rowm", [1, 128], F32))
        gmx = es.enter_context(nc.sbuf_tensor("gmx", [1, 1], F32))
        epst = es.enter_context(nc.sbuf_tensor("epst", [128, 1], F32))
        ptr = es.enter_context(nc.psum_tensor("ptr", [128, 512], F32))

        p = Prog(es, nc)
        block = p.block
        ld = p.ctr("ld")
        ca = p.ctr("ca")
        cv = p.ctr("cv")
        cp = p.ctr("cp")
        so = p.ctr("so")

        @block.sync
        def _(sync):
            ld.dma(sync.dma_start(out=sx[:], in_=x.rearrange("(t p) d -> p t d", p=128)))
            ld.dma(sync.dma_start(out=ident[:], in_=ident_d))

        @block.scalar
        def _(scalar):
            ld.wait(scalar)
            for i in range(NT):
                ser(scalar, ca, lambda i=i: scalar.activation(
                    sq[:], sx[:, i * DM:(i + 1) * DM], AF.Square, accum_out=sqs[:, i:i + 1]))

        @block.vector
        def _(vector):
            ld.wait(vector)
            ser(vector, cv, lambda: vector.memset(epst[:], EPS_LN))
            for i in range(NT):
                ser(vector, cv, lambda i=i: vector.reduce_sum(
                    sums[:, i:i + 1], sx[:, i * DM:(i + 1) * DM], axis=AX.X))
                ser(vector, cv, lambda i=i: vector.reduce_max(
                    mxs[:, i:i + 1], sx[:, i * DM:(i + 1) * DM], axis=AX.X))
            ca.wait(vector)
            ser(vector, cv, lambda: vector.tensor_scalar(stt[:, 0:NT], sums[:], INVD, None, ALU.mult))
            ser(vector, cv, lambda: vector.tensor_tensor(tmp[:], stt[:, 0:NT], stt[:, 0:NT], ALU.mult))
            ser(vector, cv, lambda: vector.tensor_scalar(sqs[:], sqs[:], INVD, None, ALU.mult))
            ser(vector, cv, lambda: vector.tensor_tensor(tmp[:], sqs[:], tmp[:], ALU.subtract))

        _ln_rstd(block, ca, cv, tmp[:], stt[:, NT:2 * NT], epst[:, 0:1])

        @block.vector
        def _(vector):
            ca.wait(vector)
            ser(vector, cv, lambda: vector.tensor_tensor(tmax[:], mxs[:], stt[:, 0:NT], ALU.subtract))
            ser(vector, cv, lambda: vector.tensor_tensor(tmax[:], tmax[:], stt[:, NT:2 * NT], ALU.mult))
            ser(vector, cv, lambda: vector.reduce_max(rmax[:], tmax[:], axis=AX.X))

        _fold_max(p, ident, ptr, rmax, rowm, gmx, cp, cv)

        @block.sync
        def _(sync):
            cv.wait(sync)
            so.dma(sync.dma_start(out=st_o, in_=stt[:]))
            so.dma(sync.dma_start(out=mx_o, in_=gmx[:]))
            so.wait(sync)
    return nc



def build_m1(nc):
    x = nc.dram_tensor("x", [S, DM], F32, kind="ExternalInput").ap()
    st_i = nc.dram_tensor("st", [128, 2 * NT], F32, kind="ExternalInput").ap()
    gmax = nc.dram_tensor("gmax", [1, 8], F32, kind="ExternalInput").ap()
    w1 = nc.dram_tensor("w1", [FF, DM], F32, kind="ExternalInput").ap()
    b1 = nc.dram_tensor("b1", [32, 128], F32, kind="ExternalInput").ap()
    ident_d = nc.dram_tensor("ident", [128, 128], F32, kind="ExternalInput").ap()
    onesf_d = nc.dram_tensor("onesf", [1, 128], F32, kind="ExternalInput").ap()
    h_o = nc.dram_tensor("H", [NF, 128, S], F32, kind="ExternalOutput").ap()
    mx_o = nc.dram_tensor("mx", [1, 1], F32, kind="ExternalOutput").ap()

    es = contextlib.ExitStack()
    with es:
        sx = es.enter_context(nc.sbuf_tensor("sx_s", [128, NT * DM], F32))
        stt = es.enter_context(nc.sbuf_tensor("stt", [128, 2 * NT], F32))
        sgm = es.enter_context(nc.sbuf_tensor("sgm", [1, 8], F32))
        srow = es.enter_context(nc.sbuf_tensor("srow", [1, 4], F32))
        sb1 = es.enter_context(nc.sbuf_tensor("sb1", [32, 128], F32))
        b1T = es.enter_context(nc.sbuf_tensor("b1T", [128, 32], F32))
        ident = es.enter_context(nc.sbuf_tensor("identS", [128, 128], F32))
        onesf = es.enter_context(nc.sbuf_tensor("onesf_s", [1, 128], F32))
        invs_b = es.enter_context(nc.sbuf_tensor("invs_b", [128, 1], F32))
        s_b = es.enter_context(nc.sbuf_tensor("s_b", [128, 1], F32))
        scl = es.enter_context(nc.sbuf_tensor("scl", [128, NT], F32))
        bia = es.enter_context(nc.sbuf_tensor("bia", [128, NT], F32))
        qsc = es.enter_context(nc.sbuf_tensor("qsc", [128, DM], F32))
        xq = es.enter_context(nc.sbuf_tensor("xq", [128, NT * DM], BF16))
        xqT = es.enter_context(nc.sbuf_tensor("xqT", [128, ND * S], BF16))
        wbuf = es.enter_context(nc.sbuf_tensor("wbuf", [128, DM], F32))
        wsc = es.enter_context(nc.sbuf_tensor("wsc", [128, DM], F32))
        wq = es.enter_context(nc.sbuf_tensor("wq", [128, DM], BF16))
        wqT = es.enter_context(nc.sbuf_tensor("wqT", [128, ND * 128], BF16))
        amx = es.enter_context(nc.sbuf_tensor("amx", [128, 1], F32))
        invw = es.enter_context(nc.sbuf_tensor("invw", [128, 1], F32))
        deq = es.enter_context(nc.sbuf_tensor("deq", [128, 1], F32))
        b1q = es.enter_context(nc.sbuf_tensor("b1q", [128, 1], F32))
        tb = es.enter_context(nc.sbuf_tensor("tb", [128, 1], F32))
        hst = es.enter_context(nc.sbuf_tensor("hst", [128, S], F32))
        rmax = es.enter_context(nc.sbuf_tensor("rmax", [128, 1], F32))
        tmax = es.enter_context(nc.sbuf_tensor("tmax", [128, 1], F32))
        rowm = es.enter_context(nc.sbuf_tensor("rowm", [1, 128], F32))
        gmx = es.enter_context(nc.sbuf_tensor("gmx", [1, 1], F32))
        pmm = [es.enter_context(nc.psum_tensor(f"pmm{j}", [128, 512], F32)) for j in range(4)]

        p = Prog(es, nc)
        block = p.block
        ld = p.ctr("ld")
        cv = p.ctr("cv")
        ca = p.ctr("ca")
        cp = p.ctr("cp")
        cxb = p.ctr("cxb")
        cwd = p.ctr("cwd")
        cmm = p.ctr("cmm")
        chd = p.ctr("chd")
        so = p.ctr("so")

        @block.sync
        def _(sync):
            ld.dma(sync.dma_start(out=sx[:], in_=x.rearrange("(t p) d -> p t d", p=128)))
            ld.dma(sync.dma_start(out=stt[:], in_=st_i))
            ld.dma(sync.dma_start(out=sgm[:], in_=gmax))
            ld.dma(sync.dma_start(out=sb1[:], in_=b1))
            ld.dma(sync.dma_start(out=ident[:], in_=ident_d))
            ld.dma(sync.dma_start(out=onesf[:], in_=onesf_d))

        @block.vector
        def _(vector):
            ld.wait(vector)

        _quant_scalars(p, sgm, srow, onesf, invs_b, s_b, pmm[0], cv, cp)

        @block.tensor
        def _(tensor):
            cv.wait(tensor)
            ld.wait(tensor)
            cp.inc(tensor.transpose(pmm[1][0:128, 0:32], sb1[:], ident[0:32, 0:32]))

        @block.vector
        def _(vector):
            cp.wait(vector)
            ser(vector, cv, lambda: vector.tensor_copy(b1T[:], pmm[1][0:128, 0:32]))
            ser(vector, cv, lambda: vector.tensor_scalar(scl[:], stt[:, NT:2 * NT], invs_b[:, 0:1], None, ALU.mult))
            ser(vector, cv, lambda: vector.tensor_tensor(bia[:], stt[:, 0:NT], scl[:], ALU.mult))
            ser(vector, cv, lambda: vector.tensor_scalar(bia[:], bia[:], -1.0, None, ALU.mult))
            ser(vector, cv, lambda: vector.memset(rmax[:], 0.0))

        p.barrier()

        for i in range(NT):
            @block.scalar
            def _(scalar, i=i):
                cv.wait(scalar)
                ser(scalar, ca, lambda: scalar.activation(
                    qsc[:], sx[:, i * DM:(i + 1) * DM], AF.Identity,
                    bias=bia[:, i:i + 1], scale=scl[:, i:i + 1]))

            @block.vector
            def _(vector, i=i):
                ca.wait(vector)
                _round_to_bf16(vector, cv, qsc, xq[:, i * DM:(i + 1) * DM], lo=0.0, hi=255.0)

            @block.sync
            def _(sync, i=i):
                cv.wait(sync)
                for c in range(0, ND, 2):
                    cxb.dma(sync.dma_start_transpose(
                        xqT[:, c * S + i * 128:c * S + (i + 1) * 128],
                        xq[:, i * DM + c * 128:i * DM + (c + 1) * 128]))

            @block.scalar
            def _(scalar, i=i):
                cv.wait(scalar)
                for c in range(1, ND, 2):
                    cxb.dma(scalar.dma_start_transpose(
                        xqT[:, c * S + i * 128:c * S + (i + 1) * 128],
                        xq[:, i * DM + c * 128:i * DM + (c + 1) * 128]))

        p.barrier()

        for fc in range(NF):
            @block.sync
            def _(sync, fc=fc):
                cwd.dma(sync.dma_start(out=wbuf[:], in_=w1[fc * 128:(fc + 1) * 128, :]))

            @block.vector
            def _(vector, fc=fc):
                cwd.wait(vector)
                ser(vector, cv, lambda: vector.reduce_max(amx[:], wbuf[:], axis=AX.X, apply_absolute_value=True))
                ser(vector, cv, lambda: vector.tensor_scalar(amx[:], amx[:], EPS_Q, None, ALU.max))
                ser(vector, cv, lambda: vector.reciprocal(invw[:], amx[:]))
                ser(vector, cv, lambda: vector.tensor_scalar(invw[:], invw[:], 127.0, None, ALU.mult))
                ser(vector, cv, lambda: vector.tensor_scalar(deq[:], amx[:], INV127, None, ALU.mult))
                ser(vector, cv, lambda: vector.tensor_tensor(deq[:], deq[:], s_b[:], ALU.mult))
                ser(vector, cv, lambda: vector.tensor_scalar(tb[:], b1T[:, fc:fc + 1], invs_b[:, 0:1], None, ALU.mult))
                ser(vector, cv, lambda: vector.tensor_tensor(tb[:], tb[:], invw[:], ALU.mult))
                ser(vector, cv, lambda: vector.tensor_copy(tb.bitcast(I32)[:], tb[:]))
                ser(vector, cv, lambda: vector.tensor_copy(b1q[:], tb.bitcast(I32)[:]))
                ser(vector, cv, lambda: vector.tensor_tensor(b1q[:], b1q[:], deq[:], ALU.mult))
                ser(vector, cv, lambda: vector.tensor_scalar(wsc[:], wbuf[:], invw[:, 0:1], None, ALU.mult))
                _round_to_bf16(vector, cv, wsc, wq[:], lo=-128.0, hi=127.0)

            @block.sync
            def _(sync, fc=fc):
                cv.wait(sync)
                for c in range(0, ND, 2):
                    cxb.dma(sync.dma_start_transpose(
                        wqT[:, c * 128:(c + 1) * 128], wq[:, c * 128:(c + 1) * 128]))

            @block.scalar
            def _(scalar, fc=fc):
                cv.wait(scalar)
                for c in range(1, ND, 2):
                    cxb.dma(scalar.dma_start_transpose(
                        wqT[:, c * 128:(c + 1) * 128], wq[:, c * 128:(c + 1) * 128]))

            @block.tensor
            def _(tensor, fc=fc):
                cxb.wait(tensor)
                chd.wait(tensor)
                for tq in range(4):
                    for d in range(ND):
                        ins = tensor.matmul(pmm[tq][:],
                                            wqT[:, d * 128:(d + 1) * 128],
                                            xqT[:, d * S + tq * 512:d * S + (tq + 1) * 512],
                                            start=(d == 0), stop=(d == ND - 1))
                        if d == ND - 1:
                            cmm.inc(ins)

            @block.scalar
            def _(scalar, fc=fc):
                cmm.wait(scalar)
                for tq in range(4):
                    ser(scalar, ca, lambda tq=tq: scalar.activation(
                        hst[:, tq * 512:(tq + 1) * 512], pmm[tq][:], AF.Relu,
                        bias=b1q[:, 0:1], scale=deq[:, 0:1]))

            @block.vector
            def _(vector, fc=fc):
                ca.wait(vector)
                ser(vector, cv, lambda: vector.reduce_max(tmax[:], hst[:], axis=AX.X))
                ser(vector, cv, lambda: vector.tensor_tensor(rmax[:], rmax[:], tmax[:], ALU.max))

            @block.sync
            def _(sync, fc=fc):
                ca.wait(sync)
                chd.dma(sync.dma_start(out=h_o[fc], in_=hst[:]))

        p.barrier()
        _fold_max(p, ident, pmm[0], rmax, rowm, gmx, cp, cv)

        @block.sync
        def _(sync):
            cv.wait(sync)
            so.dma(sync.dma_start(out=mx_o, in_=gmx[:]))
            chd.wait(sync)
            so.wait(sync)
    return nc



def _emit_m2(nc, es, p, ctx, *, h_in, w2_in, post_tile):
    block = p.block
    cv, ca, cp = ctx["cv"], ctx["ca"], ctx["cp"]
    ident, onesf = ctx["ident"], ctx["onesf"]
    invs_b, s_b = ctx["invs_b"], ctx["s_b"]
    sx = ctx["sx"]
    sb2 = ctx["sb2"]
    rows_d = ctx["rows_d"]
    pmm = ctx["pmm"]

    wbuf = es.enter_context(nc.sbuf_tensor("m2wbuf", [128, FF], F32))
    wsc = es.enter_context(nc.sbuf_tensor("m2wsc", [128, FF], F32))
    wqb = es.enter_context(nc.sbuf_tensor("m2wqb", [128, FF], BF16))
    w2q = es.enter_context(nc.sbuf_tensor("w2q_s", [128, NF * DM], BF16))
    b2T = es.enter_context(nc.sbuf_tensor("b2T", [128, 8], F32))
    sw2a = es.enter_context(nc.sbuf_tensor("sw2a", [128, 8], F32))
    w2inv = es.enter_context(nc.sbuf_tensor("w2inv", [128, 8], F32))
    w2deq = es.enter_context(nc.sbuf_tensor("w2deq", [128, 8], F32))
    b2i = es.enter_context(nc.sbuf_tensor("b2i", [128, 8], F32))
    r8a = es.enter_context(nc.sbuf_tensor("r8a", [8, 128], F32))
    r8b = es.enter_context(nc.sbuf_tensor("r8b", [8, 128], F32))
    rowln = es.enter_context(nc.sbuf_tensor("rowln", [1, DM], F32))
    rowln2 = es.enter_context(nc.sbuf_tensor("rowln2", [1, DM], F32))
    scale_b = es.enter_context(nc.sbuf_tensor("scale_b", [128, DM], F32))
    bias_b = es.enter_context(nc.sbuf_tensor("bias_b", [128, DM], F32))
    hl = wsc
    hq = wqb
    x2t = es.enter_context(nc.sbuf_tensor("x2t", [128, DM], F32))
    t1t = es.enter_context(nc.sbuf_tensor("t1t", [128, DM], F32))

    cwd = p.ctr("m2cwd")
    cwx = p.ctr("m2cwx")
    crw = p.ctr("m2crw")
    chl = p.ctr("m2chl")
    cmm = p.ctr("m2cmm")

    @block.tensor
    def _(tensor):
        cv.wait(tensor)
        ctx["ld"].wait(tensor)
        cp.inc(tensor.transpose(pmm[1][0:128, 0:8], sb2[:], ident[0:8, 0:8]))

    @block.vector
    def _(vector):
        cp.wait(vector)
        ser(vector, cv, lambda: vector.tensor_copy(b2T[:], pmm[1][0:128, 0:8]))

    p.barrier()

    for dc in range(8):
        @block.sync
        def _(sync, dc=dc):
            cwx.wait(sync)
            cwd.dma(sync.dma_start(out=wbuf[:], in_=w2_in[dc * 128:(dc + 1) * 128, :]))

        @block.vector
        def _(vector, dc=dc):
            cwd.wait(vector)
            ser(vector, cv, lambda: vector.reduce_max(sw2a[:, dc:dc + 1], wbuf[:], axis=AX.X, apply_absolute_value=True))
            ser(vector, cv, lambda: vector.tensor_scalar(sw2a[:, dc:dc + 1], sw2a[:, dc:dc + 1], EPS_Q, None, ALU.max))
            ser(vector, cv, lambda: vector.reciprocal(w2inv[:, dc:dc + 1], sw2a[:, dc:dc + 1]))
            ser(vector, cv, lambda: vector.tensor_scalar(w2inv[:, dc:dc + 1], w2inv[:, dc:dc + 1], 127.0, None, ALU.mult))
            ser(vector, cv, lambda: vector.tensor_scalar(wsc[:], wbuf[:], w2inv[:, dc:dc + 1], None, ALU.mult))
            _round_to_bf16(vector, cv, wsc, wqb[:], lo=-128.0, hi=127.0)

        @block.sync
        def _(sync, dc=dc):
            cv.wait(sync)
            for fc in range(0, NF, 2):
                cwx.dma(sync.dma_start_transpose(
                    w2q[:, fc * DM + dc * 128:fc * DM + (dc + 1) * 128],
                    wqb[:, fc * 128:(fc + 1) * 128]))

        @block.scalar
        def _(scalar, dc=dc):
            cv.wait(scalar)
            for fc in range(1, NF, 2):
                cwx.dma(scalar.dma_start_transpose(
                    w2q[:, fc * DM + dc * 128:fc * DM + (dc + 1) * 128],
                    wqb[:, fc * 128:(fc + 1) * 128]))

    p.barrier()

    @block.vector
    def _(vector):
        ser(vector, cv, lambda: vector.tensor_scalar(w2deq[:], sw2a[:], INV127, None, ALU.mult))
        ser(vector, cv, lambda: vector.tensor_scalar(w2deq[:], w2deq[:], s_b[:, 0:1], None, ALU.mult))
        ser(vector, cv, lambda: vector.tensor_scalar(w2deq[:], w2deq[:], 0.5, None, ALU.mult))
        ser(vector, cv, lambda: vector.tensor_scalar(b2i[:], b2T[:], invs_b[:, 0:1], None, ALU.mult))
        ser(vector, cv, lambda: vector.tensor_tensor(b2i[:], b2i[:], w2inv[:], ALU.mult))
        ser(vector, cv, lambda: vector.tensor_copy(b2i.bitcast(I32)[:], b2i[:]))
        ser(vector, cv, lambda: vector.tensor_copy(b2i[:], b2i.bitcast(I32)[:]))
        ser(vector, cv, lambda: vector.tensor_tensor(b2i[:], b2i[:], w2deq[:], ALU.mult))

    @block.tensor
    def _(tensor):
        cv.wait(tensor)
        tensor.transpose(pmm[0][0:8, 0:128], w2deq[:], ident[:])
        cp.inc(tensor.transpose(pmm[1][0:8, 0:128], b2i[:], ident[:]))

    @block.vector
    def _(vector):
        cp.wait(vector)
        ser(vector, cv, lambda: vector.tensor_copy(r8a[:], pmm[0][0:8, 0:128]))
        ser(vector, cv, lambda: vector.tensor_copy(r8b[:], pmm[1][0:8, 0:128]))

    @block.sync
    def _(sync):
        cv.wait(sync)
        crw.dma(sync.dma_start(out=rows_d[0], in_=r8a[:]))
        crw.dma(sync.dma_start(out=rows_d[1], in_=r8b[:]))
        crw.wait(sync)
        crw.dma(sync.dma_start(out=rowln[:], in_=rows_d.rearrange("a b c -> a (b c)")[0:1, :]))
        crw.dma(sync.dma_start(out=rowln2[:], in_=rows_d.rearrange("a b c -> a (b c)")[1:2, :]))

    @block.tensor
    def _(tensor):
        crw.wait(tensor)
        tensor.matmul(pmm[0][:, 0:512], onesf[:], rowln[:, 0:512], start=True, stop=True)
        tensor.matmul(pmm[1][:, 0:512], onesf[:], rowln[:, 512:1024], start=True, stop=True)
        tensor.matmul(pmm[2][:, 0:512], onesf[:], rowln2[:, 0:512], start=True, stop=True)
        cp.inc(tensor.matmul(pmm[3][:, 0:512], onesf[:], rowln2[:, 512:1024], start=True, stop=True))

    @block.vector
    def _(vector):
        cp.wait(vector)
        ser(vector, cv, lambda: vector.tensor_copy(scale_b[:, 0:512], pmm[0][:, 0:512]))
        ser(vector, cv, lambda: vector.tensor_copy(scale_b[:, 512:1024], pmm[1][:, 0:512]))
        ser(vector, cv, lambda: vector.tensor_copy(bias_b[:, 0:512], pmm[2][:, 0:512]))
        ser(vector, cv, lambda: vector.tensor_copy(bias_b[:, 512:1024], pmm[3][:, 0:512]))

    p.barrier()

    for i in range(NT):
        @block.sync
        def _(sync, i=i):
            p.wait_all(sync)
            chl.dma(sync.dma_start(
                out=hl[:], in_=h_in.rearrange("c p t -> p c t")[:, :, i * 128:(i + 1) * 128]))

        @block.scalar
        def _(scalar, i=i):
            chl.wait(scalar)
            ser(scalar, ca, lambda: scalar.activation(hl[:], hl[:], AF.Copy, scale=invs_b[:, 0:1]))

        @block.vector
        def _(vector, i=i):
            ca.wait(vector)
            ser(vector, cv, lambda: vector.tensor_copy(hl.bitcast(I32)[:], hl[:]))
            ser(vector, cv, lambda: vector.tensor_copy(hq[:], hl.bitcast(I32)[:]))

        @block.tensor
        def _(tensor, i=i):
            cv.wait(tensor)
            cwx.wait(tensor)
            for nb in range(2):
                for fc in range(NF):
                    ins = tensor.matmul(pmm[nb][:],
                                        hq[:, fc * 128:(fc + 1) * 128],
                                        w2q[:, fc * DM + nb * 512:fc * DM + (nb + 1) * 512],
                                        start=(fc == 0), stop=(fc == NF - 1))
                    if fc == NF - 1:
                        cmm.inc(ins)

        @block.vector
        def _(vector, i=i):
            p.wait_all(vector)
            ser(vector, cv, lambda: vector.scalar_tensor_tensor(
                t1t[:, 0:512], pmm[0][:], 1.0, scale_b[:, 0:512], ALU.mult, ALU.mult))
            ser(vector, cv, lambda: vector.scalar_tensor_tensor(
                t1t[:, 512:1024], pmm[1][:], 1.0, scale_b[:, 512:1024], ALU.mult, ALU.mult))
            ser(vector, cv, lambda: vector.tensor_tensor(t1t[:], t1t[:], bias_b[:], ALU.add))
            ser(vector, cv, lambda: vector.tensor_tensor(
                x2t[:], t1t[:], sx[:, i * DM:(i + 1) * DM], ALU.add))

        post_tile(i, x2t)



def build_m2f(nc):
    x = nc.dram_tensor("x", [S, DM], F32, kind="ExternalInput").ap()
    h_in = nc.dram_tensor("H", [NF, 128, S], F32, kind="ExternalInput").ap()
    gmax = nc.dram_tensor("gmax", [1, 8], F32, kind="ExternalInput").ap()
    w2 = nc.dram_tensor("w2", [DM, FF], F32, kind="ExternalInput").ap()
    b2 = nc.dram_tensor("b2", [8, 128], F32, kind="ExternalInput").ap()
    ident_d = nc.dram_tensor("ident", [128, 128], F32, kind="ExternalInput").ap()
    onesf_d = nc.dram_tensor("onesf", [1, 128], F32, kind="ExternalInput").ap()
    y_o = nc.dram_tensor("y", [S, DM], F32, kind="ExternalOutput").ap()
    rows_d = nc.dram_tensor("rows_scratch", [2, 8, 128], F32).ap()

    es = contextlib.ExitStack()
    with es:
        sx = es.enter_context(nc.sbuf_tensor("sx_s", [128, NT * DM], F32))
        sgm = es.enter_context(nc.sbuf_tensor("sgm", [1, 8], F32))
        srow = es.enter_context(nc.sbuf_tensor("srow", [1, 4], F32))
        sb2 = es.enter_context(nc.sbuf_tensor("sb2", [8, 128], F32))
        ident = es.enter_context(nc.sbuf_tensor("identS", [128, 128], F32))
        onesf = es.enter_context(nc.sbuf_tensor("onesf_s", [1, 128], F32))
        invs_b = es.enter_context(nc.sbuf_tensor("invs_b", [128, 1], F32))
        s_b = es.enter_context(nc.sbuf_tensor("s_b", [128, 1], F32))
        pmm = [es.enter_context(nc.psum_tensor(f"pmm{j}", [128, 512], F32)) for j in range(4)]

        p = Prog(es, nc)
        block = p.block
        ld = p.ctr("ld")
        cv = p.ctr("cv")
        ca = p.ctr("ca")
        cp = p.ctr("cp")
        so = p.ctr("so")

        @block.sync
        def _(sync):
            ld.dma(sync.dma_start(out=sx[:], in_=x.rearrange("(t p) d -> p t d", p=128)))
            ld.dma(sync.dma_start(out=sgm[:], in_=gmax))
            ld.dma(sync.dma_start(out=sb2[:], in_=b2))
            ld.dma(sync.dma_start(out=ident[:], in_=ident_d))
            ld.dma(sync.dma_start(out=onesf[:], in_=onesf_d))

        @block.vector
        def _(vector):
            ld.wait(vector)

        _quant_scalars(p, sgm, srow, onesf, invs_b, s_b, pmm[0], cv, cp)

        ctx = dict(cv=cv, ca=ca, cp=cp, ld=ld, ident=ident, onesf=onesf,
                   invs_b=invs_b, s_b=s_b, sx=sx, pmm=pmm, rows_d=rows_d,
                   sb2=sb2, srow=srow)

        def post_tile(i, x2_ap):
            @block.sync
            def _(sync, i=i):
                cv.wait(sync)
                so.dma(sync.dma_start(
                    out=y_o.rearrange("(t p) d -> t p d", p=128)[i], in_=x2_ap[:]))

        _emit_m2(nc, es, p, ctx, h_in=h_in, w2_in=w2, post_tile=post_tile)

        @block.sync
        def _(sync):
            so.wait(sync)
    return nc



def build_mid(nc):
    x = nc.dram_tensor("x", [S, DM], F32, kind="ExternalInput").ap()
    h_in = nc.dram_tensor("H", [NF, 128, S], F32, kind="ExternalInput").ap()
    gmax = nc.dram_tensor("gmax", [1, 8], F32, kind="ExternalInput").ap()
    w2 = nc.dram_tensor("w2", [DM, FF], F32, kind="ExternalInput").ap()
    b2 = nc.dram_tensor("b2", [8, 128], F32, kind="ExternalInput").ap()
    wqkv = nc.dram_tensor("wqkv", [DM, 3 * DM], F32, kind="ExternalInput").ap()
    bqkv = nc.dram_tensor("bqkv", [24, 128], F32, kind="ExternalInput").ap()
    bqkv_r = nc.dram_tensor("bqkv_r", [1, 3 * DM], F32, kind="ExternalInput").ap()
    wproj = nc.dram_tensor("wproj", [DM, DM], F32, kind="ExternalInput").ap()
    bproj_r = nc.dram_tensor("bproj_r", [1, DM], F32, kind="ExternalInput").ap()
    g_r = nc.dram_tensor("g_r", [1, DM], F32, kind="ExternalInput").ap()
    cosT = nc.dram_tensor("cosT", [512, S], BF16, kind="ExternalInput").ap()
    sinT = nc.dram_tensor("sinT", [512, S], BF16, kind="ExternalInput").ap()
    ident_d = nc.dram_tensor("ident", [128, 128], F32, kind="ExternalInput").ap()
    onesf_d = nc.dram_tensor("onesf", [1, 128], F32, kind="ExternalInput").ap()
    onesb_d = nc.dram_tensor("onesb", [1, 128], BF16, kind="ExternalInput").ap()
    x2_o = nc.dram_tensor("x2", [S, DM], F32, kind="ExternalOutput").ap()
    st_o = nc.dram_tensor("st3", [128, 2 * NT], F32, kind="ExternalOutput").ap()
    mx_o = nc.dram_tensor("mx", [1, 1], F32, kind="ExternalOutput").ap()
    rows_d = nc.dram_tensor("rows_scratch", [2, 8, 128], F32).ap()
    x1s = nc.dram_tensor("x1_scratch", [S, DM], F32).ap()

    es = contextlib.ExitStack()
    with es:
        sgm = es.enter_context(nc.sbuf_tensor("sgm", [1, 8], F32))
        srow = es.enter_context(nc.sbuf_tensor("srow", [1, 4], F32))
        sb2 = es.enter_context(nc.sbuf_tensor("sb2", [8, 128], F32))
        ident = es.enter_context(nc.sbuf_tensor("identS", [128, 128], F32))
        onesf = es.enter_context(nc.sbuf_tensor("onesf_s", [1, 128], F32))
        onesb = es.enter_context(nc.sbuf_tensor("onesb_s", [1, 128], BF16))
        invs_b = es.enter_context(nc.sbuf_tensor("invs_b", [128, 1], F32))
        s_b = es.enter_context(nc.sbuf_tensor("s_b", [128, 1], F32))
        st2 = es.enter_context(nc.sbuf_tensor("st2_s", [128, 2 * NT], F32))
        st3 = es.enter_context(nc.sbuf_tensor("st3_s", [128, 2 * NT], F32))
        sum2 = es.enter_context(nc.sbuf_tensor("sum2", [128, NT], F32))
        sq2 = es.enter_context(nc.sbuf_tensor("sq2", [128, NT], F32))
        sum3 = es.enter_context(nc.sbuf_tensor("sum3", [128, NT], F32))
        sq3 = es.enter_context(nc.sbuf_tensor("sq3", [128, NT], F32))
        mx3t = es.enter_context(nc.sbuf_tensor("mx3t", [128, NT], F32))
        tmpn = es.enter_context(nc.sbuf_tensor("tmpn", [128, NT], F32))
        gbc = es.enter_context(nc.sbuf_tensor("gbc", [128, DM], F32))
        sqd = es.enter_context(nc.sbuf_tensor("sqd", [128, DM], BF16))
        rmax = es.enter_context(nc.sbuf_tensor("rmax2", [128, 1], F32))
        rowm = es.enter_context(nc.sbuf_tensor("rowm", [1, 128], F32))
        gmx = es.enter_context(nc.sbuf_tensor("gmx", [1, 1], F32))
        epst = es.enter_context(nc.sbuf_tensor("epst", [128, 2], F32))

        p = Prog(es, nc)
        block = p.block
        ld = p.ctr("ld")
        cv = p.ctr("cv")
        ca = p.ctr("ca")
        cp = p.ctr("cp")
        cpl = p.ctr("cpl")
        so = p.ctr("so")
        cx1 = p.ctr("cx1")

        @block.sync
        def _(sync):
            ld.dma(sync.dma_start(out=sgm[:], in_=gmax))
            ld.dma(sync.dma_start(out=sb2[:], in_=b2))
            ld.dma(sync.dma_start(out=ident[:], in_=ident_d))
            ld.dma(sync.dma_start(out=onesf[:], in_=onesf_d))
            ld.dma(sync.dma_start(out=onesb[:], in_=onesb_d))

        @block.vector
        def _(vector):
            ld.wait(vector)
            ser(vector, cv, lambda: vector.memset(epst[:, 0:1], EPS_LN))
            ser(vector, cv, lambda: vector.memset(epst[:, 1:2], EPS_RMS))

        esA = contextlib.ExitStack()
        sx = esA.enter_context(nc.sbuf_tensor("sx_s", [128, NT * DM], F32))
        pmm = [esA.enter_context(nc.psum_tensor(f"pmm{j}", [128, 512], F32)) for j in range(4)]

        @block.sync
        def _(sync):
            ld.dma(sync.dma_start(out=sx[:], in_=x.rearrange("(t p) d -> p t d", p=128)))

        _quant_scalars(p, sgm, srow, onesf, invs_b, s_b, pmm[0], cv, cp)

        ctx = dict(cv=cv, ca=ca, cp=cp, ld=ld, ident=ident, onesf=onesf,
                   invs_b=invs_b, s_b=s_b, sx=sx, pmm=pmm, rows_d=rows_d,
                   sb2=sb2, srow=srow)

        def post_tile(i, x1_ap):
            @block.sync
            def _(sync, i=i):
                cv.wait(sync)
                cx1.dma(sync.dma_start(
                    out=x1s.rearrange("(t p) d -> t p d", p=128)[i], in_=x1_ap[:]))

            @block.vector
            def _(vector, i=i):
                ser(vector, cv, lambda: vector.reduce_sum(sum2[:, i:i + 1], x1_ap[:], axis=AX.X))

            @block.scalar
            def _(scalar, i=i):
                cv.wait(scalar)
                ser(scalar, ca, lambda: scalar.activation(
                    sqd[:], x1_ap[:], AF.Square, accum_out=sq2[:, i:i + 1]))

        _emit_m2(nc, esA, p, ctx, h_in=h_in, w2_in=w2, post_tile=post_tile)
        p.barrier()

        @block.vector
        def _(vector):
            ser(vector, cv, lambda: vector.tensor_scalar(st2[:, 0:NT], sum2[:], INVD, None, ALU.mult))
            ser(vector, cv, lambda: vector.tensor_tensor(tmpn[:], st2[:, 0:NT], st2[:, 0:NT], ALU.mult))
            ser(vector, cv, lambda: vector.tensor_scalar(sq2[:], sq2[:], INVD, None, ALU.mult))
            ser(vector, cv, lambda: vector.tensor_tensor(tmpn[:], sq2[:], tmpn[:], ALU.subtract))

        _ln_rstd(block, ca, cv, tmpn[:], st2[:, NT:2 * NT], epst[:, 0:1])

        @block.vector
        def _(vector):
            ca.wait(vector)
            ser(vector, cv, lambda: vector.tensor_tensor(tmpn[:], st2[:, 0:NT], st2[:, NT:2 * NT], ALU.mult))
            ser(vector, cv, lambda: vector.tensor_scalar(tmpn[:], tmpn[:], -1.0, None, ALU.mult))

        p.barrier()
        esA.close()

        esQ = contextlib.ExitStack()
        qrT = esQ.enter_context(nc.sbuf_tensor("qrT", [128, ND * S], BF16))
        krT = esQ.enter_context(nc.sbuf_tensor("krT", [128, ND * S], BF16))
        vtk = esQ.enter_context(nc.sbuf_tensor("vtk", [128, NT * DM], BF16))
        esB = contextlib.ExitStack()
        wqkv_bf = esB.enter_context(nc.sbuf_tensor("wqkv_bf", [128, ND * 3 * DM], BF16))
        sbqkv = esB.enter_context(nc.sbuf_tensor("sbqkv", [24, 128], F32))
        bqkvT = esB.enter_context(nc.sbuf_tensor("bqkvT", [128, 24], F32))
        bqkv_rb = esB.enter_context(nc.sbuf_tensor("bqkv_rb", [1, 3 * DM], BF16))
        sbq_row = esB.enter_context(nc.sbuf_tensor("sbq_row", [1, 3 * DM], F32))
        g_row = esB.enter_context(nc.sbuf_tensor("g_row", [1, DM], F32))
        x1t = esB.enter_context(nc.sbuf_tensor("x1t", [128, DM], F32))
        u2t = esB.enter_context(nc.sbuf_tensor("u2t", [128, DM], F32))
        msr = esB.enter_context(nc.sbuf_tensor("msr", [128, NT], F32))
        hbt = esB.enter_context(nc.sbuf_tensor("hbt", [128, DM], BF16))
        hTw = esB.enter_context(nc.sbuf_tensor("hTw", [128, ND * 512], BF16))
        cosw = esB.enter_context(nc.sbuf_tensor("cosw", [128, 4 * 512], BF16))
        sinw = esB.enter_context(nc.sbuf_tensor("sinw", [128, 4 * 512], BF16))
        rt1 = esB.enter_context(nc.sbuf_tensor("rt1", [128, 512], F32))
        rt2 = esB.enter_context(nc.sbuf_tensor("rt2", [128, 512], F32))
        pqk = [esB.enter_context(nc.psum_tensor(f"pqk{j}", [128, 512], F32)) for j in range(4)]

        cqkd = p.ctr("cqkd")
        cbq = p.ctr("cbq")
        cxw = p.ctr("cxw")
        chx = p.ctr("chx")
        ctw = p.ctr("ctw")
        cqm = p.ctr("cqm")

        @block.gpsimd
        def _(gpsimd):
            ld.wait(gpsimd)
            cqkd.dma(gpsimd.dma_start(
                out=wqkv_bf[:], in_=wqkv.rearrange("(c p) m -> p c m", p=128)))

        @block.sync
        def _(sync):
            ld.wait(sync)
            cbq.dma(sync.dma_start(out=sbqkv[:], in_=bqkv))
            cbq.dma(sync.dma_start(out=sbq_row[:], in_=bqkv_r))
            cbq.dma(sync.dma_start(out=g_row[:], in_=g_r))

        @block.tensor
        def _(tensor):
            cbq.wait(tensor)
            cv.wait(tensor)
            tensor.transpose(pqk[0][0:128, 0:24], sbqkv[:], ident[0:24, 0:24])
            tensor.matmul(pqk[1][:, 0:512], onesf[:], g_row[:, 0:512], start=True, stop=True)
            cp.inc(tensor.matmul(pqk[2][:, 0:512], onesf[:], g_row[:, 512:1024], start=True, stop=True))

        @block.vector
        def _(vector):
            cp.wait(vector)
            ser(vector, cv, lambda: vector.tensor_copy(bqkvT[:], pqk[0][0:128, 0:24]))
            ser(vector, cv, lambda: vector.tensor_copy(gbc[:, 0:512], pqk[1][:, 0:512]))
            ser(vector, cv, lambda: vector.tensor_copy(gbc[:, 512:1024], pqk[2][:, 0:512]))
            ser(vector, cv, lambda: vector.tensor_copy(bqkv_rb[:], sbq_row[:]))

        p.barrier()

        for tw in range(4):
            @block.sync
            def _(sync, tw=tw):
                cv.wait(sync)
                for c in range(4):
                    ctw.dma(sync.dma_start(out=cosw[:, c * 512:(c + 1) * 512],
                                           in_=cosT[c * 128:(c + 1) * 128, tw * 512:(tw + 1) * 512]))
                    ctw.dma(sync.dma_start(out=sinw[:, c * 512:(c + 1) * 512],
                                           in_=sinT[c * 128:(c + 1) * 128, tw * 512:(tw + 1) * 512]))

            for ti in range(4):
                i = tw * 4 + ti

                @block.sync
                def _(sync, i=i):
                    cv.wait(sync)
                    ca.wait(sync)
                    chx.wait(sync)
                    cxw.dma(sync.dma_start(
                        out=x1t[:], in_=x1s.rearrange("(t p) d -> t p d", p=128)[i]))

                @block.scalar
                def _(scalar, i=i):
                    cxw.wait(scalar)
                    ser(scalar, ca, lambda: scalar.activation(
                        u2t[:], x1t[:], AF.Identity,
                        bias=tmpn[:, i:i + 1], scale=st2[:, NT + i:NT + i + 1]))
                    ser(scalar, ca, lambda: scalar.activation(
                        sqd[:], u2t[:], AF.Square, accum_out=msr[:, i:i + 1]))

                @block.vector
                def _(vector, i=i):
                    ca.wait(vector)
                    ser(vector, cv, lambda: vector.tensor_scalar(
                        msr[:, i:i + 1], msr[:, i:i + 1], INVD, None, ALU.mult))

                @block.scalar
                def _(scalar, i=i):
                    cv.wait(scalar)
                    ser(scalar, ca, lambda: scalar.activation(
                        msr[:, i:i + 1], msr[:, i:i + 1], AF.Ln, bias=epst[:, 1:2]))
                    ser(scalar, ca, lambda: scalar.activation(
                        msr[:, i:i + 1], msr[:, i:i + 1], AF.Exp, scale=-0.5))
                    ser(scalar, ca, lambda: scalar.activation(
                        u2t[:], u2t[:], AF.Copy, scale=msr[:, i:i + 1]))

                @block.vector
                def _(vector, i=i):
                    ca.wait(vector)
                    ser(vector, cv, lambda: vector.tensor_tensor(hbt[:], u2t[:], gbc[:], ALU.mult))

                @block.sync
                def _(sync, i=i, ti=ti):
                    cv.wait(sync)
                    for c in range(0, ND, 2):
                        chx.dma(sync.dma_start_transpose(
                            hTw[:, c * 512 + ti * 128:c * 512 + (ti + 1) * 128],
                            hbt[:, c * 128:(c + 1) * 128]))

                @block.scalar
                def _(scalar, i=i, ti=ti):
                    cv.wait(scalar)
                    for c in range(1, ND, 2):
                        chx.dma(scalar.dma_start_transpose(
                            hTw[:, c * 512 + ti * 128:c * 512 + (ti + 1) * 128],
                            hbt[:, c * 128:(c + 1) * 128]))

            for qk in range(2):
                dst = qrT if qk == 0 else krT
                mbase = qk * 8
                for c in range(4):
                    mA, mB = mbase + c, mbase + c + 4

                    @block.tensor
                    def _(tensor, tw=tw, mA=mA, mB=mB):
                        chx.wait(tensor)
                        cqkd.wait(tensor)
                        ctw.wait(tensor)
                        cv.wait(tensor)
                        for m, bank in ((mA, 0), (mB, 1)):
                            for d in range(ND):
                                ins = tensor.matmul(
                                    pqk[bank][:],
                                    wqkv_bf[:, d * 3 * DM + m * 128:d * 3 * DM + (m + 1) * 128],
                                    hTw[:, d * 512:(d + 1) * 512],
                                    start=(d == 0), stop=(d == ND - 1))
                                if d == ND - 1:
                                    cqm.inc(ins)

                    @block.vector
                    def _(vector, tw=tw, c=c, mA=mA, mB=mB, dst=dst):
                        cqm.wait(vector)
                        colA = c * S + tw * 512
                        colB = (c + 4) * S + tw * 512
                        cw_ = cosw[:, c * 512:(c + 1) * 512]
                        sw_ = sinw[:, c * 512:(c + 1) * 512]
                        ser(vector, cv, lambda: vector.scalar_tensor_tensor(
                            rt1[:], pqk[0][:], bqkvT[:, mA:mA + 1], cw_, ALU.add, ALU.mult))
                        ser(vector, cv, lambda: vector.scalar_tensor_tensor(
                            rt2[:], pqk[1][:], bqkvT[:, mB:mB + 1], sw_, ALU.add, ALU.mult))
                        ser(vector, cv, lambda: vector.tensor_tensor(
                            dst[:, colA:colA + 512], rt1[:], rt2[:], ALU.subtract))
                        ser(vector, cv, lambda: vector.scalar_tensor_tensor(
                            rt1[:], pqk[1][:], bqkvT[:, mB:mB + 1], cw_, ALU.add, ALU.mult))
                        ser(vector, cv, lambda: vector.scalar_tensor_tensor(
                            rt2[:], pqk[0][:], bqkvT[:, mA:mA + 1], sw_, ALU.add, ALU.mult))
                        ser(vector, cv, lambda: vector.tensor_tensor(
                            dst[:, colB:colB + 512], rt1[:], rt2[:], ALU.add))

            for ti in range(4):
                i = tw * 4 + ti

                @block.tensor
                def _(tensor, tw=tw, ti=ti, i=i):
                    chx.wait(tensor)
                    ca.wait(tensor)
                    for e in range(2):
                        bank = 2 + e
                        for d in range(ND):
                            tensor.matmul(
                                pqk[bank][:],
                                hTw[:, d * 512 + ti * 128:d * 512 + (ti + 1) * 128],
                                wqkv_bf[:, d * 3 * DM + 2 * DM + e * 512:d * 3 * DM + 2 * DM + (e + 1) * 512],
                                start=(d == 0), stop=False)
                        ins = tensor.matmul(
                            pqk[bank][:], onesb[:],
                            bqkv_rb[:, 2 * DM + e * 512:2 * DM + (e + 1) * 512],
                            start=False, stop=True)
                        cqm.inc(ins)

                @block.scalar
                def _(scalar, i=i):
                    cqm.wait(scalar)
                    for e in range(2):
                        ser(scalar, ca, lambda e=e: scalar.activation(
                            vtk[:, i * DM + e * 512:i * DM + (e + 1) * 512],
                            pqk[2 + e][:], AF.Copy))

        p.barrier()
        esB.close()

        esC = contextlib.ExitStack()
        scps = esC.enter_context(nc.psum_tensor("scps", [128, S], F32))
        avp = esC.enter_context(nc.psum_tensor("avp", [128, DM], F32))
        pjp = esC.enter_context(nc.psum_tensor("pjp", [128, DM], F32))
        wproj_bf = esC.enter_context(nc.sbuf_tensor("wproj_bf", [128, ND * DM], BF16))
        bproj_rb = esC.enter_context(nc.sbuf_tensor("bproj_rb", [1, DM], BF16))
        sbp_row = esC.enter_context(nc.sbuf_tensor("sbp_row", [1, DM], F32))
        abuf = esC.enter_context(nc.sbuf_tensor("abuf", [128, S], BF16))
        atb = esC.enter_context(nc.sbuf_tensor("atb", [128, S], BF16))
        obf = esC.enter_context(nc.sbuf_tensor("obf", [128, DM], BF16))
        otb = esC.enter_context(nc.sbuf_tensor("otb", [128, DM], BF16))
        mxr = esC.enter_context(nc.sbuf_tensor("mxr", [128, 1], F32))
        mbr = esC.enter_context(nc.sbuf_tensor("mbr", [128, 1], F32))
        rsum = esC.enter_context(nc.sbuf_tensor("rsum", [128, 1], F32))
        rs2 = esC.enter_context(nc.sbuf_tensor("rs2", [128, 1], F32))
        x1c = esC.enter_context(nc.sbuf_tensor("x1c", [128, DM], F32))
        u2rc = esC.enter_context(nc.sbuf_tensor("u2rc", [128, DM], F32))
        x2c = esC.enter_context(nc.sbuf_tensor("x2c", [128, DM], F32))

        cpj = p.ctr("cpj")
        cat = p.ctr("cat")
        cot = p.ctr("cot")
        cwp = p.ctr("cwp")

        @block.gpsimd
        def _(gpsimd):
            ld.wait(gpsimd)
            cwp.dma(gpsimd.dma_start(
                out=wproj_bf[:], in_=wproj.rearrange("(c p) m -> p c m", p=128)))

        @block.sync
        def _(sync):
            ld.wait(sync)
            cwp.dma(sync.dma_start(out=sbp_row[:], in_=bproj_r))

        @block.vector
        def _(vector):
            cwp.wait(vector)
            ser(vector, cv, lambda: vector.tensor_copy(bproj_rb[:], sbp_row[:]))
            ser(vector, cv, lambda: vector.memset(rmax[:], -1e30))

        p.barrier()

        for i in range(NT):
            @block.tensor
            def _(tensor, i=i):
                cv.wait(tensor)
                ca.wait(tensor)
                for n in range(4):
                    for c in range(ND):
                        ins = tensor.matmul(
                            scps[:, n * 512:(n + 1) * 512],
                            qrT[:, c * S + i * 128:c * S + (i + 1) * 128],
                            krT[:, c * S + n * 512:c * S + (n + 1) * 512],
                            start=(c == 0), stop=(c == ND - 1))
                        if c == ND - 1:
                            cpj.inc(ins)

            @block.vector
            def _(vector, i=i):
                cpj.wait(vector)
                ser(vector, cv, lambda: vector.reduce_max(mxr[:], scps[:], axis=AX.X))
                ser(vector, cv, lambda: vector.tensor_scalar(mbr[:], mxr[:], -SCALE_SM, None, ALU.mult))

            @block.scalar
            def _(scalar, i=i):
                cv.wait(scalar)
                ser(scalar, ca, lambda: scalar.activation(
                    abuf[:], scps[:], AF.Exp, bias=mbr[:, 0:1], scale=SCALE_SM,
                    accum_out=rsum[:]))

            @block.vector
            def _(vector, i=i):
                ca.wait(vector)
                ser(vector, cv, lambda: vector.reciprocal(rs2[:], rsum[:]))

            @block.sync
            def _(sync, i=i):
                ca.wait(sync)
                cat.wait(sync)
                for c in range(0, NT, 2):
                    cat.dma(sync.dma_start_transpose(
                        atb[:, c * 128:(c + 1) * 128], abuf[:, c * 128:(c + 1) * 128]))

            @block.scalar
            def _(scalar, i=i):
                ca.wait(scalar)
                for c in range(1, NT, 2):
                    cat.dma(scalar.dma_start_transpose(
                        atb[:, c * 128:(c + 1) * 128], abuf[:, c * 128:(c + 1) * 128]))

            @block.tensor
            def _(tensor, i=i):
                cat.wait(tensor)
                ca.wait(tensor)
                for e in range(2):
                    for c in range(NT):
                        ins = tensor.matmul(
                            avp[:, e * 512:(e + 1) * 512],
                            atb[:, c * 128:(c + 1) * 128],
                            vtk[:, c * DM + e * 512:c * DM + (e + 1) * 512],
                            start=(c == 0), stop=(c == NT - 1))
                        if c == NT - 1:
                            cpj.inc(ins)

            @block.scalar
            def _(scalar, i=i):
                cpj.wait(scalar)
                cv.wait(scalar)
                ser(scalar, ca, lambda: scalar.activation(
                    obf[:], avp[:], AF.Copy, scale=rs2[:, 0:1]))

            @block.sync
            def _(sync, i=i):
                ca.wait(sync)
                cot.wait(sync)
                for e in range(0, ND, 2):
                    cot.dma(sync.dma_start_transpose(
                        otb[:, e * 128:(e + 1) * 128], obf[:, e * 128:(e + 1) * 128]))

            @block.scalar
            def _(scalar, i=i):
                ca.wait(scalar)
                for e in range(1, ND, 2):
                    cot.dma(scalar.dma_start_transpose(
                        otb[:, e * 128:(e + 1) * 128], obf[:, e * 128:(e + 1) * 128]))

            @block.tensor
            def _(tensor, i=i):
                cot.wait(tensor)
                cwp.wait(tensor)
                cv.wait(tensor)
                for n in range(2):
                    for e in range(ND):
                        tensor.matmul(
                            pjp[:, n * 512:(n + 1) * 512],
                            otb[:, e * 128:(e + 1) * 128],
                            wproj_bf[:, e * DM + n * 512:e * DM + (n + 1) * 512],
                            start=(e == 0), stop=False)
                    ins = tensor.matmul(
                        pjp[:, n * 512:(n + 1) * 512], onesb[:],
                        bproj_rb[:, n * 512:(n + 1) * 512], start=False, stop=True)
                    cpj.inc(ins)

            @block.sync
            def _(sync, i=i):
                cv.wait(sync)
                ca.wait(sync)
                cx1.dma(sync.dma_start(
                    out=x1c[:], in_=x1s.rearrange("(t p) d -> t p d", p=128)[i]))

            @block.scalar
            def _(scalar, i=i):
                cx1.wait(scalar)
                ser(scalar, ca, lambda: scalar.activation(
                    u2rc[:], x1c[:], AF.Identity,
                    bias=tmpn[:, i:i + 1], scale=st2[:, NT + i:NT + i + 1]))

            @block.vector
            def _(vector, i=i):
                cpj.wait(vector)
                ca.wait(vector)
                ser(vector, cv, lambda: vector.tensor_tensor(x2c[:], pjp[:], u2rc[:], ALU.add))
                ser(vector, cv, lambda: vector.tensor_tensor(x2c[:], x2c[:], x1c[:], ALU.add))
                ser(vector, cv, lambda: vector.reduce_sum(sum3[:, i:i + 1], x2c[:], axis=AX.X))
                ser(vector, cv, lambda: vector.reduce_max(mx3t[:, i:i + 1], x2c[:], axis=AX.X))

            @block.scalar
            def _(scalar, i=i):
                cv.wait(scalar)
                ser(scalar, ca, lambda: scalar.activation(
                    sqd[:], x2c[:], AF.Square, accum_out=sq3[:, i:i + 1]))

            @block.sync
            def _(sync, i=i):
                cv.wait(sync)
                so.dma(sync.dma_start(
                    out=x2_o.rearrange("(t p) d -> t p d", p=128)[i], in_=x2c[:]))

        p.barrier()
        esC.close()

        @block.vector
        def _(vector):
            ser(vector, cv, lambda: vector.tensor_scalar(st3[:, 0:NT], sum3[:], INVD, None, ALU.mult))
            ser(vector, cv, lambda: vector.tensor_tensor(tmpn[:], st3[:, 0:NT], st3[:, 0:NT], ALU.mult))
            ser(vector, cv, lambda: vector.tensor_scalar(sq3[:], sq3[:], INVD, None, ALU.mult))
            ser(vector, cv, lambda: vector.tensor_tensor(tmpn[:], sq3[:], tmpn[:], ALU.subtract))

        _ln_rstd(block, ca, cv, tmpn[:], st3[:, NT:2 * NT], epst[:, 0:1])

        esD = contextlib.ExitStack()
        pfold = esD.enter_context(nc.psum_tensor("pfold", [128, 512], F32))

        @block.vector
        def _(vector):
            ca.wait(vector)
            ser(vector, cv, lambda: vector.tensor_tensor(mx3t[:], mx3t[:], st3[:, 0:NT], ALU.subtract))
            ser(vector, cv, lambda: vector.tensor_tensor(mx3t[:], mx3t[:], st3[:, NT:2 * NT], ALU.mult))
            ser(vector, cv, lambda: vector.reduce_max(rmax[:], mx3t[:], axis=AX.X))

        _fold_max(p, ident, pfold, rmax, rowm, gmx, cp, cv)

        @block.sync
        def _(sync):
            cv.wait(sync)
            so.dma(sync.dma_start(out=st_o, in_=st3[:]))
            so.dma(sync.dma_start(out=mx_o, in_=gmx[:]))
            so.wait(sync)
        esD.close()
        esQ.close()
    return nc



def _rope_tables():
    half = 512
    inv_freq = 10000.0 ** (-np.arange(half, dtype=np.float64) / half)
    ang = np.arange(S, dtype=np.float64)[:, None] * inv_freq[None, :]
    cos = np.rint(np.cos(ang)).astype(ml_dtypes.bfloat16)
    sin = np.rint(np.sin(ang)).astype(ml_dtypes.bfloat16)
    return np.ascontiguousarray(cos.T), np.ascontiguousarray(sin.T)


_NC_CACHE = {}


def _get_nc(name, builder):
    if name not in _NC_CACHE:
        nc = bass.Bass("TRN2", target_bir_lowering=False)
        _NC_CACHE[name] = builder(nc)
    return _NC_CACHE[name]


def _run(name, builder, in_maps):
    nc = _get_nc(name, builder)
    res = run_bass_kernel_spmd(nc, in_maps, core_ids=list(range(8)))
    return res.results


def kernel(L0_attn_x, L0_attn_g_norm, L0_attn_w_qkv, L0_attn_b_qkv,
           L0_attn_w_proj, L0_attn_b_proj, L0_ffn_w1, L0_ffn_b1,
           L0_ffn_w2, L0_ffn_b2):
    xs = [np.ascontiguousarray(np.asarray(L0_attn_x)[c], dtype=np.float32) for c in range(8)]
    w1 = np.ascontiguousarray(np.asarray(L0_ffn_w1, dtype=np.float32))
    b1 = np.ascontiguousarray(np.asarray(L0_ffn_b1, dtype=np.float32).reshape(32, 128))
    w2 = np.ascontiguousarray(np.asarray(L0_ffn_w2, dtype=np.float32))
    b2 = np.ascontiguousarray(np.asarray(L0_ffn_b2, dtype=np.float32).reshape(8, 128))
    wqkv = np.ascontiguousarray(np.asarray(L0_attn_w_qkv, dtype=np.float32))
    bqkv = np.ascontiguousarray(np.asarray(L0_attn_b_qkv, dtype=np.float32).reshape(24, 128))
    bqkv_r = np.ascontiguousarray(np.asarray(L0_attn_b_qkv, dtype=np.float32).reshape(1, 3 * DM))
    wproj = np.ascontiguousarray(np.asarray(L0_attn_w_proj, dtype=np.float32))
    bproj_r = np.ascontiguousarray(np.asarray(L0_attn_b_proj, dtype=np.float32).reshape(1, DM))
    g_r = np.ascontiguousarray(np.asarray(L0_attn_g_norm, dtype=np.float32).reshape(1, DM))
    cosT, sinT = _rope_tables()
    ident = np.eye(128, dtype=np.float32)
    onesf = np.ones((1, 128), dtype=np.float32)
    onesb = np.ones((1, 128), dtype=ml_dtypes.bfloat16)

    r0 = _run("stats", build_stats, [
        {"x": xs[c], "ident": ident} for c in range(8)])
    gmax1 = np.concatenate([r0[c]["mx"] for c in range(8)], axis=1)

    r1 = _run("m1", build_m1, [
        {"x": xs[c], "st": r0[c]["st"], "gmax": gmax1, "w1": w1, "b1": b1,
         "ident": ident, "onesf": onesf} for c in range(8)])
    gmax2 = np.concatenate([r1[c]["mx"] for c in range(8)], axis=1)

    r2 = _run("mid", build_mid, [
        {"x": xs[c], "H": r1[c]["H"], "gmax": gmax2, "w2": w2, "b2": b2,
         "wqkv": wqkv, "bqkv": bqkv, "bqkv_r": bqkv_r, "wproj": wproj,
         "bproj_r": bproj_r, "g_r": g_r, "cosT": cosT, "sinT": sinT,
         "ident": ident, "onesf": onesf, "onesb": onesb} for c in range(8)])
    gmax3 = np.concatenate([r2[c]["mx"] for c in range(8)], axis=1)

    r3 = _run("m1", build_m1, [
        {"x": r2[c]["x2"], "st": r2[c]["st3"], "gmax": gmax3, "w1": w1, "b1": b1,
         "ident": ident, "onesf": onesf} for c in range(8)])
    gmax4 = np.concatenate([r3[c]["mx"] for c in range(8)], axis=1)

    r4 = _run("m2f", build_m2f, [
        {"x": r2[c]["x2"], "H": r3[c]["H"], "gmax": gmax4, "w2": w2, "b2": b2,
         "ident": ident, "onesf": onesf} for c in range(8)])

    out = np.stack([r4[c]["y"] for c in range(8)], axis=0)
    return out.astype(np.float32)


# version 07
# speedup vs baseline: 1.2367x; 1.2367x over previous
import contextlib

import numpy as np
import ml_dtypes

import concourse.bass as bass
import concourse.mybir as mybir
from concourse.bass_utils import run_bass_kernel_spmd

F32 = mybir.dt.float32
BF16 = mybir.dt.bfloat16
I32 = mybir.dt.int32
AF = mybir.ActivationFunctionType
ALU = mybir.AluOpType
AX = mybir.AxisListType

S = 2048
DM = 1024
FF = 4096
NT = S // 128
ND = DM // 128
NF = FF // 128
EPS_LN = 1e-5
EPS_RMS = 1e-6
EPS_Q = 1e-8
LN3 = 1.0986122886681098
INV255 = float(np.float32(1.0) / np.float32(255.0))
INV127 = float(np.float32(1.0) / np.float32(127.0))
SCALE_SM = float(np.float32(LN3) / np.float32(32.0))
INVD = 1.0 / DM


class Ctr:
    def __init__(self, es, nc, name):
        self.h = es.enter_context(nc.semaphore(name))
        self.n = 0

    def inc(self, instr, k=1):
        instr.then_inc(self.h, k)
        self.n += k
        return instr

    def dma(self, instr):
        return self.inc(instr, 16)

    def wait(self, eng, at=None):
        eng.wait_ge(self.h, self.n if at is None else at)


class Prog:

    def __init__(self, es, nc):
        self.nc = nc
        self.es = es
        self.block = es.enter_context(nc.Block())
        self.ctrs = []

    def ctr(self, name):
        c = Ctr(self.es, self.nc, name)
        self.ctrs.append(c)
        return c

    def wait_all(self, eng):
        for c in self.ctrs:
            if c.n > 0:
                c.wait(eng)

    def barrier(self, engines=("sync", "scalar", "vector", "tensor", "gpsimd")):
        b = self.block
        if "sync" in engines:
            @b.sync
            def _(sync):
                self.wait_all(sync)
        if "scalar" in engines:
            @b.scalar
            def _(scalar):
                self.wait_all(scalar)
        if "vector" in engines:
            @b.vector
            def _(vector):
                self.wait_all(vector)
        if "tensor" in engines:
            @b.tensor
            def _(tensor):
                self.wait_all(tensor)
        if "gpsimd" in engines:
            @b.gpsimd
            def _(gpsimd):
                self.wait_all(gpsimd)


def ser(eng, ctr, emit):
    ctr.wait(eng)
    return ctr.inc(emit())


def _round_to_bf16(vector, cv, scratch_f32, out_bf, lo=None, hi=None):
    if lo is not None:
        ser(vector, cv, lambda: vector.tensor_scalar(
            scratch_f32[:], scratch_f32[:], float(lo), float(hi), ALU.max, ALU.min))
    ser(vector, cv, lambda: vector.tensor_copy(scratch_f32.bitcast(I32)[:], scratch_f32[:]))
    ser(vector, cv, lambda: vector.tensor_copy(out_bf, scratch_f32.bitcast(I32)[:]))


def _ln_rstd(block, ca, cv, var_ap, out_ap, eps_ap):
    @block.scalar
    def _(scalar):
        cv.wait(scalar)
        ser(scalar, ca, lambda: scalar.activation(var_ap, var_ap, AF.Ln, bias=eps_ap))
        ser(scalar, ca, lambda: scalar.activation(out_ap, var_ap, AF.Exp, scale=-0.5))


def _fold_max(p, ident, pbank, rmax, rowm, gmx, cp, cv):
    block = p.block

    @block.tensor
    def _(tensor):
        cv.wait(tensor)
        cp.inc(tensor.transpose(pbank[0:1, 0:128], rmax[:], ident[:]))

    @block.vector
    def _(vector):
        cp.wait(vector)
        ser(vector, cv, lambda: vector.tensor_copy(rowm[:], pbank[0:1, 0:128]))
        ser(vector, cv, lambda: vector.reduce_max(gmx[:], rowm[:], axis=AX.X))


def _quant_scalars(p, sgm, srow, onesf, invs_b, s_b, pbank, cv, cp):
    block = p.block

    @block.vector
    def _(vector):
        ser(vector, cv, lambda: vector.reduce_max(srow[:, 0:1], sgm[:], axis=AX.X))
        ser(vector, cv, lambda: vector.tensor_scalar(srow[:, 0:1], srow[:, 0:1], EPS_Q, None, ALU.max))
        ser(vector, cv, lambda: vector.reciprocal(srow[:, 1:2], srow[:, 0:1]))
        ser(vector, cv, lambda: vector.tensor_scalar(srow[:, 1:2], srow[:, 1:2], 255.0, None, ALU.mult))
        ser(vector, cv, lambda: vector.tensor_scalar(srow[:, 2:3], srow[:, 0:1], INV255, None, ALU.mult))

    @block.tensor
    def _(tensor):
        cv.wait(tensor)
        tensor.matmul(pbank[:, 0:1], onesf[:], srow[:, 1:2], start=True, stop=True)
        cp.inc(tensor.matmul(pbank[:, 1:2], onesf[:], srow[:, 2:3], start=True, stop=True))

    @block.vector
    def _(vector):
        cp.wait(vector)
        ser(vector, cv, lambda: vector.tensor_copy(invs_b[:], pbank[:, 0:1]))
        ser(vector, cv, lambda: vector.tensor_copy(s_b[:], pbank[:, 1:2]))



def build_stats(nc):
    x = nc.dram_tensor("x", [S, DM], F32, kind="ExternalInput").ap()
    ident_d = nc.dram_tensor("ident", [128, 128], F32, kind="ExternalInput").ap()
    st_o = nc.dram_tensor("st", [128, 2 * NT], F32, kind="ExternalOutput").ap()
    mx_o = nc.dram_tensor("mx", [1, 1], F32, kind="ExternalOutput").ap()

    es = contextlib.ExitStack()
    with es:
        sx = es.enter_context(nc.sbuf_tensor("sx_s", [128, NT * DM], F32))
        ident = es.enter_context(nc.sbuf_tensor("identS", [128, 128], F32))
        sq = es.enter_context(nc.sbuf_tensor("sq", [128, DM], BF16))
        sums = es.enter_context(nc.sbuf_tensor("sums", [128, NT], F32))
        sqs = es.enter_context(nc.sbuf_tensor("sqs", [128, NT], F32))
        mxs = es.enter_context(nc.sbuf_tensor("mxs", [128, NT], F32))
        stt = es.enter_context(nc.sbuf_tensor("stt", [128, 2 * NT], F32))
        tmp = es.enter_context(nc.sbuf_tensor("tmp", [128, NT], F32))
        tmax = es.enter_context(nc.sbuf_tensor("tmax", [128, NT], F32))
        rmax = es.enter_context(nc.sbuf_tensor("rmax", [128, 1], F32))
        rowm = es.enter_context(nc.sbuf_tensor("rowm", [1, 128], F32))
        gmx = es.enter_context(nc.sbuf_tensor("gmx", [1, 1], F32))
        epst = es.enter_context(nc.sbuf_tensor("epst", [128, 1], F32))
        ptr = es.enter_context(nc.psum_tensor("ptr", [128, 512], F32))

        p = Prog(es, nc)
        block = p.block
        ld = p.ctr("ld")
        ca = p.ctr("ca")
        cv = p.ctr("cv")
        cp = p.ctr("cp")
        so = p.ctr("so")

        @block.sync
        def _(sync):
            ld.dma(sync.dma_start(out=sx[:], in_=x.rearrange("(t p) d -> p t d", p=128)))
            ld.dma(sync.dma_start(out=ident[:], in_=ident_d))

        @block.scalar
        def _(scalar):
            ld.wait(scalar)
            for i in range(NT):
                ser(scalar, ca, lambda i=i: scalar.activation(
                    sq[:], sx[:, i * DM:(i + 1) * DM], AF.Square, accum_out=sqs[:, i:i + 1]))

        @block.vector
        def _(vector):
            ld.wait(vector)
            ser(vector, cv, lambda: vector.memset(epst[:], EPS_LN))
            for i in range(NT):
                ser(vector, cv, lambda i=i: vector.reduce_sum(
                    sums[:, i:i + 1], sx[:, i * DM:(i + 1) * DM], axis=AX.X))
                ser(vector, cv, lambda i=i: vector.reduce_max(
                    mxs[:, i:i + 1], sx[:, i * DM:(i + 1) * DM], axis=AX.X))
            ca.wait(vector)
            ser(vector, cv, lambda: vector.tensor_scalar(stt[:, 0:NT], sums[:], INVD, None, ALU.mult))
            ser(vector, cv, lambda: vector.tensor_tensor(tmp[:], stt[:, 0:NT], stt[:, 0:NT], ALU.mult))
            ser(vector, cv, lambda: vector.tensor_scalar(sqs[:], sqs[:], INVD, None, ALU.mult))
            ser(vector, cv, lambda: vector.tensor_tensor(tmp[:], sqs[:], tmp[:], ALU.subtract))

        _ln_rstd(block, ca, cv, tmp[:], stt[:, NT:2 * NT], epst[:, 0:1])

        @block.vector
        def _(vector):
            ca.wait(vector)
            ser(vector, cv, lambda: vector.tensor_tensor(tmax[:], mxs[:], stt[:, 0:NT], ALU.subtract))
            ser(vector, cv, lambda: vector.tensor_tensor(tmax[:], tmax[:], stt[:, NT:2 * NT], ALU.mult))
            ser(vector, cv, lambda: vector.reduce_max(rmax[:], tmax[:], axis=AX.X))

        _fold_max(p, ident, ptr, rmax, rowm, gmx, cp, cv)

        @block.sync
        def _(sync):
            cv.wait(sync)
            so.dma(sync.dma_start(out=st_o, in_=stt[:]))
            so.dma(sync.dma_start(out=mx_o, in_=gmx[:]))
            so.wait(sync)
    return nc



def build_m1(nc):
    x = nc.dram_tensor("x", [S, DM], F32, kind="ExternalInput").ap()
    st_i = nc.dram_tensor("st", [128, 2 * NT], F32, kind="ExternalInput").ap()
    gmax = nc.dram_tensor("gmax", [1, 8], F32, kind="ExternalInput").ap()
    w1 = nc.dram_tensor("w1", [FF, DM], F32, kind="ExternalInput").ap()
    b1 = nc.dram_tensor("b1", [32, 128], F32, kind="ExternalInput").ap()
    ident_d = nc.dram_tensor("ident", [128, 128], F32, kind="ExternalInput").ap()
    identb_d = nc.dram_tensor("identb", [128, 128], BF16, kind="ExternalInput").ap()
    onesf_d = nc.dram_tensor("onesf", [1, 128], F32, kind="ExternalInput").ap()
    h_o = nc.dram_tensor("H", [NF, 128, S], F32, kind="ExternalOutput").ap()
    mx_o = nc.dram_tensor("mx", [1, 1], F32, kind="ExternalOutput").ap()

    es = contextlib.ExitStack()
    with es:
        sx = es.enter_context(nc.sbuf_tensor("sx_s", [128, NT * DM], F32))
        stt = es.enter_context(nc.sbuf_tensor("stt", [128, 2 * NT], F32))
        sgm = es.enter_context(nc.sbuf_tensor("sgm", [1, 8], F32))
        srow = es.enter_context(nc.sbuf_tensor("srow", [1, 4], F32))
        sb1 = es.enter_context(nc.sbuf_tensor("sb1", [32, 128], F32))
        b1T = es.enter_context(nc.sbuf_tensor("b1T", [128, 32], F32))
        ident = es.enter_context(nc.sbuf_tensor("identS", [128, 128], F32))
        identb = es.enter_context(nc.sbuf_tensor("identbS", [128, 128], BF16))
        onesf = es.enter_context(nc.sbuf_tensor("onesf_s", [1, 128], F32))
        invs_b = es.enter_context(nc.sbuf_tensor("invs_b", [128, 1], F32))
        s_b = es.enter_context(nc.sbuf_tensor("s_b", [128, 1], F32))
        scl = es.enter_context(nc.sbuf_tensor("scl", [128, NT], F32))
        bia = es.enter_context(nc.sbuf_tensor("bia", [128, NT], F32))
        qsc = es.enter_context(nc.sbuf_tensor("qsc", [128, DM], F32))
        xq = es.enter_context(nc.sbuf_tensor("xq", [128, NT * DM], BF16))
        xqT = es.enter_context(nc.sbuf_tensor("xqT", [128, ND * S], BF16))
        wbuf = [es.enter_context(nc.sbuf_tensor(f"wbuf{k}", [128, DM], F32)) for k in range(2)]
        wsc = [es.enter_context(nc.sbuf_tensor(f"wsc{k}", [128, DM], F32)) for k in range(2)]
        wq = [es.enter_context(nc.sbuf_tensor(f"wq{k}", [128, DM], BF16)) for k in range(2)]
        wqT = [es.enter_context(nc.sbuf_tensor(f"wqT{k}", [128, ND * 128], BF16)) for k in range(2)]
        amx = [es.enter_context(nc.sbuf_tensor(f"amx{k}", [128, 1], F32)) for k in range(2)]
        invw = [es.enter_context(nc.sbuf_tensor(f"invw{k}", [128, 1], F32)) for k in range(2)]
        deq = [es.enter_context(nc.sbuf_tensor(f"deq{k}", [128, 1], F32)) for k in range(2)]
        b1q = [es.enter_context(nc.sbuf_tensor(f"b1q{k}", [128, 1], F32)) for k in range(2)]
        tb = [es.enter_context(nc.sbuf_tensor(f"tb{k}", [128, 1], F32)) for k in range(2)]
        deq_all = es.enter_context(nc.sbuf_tensor("deq_all", [128, NF], F32))
        b1q_all = es.enter_context(nc.sbuf_tensor("b1q_all", [128, NF], F32))
        rawmax = es.enter_context(nc.sbuf_tensor("rawmax", [128, NF], F32))
        pm4 = es.enter_context(nc.sbuf_tensor("pm4", [128, 4], F32))
        hst = [es.enter_context(nc.sbuf_tensor(f"hst{k}", [128, S], F32)) for k in range(2)]
        rmax = es.enter_context(nc.sbuf_tensor("rmax", [128, 1], F32))
        tmax = [es.enter_context(nc.sbuf_tensor(f"tmax{k}", [128, 1], F32)) for k in range(2)]
        rowm = es.enter_context(nc.sbuf_tensor("rowm", [1, 128], F32))
        gmx = es.enter_context(nc.sbuf_tensor("gmx", [1, 1], F32))
        pful = [es.enter_context(nc.psum_tensor(f"pful{j}", [128, 2048], F32)) for j in range(2)]
        pmm = [pful[b // 4][:, (b % 4) * 512:(b % 4 + 1) * 512] for b in range(8)]

        p = Prog(es, nc)
        block = p.block
        ld = p.ctr("ld")
        cv = p.ctr("cv")
        ca = p.ctr("ca")
        cp = p.ctr("cp")
        cxb = p.ctr("cxb")
        cpm = p.ctr("cpm")
        cwt = p.ctr("cwt")
        cwte = p.ctr("cwte")
        cwd = [p.ctr(f"cwd{k}") for k in range(2)]
        cwv = [p.ctr(f"cwv{k}") for k in range(2)]
        cwx = [p.ctr(f"cwx{k}") for k in range(2)]
        cmm = [p.ctr(f"cmm{k}") for k in range(8)]
        cev = [p.ctr(f"cev{k}") for k in range(8)]
        chd = [p.ctr(f"chd{k}") for k in range(2)]
        cmx = [p.ctr(f"cmx{k}") for k in range(2)]
        so = p.ctr("so")

        @block.sync
        def _(sync):
            ld.dma(sync.dma_start(out=sx[:], in_=x.rearrange("(t p) d -> p t d", p=128)))
            ld.dma(sync.dma_start(out=stt[:], in_=st_i))
            ld.dma(sync.dma_start(out=sgm[:], in_=gmax))
            ld.dma(sync.dma_start(out=sb1[:], in_=b1))
            ld.dma(sync.dma_start(out=ident[:], in_=ident_d))
            ld.dma(sync.dma_start(out=identb[:], in_=identb_d))
            ld.dma(sync.dma_start(out=onesf[:], in_=onesf_d))

        @block.vector
        def _(vector):
            ld.wait(vector)

        _quant_scalars(p, sgm, srow, onesf, invs_b, s_b, pmm[0], cv, cp)

        @block.tensor
        def _(tensor):
            cv.wait(tensor)
            ld.wait(tensor)
            cp.inc(tensor.transpose(pmm[1][0:128, 0:32], sb1[:], ident[0:32, 0:32]))

        @block.vector
        def _(vector):
            cp.wait(vector)
            ser(vector, cv, lambda: vector.tensor_copy(b1T[:], pmm[1][0:128, 0:32]))
            ser(vector, cv, lambda: vector.tensor_scalar(scl[:], stt[:, NT:2 * NT], invs_b[:, 0:1], None, ALU.mult))
            ser(vector, cv, lambda: vector.tensor_tensor(bia[:], stt[:, 0:NT], scl[:], ALU.mult))
            ser(vector, cv, lambda: vector.tensor_scalar(bia[:], bia[:], -1.0, None, ALU.mult))
            ser(vector, cv, lambda: vector.memset(rmax[:], 0.0))
            cev[0].inc(vector.engine_nop())
            cev[1].inc(vector.engine_nop())

        p.barrier()

        for i in range(NT):
            @block.scalar
            def _(scalar, i=i):
                cv.wait(scalar)
                ser(scalar, ca, lambda: scalar.activation(
                    qsc[:], sx[:, i * DM:(i + 1) * DM], AF.Identity,
                    bias=bia[:, i:i + 1], scale=scl[:, i:i + 1]))

            @block.vector
            def _(vector, i=i):
                ca.wait(vector)
                _round_to_bf16(vector, cv, qsc, xq[:, i * DM:(i + 1) * DM], lo=0.0, hi=255.0)

            @block.sync
            def _(sync, i=i):
                cv.wait(sync)
                for c in range(0, ND, 2):
                    cxb.dma(sync.dma_start_transpose(
                        xqT[:, c * S + i * 128:c * S + (i + 1) * 128],
                        xq[:, i * DM + c * 128:i * DM + (c + 1) * 128]))

            @block.scalar
            def _(scalar, i=i):
                cv.wait(scalar)
                for c in range(1, ND, 2):
                    cxb.dma(scalar.dma_start_transpose(
                        xqT[:, c * S + i * 128:c * S + (i + 1) * 128],
                        xq[:, i * DM + c * 128:i * DM + (c + 1) * 128]))

        p.barrier()

        XQT_ALL = cxb.n
        CEV_BASE = [1, 1, 0, 0, 0, 0, 0, 0]
        def emit_lqx(fc):
            j = fc % 2
            use = fc // 2
            pg = j * 4

            @block.sync
            def _(sync, fc=fc, j=j, use=use):
                if use > 0:
                    cwv[j].wait(sync, at=use)
                cwd[j].dma(sync.dma_start(out=wbuf[j][:], in_=w1[fc * 128:(fc + 1) * 128, :]))

            @block.vector
            def _(vector, fc=fc, j=j, use=use):
                cwd[j].wait(vector, at=16 * (use + 1))
                if use > 0:
                    cwt.wait(vector, at=fc - 1)
                ser(vector, cv, lambda: vector.reduce_max(amx[j][:], wbuf[j][:], axis=AX.X, apply_absolute_value=True))
                ser(vector, cv, lambda: vector.tensor_scalar(amx[j][:], amx[j][:], EPS_Q, None, ALU.max))
                ser(vector, cv, lambda: vector.reciprocal(invw[j][:], amx[j][:]))
                ser(vector, cv, lambda: vector.tensor_scalar(invw[j][:], invw[j][:], 127.0, None, ALU.mult))
                ser(vector, cv, lambda: vector.tensor_scalar(deq[j][:], amx[j][:], INV127, s_b[:, 0:1], ALU.mult, ALU.mult))
                ser(vector, cv, lambda: vector.tensor_scalar(tb[j][:], b1T[:, fc:fc + 1], invs_b[:, 0:1], invw[j][:, 0:1], ALU.mult, ALU.mult))
                ser(vector, cv, lambda: vector.tensor_copy(tb[j].bitcast(I32)[:], tb[j][:]))
                ser(vector, cv, lambda: vector.tensor_copy(b1q[j][:], tb[j].bitcast(I32)[:]))
                ser(vector, cv, lambda: vector.tensor_tensor(b1q[j][:], b1q[j][:], deq[j][:], ALU.mult))
                ser(vector, cv, lambda: vector.tensor_copy(deq_all[:, fc:fc + 1], deq[j][:]))
                ser(vector, cv, lambda: vector.tensor_copy(b1q_all[:, fc:fc + 1], b1q[j][:]))
                ser(vector, cv, lambda: vector.tensor_scalar(wsc[j][:], wbuf[j][:], invw[j][:, 0:1], None, ALU.mult))
                _round_to_bf16(vector, cv, wsc[j], wq[j][:], lo=-128.0, hi=127.0)
                cwv[j].inc(vector.engine_nop())

            CVQS[fc] = cv.n

        def emit_post(fc):
            j = fc % 2
            use = fc // 2
            pg = j * 4

            @block.vector
            def _(vector, fc=fc, j=j, use=use, pg=pg):
                for tq in range(4):
                    cmm[pg + tq].wait(vector, at=use + 1)
                cv.wait(vector)
                cpm.inc(vector.reduce_max(rawmax[:, fc:fc + 1], pful[j][:], axis=AX.X))

            @block.scalar
            def _(scalar, fc=fc, j=j, use=use, pg=pg):
                cpm.wait(scalar, at=fc + 1)
                if use > 0:
                    chd[j].wait(scalar, at=16 * use)
                for tq in range(4):
                    bank = pg + tq
                    cmm[bank].wait(scalar, at=use + 1)
                    cev[bank].inc(scalar.activation(hst[j][:, tq * 512:(tq + 1) * 512],
                                                    pmm[bank][:], AF.Relu,
                                                    bias=b1q_all[:, fc:fc + 1],
                                                    scale=deq_all[:, fc:fc + 1]))

            @block.sync
            def _(sync, fc=fc, j=j, use=use, pg=pg):
                for tq in range(4):
                    cev[pg + tq].wait(sync, at=CEV_BASE[pg + tq] + use + 1)
                chd[j].dma(sync.dma_start(out=h_o[fc], in_=hst[j][:]))

        def emit_wt(fc):
            j = fc % 2
            use = fc // 2
            pg = j * 4

            @block.tensor
            def _(tensor, fc=fc, j=j, use=use, pg=pg):
                cv.wait(tensor, at=CVQS[fc])
                cev[pg].wait(tensor, at=CEV_BASE[pg] + use)
                pview = pmm[pg].bitcast(BF16)
                for c in range(ND):
                    ins = tensor.transpose(pview[:, c * 128:(c + 1) * 128],
                                           wq[j][:, c * 128:(c + 1) * 128], identb[:])
                    if c == ND - 1:
                        cwt.inc(ins)

            @block.vector
            def _(vector, fc=fc, j=j, pg=pg):
                cwt.wait(vector, at=fc + 1)
                cv.wait(vector)
                cwte.inc(vector.tensor_copy(wqT[j][:], pmm[pg].bitcast(BF16)[:, 0:ND * 128]))

        def emit_mms(fc):
            j = fc % 2
            use = fc // 2
            pg = j * 4

            @block.tensor
            def _(tensor, fc=fc, j=j, use=use, pg=pg):
                cwte.wait(tensor, at=fc + 1)
                cxb.wait(tensor, at=XQT_ALL)
                for tq in range(4):
                    bank = pg + tq
                    cev[bank].wait(tensor, at=CEV_BASE[bank] + use)
                    for d in range(ND):
                        ins = tensor.matmul(pmm[bank][:],
                                            wqT[j][:, d * 128:(d + 1) * 128],
                                            xqT[:, d * S + tq * 512:d * S + (tq + 1) * 512],
                                            start=(d == 0), stop=(d == ND - 1))
                        if d == ND - 1:
                            cmm[bank].inc(ins)

        CVQS = {}
        emit_lqx(0)
        emit_lqx(1)
        emit_wt(0)
        emit_wt(1)
        for k2 in range(0, NF, 2):
            if k2 + 2 < NF:
                emit_lqx(k2 + 2)
                emit_lqx(k2 + 3)
            emit_mms(k2)
            emit_mms(k2 + 1)
            emit_post(k2)
            emit_post(k2 + 1)
            if k2 + 2 < NF:
                emit_wt(k2 + 2)
                emit_wt(k2 + 3)

        @block.vector
        def _(vector):
            ser(vector, cv, lambda: vector.tensor_tensor(rawmax[:], rawmax[:], deq_all[:], ALU.mult))
            ser(vector, cv, lambda: vector.tensor_tensor(rawmax[:], rawmax[:], b1q_all[:], ALU.add))
            ser(vector, cv, lambda: vector.tensor_scalar(rawmax[:], rawmax[:], 0.0, None, ALU.max))
            ser(vector, cv, lambda: vector.reduce_max(rmax[:], rawmax[:], axis=AX.X))

        p.barrier()
        _fold_max(p, ident, pmm[0], rmax, rowm, gmx, cp, cv)

        @block.sync
        def _(sync):
            cv.wait(sync)
            so.dma(sync.dma_start(out=mx_o, in_=gmx[:]))
            chd[0].wait(sync)
            chd[1].wait(sync)
            so.wait(sync)
    return nc



def _emit_m2(nc, es, p, ctx, *, h_in, w2_in, post_tile):
    block = p.block
    cv, ca, cp = ctx["cv"], ctx["ca"], ctx["cp"]
    ident, onesf = ctx["ident"], ctx["onesf"]
    invs_b, s_b = ctx["invs_b"], ctx["s_b"]
    sx = ctx["sx"]
    sb2 = ctx["sb2"]
    rows_d = ctx["rows_d"]
    pmm = ctx["pmm"]

    wbuf = es.enter_context(nc.sbuf_tensor("m2wbuf", [128, FF], F32))
    wsc = es.enter_context(nc.sbuf_tensor("m2wsc", [128, FF], F32))
    wqb = es.enter_context(nc.sbuf_tensor("m2wqb", [128, FF], BF16))
    w2q = es.enter_context(nc.sbuf_tensor("w2q_s", [128, NF * DM], BF16))
    b2T = es.enter_context(nc.sbuf_tensor("b2T", [128, 8], F32))
    sw2a = es.enter_context(nc.sbuf_tensor("sw2a", [128, 8], F32))
    w2inv = es.enter_context(nc.sbuf_tensor("w2inv", [128, 8], F32))
    w2deq = es.enter_context(nc.sbuf_tensor("w2deq", [128, 8], F32))
    b2i = es.enter_context(nc.sbuf_tensor("b2i", [128, 8], F32))
    r8a = es.enter_context(nc.sbuf_tensor("r8a", [8, 128], F32))
    r8b = es.enter_context(nc.sbuf_tensor("r8b", [8, 128], F32))
    rowln = es.enter_context(nc.sbuf_tensor("rowln", [1, DM], F32))
    rowln2 = es.enter_context(nc.sbuf_tensor("rowln2", [1, DM], F32))
    scale_b = es.enter_context(nc.sbuf_tensor("scale_b", [128, DM], F32))
    bias_b = es.enter_context(nc.sbuf_tensor("bias_b", [128, DM], F32))
    hl = wsc
    hq = wqb
    x2t = es.enter_context(nc.sbuf_tensor("x2t", [128, DM], F32))
    t1t = es.enter_context(nc.sbuf_tensor("t1t", [128, DM], F32))
    wsg = [x2t.bitcast(BF16), t1t.bitcast(BF16)]

    cwd = p.ctr("m2cwd")
    cwx = p.ctr("m2cwx")
    crw = p.ctr("m2crw")
    chl = p.ctr("m2chl")
    cmm = p.ctr("m2cmm")

    @block.tensor
    def _(tensor):
        cv.wait(tensor)
        ctx["ld"].wait(tensor)
        cp.inc(tensor.transpose(pmm[1][0:128, 0:8], sb2[:], ident[0:8, 0:8]))

    @block.vector
    def _(vector):
        cp.wait(vector)
        ser(vector, cv, lambda: vector.tensor_copy(b2T[:], pmm[1][0:128, 0:8]))

    p.barrier()

    identb = ctx["identb"]
    cpt = p.ctr("m2cpt")
    CVQ = {}
    CVB = {}
    for dc in range(8):
        @block.sync
        def _(sync, dc=dc):
            if dc > 0:
                cv.wait(sync, at=CVQ[dc - 1])
            cwd.dma(sync.dma_start(out=wbuf[:], in_=w2_in[dc * 128:(dc + 1) * 128, :]))

        @block.vector
        def _(vector, dc=dc):
            cwd.wait(vector)
            if dc > 0:
                cpt.wait(vector, at=dc)
            ser(vector, cv, lambda: vector.reduce_max(sw2a[:, dc:dc + 1], wbuf[:], axis=AX.X, apply_absolute_value=True))
            ser(vector, cv, lambda: vector.tensor_scalar(sw2a[:, dc:dc + 1], sw2a[:, dc:dc + 1], EPS_Q, None, ALU.max))
            ser(vector, cv, lambda: vector.reciprocal(w2inv[:, dc:dc + 1], sw2a[:, dc:dc + 1]))
            ser(vector, cv, lambda: vector.tensor_scalar(w2inv[:, dc:dc + 1], w2inv[:, dc:dc + 1], 127.0, None, ALU.mult))
            ser(vector, cv, lambda: vector.tensor_scalar(wsc[:], wbuf[:], w2inv[:, dc:dc + 1], None, ALU.mult))
            _round_to_bf16(vector, cv, wsc, wqb[:], lo=-128.0, hi=127.0)
            CVQ[dc] = cv.n

        @block.tensor
        def _(tensor, dc=dc):
            cv.wait(tensor, at=CVQ[dc])
            if dc > 0:
                cv.wait(tensor, at=CVB[dc - 1])
            for b in range(4):
                pv = pmm[b].bitcast(BF16)
                for c in range(8):
                    fc = b * 8 + c
                    ins = tensor.transpose(pv[:, c * 128:(c + 1) * 128],
                                           wqb[:, fc * 128:(fc + 1) * 128], identb[:])
            cpt.inc(ins)

        @block.vector
        def _(vector, dc=dc):
            cpt.wait(vector, at=dc + 1)
            if dc > 0:
                cwx.wait(vector, at=512 * dc)
            for b in range(4):
                ser(vector, cv, lambda b=b: vector.tensor_copy(
                    wsg[b // 2][:, (b % 2) * 1024:(b % 2 + 1) * 1024],
                    pmm[b].bitcast(BF16)[:, 0:1024]))
            CVB[dc] = cv.n

        @block.sync
        def _(sync, dc=dc):
            cv.wait(sync, at=CVB[dc])
            for fc in range(0, NF, 2):
                cwx.dma(sync.dma_start(
                    out=w2q[:, fc * DM + dc * 128:fc * DM + (dc + 1) * 128],
                    in_=wsg[fc // 16][:, (fc % 16) * 128:(fc % 16 + 1) * 128]))

        @block.scalar
        def _(scalar, dc=dc):
            cv.wait(scalar, at=CVB[dc])
            for fc in range(1, NF, 2):
                cwx.dma(scalar.dma_start(
                    out=w2q[:, fc * DM + dc * 128:fc * DM + (dc + 1) * 128],
                    in_=wsg[fc // 16][:, (fc % 16) * 128:(fc % 16 + 1) * 128]))

    p.barrier()

    @block.vector
    def _(vector):
        ser(vector, cv, lambda: vector.tensor_scalar(w2deq[:], sw2a[:], INV127, None, ALU.mult))
        ser(vector, cv, lambda: vector.tensor_scalar(w2deq[:], w2deq[:], s_b[:, 0:1], None, ALU.mult))
        ser(vector, cv, lambda: vector.tensor_scalar(w2deq[:], w2deq[:], 0.5, None, ALU.mult))
        ser(vector, cv, lambda: vector.tensor_scalar(b2i[:], b2T[:], invs_b[:, 0:1], None, ALU.mult))
        ser(vector, cv, lambda: vector.tensor_tensor(b2i[:], b2i[:], w2inv[:], ALU.mult))
        ser(vector, cv, lambda: vector.tensor_copy(b2i.bitcast(I32)[:], b2i[:]))
        ser(vector, cv, lambda: vector.tensor_copy(b2i[:], b2i.bitcast(I32)[:]))
        ser(vector, cv, lambda: vector.tensor_tensor(b2i[:], b2i[:], w2deq[:], ALU.mult))

    @block.tensor
    def _(tensor):
        cv.wait(tensor)
        tensor.transpose(pmm[0][0:8, 0:128], w2deq[:], ident[:])
        cp.inc(tensor.transpose(pmm[1][0:8, 0:128], b2i[:], ident[:]))

    @block.vector
    def _(vector):
        cp.wait(vector)
        ser(vector, cv, lambda: vector.tensor_copy(r8a[:], pmm[0][0:8, 0:128]))
        ser(vector, cv, lambda: vector.tensor_copy(r8b[:], pmm[1][0:8, 0:128]))

    @block.sync
    def _(sync):
        cv.wait(sync)
        crw.dma(sync.dma_start(out=rows_d[0], in_=r8a[:]))
        crw.dma(sync.dma_start(out=rows_d[1], in_=r8b[:]))
        crw.wait(sync)
        crw.dma(sync.dma_start(out=rowln[:], in_=rows_d.rearrange("a b c -> a (b c)")[0:1, :]))
        crw.dma(sync.dma_start(out=rowln2[:], in_=rows_d.rearrange("a b c -> a (b c)")[1:2, :]))

    @block.tensor
    def _(tensor):
        crw.wait(tensor)
        tensor.matmul(pmm[0][:, 0:512], onesf[:], rowln[:, 0:512], start=True, stop=True)
        tensor.matmul(pmm[1][:, 0:512], onesf[:], rowln[:, 512:1024], start=True, stop=True)
        tensor.matmul(pmm[2][:, 0:512], onesf[:], rowln2[:, 0:512], start=True, stop=True)
        cp.inc(tensor.matmul(pmm[3][:, 0:512], onesf[:], rowln2[:, 512:1024], start=True, stop=True))

    @block.vector
    def _(vector):
        cp.wait(vector)
        ser(vector, cv, lambda: vector.tensor_copy(scale_b[:, 0:512], pmm[0][:, 0:512]))
        ser(vector, cv, lambda: vector.tensor_copy(scale_b[:, 512:1024], pmm[1][:, 0:512]))
        ser(vector, cv, lambda: vector.tensor_copy(bias_b[:, 0:512], pmm[2][:, 0:512]))
        ser(vector, cv, lambda: vector.tensor_copy(bias_b[:, 512:1024], pmm[3][:, 0:512]))

    p.barrier()

    for i in range(NT):
        @block.sync
        def _(sync, i=i):
            p.wait_all(sync)
            chl.dma(sync.dma_start(
                out=hl[:], in_=h_in.rearrange("c p t -> p c t")[:, :, i * 128:(i + 1) * 128]))

        @block.scalar
        def _(scalar, i=i):
            chl.wait(scalar)
            ser(scalar, ca, lambda: scalar.activation(hl[:], hl[:], AF.Copy, scale=invs_b[:, 0:1]))

        @block.vector
        def _(vector, i=i):
            ca.wait(vector)
            ser(vector, cv, lambda: vector.tensor_copy(hl.bitcast(I32)[:], hl[:]))
            ser(vector, cv, lambda: vector.tensor_copy(hq[:], hl.bitcast(I32)[:]))

        @block.tensor
        def _(tensor, i=i):
            cv.wait(tensor)
            cwx.wait(tensor)
            for nb in range(2):
                for fc in range(NF):
                    ins = tensor.matmul(pmm[nb][:],
                                        hq[:, fc * 128:(fc + 1) * 128],
                                        w2q[:, fc * DM + nb * 512:fc * DM + (nb + 1) * 512],
                                        start=(fc == 0), stop=(fc == NF - 1))
                    if fc == NF - 1:
                        cmm.inc(ins)

        @block.vector
        def _(vector, i=i):
            p.wait_all(vector)
            ser(vector, cv, lambda: vector.scalar_tensor_tensor(
                t1t[:, 0:512], pmm[0][:], 1.0, scale_b[:, 0:512], ALU.mult, ALU.mult))
            ser(vector, cv, lambda: vector.scalar_tensor_tensor(
                t1t[:, 512:1024], pmm[1][:], 1.0, scale_b[:, 512:1024], ALU.mult, ALU.mult))
            ser(vector, cv, lambda: vector.tensor_tensor(t1t[:], t1t[:], bias_b[:], ALU.add))
            ser(vector, cv, lambda: vector.tensor_tensor(
                x2t[:], t1t[:], sx[:, i * DM:(i + 1) * DM], ALU.add))

        post_tile(i, x2t)



def build_m2f(nc):
    x = nc.dram_tensor("x", [S, DM], F32, kind="ExternalInput").ap()
    h_in = nc.dram_tensor("H", [NF, 128, S], F32, kind="ExternalInput").ap()
    gmax = nc.dram_tensor("gmax", [1, 8], F32, kind="ExternalInput").ap()
    w2 = nc.dram_tensor("w2", [DM, FF], F32, kind="ExternalInput").ap()
    b2 = nc.dram_tensor("b2", [8, 128], F32, kind="ExternalInput").ap()
    ident_d = nc.dram_tensor("ident", [128, 128], F32, kind="ExternalInput").ap()
    identb_d = nc.dram_tensor("identb", [128, 128], BF16, kind="ExternalInput").ap()
    onesf_d = nc.dram_tensor("onesf", [1, 128], F32, kind="ExternalInput").ap()
    y_o = nc.dram_tensor("y", [S, DM], F32, kind="ExternalOutput").ap()
    rows_d = nc.dram_tensor("rows_scratch", [2, 8, 128], F32).ap()

    es = contextlib.ExitStack()
    with es:
        sx = es.enter_context(nc.sbuf_tensor("sx_s", [128, NT * DM], F32))
        sgm = es.enter_context(nc.sbuf_tensor("sgm", [1, 8], F32))
        srow = es.enter_context(nc.sbuf_tensor("srow", [1, 4], F32))
        sb2 = es.enter_context(nc.sbuf_tensor("sb2", [8, 128], F32))
        ident = es.enter_context(nc.sbuf_tensor("identS", [128, 128], F32))
        identb = es.enter_context(nc.sbuf_tensor("identbS", [128, 128], BF16))
        onesf = es.enter_context(nc.sbuf_tensor("onesf_s", [1, 128], F32))
        invs_b = es.enter_context(nc.sbuf_tensor("invs_b", [128, 1], F32))
        s_b = es.enter_context(nc.sbuf_tensor("s_b", [128, 1], F32))
        pmm = [es.enter_context(nc.psum_tensor(f"pmm{j}", [128, 512], F32)) for j in range(4)]

        p = Prog(es, nc)
        block = p.block
        ld = p.ctr("ld")
        cv = p.ctr("cv")
        ca = p.ctr("ca")
        cp = p.ctr("cp")
        so = p.ctr("so")

        @block.sync
        def _(sync):
            ld.dma(sync.dma_start(out=sx[:], in_=x.rearrange("(t p) d -> p t d", p=128)))
            ld.dma(sync.dma_start(out=sgm[:], in_=gmax))
            ld.dma(sync.dma_start(out=sb2[:], in_=b2))
            ld.dma(sync.dma_start(out=ident[:], in_=ident_d))
            ld.dma(sync.dma_start(out=identb[:], in_=identb_d))
            ld.dma(sync.dma_start(out=onesf[:], in_=onesf_d))

        @block.vector
        def _(vector):
            ld.wait(vector)

        _quant_scalars(p, sgm, srow, onesf, invs_b, s_b, pmm[0], cv, cp)

        ctx = dict(cv=cv, ca=ca, cp=cp, ld=ld, ident=ident, identb=identb,
                   onesf=onesf, invs_b=invs_b, s_b=s_b, sx=sx, pmm=pmm,
                   rows_d=rows_d, sb2=sb2, srow=srow)

        def post_tile(i, x2_ap):
            @block.sync
            def _(sync, i=i):
                cv.wait(sync)
                so.dma(sync.dma_start(
                    out=y_o.rearrange("(t p) d -> t p d", p=128)[i], in_=x2_ap[:]))

        _emit_m2(nc, es, p, ctx, h_in=h_in, w2_in=w2, post_tile=post_tile)

        @block.sync
        def _(sync):
            so.wait(sync)
    return nc



def build_mid(nc):
    x = nc.dram_tensor("x", [S, DM], F32, kind="ExternalInput").ap()
    h_in = nc.dram_tensor("H", [NF, 128, S], F32, kind="ExternalInput").ap()
    gmax = nc.dram_tensor("gmax", [1, 8], F32, kind="ExternalInput").ap()
    w2 = nc.dram_tensor("w2", [DM, FF], F32, kind="ExternalInput").ap()
    b2 = nc.dram_tensor("b2", [8, 128], F32, kind="ExternalInput").ap()
    wqkv = nc.dram_tensor("wqkv", [DM, 3 * DM], F32, kind="ExternalInput").ap()
    bqkv = nc.dram_tensor("bqkv", [24, 128], F32, kind="ExternalInput").ap()
    bqkv_r = nc.dram_tensor("bqkv_r", [1, 3 * DM], F32, kind="ExternalInput").ap()
    wproj = nc.dram_tensor("wproj", [DM, DM], F32, kind="ExternalInput").ap()
    bproj_r = nc.dram_tensor("bproj_r", [1, DM], F32, kind="ExternalInput").ap()
    g_r = nc.dram_tensor("g_r", [1, DM], F32, kind="ExternalInput").ap()
    cosT = nc.dram_tensor("cosT", [512, S], BF16, kind="ExternalInput").ap()
    sinT = nc.dram_tensor("sinT", [512, S], BF16, kind="ExternalInput").ap()
    ident_d = nc.dram_tensor("ident", [128, 128], F32, kind="ExternalInput").ap()
    identb_d = nc.dram_tensor("identb", [128, 128], BF16, kind="ExternalInput").ap()
    onesf_d = nc.dram_tensor("onesf", [1, 128], F32, kind="ExternalInput").ap()
    onesb_d = nc.dram_tensor("onesb", [1, 128], BF16, kind="ExternalInput").ap()
    x2_o = nc.dram_tensor("x2", [S, DM], F32, kind="ExternalOutput").ap()
    st_o = nc.dram_tensor("st3", [128, 2 * NT], F32, kind="ExternalOutput").ap()
    mx_o = nc.dram_tensor("mx", [1, 1], F32, kind="ExternalOutput").ap()
    rows_d = nc.dram_tensor("rows_scratch", [2, 8, 128], F32).ap()
    x1s = nc.dram_tensor("x1_scratch", [S, DM], F32).ap()

    es = contextlib.ExitStack()
    with es:
        sgm = es.enter_context(nc.sbuf_tensor("sgm", [1, 8], F32))
        srow = es.enter_context(nc.sbuf_tensor("srow", [1, 4], F32))
        sb2 = es.enter_context(nc.sbuf_tensor("sb2", [8, 128], F32))
        ident = es.enter_context(nc.sbuf_tensor("identS", [128, 128], F32))
        identb = es.enter_context(nc.sbuf_tensor("identbS", [128, 128], BF16))
        onesf = es.enter_context(nc.sbuf_tensor("onesf_s", [1, 128], F32))
        onesb = es.enter_context(nc.sbuf_tensor("onesb_s", [1, 128], BF16))
        invs_b = es.enter_context(nc.sbuf_tensor("invs_b", [128, 1], F32))
        s_b = es.enter_context(nc.sbuf_tensor("s_b", [128, 1], F32))
        st2 = es.enter_context(nc.sbuf_tensor("st2_s", [128, 2 * NT], F32))
        st3 = es.enter_context(nc.sbuf_tensor("st3_s", [128, 2 * NT], F32))
        sum2 = es.enter_context(nc.sbuf_tensor("sum2", [128, NT], F32))
        sq2 = es.enter_context(nc.sbuf_tensor("sq2", [128, NT], F32))
        sum3 = es.enter_context(nc.sbuf_tensor("sum3", [128, NT], F32))
        sq3 = es.enter_context(nc.sbuf_tensor("sq3", [128, NT], F32))
        mx3t = es.enter_context(nc.sbuf_tensor("mx3t", [128, NT], F32))
        tmpn = es.enter_context(nc.sbuf_tensor("tmpn", [128, NT], F32))
        gbc = es.enter_context(nc.sbuf_tensor("gbc", [128, DM], F32))
        sqd = es.enter_context(nc.sbuf_tensor("sqd", [128, DM], BF16))
        rmax = es.enter_context(nc.sbuf_tensor("rmax2", [128, 1], F32))
        rowm = es.enter_context(nc.sbuf_tensor("rowm", [1, 128], F32))
        gmx = es.enter_context(nc.sbuf_tensor("gmx", [1, 1], F32))
        epst = es.enter_context(nc.sbuf_tensor("epst", [128, 2], F32))

        p = Prog(es, nc)
        block = p.block
        ld = p.ctr("ld")
        cv = p.ctr("cv")
        ca = p.ctr("ca")
        cp = p.ctr("cp")
        cpl = p.ctr("cpl")
        so = p.ctr("so")
        cx1 = p.ctr("cx1")

        @block.sync
        def _(sync):
            ld.dma(sync.dma_start(out=sgm[:], in_=gmax))
            ld.dma(sync.dma_start(out=sb2[:], in_=b2))
            ld.dma(sync.dma_start(out=ident[:], in_=ident_d))
            ld.dma(sync.dma_start(out=identb[:], in_=identb_d))
            ld.dma(sync.dma_start(out=onesf[:], in_=onesf_d))
            ld.dma(sync.dma_start(out=onesb[:], in_=onesb_d))

        @block.vector
        def _(vector):
            ld.wait(vector)
            ser(vector, cv, lambda: vector.memset(epst[:, 0:1], EPS_LN))
            ser(vector, cv, lambda: vector.memset(epst[:, 1:2], EPS_RMS))

        esA = contextlib.ExitStack()
        sx = esA.enter_context(nc.sbuf_tensor("sx_s", [128, NT * DM], F32))
        pmm = [esA.enter_context(nc.psum_tensor(f"pmm{j}", [128, 512], F32)) for j in range(4)]
        clsx = p.ctr("clsx")

        @block.sync
        def _(sync):
            clsx.dma(sync.dma_start(out=sx[:], in_=x.rearrange("(t p) d -> p t d", p=128)))

        @block.vector
        def _(vector):
            clsx.wait(vector)

        @block.scalar
        def _(scalar):
            clsx.wait(scalar)

        @block.tensor
        def _(tensor):
            clsx.wait(tensor)

        _quant_scalars(p, sgm, srow, onesf, invs_b, s_b, pmm[0], cv, cp)

        ctx = dict(cv=cv, ca=ca, cp=cp, ld=ld, ident=ident, identb=identb,
                   onesf=onesf, invs_b=invs_b, s_b=s_b, sx=sx, pmm=pmm,
                   rows_d=rows_d, sb2=sb2, srow=srow)

        def post_tile(i, x1_ap):
            @block.sync
            def _(sync, i=i):
                cv.wait(sync)
                cx1.dma(sync.dma_start(
                    out=x1s.rearrange("(t p) d -> t p d", p=128)[i], in_=x1_ap[:]))

            @block.vector
            def _(vector, i=i):
                ser(vector, cv, lambda: vector.reduce_sum(sum2[:, i:i + 1], x1_ap[:], axis=AX.X))

            @block.scalar
            def _(scalar, i=i):
                cv.wait(scalar)
                ser(scalar, ca, lambda: scalar.activation(
                    sqd[:], x1_ap[:], AF.Square, accum_out=sq2[:, i:i + 1]))

        _emit_m2(nc, esA, p, ctx, h_in=h_in, w2_in=w2, post_tile=post_tile)
        p.barrier()

        @block.vector
        def _(vector):
            ser(vector, cv, lambda: vector.tensor_scalar(st2[:, 0:NT], sum2[:], INVD, None, ALU.mult))
            ser(vector, cv, lambda: vector.tensor_tensor(tmpn[:], st2[:, 0:NT], st2[:, 0:NT], ALU.mult))
            ser(vector, cv, lambda: vector.tensor_scalar(sq2[:], sq2[:], INVD, None, ALU.mult))
            ser(vector, cv, lambda: vector.tensor_tensor(tmpn[:], sq2[:], tmpn[:], ALU.subtract))

        _ln_rstd(block, ca, cv, tmpn[:], st2[:, NT:2 * NT], epst[:, 0:1])

        @block.vector
        def _(vector):
            ca.wait(vector)
            ser(vector, cv, lambda: vector.tensor_tensor(tmpn[:], st2[:, 0:NT], st2[:, NT:2 * NT], ALU.mult))
            ser(vector, cv, lambda: vector.tensor_scalar(tmpn[:], tmpn[:], -1.0, None, ALU.mult))

        p.barrier()
        esA.close()

        esQ = contextlib.ExitStack()
        qrT = esQ.enter_context(nc.sbuf_tensor("qrT", [128, ND * S], BF16))
        krT = esQ.enter_context(nc.sbuf_tensor("krT", [128, ND * S], BF16))
        vtk = esQ.enter_context(nc.sbuf_tensor("vtk", [128, NT * DM], BF16))
        esB = contextlib.ExitStack()
        wqkv_bf = esB.enter_context(nc.sbuf_tensor("wqkv_bf", [128, ND * 3 * DM], BF16))
        sbqkv = esB.enter_context(nc.sbuf_tensor("sbqkv", [24, 128], F32))
        bqkvT = esB.enter_context(nc.sbuf_tensor("bqkvT", [128, 24], F32))
        bqkv_rb = esB.enter_context(nc.sbuf_tensor("bqkv_rb", [1, 3 * DM], BF16))
        sbq_row = esB.enter_context(nc.sbuf_tensor("sbq_row", [1, 3 * DM], F32))
        g_row = esB.enter_context(nc.sbuf_tensor("g_row", [1, DM], F32))
        x1t = esB.enter_context(nc.sbuf_tensor("x1t", [128, DM], F32))
        u2t = esB.enter_context(nc.sbuf_tensor("u2t", [128, DM], F32))
        msr = esB.enter_context(nc.sbuf_tensor("msr", [128, NT], F32))
        hbt = esB.enter_context(nc.sbuf_tensor("hbt", [128, DM], BF16))
        hTw = esB.enter_context(nc.sbuf_tensor("hTw", [128, ND * 512], BF16))
        cosw = esB.enter_context(nc.sbuf_tensor("cosw", [128, 4 * 512], BF16))
        sinw = esB.enter_context(nc.sbuf_tensor("sinw", [128, 4 * 512], BF16))
        rt1 = esB.enter_context(nc.sbuf_tensor("rt1", [128, 512], F32))
        rt2 = esB.enter_context(nc.sbuf_tensor("rt2", [128, 512], F32))
        pqk = [esB.enter_context(nc.psum_tensor(f"pqk{j}", [128, 512], F32)) for j in range(4)]

        cqkd = p.ctr("cqkd")
        cbq = p.ctr("cbq")
        cxw = p.ctr("cxw")
        cht = p.ctr("cht")
        ctw = p.ctr("ctw")
        cqm = p.ctr("cqm")
        CVH, CVHE, CVROPE, CQMW = {}, {}, {}, {}

        @block.gpsimd
        def _(gpsimd):
            ld.wait(gpsimd)
            cqkd.dma(gpsimd.dma_start(
                out=wqkv_bf[:], in_=wqkv.rearrange("(c p) m -> p c m", p=128)))

        @block.sync
        def _(sync):
            ld.wait(sync)
            cbq.dma(sync.dma_start(out=sbqkv[:], in_=bqkv))
            cbq.dma(sync.dma_start(out=sbq_row[:], in_=bqkv_r))
            cbq.dma(sync.dma_start(out=g_row[:], in_=g_r))

        @block.tensor
        def _(tensor):
            cbq.wait(tensor)
            cv.wait(tensor)
            tensor.transpose(pqk[0][0:128, 0:24], sbqkv[:], ident[0:24, 0:24])
            tensor.matmul(pqk[1][:, 0:512], onesf[:], g_row[:, 0:512], start=True, stop=True)
            cp.inc(tensor.matmul(pqk[2][:, 0:512], onesf[:], g_row[:, 512:1024], start=True, stop=True))

        @block.vector
        def _(vector):
            cp.wait(vector)
            ser(vector, cv, lambda: vector.tensor_copy(bqkvT[:], pqk[0][0:128, 0:24]))
            ser(vector, cv, lambda: vector.tensor_copy(gbc[:, 0:512], pqk[1][:, 0:512]))
            ser(vector, cv, lambda: vector.tensor_copy(gbc[:, 512:1024], pqk[2][:, 0:512]))
            ser(vector, cv, lambda: vector.tensor_copy(bqkv_rb[:], sbq_row[:]))

        p.barrier()

        for tw in range(4):
            @block.sync
            def _(sync, tw=tw):
                cv.wait(sync)
                for c in range(4):
                    ctw.dma(sync.dma_start(out=cosw[:, c * 512:(c + 1) * 512],
                                           in_=cosT[c * 128:(c + 1) * 128, tw * 512:(tw + 1) * 512]))
                    ctw.dma(sync.dma_start(out=sinw[:, c * 512:(c + 1) * 512],
                                           in_=sinT[c * 128:(c + 1) * 128, tw * 512:(tw + 1) * 512]))

            for ti in range(4):
                i = tw * 4 + ti

                @block.sync
                def _(sync, i=i):
                    cv.wait(sync)
                    ca.wait(sync)
                    cxw.dma(sync.dma_start(
                        out=x1t[:], in_=x1s.rearrange("(t p) d -> t p d", p=128)[i]))

                @block.scalar
                def _(scalar, i=i):
                    cxw.wait(scalar)
                    ser(scalar, ca, lambda: scalar.activation(
                        u2t[:], x1t[:], AF.Identity,
                        bias=tmpn[:, i:i + 1], scale=st2[:, NT + i:NT + i + 1]))
                    ser(scalar, ca, lambda: scalar.activation(
                        sqd[:], u2t[:], AF.Square, accum_out=msr[:, i:i + 1]))

                @block.vector
                def _(vector, i=i):
                    ca.wait(vector)
                    ser(vector, cv, lambda: vector.tensor_scalar(
                        msr[:, i:i + 1], msr[:, i:i + 1], INVD, None, ALU.mult))

                @block.scalar
                def _(scalar, i=i):
                    cv.wait(scalar)
                    ser(scalar, ca, lambda: scalar.activation(
                        msr[:, i:i + 1], msr[:, i:i + 1], AF.Ln, bias=epst[:, 1:2]))
                    ser(scalar, ca, lambda: scalar.activation(
                        msr[:, i:i + 1], msr[:, i:i + 1], AF.Exp, scale=-0.5))
                    ser(scalar, ca, lambda: scalar.activation(
                        u2t[:], u2t[:], AF.Copy, scale=msr[:, i:i + 1]))

                @block.vector
                def _(vector, i=i):
                    ca.wait(vector)
                    cht.wait(vector, at=i)
                    ser(vector, cv, lambda: vector.tensor_tensor(hbt[:], u2t[:], gbc[:], ALU.mult))
                CVH[i] = cv.n

                @block.tensor
                def _(tensor, tw=tw, ti=ti, i=i):
                    cv.wait(tensor, at=CVH[i])
                    if tw > 0:
                        cv.wait(tensor, at=CVROPE[tw - 1])
                    if i >= 2:
                        cv.wait(tensor, at=CVHE[i - 2])
                    pv = pqk[ti % 2].bitcast(BF16)
                    for c in range(ND):
                        ins = tensor.transpose(pv[:, c * 128:(c + 1) * 128],
                                               hbt[:, c * 128:(c + 1) * 128], identb[:])
                    cht.inc(ins)

                @block.vector
                def _(vector, tw=tw, ti=ti, i=i):
                    cht.wait(vector, at=i + 1)
                    if tw > 0:
                        cqm.wait(vector, at=CQMW[tw - 1])
                    pv = pqk[ti % 2].bitcast(BF16)
                    for c in range(ND):
                        ser(vector, cv, lambda c=c: vector.tensor_copy(
                            hTw[:, c * 512 + ti * 128:c * 512 + (ti + 1) * 128],
                            pv[:, c * 128:(c + 1) * 128]))
                    CVHE[i] = cv.n

            for qk in range(2):
                dst = qrT if qk == 0 else krT
                mbase = qk * 8
                for c in range(4):
                    mA, mB = mbase + c, mbase + c + 4

                    @block.tensor
                    def _(tensor, tw=tw, mA=mA, mB=mB):
                        cv.wait(tensor, at=CVHE[tw * 4 + 3])
                        cqkd.wait(tensor)
                        ctw.wait(tensor)
                        cv.wait(tensor)
                        for m, bank in ((mA, 0), (mB, 1)):
                            for d in range(ND):
                                ins = tensor.matmul(
                                    pqk[bank][:],
                                    wqkv_bf[:, d * 3 * DM + m * 128:d * 3 * DM + (m + 1) * 128],
                                    hTw[:, d * 512:(d + 1) * 512],
                                    start=(d == 0), stop=(d == ND - 1))
                                if d == ND - 1:
                                    cqm.inc(ins)

                    @block.vector
                    def _(vector, tw=tw, c=c, mA=mA, mB=mB, dst=dst):
                        cqm.wait(vector)
                        colA = c * S + tw * 512
                        colB = (c + 4) * S + tw * 512
                        cw_ = cosw[:, c * 512:(c + 1) * 512]
                        sw_ = sinw[:, c * 512:(c + 1) * 512]
                        ser(vector, cv, lambda: vector.scalar_tensor_tensor(
                            rt1[:], pqk[0][:], bqkvT[:, mA:mA + 1], cw_, ALU.add, ALU.mult))
                        ser(vector, cv, lambda: vector.scalar_tensor_tensor(
                            rt2[:], pqk[1][:], bqkvT[:, mB:mB + 1], sw_, ALU.add, ALU.mult))
                        ser(vector, cv, lambda: vector.tensor_tensor(
                            dst[:, colA:colA + 512], rt1[:], rt2[:], ALU.subtract))
                        ser(vector, cv, lambda: vector.scalar_tensor_tensor(
                            rt1[:], pqk[1][:], bqkvT[:, mB:mB + 1], cw_, ALU.add, ALU.mult))
                        ser(vector, cv, lambda: vector.scalar_tensor_tensor(
                            rt2[:], pqk[0][:], bqkvT[:, mA:mA + 1], sw_, ALU.add, ALU.mult))
                        ser(vector, cv, lambda: vector.tensor_tensor(
                            dst[:, colB:colB + 512], rt1[:], rt2[:], ALU.add))

            for ti in range(4):
                i = tw * 4 + ti

                @block.tensor
                def _(tensor, tw=tw, ti=ti, i=i):
                    ca.wait(tensor)
                    for e in range(2):
                        bank = 2 + e
                        for d in range(ND):
                            tensor.matmul(
                                pqk[bank][:],
                                hTw[:, d * 512 + ti * 128:d * 512 + (ti + 1) * 128],
                                wqkv_bf[:, d * 3 * DM + 2 * DM + e * 512:d * 3 * DM + 2 * DM + (e + 1) * 512],
                                start=(d == 0), stop=False)
                        ins = tensor.matmul(
                            pqk[bank][:], onesb[:],
                            bqkv_rb[:, 2 * DM + e * 512:2 * DM + (e + 1) * 512],
                            start=False, stop=True)
                        cqm.inc(ins)

                @block.scalar
                def _(scalar, i=i):
                    cqm.wait(scalar)
                    for e in range(2):
                        ser(scalar, ca, lambda e=e: scalar.activation(
                            vtk[:, i * DM + e * 512:i * DM + (e + 1) * 512],
                            pqk[2 + e][:], AF.Copy))

            CVROPE[tw] = cv.n
            CQMW[tw] = cqm.n

        p.barrier()
        esB.close()

        esC = contextlib.ExitStack()
        scps = esC.enter_context(nc.psum_tensor("scps", [128, S], F32))
        avp = esC.enter_context(nc.psum_tensor("avp", [128, DM], F32))
        pjp = esC.enter_context(nc.psum_tensor("pjp", [128, DM], F32))
        wproj_bf = esC.enter_context(nc.sbuf_tensor("wproj_bf", [128, ND * DM], BF16))
        bproj_rb = esC.enter_context(nc.sbuf_tensor("bproj_rb", [1, DM], BF16))
        sbp_row = esC.enter_context(nc.sbuf_tensor("sbp_row", [1, DM], F32))
        abuf = esC.enter_context(nc.sbuf_tensor("abuf", [128, S], BF16))
        atb = esC.enter_context(nc.sbuf_tensor("atb", [128, S], BF16))
        obf = esC.enter_context(nc.sbuf_tensor("obf", [128, DM], BF16))
        otb = esC.enter_context(nc.sbuf_tensor("otb", [128, DM], BF16))
        mxr = esC.enter_context(nc.sbuf_tensor("mxr", [128, 1], F32))
        mbr = esC.enter_context(nc.sbuf_tensor("mbr", [128, 1], F32))
        rsum = esC.enter_context(nc.sbuf_tensor("rsum", [128, 1], F32))
        rs2 = esC.enter_context(nc.sbuf_tensor("rs2", [128, 1], F32))
        x1c = esC.enter_context(nc.sbuf_tensor("x1c", [128, DM], F32))
        u2rc = esC.enter_context(nc.sbuf_tensor("u2rc", [128, DM], F32))
        x2c = esC.enter_context(nc.sbuf_tensor("x2c", [128, DM], F32))

        cpj = p.ctr("cpj")
        ctp = p.ctr("ctp")
        cto = p.ctr("cto")
        cwp = p.ctr("cwp")
        cbp = p.ctr("cbp")
        CAB, CVAT, CAO, CVOT = {}, {}, {}, {}

        @block.gpsimd
        def _(gpsimd):
            ld.wait(gpsimd)
            cwp.dma(gpsimd.dma_start(
                out=wproj_bf[:], in_=wproj.rearrange("(c p) m -> p c m", p=128)))

        @block.sync
        def _(sync):
            ld.wait(sync)
            cbp.dma(sync.dma_start(out=sbp_row[:], in_=bproj_r))

        @block.vector
        def _(vector):
            cwp.wait(vector)
            cbp.wait(vector)
            ser(vector, cv, lambda: vector.tensor_copy(bproj_rb[:], sbp_row[:]))
            ser(vector, cv, lambda: vector.memset(rmax[:], -1e30))

        p.barrier()

        for i in range(NT):
            @block.tensor
            def _(tensor, i=i):
                p.wait_all(tensor)
                for n in range(4):
                    for c in range(ND):
                        ins = tensor.matmul(
                            scps[:, n * 512:(n + 1) * 512],
                            qrT[:, c * S + i * 128:c * S + (i + 1) * 128],
                            krT[:, c * S + n * 512:c * S + (n + 1) * 512],
                            start=(c == 0), stop=(c == ND - 1))
                        if c == ND - 1:
                            cpj.inc(ins)

            @block.vector
            def _(vector, i=i):
                cpj.wait(vector)
                ser(vector, cv, lambda: vector.reduce_max(mxr[:], scps[:], axis=AX.X))
                ser(vector, cv, lambda: vector.tensor_scalar(mbr[:], mxr[:], -SCALE_SM, None, ALU.mult))

            @block.scalar
            def _(scalar, i=i):
                cv.wait(scalar)
                ser(scalar, ca, lambda: scalar.activation(
                    abuf[:], scps[:], AF.Exp, bias=mbr[:, 0:1], scale=SCALE_SM,
                    accum_out=rsum[:]))
            CAB[i] = ca.n

            @block.vector
            def _(vector, i=i):
                ca.wait(vector)
                ser(vector, cv, lambda: vector.reciprocal(rs2[:], rsum[:]))

            @block.tensor
            def _(tensor, i=i):
                ca.wait(tensor, at=CAB[i])
                pv = scps.bitcast(BF16)
                for c in range(NT):
                    ins = tensor.transpose(pv[:, c * 128:(c + 1) * 128],
                                           abuf[:, c * 128:(c + 1) * 128], identb[:])
                ctp.inc(ins)

            @block.vector
            def _(vector, i=i):
                ctp.wait(vector, at=i + 1)
                pv = scps.bitcast(BF16)
                ser(vector, cv, lambda: vector.tensor_copy(atb[:, 0:1024], pv[:, 0:1024]))
                ser(vector, cv, lambda: vector.tensor_copy(atb[:, 1024:2048], pv[:, 1024:2048]))
                CVAT[i] = cv.n

            @block.tensor
            def _(tensor, i=i):
                cv.wait(tensor, at=CVAT[i])
                ca.wait(tensor)
                for e in range(2):
                    for c in range(NT):
                        ins = tensor.matmul(
                            avp[:, e * 512:(e + 1) * 512],
                            atb[:, c * 128:(c + 1) * 128],
                            vtk[:, c * DM + e * 512:c * DM + (e + 1) * 512],
                            start=(c == 0), stop=(c == NT - 1))
                        if c == NT - 1:
                            cpj.inc(ins)

            @block.scalar
            def _(scalar, i=i):
                cpj.wait(scalar)
                cv.wait(scalar)
                ser(scalar, ca, lambda: scalar.activation(
                    obf[:], avp[:], AF.Copy, scale=rs2[:, 0:1]))
            CAO[i] = ca.n

            @block.tensor
            def _(tensor, i=i):
                ca.wait(tensor, at=CAO[i])
                pv = avp.bitcast(BF16)
                for e in range(ND):
                    ins = tensor.transpose(pv[:, e * 128:(e + 1) * 128],
                                           obf[:, e * 128:(e + 1) * 128], identb[:])
                cto.inc(ins)

            @block.vector
            def _(vector, i=i):
                cto.wait(vector, at=i + 1)
                ser(vector, cv, lambda: vector.tensor_copy(otb[:], avp.bitcast(BF16)[:, 0:1024]))
                CVOT[i] = cv.n

            @block.tensor
            def _(tensor, i=i):
                cv.wait(tensor, at=CVOT[i])
                cwp.wait(tensor)
                cv.wait(tensor)
                for n in range(2):
                    for e in range(ND):
                        tensor.matmul(
                            pjp[:, n * 512:(n + 1) * 512],
                            otb[:, e * 128:(e + 1) * 128],
                            wproj_bf[:, e * DM + n * 512:e * DM + (n + 1) * 512],
                            start=(e == 0), stop=False)
                    ins = tensor.matmul(
                        pjp[:, n * 512:(n + 1) * 512], onesb[:],
                        bproj_rb[:, n * 512:(n + 1) * 512], start=False, stop=True)
                    cpj.inc(ins)

            @block.sync
            def _(sync, i=i):
                cv.wait(sync)
                ca.wait(sync)
                cx1.dma(sync.dma_start(
                    out=x1c[:], in_=x1s.rearrange("(t p) d -> t p d", p=128)[i]))

            @block.scalar
            def _(scalar, i=i):
                cx1.wait(scalar)
                ser(scalar, ca, lambda: scalar.activation(
                    u2rc[:], x1c[:], AF.Identity,
                    bias=tmpn[:, i:i + 1], scale=st2[:, NT + i:NT + i + 1]))

            @block.vector
            def _(vector, i=i):
                p.wait_all(vector)
                ser(vector, cv, lambda: vector.tensor_tensor(x2c[:], pjp[:], u2rc[:], ALU.add))
                ser(vector, cv, lambda: vector.tensor_tensor(x2c[:], x2c[:], x1c[:], ALU.add))
                ser(vector, cv, lambda: vector.reduce_sum(sum3[:, i:i + 1], x2c[:], axis=AX.X))
                ser(vector, cv, lambda: vector.reduce_max(mx3t[:, i:i + 1], x2c[:], axis=AX.X))

            @block.scalar
            def _(scalar, i=i):
                cv.wait(scalar)
                ser(scalar, ca, lambda: scalar.activation(
                    sqd[:], x2c[:], AF.Square, accum_out=sq3[:, i:i + 1]))

            @block.sync
            def _(sync, i=i):
                cv.wait(sync)
                so.dma(sync.dma_start(
                    out=x2_o.rearrange("(t p) d -> t p d", p=128)[i], in_=x2c[:]))

        p.barrier()
        esC.close()

        @block.vector
        def _(vector):
            ser(vector, cv, lambda: vector.tensor_scalar(st3[:, 0:NT], sum3[:], INVD, None, ALU.mult))
            ser(vector, cv, lambda: vector.tensor_tensor(tmpn[:], st3[:, 0:NT], st3[:, 0:NT], ALU.mult))
            ser(vector, cv, lambda: vector.tensor_scalar(sq3[:], sq3[:], INVD, None, ALU.mult))
            ser(vector, cv, lambda: vector.tensor_tensor(tmpn[:], sq3[:], tmpn[:], ALU.subtract))

        _ln_rstd(block, ca, cv, tmpn[:], st3[:, NT:2 * NT], epst[:, 0:1])

        esD = contextlib.ExitStack()
        pfold = esD.enter_context(nc.psum_tensor("pfold", [128, 512], F32))

        @block.vector
        def _(vector):
            ca.wait(vector)
            ser(vector, cv, lambda: vector.tensor_tensor(mx3t[:], mx3t[:], st3[:, 0:NT], ALU.subtract))
            ser(vector, cv, lambda: vector.tensor_tensor(mx3t[:], mx3t[:], st3[:, NT:2 * NT], ALU.mult))
            ser(vector, cv, lambda: vector.reduce_max(rmax[:], mx3t[:], axis=AX.X))

        _fold_max(p, ident, pfold, rmax, rowm, gmx, cp, cv)

        @block.sync
        def _(sync):
            cv.wait(sync)
            so.dma(sync.dma_start(out=st_o, in_=st3[:]))
            so.dma(sync.dma_start(out=mx_o, in_=gmx[:]))
            so.wait(sync)
        esD.close()
        esQ.close()
    return nc



def _rope_tables():
    half = 512
    inv_freq = 10000.0 ** (-np.arange(half, dtype=np.float64) / half)
    ang = np.arange(S, dtype=np.float64)[:, None] * inv_freq[None, :]
    cos = np.rint(np.cos(ang)).astype(ml_dtypes.bfloat16)
    sin = np.rint(np.sin(ang)).astype(ml_dtypes.bfloat16)
    return np.ascontiguousarray(cos.T), np.ascontiguousarray(sin.T)


_NC_CACHE = {}


def _get_nc(name, builder):
    if name not in _NC_CACHE:
        nc = bass.Bass("TRN2", target_bir_lowering=False)
        _NC_CACHE[name] = builder(nc)
    return _NC_CACHE[name]


def _run(name, builder, in_maps):
    nc = _get_nc(name, builder)
    res = run_bass_kernel_spmd(nc, in_maps, core_ids=list(range(8)))
    return res.results


def kernel(L0_attn_x, L0_attn_g_norm, L0_attn_w_qkv, L0_attn_b_qkv,
           L0_attn_w_proj, L0_attn_b_proj, L0_ffn_w1, L0_ffn_b1,
           L0_ffn_w2, L0_ffn_b2):
    xs = [np.ascontiguousarray(np.asarray(L0_attn_x)[c], dtype=np.float32) for c in range(8)]
    w1 = np.ascontiguousarray(np.asarray(L0_ffn_w1, dtype=np.float32))
    b1 = np.ascontiguousarray(np.asarray(L0_ffn_b1, dtype=np.float32).reshape(32, 128))
    w2 = np.ascontiguousarray(np.asarray(L0_ffn_w2, dtype=np.float32))
    b2 = np.ascontiguousarray(np.asarray(L0_ffn_b2, dtype=np.float32).reshape(8, 128))
    wqkv = np.ascontiguousarray(np.asarray(L0_attn_w_qkv, dtype=np.float32))
    bqkv = np.ascontiguousarray(np.asarray(L0_attn_b_qkv, dtype=np.float32).reshape(24, 128))
    bqkv_r = np.ascontiguousarray(np.asarray(L0_attn_b_qkv, dtype=np.float32).reshape(1, 3 * DM))
    wproj = np.ascontiguousarray(np.asarray(L0_attn_w_proj, dtype=np.float32))
    bproj_r = np.ascontiguousarray(np.asarray(L0_attn_b_proj, dtype=np.float32).reshape(1, DM))
    g_r = np.ascontiguousarray(np.asarray(L0_attn_g_norm, dtype=np.float32).reshape(1, DM))
    cosT, sinT = _rope_tables()
    ident = np.eye(128, dtype=np.float32)
    identb = np.eye(128, dtype=ml_dtypes.bfloat16)
    onesf = np.ones((1, 128), dtype=np.float32)
    onesb = np.ones((1, 128), dtype=ml_dtypes.bfloat16)

    r0 = _run("stats", build_stats, [
        {"x": xs[c], "ident": ident} for c in range(8)])
    gmax1 = np.concatenate([r0[c]["mx"] for c in range(8)], axis=1)

    r1 = _run("m1", build_m1, [
        {"x": xs[c], "st": r0[c]["st"], "gmax": gmax1, "w1": w1, "b1": b1,
         "ident": ident, "identb": identb, "onesf": onesf} for c in range(8)])
    gmax2 = np.concatenate([r1[c]["mx"] for c in range(8)], axis=1)

    r2 = _run("mid", build_mid, [
        {"x": xs[c], "H": r1[c]["H"], "gmax": gmax2, "w2": w2, "b2": b2,
         "wqkv": wqkv, "bqkv": bqkv, "bqkv_r": bqkv_r, "wproj": wproj,
         "bproj_r": bproj_r, "g_r": g_r, "cosT": cosT, "sinT": sinT,
         "ident": ident, "identb": identb, "onesf": onesf, "onesb": onesb} for c in range(8)])
    gmax3 = np.concatenate([r2[c]["mx"] for c in range(8)], axis=1)

    r3 = _run("m1", build_m1, [
        {"x": r2[c]["x2"], "st": r2[c]["st3"], "gmax": gmax3, "w1": w1, "b1": b1,
         "ident": ident, "identb": identb, "onesf": onesf} for c in range(8)])
    gmax4 = np.concatenate([r3[c]["mx"] for c in range(8)], axis=1)

    r4 = _run("m2f", build_m2f, [
        {"x": r2[c]["x2"], "H": r3[c]["H"], "gmax": gmax4, "w2": w2, "b2": b2,
         "ident": ident, "identb": identb, "onesf": onesf} for c in range(8)])

    out = np.stack([r4[c]["y"] for c in range(8)], axis=0)
    return out.astype(np.float32)


# version 08
# speedup vs baseline: 1.4254x; 1.1526x over previous
import contextlib

import numpy as np
import ml_dtypes

import concourse.bass as bass
import concourse.mybir as mybir
from concourse.bass_utils import run_bass_kernel_spmd

F32 = mybir.dt.float32
BF16 = mybir.dt.bfloat16
I32 = mybir.dt.int32
AF = mybir.ActivationFunctionType
ALU = mybir.AluOpType
AX = mybir.AxisListType

S = 2048
DM = 1024
FF = 4096
NT = S // 128
ND = DM // 128
NF = FF // 128
EPS_LN = 1e-5
EPS_RMS = 1e-6
EPS_Q = 1e-8
LN3 = 1.0986122886681098
INV255 = float(np.float32(1.0) / np.float32(255.0))
INV127 = float(np.float32(1.0) / np.float32(127.0))
SCALE_SM = float(np.float32(LN3) / np.float32(32.0))
INVD = 1.0 / DM


class Ctr:
    def __init__(self, es, nc, name):
        self.h = es.enter_context(nc.semaphore(name))
        self.n = 0

    def inc(self, instr, k=1):
        instr.then_inc(self.h, k)
        self.n += k
        return instr

    def dma(self, instr):
        return self.inc(instr, 16)

    def wait(self, eng, at=None):
        eng.wait_ge(self.h, self.n if at is None else at)


class Prog:

    def __init__(self, es, nc):
        self.nc = nc
        self.es = es
        self.block = es.enter_context(nc.Block())
        self.ctrs = []

    def ctr(self, name):
        c = Ctr(self.es, self.nc, name)
        self.ctrs.append(c)
        return c

    def wait_all(self, eng):
        for c in self.ctrs:
            if c.n > 0:
                c.wait(eng)

    def barrier(self, engines=("sync", "scalar", "vector", "tensor", "gpsimd")):
        b = self.block
        if "sync" in engines:
            @b.sync
            def _(sync):
                self.wait_all(sync)
        if "scalar" in engines:
            @b.scalar
            def _(scalar):
                self.wait_all(scalar)
        if "vector" in engines:
            @b.vector
            def _(vector):
                self.wait_all(vector)
        if "tensor" in engines:
            @b.tensor
            def _(tensor):
                self.wait_all(tensor)
        if "gpsimd" in engines:
            @b.gpsimd
            def _(gpsimd):
                self.wait_all(gpsimd)


def ser(eng, ctr, emit):
    ctr.wait(eng)
    return ctr.inc(emit())


def _round_to_bf16(vector, cv, scratch_f32, out_bf, lo=None, hi=None):
    if lo is not None:
        ser(vector, cv, lambda: vector.tensor_scalar(
            scratch_f32[:], scratch_f32[:], float(lo), float(hi), ALU.max, ALU.min))
    ser(vector, cv, lambda: vector.tensor_copy(scratch_f32.bitcast(I32)[:], scratch_f32[:]))
    ser(vector, cv, lambda: vector.tensor_copy(out_bf, scratch_f32.bitcast(I32)[:]))


def _ln_rstd(block, ca, cv, var_ap, out_ap, eps_ap):
    @block.scalar
    def _(scalar):
        cv.wait(scalar)
        ser(scalar, ca, lambda: scalar.activation(var_ap, var_ap, AF.Ln, bias=eps_ap))
        ser(scalar, ca, lambda: scalar.activation(out_ap, var_ap, AF.Exp, scale=-0.5))


def _fold_max(p, ident, pbank, rmax, rowm, gmx, cp, cv):
    block = p.block

    @block.tensor
    def _(tensor):
        cv.wait(tensor)
        cp.inc(tensor.transpose(pbank[0:1, 0:128], rmax[:], ident[:]))

    @block.vector
    def _(vector):
        cp.wait(vector)
        ser(vector, cv, lambda: vector.tensor_copy(rowm[:], pbank[0:1, 0:128]))
        ser(vector, cv, lambda: vector.reduce_max(gmx[:], rowm[:], axis=AX.X))


def _quant_scalars(p, sgm, srow, onesf, invs_b, s_b, pbank, cv, cp):
    block = p.block

    @block.vector
    def _(vector):
        ser(vector, cv, lambda: vector.reduce_max(srow[:, 0:1], sgm[:], axis=AX.X))
        ser(vector, cv, lambda: vector.tensor_scalar(srow[:, 0:1], srow[:, 0:1], EPS_Q, None, ALU.max))
        ser(vector, cv, lambda: vector.reciprocal(srow[:, 1:2], srow[:, 0:1]))
        ser(vector, cv, lambda: vector.tensor_scalar(srow[:, 1:2], srow[:, 1:2], 255.0, None, ALU.mult))
        ser(vector, cv, lambda: vector.tensor_scalar(srow[:, 2:3], srow[:, 0:1], INV255, None, ALU.mult))

    @block.tensor
    def _(tensor):
        cv.wait(tensor)
        tensor.matmul(pbank[:, 0:1], onesf[:], srow[:, 1:2], start=True, stop=True)
        cp.inc(tensor.matmul(pbank[:, 1:2], onesf[:], srow[:, 2:3], start=True, stop=True))

    @block.vector
    def _(vector):
        cp.wait(vector)
        ser(vector, cv, lambda: vector.tensor_copy(invs_b[:], pbank[:, 0:1]))
        ser(vector, cv, lambda: vector.tensor_copy(s_b[:], pbank[:, 1:2]))



def build_stats(nc):
    x = nc.dram_tensor("x", [S, DM], F32, kind="ExternalInput").ap()
    ident_d = nc.dram_tensor("ident", [128, 128], F32, kind="ExternalInput").ap()
    st_o = nc.dram_tensor("st", [128, 2 * NT], F32, kind="ExternalOutput").ap()
    mx_o = nc.dram_tensor("mx", [1, 1], F32, kind="ExternalOutput").ap()

    es = contextlib.ExitStack()
    with es:
        sx = es.enter_context(nc.sbuf_tensor("sx_s", [128, NT * DM], F32))
        ident = es.enter_context(nc.sbuf_tensor("identS", [128, 128], F32))
        sq = es.enter_context(nc.sbuf_tensor("sq", [128, DM], BF16))
        sums = es.enter_context(nc.sbuf_tensor("sums", [128, NT], F32))
        sqs = es.enter_context(nc.sbuf_tensor("sqs", [128, NT], F32))
        mxs = es.enter_context(nc.sbuf_tensor("mxs", [128, NT], F32))
        stt = es.enter_context(nc.sbuf_tensor("stt", [128, 2 * NT], F32))
        tmp = es.enter_context(nc.sbuf_tensor("tmp", [128, NT], F32))
        tmax = es.enter_context(nc.sbuf_tensor("tmax", [128, NT], F32))
        rmax = es.enter_context(nc.sbuf_tensor("rmax", [128, 1], F32))
        rowm = es.enter_context(nc.sbuf_tensor("rowm", [1, 128], F32))
        gmx = es.enter_context(nc.sbuf_tensor("gmx", [1, 1], F32))
        epst = es.enter_context(nc.sbuf_tensor("epst", [128, 1], F32))
        ptr = es.enter_context(nc.psum_tensor("ptr", [128, 512], F32))

        p = Prog(es, nc)
        block = p.block
        ld = p.ctr("ld")
        ca = p.ctr("ca")
        cv = p.ctr("cv")
        cp = p.ctr("cp")
        so = p.ctr("so")

        @block.sync
        def _(sync):
            ld.dma(sync.dma_start(out=sx[:], in_=x.rearrange("(t p) d -> p t d", p=128)))
            ld.dma(sync.dma_start(out=ident[:], in_=ident_d))

        @block.scalar
        def _(scalar):
            ld.wait(scalar)
            for i in range(NT):
                ser(scalar, ca, lambda i=i: scalar.activation(
                    sq[:], sx[:, i * DM:(i + 1) * DM], AF.Square, accum_out=sqs[:, i:i + 1]))

        @block.vector
        def _(vector):
            ld.wait(vector)
            ser(vector, cv, lambda: vector.memset(epst[:], EPS_LN))
            for i in range(NT):
                ser(vector, cv, lambda i=i: vector.reduce_sum(
                    sums[:, i:i + 1], sx[:, i * DM:(i + 1) * DM], axis=AX.X))
                ser(vector, cv, lambda i=i: vector.reduce_max(
                    mxs[:, i:i + 1], sx[:, i * DM:(i + 1) * DM], axis=AX.X))
            ca.wait(vector)
            ser(vector, cv, lambda: vector.tensor_scalar(stt[:, 0:NT], sums[:], INVD, None, ALU.mult))
            ser(vector, cv, lambda: vector.tensor_tensor(tmp[:], stt[:, 0:NT], stt[:, 0:NT], ALU.mult))
            ser(vector, cv, lambda: vector.tensor_scalar(sqs[:], sqs[:], INVD, None, ALU.mult))
            ser(vector, cv, lambda: vector.tensor_tensor(tmp[:], sqs[:], tmp[:], ALU.subtract))

        _ln_rstd(block, ca, cv, tmp[:], stt[:, NT:2 * NT], epst[:, 0:1])

        @block.vector
        def _(vector):
            ca.wait(vector)
            ser(vector, cv, lambda: vector.tensor_tensor(tmax[:], mxs[:], stt[:, 0:NT], ALU.subtract))
            ser(vector, cv, lambda: vector.tensor_tensor(tmax[:], tmax[:], stt[:, NT:2 * NT], ALU.mult))
            ser(vector, cv, lambda: vector.reduce_max(rmax[:], tmax[:], axis=AX.X))

        _fold_max(p, ident, ptr, rmax, rowm, gmx, cp, cv)

        @block.sync
        def _(sync):
            cv.wait(sync)
            so.dma(sync.dma_start(out=st_o, in_=stt[:]))
            so.dma(sync.dma_start(out=mx_o, in_=gmx[:]))
            so.wait(sync)
    return nc



def build_m1(nc):
    x = nc.dram_tensor("x", [S, DM], F32, kind="ExternalInput").ap()
    st_i = nc.dram_tensor("st", [128, 2 * NT], F32, kind="ExternalInput").ap()
    gmax = nc.dram_tensor("gmax", [1, 8], F32, kind="ExternalInput").ap()
    w1 = nc.dram_tensor("w1", [FF, DM], F32, kind="ExternalInput").ap()
    b1 = nc.dram_tensor("b1", [32, 128], F32, kind="ExternalInput").ap()
    ident_d = nc.dram_tensor("ident", [128, 128], F32, kind="ExternalInput").ap()
    identb_d = nc.dram_tensor("identb", [128, 128], BF16, kind="ExternalInput").ap()
    onesf_d = nc.dram_tensor("onesf", [1, 128], F32, kind="ExternalInput").ap()
    h_o = nc.dram_tensor("H", [NF, 128, S], F32, kind="ExternalOutput").ap()
    mx_o = nc.dram_tensor("mx", [1, 1], F32, kind="ExternalOutput").ap()

    es = contextlib.ExitStack()
    with es:
        sx = es.enter_context(nc.sbuf_tensor("sx_s", [128, NT * DM], F32))
        stt = es.enter_context(nc.sbuf_tensor("stt", [128, 2 * NT], F32))
        sgm = es.enter_context(nc.sbuf_tensor("sgm", [1, 8], F32))
        srow = es.enter_context(nc.sbuf_tensor("srow", [1, 4], F32))
        sb1 = es.enter_context(nc.sbuf_tensor("sb1", [32, 128], F32))
        b1T = es.enter_context(nc.sbuf_tensor("b1T", [128, 32], F32))
        ident = es.enter_context(nc.sbuf_tensor("identS", [128, 128], F32))
        identb = es.enter_context(nc.sbuf_tensor("identbS", [128, 128], BF16))
        onesf = es.enter_context(nc.sbuf_tensor("onesf_s", [1, 128], F32))
        invs_b = es.enter_context(nc.sbuf_tensor("invs_b", [128, 1], F32))
        s_b = es.enter_context(nc.sbuf_tensor("s_b", [128, 1], F32))
        scl = es.enter_context(nc.sbuf_tensor("scl", [128, NT], F32))
        bia = es.enter_context(nc.sbuf_tensor("bia", [128, NT], F32))
        qsc = es.enter_context(nc.sbuf_tensor("qsc", [128, DM], F32))
        xq = es.enter_context(nc.sbuf_tensor("xq", [128, NT * DM], BF16))
        xqT = es.enter_context(nc.sbuf_tensor("xqT", [128, ND * S], BF16))
        wbuf = [es.enter_context(nc.sbuf_tensor(f"wbuf{k}", [128, DM], F32)) for k in range(2)]
        wsc = [es.enter_context(nc.sbuf_tensor(f"wsc{k}", [128, DM], F32)) for k in range(2)]
        wq = [es.enter_context(nc.sbuf_tensor(f"wq{k}", [128, DM], BF16)) for k in range(2)]
        wqT = [es.enter_context(nc.sbuf_tensor(f"wqT{k}", [128, ND * 128], BF16)) for k in range(2)]
        amx = [es.enter_context(nc.sbuf_tensor(f"amx{k}", [128, 1], F32)) for k in range(2)]
        invw = [es.enter_context(nc.sbuf_tensor(f"invw{k}", [128, 1], F32)) for k in range(2)]
        deq = [es.enter_context(nc.sbuf_tensor(f"deq{k}", [128, 1], F32)) for k in range(2)]
        b1q = [es.enter_context(nc.sbuf_tensor(f"b1q{k}", [128, 1], F32)) for k in range(2)]
        tb = [es.enter_context(nc.sbuf_tensor(f"tb{k}", [128, 1], F32)) for k in range(2)]
        deq_all = es.enter_context(nc.sbuf_tensor("deq_all", [128, NF], F32))
        b1q_all = es.enter_context(nc.sbuf_tensor("b1q_all", [128, NF], F32))
        rawmax = es.enter_context(nc.sbuf_tensor("rawmax", [128, NF], F32))
        pm4 = es.enter_context(nc.sbuf_tensor("pm4", [128, 4], F32))
        hst = [es.enter_context(nc.sbuf_tensor(f"hst{k}", [128, S], F32)) for k in range(2)]
        rmax = es.enter_context(nc.sbuf_tensor("rmax", [128, 1], F32))
        tmax = [es.enter_context(nc.sbuf_tensor(f"tmax{k}", [128, 1], F32)) for k in range(2)]
        rowm = es.enter_context(nc.sbuf_tensor("rowm", [1, 128], F32))
        gmx = es.enter_context(nc.sbuf_tensor("gmx", [1, 1], F32))
        pful = [es.enter_context(nc.psum_tensor(f"pful{j}", [128, 2048], F32)) for j in range(2)]
        pmm = [pful[b // 4][:, (b % 4) * 512:(b % 4 + 1) * 512] for b in range(8)]

        p = Prog(es, nc)
        block = p.block
        ld = p.ctr("ld")
        cv = p.ctr("cv")
        ca = p.ctr("ca")
        cp = p.ctr("cp")
        cxb = p.ctr("cxb")
        cpm = p.ctr("cpm")
        cwt = p.ctr("cwt")
        cwte = p.ctr("cwte")
        cwd = [p.ctr(f"cwd{k}") for k in range(2)]
        cwv = [p.ctr(f"cwv{k}") for k in range(2)]
        cwx = [p.ctr(f"cwx{k}") for k in range(2)]
        cmm = [p.ctr(f"cmm{k}") for k in range(8)]
        cev = [p.ctr(f"cev{k}") for k in range(8)]
        chd = [p.ctr(f"chd{k}") for k in range(2)]
        cmx = [p.ctr(f"cmx{k}") for k in range(2)]
        so = p.ctr("so")

        @block.sync
        def _(sync):
            ld.dma(sync.dma_start(out=sx[:], in_=x.rearrange("(t p) d -> p t d", p=128)))
            ld.dma(sync.dma_start(out=stt[:], in_=st_i))
            ld.dma(sync.dma_start(out=sgm[:], in_=gmax))
            ld.dma(sync.dma_start(out=sb1[:], in_=b1))
            ld.dma(sync.dma_start(out=ident[:], in_=ident_d))
            ld.dma(sync.dma_start(out=identb[:], in_=identb_d))
            ld.dma(sync.dma_start(out=onesf[:], in_=onesf_d))

        @block.vector
        def _(vector):
            ld.wait(vector)

        _quant_scalars(p, sgm, srow, onesf, invs_b, s_b, pmm[0], cv, cp)

        @block.tensor
        def _(tensor):
            cv.wait(tensor)
            ld.wait(tensor)
            cp.inc(tensor.transpose(pmm[1][0:128, 0:32], sb1[:], ident[0:32, 0:32]))

        @block.vector
        def _(vector):
            cp.wait(vector)
            ser(vector, cv, lambda: vector.tensor_copy(b1T[:], pmm[1][0:128, 0:32]))
            ser(vector, cv, lambda: vector.tensor_scalar(scl[:], stt[:, NT:2 * NT], invs_b[:, 0:1], None, ALU.mult))
            ser(vector, cv, lambda: vector.tensor_tensor(bia[:], stt[:, 0:NT], scl[:], ALU.mult))
            ser(vector, cv, lambda: vector.tensor_scalar(bia[:], bia[:], -1.0, None, ALU.mult))
            ser(vector, cv, lambda: vector.memset(rmax[:], 0.0))
            cev[0].inc(vector.engine_nop())
            cev[1].inc(vector.engine_nop())

        p.barrier()

        for i in range(NT):
            @block.scalar
            def _(scalar, i=i):
                cv.wait(scalar)
                ser(scalar, ca, lambda: scalar.activation(
                    qsc[:], sx[:, i * DM:(i + 1) * DM], AF.Identity,
                    bias=bia[:, i:i + 1], scale=scl[:, i:i + 1]))

            @block.vector
            def _(vector, i=i):
                ca.wait(vector)
                _round_to_bf16(vector, cv, qsc, xq[:, i * DM:(i + 1) * DM], lo=0.0, hi=255.0)

            @block.sync
            def _(sync, i=i):
                cv.wait(sync)
                for c in range(0, ND, 2):
                    cxb.dma(sync.dma_start_transpose(
                        xqT[:, c * S + i * 128:c * S + (i + 1) * 128],
                        xq[:, i * DM + c * 128:i * DM + (c + 1) * 128]))

            @block.scalar
            def _(scalar, i=i):
                cv.wait(scalar)
                for c in range(1, ND, 2):
                    cxb.dma(scalar.dma_start_transpose(
                        xqT[:, c * S + i * 128:c * S + (i + 1) * 128],
                        xq[:, i * DM + c * 128:i * DM + (c + 1) * 128]))

        p.barrier()

        XQT_ALL = cxb.n
        CEV_BASE = [1, 1, 0, 0, 0, 0, 0, 0]
        def emit_lqx(fc):
            j = fc % 2
            use = fc // 2
            pg = j * 4

            @block.sync
            def _(sync, fc=fc, j=j, use=use):
                if use > 0:
                    cwv[j].wait(sync, at=use)
                cwd[j].dma(sync.dma_start(out=wbuf[j][:], in_=w1[fc * 128:(fc + 1) * 128, :]))

            @block.vector
            def _(vector, fc=fc, j=j, use=use):
                cwd[j].wait(vector, at=16 * (use + 1))
                if use > 0:
                    cwt.wait(vector, at=fc - 1)
                ser(vector, cv, lambda: vector.reduce_max(amx[j][:], wbuf[j][:], axis=AX.X, apply_absolute_value=True))
                ser(vector, cv, lambda: vector.tensor_scalar(amx[j][:], amx[j][:], EPS_Q, None, ALU.max))
                ser(vector, cv, lambda: vector.reciprocal(invw[j][:], amx[j][:]))
                ser(vector, cv, lambda: vector.tensor_scalar(invw[j][:], invw[j][:], 127.0, None, ALU.mult))
                ser(vector, cv, lambda: vector.tensor_scalar(deq[j][:], amx[j][:], INV127, s_b[:, 0:1], ALU.mult, ALU.mult))
                ser(vector, cv, lambda: vector.tensor_scalar(tb[j][:], b1T[:, fc:fc + 1], invs_b[:, 0:1], invw[j][:, 0:1], ALU.mult, ALU.mult))
                ser(vector, cv, lambda: vector.tensor_copy(tb[j].bitcast(I32)[:], tb[j][:]))
                ser(vector, cv, lambda: vector.tensor_copy(b1q[j][:], tb[j].bitcast(I32)[:]))
                ser(vector, cv, lambda: vector.tensor_tensor(b1q[j][:], b1q[j][:], deq[j][:], ALU.mult))
                ser(vector, cv, lambda: vector.tensor_copy(deq_all[:, fc:fc + 1], deq[j][:]))
                ser(vector, cv, lambda: vector.tensor_copy(b1q_all[:, fc:fc + 1], b1q[j][:]))
                ser(vector, cv, lambda: vector.tensor_scalar(wsc[j][:], wbuf[j][:], invw[j][:, 0:1], None, ALU.mult))
                _round_to_bf16(vector, cv, wsc[j], wq[j][:], lo=-128.0, hi=127.0)
                cwv[j].inc(vector.engine_nop())

            CVQS[fc] = cv.n

        def emit_post(fc):
            j = fc % 2
            use = fc // 2
            pg = j * 4

            @block.vector
            def _(vector, fc=fc, j=j, use=use, pg=pg):
                for tq in range(4):
                    cmm[pg + tq].wait(vector, at=use + 1)
                cv.wait(vector)
                cpm.inc(vector.reduce_max(rawmax[:, fc:fc + 1], pful[j][:], axis=AX.X))

            @block.scalar
            def _(scalar, fc=fc, j=j, use=use, pg=pg):
                cpm.wait(scalar, at=fc + 1)
                if use > 0:
                    chd[j].wait(scalar, at=16 * use)
                for tq in range(4):
                    bank = pg + tq
                    cmm[bank].wait(scalar, at=use + 1)
                    cev[bank].inc(scalar.activation(hst[j][:, tq * 512:(tq + 1) * 512],
                                                    pmm[bank][:], AF.Relu,
                                                    bias=b1q_all[:, fc:fc + 1],
                                                    scale=deq_all[:, fc:fc + 1]))

            @block.sync
            def _(sync, fc=fc, j=j, use=use, pg=pg):
                for tq in range(4):
                    cev[pg + tq].wait(sync, at=CEV_BASE[pg + tq] + use + 1)
                chd[j].dma(sync.dma_start(out=h_o[fc], in_=hst[j][:]))

        def emit_wt(fc):
            j = fc % 2
            use = fc // 2
            pg = j * 4

            @block.tensor
            def _(tensor, fc=fc, j=j, use=use, pg=pg):
                cv.wait(tensor, at=CVQS[fc])
                cev[pg].wait(tensor, at=CEV_BASE[pg] + use)
                pview = pmm[pg].bitcast(BF16)
                for c in range(ND):
                    ins = tensor.transpose(pview[:, c * 128:(c + 1) * 128],
                                           wq[j][:, c * 128:(c + 1) * 128], identb[:])
                    if c == ND - 1:
                        cwt.inc(ins)

            @block.vector
            def _(vector, fc=fc, j=j, pg=pg):
                cwt.wait(vector, at=fc + 1)
                cv.wait(vector)
                cwte.inc(vector.tensor_copy(wqT[j][:], pmm[pg].bitcast(BF16)[:, 0:ND * 128]))

        def emit_mms(fc):
            j = fc % 2
            use = fc // 2
            pg = j * 4

            @block.tensor
            def _(tensor, fc=fc, j=j, use=use, pg=pg):
                cwte.wait(tensor, at=fc + 1)
                cxb.wait(tensor, at=XQT_ALL)
                for tq in range(4):
                    bank = pg + tq
                    cev[bank].wait(tensor, at=CEV_BASE[bank] + use)
                    for d in range(ND):
                        ins = tensor.matmul(pmm[bank][:],
                                            wqT[j][:, d * 128:(d + 1) * 128],
                                            xqT[:, d * S + tq * 512:d * S + (tq + 1) * 512],
                                            start=(d == 0), stop=(d == ND - 1))
                        if d == ND - 1:
                            cmm[bank].inc(ins)

        CVQS = {}
        emit_lqx(0)
        emit_lqx(1)
        emit_wt(0)
        emit_wt(1)
        for k2 in range(0, NF, 2):
            if k2 + 2 < NF:
                emit_lqx(k2 + 2)
                emit_lqx(k2 + 3)
            emit_mms(k2)
            emit_mms(k2 + 1)
            emit_post(k2)
            emit_post(k2 + 1)
            if k2 + 2 < NF:
                emit_wt(k2 + 2)
                emit_wt(k2 + 3)

        @block.vector
        def _(vector):
            ser(vector, cv, lambda: vector.tensor_tensor(rawmax[:], rawmax[:], deq_all[:], ALU.mult))
            ser(vector, cv, lambda: vector.tensor_tensor(rawmax[:], rawmax[:], b1q_all[:], ALU.add))
            ser(vector, cv, lambda: vector.tensor_scalar(rawmax[:], rawmax[:], 0.0, None, ALU.max))
            ser(vector, cv, lambda: vector.reduce_max(rmax[:], rawmax[:], axis=AX.X))

        p.barrier()
        _fold_max(p, ident, pmm[0], rmax, rowm, gmx, cp, cv)

        @block.sync
        def _(sync):
            cv.wait(sync)
            so.dma(sync.dma_start(out=mx_o, in_=gmx[:]))
            chd[0].wait(sync)
            chd[1].wait(sync)
            so.wait(sync)
    return nc



def _emit_m2(nc, es, p, ctx, *, h_in, w2_in, x_in, post_tile):
    block = p.block
    cv, ca, cp = ctx["cv"], ctx["ca"], ctx["cp"]
    ident, onesf = ctx["ident"], ctx["onesf"]
    invs_b, s_b = ctx["invs_b"], ctx["s_b"]
    sb2 = ctx["sb2"]
    rows_d = ctx["rows_d"]
    pmm = ctx["pmm"]

    wbuf = es.enter_context(nc.sbuf_tensor("m2wbuf", [128, FF], F32))
    wsc = es.enter_context(nc.sbuf_tensor("m2wsc", [128, FF], F32))
    wqb = es.enter_context(nc.sbuf_tensor("m2wqb", [128, FF], BF16))
    w2q = es.enter_context(nc.sbuf_tensor("w2q_s", [128, NF * DM], BF16))
    hbuf = [es.enter_context(nc.sbuf_tensor(f"m2hb{k}", [128, FF], F32)) for k in range(2)]
    hqb = [es.enter_context(nc.sbuf_tensor(f"m2hq{k}", [128, FF], BF16)) for k in range(2)]
    xtb = [es.enter_context(nc.sbuf_tensor(f"m2xt{k}", [128, DM], F32)) for k in range(2)]
    b2T = es.enter_context(nc.sbuf_tensor("b2T", [128, 8], F32))
    sw2a = es.enter_context(nc.sbuf_tensor("sw2a", [128, 8], F32))
    w2inv = es.enter_context(nc.sbuf_tensor("w2inv", [128, 8], F32))
    w2deq = es.enter_context(nc.sbuf_tensor("w2deq", [128, 8], F32))
    b2i = es.enter_context(nc.sbuf_tensor("b2i", [128, 8], F32))
    r8a = es.enter_context(nc.sbuf_tensor("r8a", [8, 128], F32))
    r8b = es.enter_context(nc.sbuf_tensor("r8b", [8, 128], F32))
    rowln = es.enter_context(nc.sbuf_tensor("rowln", [1, DM], F32))
    rowln2 = es.enter_context(nc.sbuf_tensor("rowln2", [1, DM], F32))
    scale_b = es.enter_context(nc.sbuf_tensor("scale_b", [128, DM], F32))
    bias_b = es.enter_context(nc.sbuf_tensor("bias_b", [128, DM], F32))
    x2s = [es.enter_context(nc.sbuf_tensor(f"x2s{k}", [128, DM], F32)) for k in range(2)]
    wsg = [x2s[0].bitcast(BF16), x2s[1].bitcast(BF16)]

    cwd = p.ctr("m2cwd")
    cwx = p.ctr("m2cwx")
    crw = p.ctr("m2crw")
    chg = [p.ctr(f"m2chg{k}") for k in range(2)]
    cxg = [p.ctr(f"m2cxg{k}") for k in range(2)]
    cmm = p.ctr("m2cmm")

    @block.tensor
    def _(tensor):
        cv.wait(tensor)
        ctx["ld"].wait(tensor)
        cp.inc(tensor.transpose(pmm[1][0:128, 0:8], sb2[:], ident[0:8, 0:8]))

    @block.vector
    def _(vector):
        cp.wait(vector)
        ser(vector, cv, lambda: vector.tensor_copy(b2T[:], pmm[1][0:128, 0:8]))

    p.barrier()

    identb = ctx["identb"]
    cpt = p.ctr("m2cpt")
    CVQ = {}
    CVB = {}
    for dc in range(8):
        @block.sync
        def _(sync, dc=dc):
            if dc > 0:
                cv.wait(sync, at=CVQ[dc - 1])
            cwd.dma(sync.dma_start(out=wbuf[:], in_=w2_in[dc * 128:(dc + 1) * 128, :]))

        @block.vector
        def _(vector, dc=dc):
            cwd.wait(vector)
            if dc > 0:
                cpt.wait(vector, at=dc)
            ser(vector, cv, lambda: vector.reduce_max(sw2a[:, dc:dc + 1], wbuf[:], axis=AX.X, apply_absolute_value=True))
            ser(vector, cv, lambda: vector.tensor_scalar(sw2a[:, dc:dc + 1], sw2a[:, dc:dc + 1], EPS_Q, None, ALU.max))
            ser(vector, cv, lambda: vector.reciprocal(w2inv[:, dc:dc + 1], sw2a[:, dc:dc + 1]))
            ser(vector, cv, lambda: vector.tensor_scalar(w2inv[:, dc:dc + 1], w2inv[:, dc:dc + 1], 127.0, None, ALU.mult))
            ser(vector, cv, lambda: vector.tensor_scalar(wsc[:], wbuf[:], w2inv[:, dc:dc + 1], None, ALU.mult))
            _round_to_bf16(vector, cv, wsc, wqb[:], lo=-128.0, hi=127.0)
            CVQ[dc] = cv.n

        @block.tensor
        def _(tensor, dc=dc):
            cv.wait(tensor, at=CVQ[dc])
            if dc > 0:
                cv.wait(tensor, at=CVB[dc - 1])
            for b in range(4):
                pv = pmm[b].bitcast(BF16)
                for c in range(8):
                    fc = b * 8 + c
                    ins = tensor.transpose(pv[:, c * 128:(c + 1) * 128],
                                           wqb[:, fc * 128:(fc + 1) * 128], identb[:])
            cpt.inc(ins)

        @block.vector
        def _(vector, dc=dc):
            cpt.wait(vector, at=dc + 1)
            if dc > 0:
                cwx.wait(vector, at=512 * dc)
            for b in range(4):
                ser(vector, cv, lambda b=b: vector.tensor_copy(
                    wsg[b // 2][:, (b % 2) * 1024:(b % 2 + 1) * 1024],
                    pmm[b].bitcast(BF16)[:, 0:1024]))
            CVB[dc] = cv.n

        @block.sync
        def _(sync, dc=dc):
            cv.wait(sync, at=CVB[dc])
            for fc in range(0, NF, 2):
                cwx.dma(sync.dma_start(
                    out=w2q[:, fc * DM + dc * 128:fc * DM + (dc + 1) * 128],
                    in_=wsg[fc // 16][:, (fc % 16) * 128:(fc % 16 + 1) * 128]))

        @block.scalar
        def _(scalar, dc=dc):
            cv.wait(scalar, at=CVB[dc])
            for fc in range(1, NF, 2):
                cwx.dma(scalar.dma_start(
                    out=w2q[:, fc * DM + dc * 128:fc * DM + (dc + 1) * 128],
                    in_=wsg[fc // 16][:, (fc % 16) * 128:(fc % 16 + 1) * 128]))

    p.barrier()

    @block.vector
    def _(vector):
        ser(vector, cv, lambda: vector.tensor_scalar(w2deq[:], sw2a[:], INV127, None, ALU.mult))
        ser(vector, cv, lambda: vector.tensor_scalar(w2deq[:], w2deq[:], s_b[:, 0:1], None, ALU.mult))
        ser(vector, cv, lambda: vector.tensor_scalar(w2deq[:], w2deq[:], 0.5, None, ALU.mult))
        ser(vector, cv, lambda: vector.tensor_scalar(b2i[:], b2T[:], invs_b[:, 0:1], None, ALU.mult))
        ser(vector, cv, lambda: vector.tensor_tensor(b2i[:], b2i[:], w2inv[:], ALU.mult))
        ser(vector, cv, lambda: vector.tensor_copy(b2i.bitcast(I32)[:], b2i[:]))
        ser(vector, cv, lambda: vector.tensor_copy(b2i[:], b2i.bitcast(I32)[:]))
        ser(vector, cv, lambda: vector.tensor_tensor(b2i[:], b2i[:], w2deq[:], ALU.mult))

    @block.tensor
    def _(tensor):
        cv.wait(tensor)
        tensor.transpose(pmm[0][0:8, 0:128], w2deq[:], ident[:])
        cp.inc(tensor.transpose(pmm[1][0:8, 0:128], b2i[:], ident[:]))

    @block.vector
    def _(vector):
        cp.wait(vector)
        ser(vector, cv, lambda: vector.tensor_copy(r8a[:], pmm[0][0:8, 0:128]))
        ser(vector, cv, lambda: vector.tensor_copy(r8b[:], pmm[1][0:8, 0:128]))

    @block.sync
    def _(sync):
        cv.wait(sync)
        crw.dma(sync.dma_start(out=rows_d[0], in_=r8a[:]))
        crw.dma(sync.dma_start(out=rows_d[1], in_=r8b[:]))
        crw.wait(sync)
        crw.dma(sync.dma_start(out=rowln[:], in_=rows_d.rearrange("a b c -> a (b c)")[0:1, :]))
        crw.dma(sync.dma_start(out=rowln2[:], in_=rows_d.rearrange("a b c -> a (b c)")[1:2, :]))

    @block.tensor
    def _(tensor):
        crw.wait(tensor)
        tensor.matmul(pmm[0][:, 0:512], onesf[:], rowln[:, 0:512], start=True, stop=True)
        tensor.matmul(pmm[1][:, 0:512], onesf[:], rowln[:, 512:1024], start=True, stop=True)
        tensor.matmul(pmm[2][:, 0:512], onesf[:], rowln2[:, 0:512], start=True, stop=True)
        cp.inc(tensor.matmul(pmm[3][:, 0:512], onesf[:], rowln2[:, 512:1024], start=True, stop=True))

    @block.vector
    def _(vector):
        cp.wait(vector)
        ser(vector, cv, lambda: vector.tensor_copy(scale_b[:, 0:512], pmm[0][:, 0:512]))
        ser(vector, cv, lambda: vector.tensor_copy(scale_b[:, 512:1024], pmm[1][:, 0:512]))
        ser(vector, cv, lambda: vector.tensor_copy(bias_b[:, 0:512], pmm[2][:, 0:512]))
        ser(vector, cv, lambda: vector.tensor_copy(bias_b[:, 512:1024], pmm[3][:, 0:512]))

    CVROWS = cv.n

    CAH, CVQ2, CVE, PT = {}, {}, {}, {}

    def emit_load(i):
        j = i % 2

        @block.gpsimd
        def _(gpsimd, i=i, j=j):
            if i >= 2:
                cv.wait(gpsimd, at=CVE[i - 2])
            chg[j].dma(gpsimd.dma_start(
                out=hbuf[j][:],
                in_=h_in.rearrange("c p t -> p c t")[:, :, i * 128:(i + 1) * 128]))
            cxg[j].dma(gpsimd.dma_start(
                out=xtb[j][:], in_=x_in.rearrange("(t p) d -> t p d", p=128)[i]))

    def emit_quant(i):
        j = i % 2

        @block.scalar
        def _(scalar, i=i, j=j):
            chg[j].wait(scalar, at=16 * (i // 2 + 1))
            if i >= 2:
                cv.wait(scalar, at=CVQ2[i - 2])
            ser(scalar, ca, lambda: scalar.activation(
                hbuf[j].bitcast(I32)[:], hbuf[j][:], AF.Copy, scale=invs_b[:, 0:1]))
        CAH[i] = ca.n

        @block.vector
        def _(vector, i=i, j=j):
            ca.wait(vector, at=CAH[i])
            if i >= 2:
                cmm.wait(vector, at=2 * (i - 1))
            ser(vector, cv, lambda: vector.tensor_copy(hqb[j][:], hbuf[j].bitcast(I32)[:]))
        CVQ2[i] = cv.n

    def emit_mm(i):
        j = i % 2

        @block.tensor
        def _(tensor, i=i, j=j):
            cv.wait(tensor, at=CVQ2[i])
            cv.wait(tensor, at=CVE[i - 2] if i >= 2 else CVROWS)
            cwx.wait(tensor)
            for nb in range(2):
                bank = 2 * j + nb
                for fc in range(NF):
                    ins = tensor.matmul(pmm[bank][:],
                                        hqb[j][:, fc * 128:(fc + 1) * 128],
                                        w2q[:, fc * DM + nb * 512:fc * DM + (nb + 1) * 512],
                                        start=(fc == 0), stop=(fc == NF - 1))
                    if fc == NF - 1:
                        cmm.inc(ins)

    def emit_epi(i):
        j = i % 2

        @block.vector
        def _(vector, i=i, j=j):
            cmm.wait(vector, at=2 * (i + 1))
            cxg[j].wait(vector, at=16 * (i // 2 + 1))
            for ctr_, at_ in PT.get(i - 2, []):
                ctr_.wait(vector, at=at_)
            ser(vector, cv, lambda: vector.scalar_tensor_tensor(
                x2s[j][:, 0:512], pmm[2 * j][:], 1.0, scale_b[:, 0:512], ALU.mult, ALU.mult))
            ser(vector, cv, lambda: vector.scalar_tensor_tensor(
                x2s[j][:, 512:1024], pmm[2 * j + 1][:], 1.0, scale_b[:, 512:1024], ALU.mult, ALU.mult))
            ser(vector, cv, lambda: vector.tensor_tensor(x2s[j][:], x2s[j][:], bias_b[:], ALU.add))
            ser(vector, cv, lambda: vector.tensor_tensor(x2s[j][:], x2s[j][:], xtb[j][:], ALU.add))
        CVE[i] = cv.n
        PT[i] = post_tile(i, x2s[j]) or []

    emit_load(0)
    emit_load(1)
    emit_quant(0)
    emit_quant(1)
    for i in range(NT):
        emit_mm(i)
        emit_epi(i)
        if i + 2 < NT:
            emit_load(i + 2)
            emit_quant(i + 2)



def build_m2f(nc):
    x = nc.dram_tensor("x", [S, DM], F32, kind="ExternalInput").ap()
    h_in = nc.dram_tensor("H", [NF, 128, S], F32, kind="ExternalInput").ap()
    gmax = nc.dram_tensor("gmax", [1, 8], F32, kind="ExternalInput").ap()
    w2 = nc.dram_tensor("w2", [DM, FF], F32, kind="ExternalInput").ap()
    b2 = nc.dram_tensor("b2", [8, 128], F32, kind="ExternalInput").ap()
    ident_d = nc.dram_tensor("ident", [128, 128], F32, kind="ExternalInput").ap()
    identb_d = nc.dram_tensor("identb", [128, 128], BF16, kind="ExternalInput").ap()
    onesf_d = nc.dram_tensor("onesf", [1, 128], F32, kind="ExternalInput").ap()
    y_o = nc.dram_tensor("y", [S, DM], F32, kind="ExternalOutput").ap()
    rows_d = nc.dram_tensor("rows_scratch", [2, 8, 128], F32).ap()

    es = contextlib.ExitStack()
    with es:
        sgm = es.enter_context(nc.sbuf_tensor("sgm", [1, 8], F32))
        srow = es.enter_context(nc.sbuf_tensor("srow", [1, 4], F32))
        sb2 = es.enter_context(nc.sbuf_tensor("sb2", [8, 128], F32))
        ident = es.enter_context(nc.sbuf_tensor("identS", [128, 128], F32))
        identb = es.enter_context(nc.sbuf_tensor("identbS", [128, 128], BF16))
        onesf = es.enter_context(nc.sbuf_tensor("onesf_s", [1, 128], F32))
        invs_b = es.enter_context(nc.sbuf_tensor("invs_b", [128, 1], F32))
        s_b = es.enter_context(nc.sbuf_tensor("s_b", [128, 1], F32))
        pmm = [es.enter_context(nc.psum_tensor(f"pmm{j}", [128, 512], F32)) for j in range(4)]

        p = Prog(es, nc)
        block = p.block
        ld = p.ctr("ld")
        cv = p.ctr("cv")
        ca = p.ctr("ca")
        cp = p.ctr("cp")
        so = [p.ctr(f"so{k}") for k in range(2)]

        @block.sync
        def _(sync):
            ld.dma(sync.dma_start(out=sgm[:], in_=gmax))
            ld.dma(sync.dma_start(out=sb2[:], in_=b2))
            ld.dma(sync.dma_start(out=ident[:], in_=ident_d))
            ld.dma(sync.dma_start(out=identb[:], in_=identb_d))
            ld.dma(sync.dma_start(out=onesf[:], in_=onesf_d))

        @block.vector
        def _(vector):
            ld.wait(vector)

        _quant_scalars(p, sgm, srow, onesf, invs_b, s_b, pmm[0], cv, cp)

        ctx = dict(cv=cv, ca=ca, cp=cp, ld=ld, ident=ident, identb=identb,
                   onesf=onesf, invs_b=invs_b, s_b=s_b, pmm=pmm,
                   rows_d=rows_d, sb2=sb2, srow=srow)

        def post_tile(i, x2_ap):
            j = i % 2

            @block.sync
            def _(sync, i=i, j=j):
                cv.wait(sync)
                so[j].dma(sync.dma_start(
                    out=y_o.rearrange("(t p) d -> t p d", p=128)[i], in_=x2_ap[:]))
            return [(so[j], 16 * (i // 2 + 1))]

        _emit_m2(nc, es, p, ctx, h_in=h_in, w2_in=w2, x_in=x, post_tile=post_tile)

        @block.sync
        def _(sync):
            so[0].wait(sync)
            so[1].wait(sync)
    return nc



def build_mid(nc):
    x = nc.dram_tensor("x", [S, DM], F32, kind="ExternalInput").ap()
    h_in = nc.dram_tensor("H", [NF, 128, S], F32, kind="ExternalInput").ap()
    gmax = nc.dram_tensor("gmax", [1, 8], F32, kind="ExternalInput").ap()
    w2 = nc.dram_tensor("w2", [DM, FF], F32, kind="ExternalInput").ap()
    b2 = nc.dram_tensor("b2", [8, 128], F32, kind="ExternalInput").ap()
    wqkv = nc.dram_tensor("wqkv", [DM, 3 * DM], F32, kind="ExternalInput").ap()
    bqkv = nc.dram_tensor("bqkv", [24, 128], F32, kind="ExternalInput").ap()
    bqkv_r = nc.dram_tensor("bqkv_r", [1, 3 * DM], F32, kind="ExternalInput").ap()
    wproj = nc.dram_tensor("wproj", [DM, DM], F32, kind="ExternalInput").ap()
    bproj_r = nc.dram_tensor("bproj_r", [1, DM], F32, kind="ExternalInput").ap()
    g_r = nc.dram_tensor("g_r", [1, DM], F32, kind="ExternalInput").ap()
    cosT = nc.dram_tensor("cosT", [512, S], BF16, kind="ExternalInput").ap()
    sinT = nc.dram_tensor("sinT", [512, S], BF16, kind="ExternalInput").ap()
    ident_d = nc.dram_tensor("ident", [128, 128], F32, kind="ExternalInput").ap()
    identb_d = nc.dram_tensor("identb", [128, 128], BF16, kind="ExternalInput").ap()
    onesf_d = nc.dram_tensor("onesf", [1, 128], F32, kind="ExternalInput").ap()
    onesb_d = nc.dram_tensor("onesb", [1, 128], BF16, kind="ExternalInput").ap()
    x2_o = nc.dram_tensor("x2", [S, DM], F32, kind="ExternalOutput").ap()
    st_o = nc.dram_tensor("st3", [128, 2 * NT], F32, kind="ExternalOutput").ap()
    mx_o = nc.dram_tensor("mx", [1, 1], F32, kind="ExternalOutput").ap()
    rows_d = nc.dram_tensor("rows_scratch", [2, 8, 128], F32).ap()
    x1s = nc.dram_tensor("x1_scratch", [S, DM], F32).ap()

    es = contextlib.ExitStack()
    with es:
        sgm = es.enter_context(nc.sbuf_tensor("sgm", [1, 8], F32))
        srow = es.enter_context(nc.sbuf_tensor("srow", [1, 4], F32))
        sb2 = es.enter_context(nc.sbuf_tensor("sb2", [8, 128], F32))
        ident = es.enter_context(nc.sbuf_tensor("identS", [128, 128], F32))
        identb = es.enter_context(nc.sbuf_tensor("identbS", [128, 128], BF16))
        onesf = es.enter_context(nc.sbuf_tensor("onesf_s", [1, 128], F32))
        onesb = es.enter_context(nc.sbuf_tensor("onesb_s", [1, 128], BF16))
        invs_b = es.enter_context(nc.sbuf_tensor("invs_b", [128, 1], F32))
        s_b = es.enter_context(nc.sbuf_tensor("s_b", [128, 1], F32))
        st2 = es.enter_context(nc.sbuf_tensor("st2_s", [128, 2 * NT], F32))
        st3 = es.enter_context(nc.sbuf_tensor("st3_s", [128, 2 * NT], F32))
        sum2 = es.enter_context(nc.sbuf_tensor("sum2", [128, NT], F32))
        sq2 = es.enter_context(nc.sbuf_tensor("sq2", [128, NT], F32))
        sum3 = es.enter_context(nc.sbuf_tensor("sum3", [128, NT], F32))
        sq3 = es.enter_context(nc.sbuf_tensor("sq3", [128, NT], F32))
        mx3t = es.enter_context(nc.sbuf_tensor("mx3t", [128, NT], F32))
        tmpn = es.enter_context(nc.sbuf_tensor("tmpn", [128, NT], F32))
        gbc = es.enter_context(nc.sbuf_tensor("gbc", [128, DM], F32))
        sqd = es.enter_context(nc.sbuf_tensor("sqd", [128, DM], BF16))
        rmax = es.enter_context(nc.sbuf_tensor("rmax2", [128, 1], F32))
        rowm = es.enter_context(nc.sbuf_tensor("rowm", [1, 128], F32))
        gmx = es.enter_context(nc.sbuf_tensor("gmx", [1, 1], F32))
        epst = es.enter_context(nc.sbuf_tensor("epst", [128, 2], F32))

        p = Prog(es, nc)
        block = p.block
        ld = p.ctr("ld")
        cv = p.ctr("cv")
        ca = p.ctr("ca")
        cp = p.ctr("cp")
        cpl = p.ctr("cpl")
        so = p.ctr("so")
        cx1 = p.ctr("cx1")
        cxs = [p.ctr(f"cxs{k}") for k in range(2)]

        @block.sync
        def _(sync):
            ld.dma(sync.dma_start(out=sgm[:], in_=gmax))
            ld.dma(sync.dma_start(out=sb2[:], in_=b2))
            ld.dma(sync.dma_start(out=ident[:], in_=ident_d))
            ld.dma(sync.dma_start(out=identb[:], in_=identb_d))
            ld.dma(sync.dma_start(out=onesf[:], in_=onesf_d))
            ld.dma(sync.dma_start(out=onesb[:], in_=onesb_d))

        @block.vector
        def _(vector):
            ld.wait(vector)
            ser(vector, cv, lambda: vector.memset(epst[:, 0:1], EPS_LN))
            ser(vector, cv, lambda: vector.memset(epst[:, 1:2], EPS_RMS))

        esA = contextlib.ExitStack()
        pmm = [esA.enter_context(nc.psum_tensor(f"pmm{j}", [128, 512], F32)) for j in range(4)]

        _quant_scalars(p, sgm, srow, onesf, invs_b, s_b, pmm[0], cv, cp)

        ctx = dict(cv=cv, ca=ca, cp=cp, ld=ld, ident=ident, identb=identb,
                   onesf=onesf, invs_b=invs_b, s_b=s_b, pmm=pmm,
                   rows_d=rows_d, sb2=sb2, srow=srow)

        def post_tile(i, x1_ap):
            j = i % 2

            @block.sync
            def _(sync, i=i, j=j):
                cv.wait(sync)
                cxs[j].dma(sync.dma_start(
                    out=x1s.rearrange("(t p) d -> t p d", p=128)[i], in_=x1_ap[:]))

            @block.vector
            def _(vector, i=i):
                ser(vector, cv, lambda: vector.reduce_sum(sum2[:, i:i + 1], x1_ap[:], axis=AX.X))

            @block.scalar
            def _(scalar, i=i):
                cv.wait(scalar)
                ser(scalar, ca, lambda: scalar.activation(
                    sqd[:], x1_ap[:], AF.Square, accum_out=sq2[:, i:i + 1]))
            return [(cxs[j], 16 * (i // 2 + 1)), (ca, ca.n)]

        _emit_m2(nc, esA, p, ctx, h_in=h_in, w2_in=w2, x_in=x, post_tile=post_tile)
        p.barrier()

        @block.vector
        def _(vector):
            ser(vector, cv, lambda: vector.tensor_scalar(st2[:, 0:NT], sum2[:], INVD, None, ALU.mult))
            ser(vector, cv, lambda: vector.tensor_tensor(tmpn[:], st2[:, 0:NT], st2[:, 0:NT], ALU.mult))
            ser(vector, cv, lambda: vector.tensor_scalar(sq2[:], sq2[:], INVD, None, ALU.mult))
            ser(vector, cv, lambda: vector.tensor_tensor(tmpn[:], sq2[:], tmpn[:], ALU.subtract))

        _ln_rstd(block, ca, cv, tmpn[:], st2[:, NT:2 * NT], epst[:, 0:1])

        @block.vector
        def _(vector):
            ca.wait(vector)
            ser(vector, cv, lambda: vector.tensor_tensor(tmpn[:], st2[:, 0:NT], st2[:, NT:2 * NT], ALU.mult))
            ser(vector, cv, lambda: vector.tensor_scalar(tmpn[:], tmpn[:], -1.0, None, ALU.mult))

        p.barrier()
        esA.close()

        esQ = contextlib.ExitStack()
        qrT = esQ.enter_context(nc.sbuf_tensor("qrT", [128, ND * S], BF16))
        krT = esQ.enter_context(nc.sbuf_tensor("krT", [128, ND * S], BF16))
        vtk = esQ.enter_context(nc.sbuf_tensor("vtk", [128, NT * DM], BF16))
        esB = contextlib.ExitStack()
        wqkv_bf = esB.enter_context(nc.sbuf_tensor("wqkv_bf", [128, ND * 3 * DM], BF16))
        sbqkv = esB.enter_context(nc.sbuf_tensor("sbqkv", [24, 128], F32))
        bqkvT = esB.enter_context(nc.sbuf_tensor("bqkvT", [128, 24], F32))
        bqkv_rb = esB.enter_context(nc.sbuf_tensor("bqkv_rb", [1, 3 * DM], BF16))
        sbq_row = esB.enter_context(nc.sbuf_tensor("sbq_row", [1, 3 * DM], F32))
        g_row = esB.enter_context(nc.sbuf_tensor("g_row", [1, DM], F32))
        x1t = esB.enter_context(nc.sbuf_tensor("x1t", [128, DM], F32))
        u2t = esB.enter_context(nc.sbuf_tensor("u2t", [128, DM], F32))
        msr = esB.enter_context(nc.sbuf_tensor("msr", [128, NT], F32))
        hbt = esB.enter_context(nc.sbuf_tensor("hbt", [128, DM], BF16))
        hTw = esB.enter_context(nc.sbuf_tensor("hTw", [128, ND * 512], BF16))
        cosw = esB.enter_context(nc.sbuf_tensor("cosw", [128, 4 * 512], BF16))
        sinw = esB.enter_context(nc.sbuf_tensor("sinw", [128, 4 * 512], BF16))
        rt1 = esB.enter_context(nc.sbuf_tensor("rt1", [128, 512], F32))
        rt2 = esB.enter_context(nc.sbuf_tensor("rt2", [128, 512], F32))
        pqk = [esB.enter_context(nc.psum_tensor(f"pqk{j}", [128, 512], F32)) for j in range(4)]

        cqkd = p.ctr("cqkd")
        cbq = p.ctr("cbq")
        cxw = p.ctr("cxw")
        cht = p.ctr("cht")
        ctw = p.ctr("ctw")
        cqm = p.ctr("cqm")
        CVH, CVHE, CVROPE, CQMW = {}, {}, {}, {}

        @block.gpsimd
        def _(gpsimd):
            ld.wait(gpsimd)
            cqkd.dma(gpsimd.dma_start(
                out=wqkv_bf[:], in_=wqkv.rearrange("(c p) m -> p c m", p=128)))

        @block.sync
        def _(sync):
            ld.wait(sync)
            cbq.dma(sync.dma_start(out=sbqkv[:], in_=bqkv))
            cbq.dma(sync.dma_start(out=sbq_row[:], in_=bqkv_r))
            cbq.dma(sync.dma_start(out=g_row[:], in_=g_r))

        @block.tensor
        def _(tensor):
            cbq.wait(tensor)
            cv.wait(tensor)
            tensor.transpose(pqk[0][0:128, 0:24], sbqkv[:], ident[0:24, 0:24])
            tensor.matmul(pqk[1][:, 0:512], onesf[:], g_row[:, 0:512], start=True, stop=True)
            cp.inc(tensor.matmul(pqk[2][:, 0:512], onesf[:], g_row[:, 512:1024], start=True, stop=True))

        @block.vector
        def _(vector):
            cp.wait(vector)
            ser(vector, cv, lambda: vector.tensor_copy(bqkvT[:], pqk[0][0:128, 0:24]))
            ser(vector, cv, lambda: vector.tensor_copy(gbc[:, 0:512], pqk[1][:, 0:512]))
            ser(vector, cv, lambda: vector.tensor_copy(gbc[:, 512:1024], pqk[2][:, 0:512]))
            ser(vector, cv, lambda: vector.tensor_copy(bqkv_rb[:], sbq_row[:]))

        p.barrier()

        for tw in range(4):
            @block.sync
            def _(sync, tw=tw):
                cv.wait(sync)
                for c in range(4):
                    ctw.dma(sync.dma_start(out=cosw[:, c * 512:(c + 1) * 512],
                                           in_=cosT[c * 128:(c + 1) * 128, tw * 512:(tw + 1) * 512]))
                    ctw.dma(sync.dma_start(out=sinw[:, c * 512:(c + 1) * 512],
                                           in_=sinT[c * 128:(c + 1) * 128, tw * 512:(tw + 1) * 512]))

            for ti in range(4):
                i = tw * 4 + ti

                @block.sync
                def _(sync, i=i):
                    cv.wait(sync)
                    ca.wait(sync)
                    cxw.dma(sync.dma_start(
                        out=x1t[:], in_=x1s.rearrange("(t p) d -> t p d", p=128)[i]))

                @block.scalar
                def _(scalar, i=i):
                    cxw.wait(scalar)
                    ser(scalar, ca, lambda: scalar.activation(
                        u2t[:], x1t[:], AF.Identity,
                        bias=tmpn[:, i:i + 1], scale=st2[:, NT + i:NT + i + 1]))
                    ser(scalar, ca, lambda: scalar.activation(
                        sqd[:], u2t[:], AF.Square, accum_out=msr[:, i:i + 1]))

                @block.vector
                def _(vector, i=i):
                    ca.wait(vector)
                    ser(vector, cv, lambda: vector.tensor_scalar(
                        msr[:, i:i + 1], msr[:, i:i + 1], INVD, None, ALU.mult))

                @block.scalar
                def _(scalar, i=i):
                    cv.wait(scalar)
                    ser(scalar, ca, lambda: scalar.activation(
                        msr[:, i:i + 1], msr[:, i:i + 1], AF.Ln, bias=epst[:, 1:2]))
                    ser(scalar, ca, lambda: scalar.activation(
                        msr[:, i:i + 1], msr[:, i:i + 1], AF.Exp, scale=-0.5))
                    ser(scalar, ca, lambda: scalar.activation(
                        u2t[:], u2t[:], AF.Copy, scale=msr[:, i:i + 1]))

                @block.vector
                def _(vector, i=i):
                    ca.wait(vector)
                    cht.wait(vector, at=i)
                    ser(vector, cv, lambda: vector.tensor_tensor(hbt[:], u2t[:], gbc[:], ALU.mult))
                CVH[i] = cv.n

                @block.tensor
                def _(tensor, tw=tw, ti=ti, i=i):
                    cv.wait(tensor, at=CVH[i])
                    if tw > 0:
                        cv.wait(tensor, at=CVROPE[tw - 1])
                    if i >= 2:
                        cv.wait(tensor, at=CVHE[i - 2])
                    pv = pqk[ti % 2].bitcast(BF16)
                    for c in range(ND):
                        ins = tensor.transpose(pv[:, c * 128:(c + 1) * 128],
                                               hbt[:, c * 128:(c + 1) * 128], identb[:])
                    cht.inc(ins)

                @block.vector
                def _(vector, tw=tw, ti=ti, i=i):
                    cht.wait(vector, at=i + 1)
                    if tw > 0:
                        cqm.wait(vector, at=CQMW[tw - 1])
                    pv = pqk[ti % 2].bitcast(BF16)
                    for c in range(ND):
                        ser(vector, cv, lambda c=c: vector.tensor_copy(
                            hTw[:, c * 512 + ti * 128:c * 512 + (ti + 1) * 128],
                            pv[:, c * 128:(c + 1) * 128]))
                    CVHE[i] = cv.n

            for qk in range(2):
                dst = qrT if qk == 0 else krT
                mbase = qk * 8
                for c in range(4):
                    mA, mB = mbase + c, mbase + c + 4

                    @block.tensor
                    def _(tensor, tw=tw, mA=mA, mB=mB):
                        cv.wait(tensor, at=CVHE[tw * 4 + 3])
                        cqkd.wait(tensor)
                        ctw.wait(tensor)
                        cv.wait(tensor)
                        for m, bank in ((mA, 0), (mB, 1)):
                            for d in range(ND):
                                ins = tensor.matmul(
                                    pqk[bank][:],
                                    wqkv_bf[:, d * 3 * DM + m * 128:d * 3 * DM + (m + 1) * 128],
                                    hTw[:, d * 512:(d + 1) * 512],
                                    start=(d == 0), stop=(d == ND - 1))
                                if d == ND - 1:
                                    cqm.inc(ins)

                    @block.vector
                    def _(vector, tw=tw, c=c, mA=mA, mB=mB, dst=dst):
                        cqm.wait(vector)
                        colA = c * S + tw * 512
                        colB = (c + 4) * S + tw * 512
                        cw_ = cosw[:, c * 512:(c + 1) * 512]
                        sw_ = sinw[:, c * 512:(c + 1) * 512]
                        ser(vector, cv, lambda: vector.scalar_tensor_tensor(
                            rt1[:], pqk[0][:], bqkvT[:, mA:mA + 1], cw_, ALU.add, ALU.mult))
                        ser(vector, cv, lambda: vector.scalar_tensor_tensor(
                            rt2[:], pqk[1][:], bqkvT[:, mB:mB + 1], sw_, ALU.add, ALU.mult))
                        ser(vector, cv, lambda: vector.tensor_tensor(
                            dst[:, colA:colA + 512], rt1[:], rt2[:], ALU.subtract))
                        ser(vector, cv, lambda: vector.scalar_tensor_tensor(
                            rt1[:], pqk[1][:], bqkvT[:, mB:mB + 1], cw_, ALU.add, ALU.mult))
                        ser(vector, cv, lambda: vector.scalar_tensor_tensor(
                            rt2[:], pqk[0][:], bqkvT[:, mA:mA + 1], sw_, ALU.add, ALU.mult))
                        ser(vector, cv, lambda: vector.tensor_tensor(
                            dst[:, colB:colB + 512], rt1[:], rt2[:], ALU.add))

            for ti in range(4):
                i = tw * 4 + ti

                @block.tensor
                def _(tensor, tw=tw, ti=ti, i=i):
                    ca.wait(tensor)
                    for e in range(2):
                        bank = 2 + e
                        for d in range(ND):
                            tensor.matmul(
                                pqk[bank][:],
                                hTw[:, d * 512 + ti * 128:d * 512 + (ti + 1) * 128],
                                wqkv_bf[:, d * 3 * DM + 2 * DM + e * 512:d * 3 * DM + 2 * DM + (e + 1) * 512],
                                start=(d == 0), stop=False)
                        ins = tensor.matmul(
                            pqk[bank][:], onesb[:],
                            bqkv_rb[:, 2 * DM + e * 512:2 * DM + (e + 1) * 512],
                            start=False, stop=True)
                        cqm.inc(ins)

                @block.scalar
                def _(scalar, i=i):
                    cqm.wait(scalar)
                    for e in range(2):
                        ser(scalar, ca, lambda e=e: scalar.activation(
                            vtk[:, i * DM + e * 512:i * DM + (e + 1) * 512],
                            pqk[2 + e][:], AF.Copy))

            CVROPE[tw] = cv.n
            CQMW[tw] = cqm.n

        p.barrier()
        esB.close()

        esC = contextlib.ExitStack()
        scps = esC.enter_context(nc.psum_tensor("scps", [128, S], F32))
        avp = esC.enter_context(nc.psum_tensor("avp", [128, DM], F32))
        pjp = esC.enter_context(nc.psum_tensor("pjp", [128, DM], F32))
        wproj_bf = esC.enter_context(nc.sbuf_tensor("wproj_bf", [128, ND * DM], BF16))
        bproj_rb = esC.enter_context(nc.sbuf_tensor("bproj_rb", [1, DM], BF16))
        sbp_row = esC.enter_context(nc.sbuf_tensor("sbp_row", [1, DM], F32))
        abuf = esC.enter_context(nc.sbuf_tensor("abuf", [128, S], BF16))
        atb = esC.enter_context(nc.sbuf_tensor("atb", [128, S], BF16))
        obf = esC.enter_context(nc.sbuf_tensor("obf", [128, DM], BF16))
        otb = esC.enter_context(nc.sbuf_tensor("otb", [128, DM], BF16))
        mxr = esC.enter_context(nc.sbuf_tensor("mxr", [128, 1], F32))
        mbr = esC.enter_context(nc.sbuf_tensor("mbr", [128, 1], F32))
        rsum = esC.enter_context(nc.sbuf_tensor("rsum", [128, 1], F32))
        rs2 = esC.enter_context(nc.sbuf_tensor("rs2", [128, 1], F32))
        x1c = esC.enter_context(nc.sbuf_tensor("x1c", [128, DM], F32))
        u2rc = esC.enter_context(nc.sbuf_tensor("u2rc", [128, DM], F32))
        x2c = esC.enter_context(nc.sbuf_tensor("x2c", [128, DM], F32))

        cpj = p.ctr("cpj")
        ctp = p.ctr("ctp")
        cto = p.ctr("cto")
        cwp = p.ctr("cwp")
        cbp = p.ctr("cbp")
        CAB, CVAT, CAO, CVOT = {}, {}, {}, {}

        @block.gpsimd
        def _(gpsimd):
            ld.wait(gpsimd)
            cwp.dma(gpsimd.dma_start(
                out=wproj_bf[:], in_=wproj.rearrange("(c p) m -> p c m", p=128)))

        @block.sync
        def _(sync):
            ld.wait(sync)
            cbp.dma(sync.dma_start(out=sbp_row[:], in_=bproj_r))

        @block.vector
        def _(vector):
            cwp.wait(vector)
            cbp.wait(vector)
            ser(vector, cv, lambda: vector.tensor_copy(bproj_rb[:], sbp_row[:]))
            ser(vector, cv, lambda: vector.memset(rmax[:], -1e30))

        p.barrier()

        for i in range(NT):
            @block.tensor
            def _(tensor, i=i):
                p.wait_all(tensor)
                for n in range(4):
                    for c in range(ND):
                        ins = tensor.matmul(
                            scps[:, n * 512:(n + 1) * 512],
                            qrT[:, c * S + i * 128:c * S + (i + 1) * 128],
                            krT[:, c * S + n * 512:c * S + (n + 1) * 512],
                            start=(c == 0), stop=(c == ND - 1))
                        if c == ND - 1:
                            cpj.inc(ins)

            @block.vector
            def _(vector, i=i):
                cpj.wait(vector)
                ser(vector, cv, lambda: vector.reduce_max(mxr[:], scps[:], axis=AX.X))
                ser(vector, cv, lambda: vector.tensor_scalar(mbr[:], mxr[:], -SCALE_SM, None, ALU.mult))

            @block.scalar
            def _(scalar, i=i):
                cv.wait(scalar)
                ser(scalar, ca, lambda: scalar.activation(
                    abuf[:], scps[:], AF.Exp, bias=mbr[:, 0:1], scale=SCALE_SM,
                    accum_out=rsum[:]))
            CAB[i] = ca.n

            @block.vector
            def _(vector, i=i):
                ca.wait(vector)
                ser(vector, cv, lambda: vector.reciprocal(rs2[:], rsum[:]))

            @block.tensor
            def _(tensor, i=i):
                ca.wait(tensor, at=CAB[i])
                pv = scps.bitcast(BF16)
                for c in range(NT):
                    ins = tensor.transpose(pv[:, c * 128:(c + 1) * 128],
                                           abuf[:, c * 128:(c + 1) * 128], identb[:])
                ctp.inc(ins)

            @block.vector
            def _(vector, i=i):
                ctp.wait(vector, at=i + 1)
                pv = scps.bitcast(BF16)
                ser(vector, cv, lambda: vector.tensor_copy(atb[:, 0:1024], pv[:, 0:1024]))
                ser(vector, cv, lambda: vector.tensor_copy(atb[:, 1024:2048], pv[:, 1024:2048]))
                CVAT[i] = cv.n

            @block.tensor
            def _(tensor, i=i):
                cv.wait(tensor, at=CVAT[i])
                ca.wait(tensor)
                for e in range(2):
                    for c in range(NT):
                        ins = tensor.matmul(
                            avp[:, e * 512:(e + 1) * 512],
                            atb[:, c * 128:(c + 1) * 128],
                            vtk[:, c * DM + e * 512:c * DM + (e + 1) * 512],
                            start=(c == 0), stop=(c == NT - 1))
                        if c == NT - 1:
                            cpj.inc(ins)

            @block.scalar
            def _(scalar, i=i):
                cpj.wait(scalar)
                cv.wait(scalar)
                ser(scalar, ca, lambda: scalar.activation(
                    obf[:], avp[:], AF.Copy, scale=rs2[:, 0:1]))
            CAO[i] = ca.n

            @block.tensor
            def _(tensor, i=i):
                ca.wait(tensor, at=CAO[i])
                pv = avp.bitcast(BF16)
                for e in range(ND):
                    ins = tensor.transpose(pv[:, e * 128:(e + 1) * 128],
                                           obf[:, e * 128:(e + 1) * 128], identb[:])
                cto.inc(ins)

            @block.vector
            def _(vector, i=i):
                cto.wait(vector, at=i + 1)
                ser(vector, cv, lambda: vector.tensor_copy(otb[:], avp.bitcast(BF16)[:, 0:1024]))
                CVOT[i] = cv.n

            @block.tensor
            def _(tensor, i=i):
                cv.wait(tensor, at=CVOT[i])
                cwp.wait(tensor)
                cv.wait(tensor)
                for n in range(2):
                    for e in range(ND):
                        tensor.matmul(
                            pjp[:, n * 512:(n + 1) * 512],
                            otb[:, e * 128:(e + 1) * 128],
                            wproj_bf[:, e * DM + n * 512:e * DM + (n + 1) * 512],
                            start=(e == 0), stop=False)
                    ins = tensor.matmul(
                        pjp[:, n * 512:(n + 1) * 512], onesb[:],
                        bproj_rb[:, n * 512:(n + 1) * 512], start=False, stop=True)
                    cpj.inc(ins)

            @block.sync
            def _(sync, i=i):
                cv.wait(sync)
                ca.wait(sync)
                cx1.dma(sync.dma_start(
                    out=x1c[:], in_=x1s.rearrange("(t p) d -> t p d", p=128)[i]))

            @block.scalar
            def _(scalar, i=i):
                cx1.wait(scalar)
                ser(scalar, ca, lambda: scalar.activation(
                    u2rc[:], x1c[:], AF.Identity,
                    bias=tmpn[:, i:i + 1], scale=st2[:, NT + i:NT + i + 1]))

            @block.vector
            def _(vector, i=i):
                p.wait_all(vector)
                ser(vector, cv, lambda: vector.tensor_tensor(x2c[:], pjp[:], u2rc[:], ALU.add))
                ser(vector, cv, lambda: vector.tensor_tensor(x2c[:], x2c[:], x1c[:], ALU.add))
                ser(vector, cv, lambda: vector.reduce_sum(sum3[:, i:i + 1], x2c[:], axis=AX.X))
                ser(vector, cv, lambda: vector.reduce_max(mx3t[:, i:i + 1], x2c[:], axis=AX.X))

            @block.scalar
            def _(scalar, i=i):
                cv.wait(scalar)
                ser(scalar, ca, lambda: scalar.activation(
                    sqd[:], x2c[:], AF.Square, accum_out=sq3[:, i:i + 1]))

            @block.sync
            def _(sync, i=i):
                cv.wait(sync)
                so.dma(sync.dma_start(
                    out=x2_o.rearrange("(t p) d -> t p d", p=128)[i], in_=x2c[:]))

        p.barrier()
        esC.close()

        @block.vector
        def _(vector):
            ser(vector, cv, lambda: vector.tensor_scalar(st3[:, 0:NT], sum3[:], INVD, None, ALU.mult))
            ser(vector, cv, lambda: vector.tensor_tensor(tmpn[:], st3[:, 0:NT], st3[:, 0:NT], ALU.mult))
            ser(vector, cv, lambda: vector.tensor_scalar(sq3[:], sq3[:], INVD, None, ALU.mult))
            ser(vector, cv, lambda: vector.tensor_tensor(tmpn[:], sq3[:], tmpn[:], ALU.subtract))

        _ln_rstd(block, ca, cv, tmpn[:], st3[:, NT:2 * NT], epst[:, 0:1])

        esD = contextlib.ExitStack()
        pfold = esD.enter_context(nc.psum_tensor("pfold", [128, 512], F32))

        @block.vector
        def _(vector):
            ca.wait(vector)
            ser(vector, cv, lambda: vector.tensor_tensor(mx3t[:], mx3t[:], st3[:, 0:NT], ALU.subtract))
            ser(vector, cv, lambda: vector.tensor_tensor(mx3t[:], mx3t[:], st3[:, NT:2 * NT], ALU.mult))
            ser(vector, cv, lambda: vector.reduce_max(rmax[:], mx3t[:], axis=AX.X))

        _fold_max(p, ident, pfold, rmax, rowm, gmx, cp, cv)

        @block.sync
        def _(sync):
            cv.wait(sync)
            so.dma(sync.dma_start(out=st_o, in_=st3[:]))
            so.dma(sync.dma_start(out=mx_o, in_=gmx[:]))
            so.wait(sync)
        esD.close()
        esQ.close()
    return nc



def _rope_tables():
    half = 512
    inv_freq = 10000.0 ** (-np.arange(half, dtype=np.float64) / half)
    ang = np.arange(S, dtype=np.float64)[:, None] * inv_freq[None, :]
    cos = np.rint(np.cos(ang)).astype(ml_dtypes.bfloat16)
    sin = np.rint(np.sin(ang)).astype(ml_dtypes.bfloat16)
    return np.ascontiguousarray(cos.T), np.ascontiguousarray(sin.T)


_NC_CACHE = {}


def _get_nc(name, builder):
    if name not in _NC_CACHE:
        nc = bass.Bass("TRN2", target_bir_lowering=False)
        _NC_CACHE[name] = builder(nc)
    return _NC_CACHE[name]


def _run(name, builder, in_maps):
    nc = _get_nc(name, builder)
    res = run_bass_kernel_spmd(nc, in_maps, core_ids=list(range(8)))
    return res.results


def kernel(L0_attn_x, L0_attn_g_norm, L0_attn_w_qkv, L0_attn_b_qkv,
           L0_attn_w_proj, L0_attn_b_proj, L0_ffn_w1, L0_ffn_b1,
           L0_ffn_w2, L0_ffn_b2):
    xs = [np.ascontiguousarray(np.asarray(L0_attn_x)[c], dtype=np.float32) for c in range(8)]
    w1 = np.ascontiguousarray(np.asarray(L0_ffn_w1, dtype=np.float32))
    b1 = np.ascontiguousarray(np.asarray(L0_ffn_b1, dtype=np.float32).reshape(32, 128))
    w2 = np.ascontiguousarray(np.asarray(L0_ffn_w2, dtype=np.float32))
    b2 = np.ascontiguousarray(np.asarray(L0_ffn_b2, dtype=np.float32).reshape(8, 128))
    wqkv = np.ascontiguousarray(np.asarray(L0_attn_w_qkv, dtype=np.float32))
    bqkv = np.ascontiguousarray(np.asarray(L0_attn_b_qkv, dtype=np.float32).reshape(24, 128))
    bqkv_r = np.ascontiguousarray(np.asarray(L0_attn_b_qkv, dtype=np.float32).reshape(1, 3 * DM))
    wproj = np.ascontiguousarray(np.asarray(L0_attn_w_proj, dtype=np.float32))
    bproj_r = np.ascontiguousarray(np.asarray(L0_attn_b_proj, dtype=np.float32).reshape(1, DM))
    g_r = np.ascontiguousarray(np.asarray(L0_attn_g_norm, dtype=np.float32).reshape(1, DM))
    cosT, sinT = _rope_tables()
    ident = np.eye(128, dtype=np.float32)
    identb = np.eye(128, dtype=ml_dtypes.bfloat16)
    onesf = np.ones((1, 128), dtype=np.float32)
    onesb = np.ones((1, 128), dtype=ml_dtypes.bfloat16)

    r0 = _run("stats", build_stats, [
        {"x": xs[c], "ident": ident} for c in range(8)])
    gmax1 = np.concatenate([r0[c]["mx"] for c in range(8)], axis=1)

    r1 = _run("m1", build_m1, [
        {"x": xs[c], "st": r0[c]["st"], "gmax": gmax1, "w1": w1, "b1": b1,
         "ident": ident, "identb": identb, "onesf": onesf} for c in range(8)])
    gmax2 = np.concatenate([r1[c]["mx"] for c in range(8)], axis=1)

    r2 = _run("mid", build_mid, [
        {"x": xs[c], "H": r1[c]["H"], "gmax": gmax2, "w2": w2, "b2": b2,
         "wqkv": wqkv, "bqkv": bqkv, "bqkv_r": bqkv_r, "wproj": wproj,
         "bproj_r": bproj_r, "g_r": g_r, "cosT": cosT, "sinT": sinT,
         "ident": ident, "identb": identb, "onesf": onesf, "onesb": onesb} for c in range(8)])
    gmax3 = np.concatenate([r2[c]["mx"] for c in range(8)], axis=1)

    r3 = _run("m1", build_m1, [
        {"x": r2[c]["x2"], "st": r2[c]["st3"], "gmax": gmax3, "w1": w1, "b1": b1,
         "ident": ident, "identb": identb, "onesf": onesf} for c in range(8)])
    gmax4 = np.concatenate([r3[c]["mx"] for c in range(8)], axis=1)

    r4 = _run("m2f", build_m2f, [
        {"x": r2[c]["x2"], "H": r3[c]["H"], "gmax": gmax4, "w2": w2, "b2": b2,
         "ident": ident, "identb": identb, "onesf": onesf} for c in range(8)])

    out = np.stack([r4[c]["y"] for c in range(8)], axis=0)
    return out.astype(np.float32)


# version 09
# speedup vs baseline: 1.4790x; 1.0376x over previous
import contextlib

import numpy as np
import ml_dtypes

import concourse.bass as bass
import concourse.mybir as mybir
from concourse.bass_utils import run_bass_kernel_spmd

F32 = mybir.dt.float32
BF16 = mybir.dt.bfloat16
I32 = mybir.dt.int32
AF = mybir.ActivationFunctionType
ALU = mybir.AluOpType
AX = mybir.AxisListType

S = 2048
DM = 1024
FF = 4096
NT = S // 128
ND = DM // 128
NF = FF // 128
EPS_LN = 1e-5
EPS_RMS = 1e-6
EPS_Q = 1e-8
LN3 = 1.0986122886681098
INV255 = float(np.float32(1.0) / np.float32(255.0))
INV127 = float(np.float32(1.0) / np.float32(127.0))
SCALE_SM = float(np.float32(LN3) / np.float32(32.0))
INVD = 1.0 / DM


class Ctr:
    def __init__(self, es, nc, name):
        self.h = es.enter_context(nc.semaphore(name))
        self.n = 0

    def inc(self, instr, k=1):
        instr.then_inc(self.h, k)
        self.n += k
        return instr

    def dma(self, instr):
        return self.inc(instr, 16)

    def wait(self, eng, at=None):
        eng.wait_ge(self.h, self.n if at is None else at)


class Prog:

    def __init__(self, es, nc):
        self.nc = nc
        self.es = es
        self.block = es.enter_context(nc.Block())
        self.ctrs = []

    def ctr(self, name):
        c = Ctr(self.es, self.nc, name)
        self.ctrs.append(c)
        return c

    def wait_all(self, eng):
        for c in self.ctrs:
            if c.n > 0:
                c.wait(eng)

    def barrier(self, engines=("sync", "scalar", "vector", "tensor", "gpsimd")):
        b = self.block
        if "sync" in engines:
            @b.sync
            def _(sync):
                self.wait_all(sync)
        if "scalar" in engines:
            @b.scalar
            def _(scalar):
                self.wait_all(scalar)
        if "vector" in engines:
            @b.vector
            def _(vector):
                self.wait_all(vector)
        if "tensor" in engines:
            @b.tensor
            def _(tensor):
                self.wait_all(tensor)
        if "gpsimd" in engines:
            @b.gpsimd
            def _(gpsimd):
                self.wait_all(gpsimd)


def ser(eng, ctr, emit):
    ctr.wait(eng)
    return ctr.inc(emit())


def _round_to_bf16(vector, cv, scratch_f32, out_bf, lo=None, hi=None):
    if lo is not None:
        ser(vector, cv, lambda: vector.tensor_scalar(
            scratch_f32[:], scratch_f32[:], float(lo), float(hi), ALU.max, ALU.min))
    ser(vector, cv, lambda: vector.tensor_copy(scratch_f32.bitcast(I32)[:], scratch_f32[:]))
    ser(vector, cv, lambda: vector.tensor_copy(out_bf, scratch_f32.bitcast(I32)[:]))


def _ln_rstd(block, ca, cv, var_ap, out_ap, eps_ap):
    @block.scalar
    def _(scalar):
        cv.wait(scalar)
        ser(scalar, ca, lambda: scalar.activation(var_ap, var_ap, AF.Ln, bias=eps_ap))
        ser(scalar, ca, lambda: scalar.activation(out_ap, var_ap, AF.Exp, scale=-0.5))


def _fold_max(p, ident, pbank, rmax, rowm, gmx, cp, cv):
    block = p.block

    @block.tensor
    def _(tensor):
        cv.wait(tensor)
        cp.inc(tensor.transpose(pbank[0:1, 0:128], rmax[:], ident[:]))

    @block.vector
    def _(vector):
        cp.wait(vector)
        ser(vector, cv, lambda: vector.tensor_copy(rowm[:], pbank[0:1, 0:128]))
        ser(vector, cv, lambda: vector.reduce_max(gmx[:], rowm[:], axis=AX.X))


def _quant_scalars(p, sgm, srow, onesf, invs_b, s_b, pbank, cv, cp):
    block = p.block

    @block.vector
    def _(vector):
        ser(vector, cv, lambda: vector.reduce_max(srow[:, 0:1], sgm[:], axis=AX.X))
        ser(vector, cv, lambda: vector.tensor_scalar(srow[:, 0:1], srow[:, 0:1], EPS_Q, None, ALU.max))
        ser(vector, cv, lambda: vector.reciprocal(srow[:, 1:2], srow[:, 0:1]))
        ser(vector, cv, lambda: vector.tensor_scalar(srow[:, 1:2], srow[:, 1:2], 255.0, None, ALU.mult))
        ser(vector, cv, lambda: vector.tensor_scalar(srow[:, 2:3], srow[:, 0:1], INV255, None, ALU.mult))

    @block.tensor
    def _(tensor):
        cv.wait(tensor)
        tensor.matmul(pbank[:, 0:1], onesf[:], srow[:, 1:2], start=True, stop=True)
        cp.inc(tensor.matmul(pbank[:, 1:2], onesf[:], srow[:, 2:3], start=True, stop=True))

    @block.vector
    def _(vector):
        cp.wait(vector)
        ser(vector, cv, lambda: vector.tensor_copy(invs_b[:], pbank[:, 0:1]))
        ser(vector, cv, lambda: vector.tensor_copy(s_b[:], pbank[:, 1:2]))



def build_stats(nc):
    x = nc.dram_tensor("x", [S, DM], F32, kind="ExternalInput").ap()
    ident_d = nc.dram_tensor("ident", [128, 128], F32, kind="ExternalInput").ap()
    st_o = nc.dram_tensor("st", [128, 2 * NT], F32, kind="ExternalOutput").ap()
    mx_o = nc.dram_tensor("mx", [1, 1], F32, kind="ExternalOutput").ap()

    es = contextlib.ExitStack()
    with es:
        sx = es.enter_context(nc.sbuf_tensor("sx_s", [128, NT * DM], F32))
        ident = es.enter_context(nc.sbuf_tensor("identS", [128, 128], F32))
        sq = es.enter_context(nc.sbuf_tensor("sq", [128, DM], BF16))
        sums = es.enter_context(nc.sbuf_tensor("sums", [128, NT], F32))
        sqs = es.enter_context(nc.sbuf_tensor("sqs", [128, NT], F32))
        mxs = es.enter_context(nc.sbuf_tensor("mxs", [128, NT], F32))
        stt = es.enter_context(nc.sbuf_tensor("stt", [128, 2 * NT], F32))
        tmp = es.enter_context(nc.sbuf_tensor("tmp", [128, NT], F32))
        tmax = es.enter_context(nc.sbuf_tensor("tmax", [128, NT], F32))
        rmax = es.enter_context(nc.sbuf_tensor("rmax", [128, 1], F32))
        rowm = es.enter_context(nc.sbuf_tensor("rowm", [1, 128], F32))
        gmx = es.enter_context(nc.sbuf_tensor("gmx", [1, 1], F32))
        epst = es.enter_context(nc.sbuf_tensor("epst", [128, 1], F32))
        ptr = es.enter_context(nc.psum_tensor("ptr", [128, 512], F32))

        p = Prog(es, nc)
        block = p.block
        ld = p.ctr("ld")
        ca = p.ctr("ca")
        cv = p.ctr("cv")
        cp = p.ctr("cp")
        so = p.ctr("so")

        @block.sync
        def _(sync):
            ld.dma(sync.dma_start(out=sx[:], in_=x.rearrange("(t p) d -> p t d", p=128)))
            ld.dma(sync.dma_start(out=ident[:], in_=ident_d))

        @block.scalar
        def _(scalar):
            ld.wait(scalar)
            for i in range(NT):
                ser(scalar, ca, lambda i=i: scalar.activation(
                    sq[:], sx[:, i * DM:(i + 1) * DM], AF.Square, accum_out=sqs[:, i:i + 1]))

        @block.vector
        def _(vector):
            ld.wait(vector)
            ser(vector, cv, lambda: vector.memset(epst[:], EPS_LN))
            for i in range(NT):
                ser(vector, cv, lambda i=i: vector.reduce_sum(
                    sums[:, i:i + 1], sx[:, i * DM:(i + 1) * DM], axis=AX.X))
                ser(vector, cv, lambda i=i: vector.reduce_max(
                    mxs[:, i:i + 1], sx[:, i * DM:(i + 1) * DM], axis=AX.X))
            ca.wait(vector)
            ser(vector, cv, lambda: vector.tensor_scalar(stt[:, 0:NT], sums[:], INVD, None, ALU.mult))
            ser(vector, cv, lambda: vector.tensor_tensor(tmp[:], stt[:, 0:NT], stt[:, 0:NT], ALU.mult))
            ser(vector, cv, lambda: vector.tensor_scalar(sqs[:], sqs[:], INVD, None, ALU.mult))
            ser(vector, cv, lambda: vector.tensor_tensor(tmp[:], sqs[:], tmp[:], ALU.subtract))

        _ln_rstd(block, ca, cv, tmp[:], stt[:, NT:2 * NT], epst[:, 0:1])

        @block.vector
        def _(vector):
            ca.wait(vector)
            ser(vector, cv, lambda: vector.tensor_tensor(tmax[:], mxs[:], stt[:, 0:NT], ALU.subtract))
            ser(vector, cv, lambda: vector.tensor_tensor(tmax[:], tmax[:], stt[:, NT:2 * NT], ALU.mult))
            ser(vector, cv, lambda: vector.reduce_max(rmax[:], tmax[:], axis=AX.X))

        _fold_max(p, ident, ptr, rmax, rowm, gmx, cp, cv)

        @block.sync
        def _(sync):
            cv.wait(sync)
            so.dma(sync.dma_start(out=st_o, in_=stt[:]))
            so.dma(sync.dma_start(out=mx_o, in_=gmx[:]))
            so.wait(sync)
    return nc



def build_m1(nc):
    x = nc.dram_tensor("x", [S, DM], F32, kind="ExternalInput").ap()
    st_i = nc.dram_tensor("st", [128, 2 * NT], F32, kind="ExternalInput").ap()
    gmax = nc.dram_tensor("gmax", [1, 8], F32, kind="ExternalInput").ap()
    w1 = nc.dram_tensor("w1", [FF, DM], F32, kind="ExternalInput").ap()
    b1 = nc.dram_tensor("b1", [32, 128], F32, kind="ExternalInput").ap()
    ident_d = nc.dram_tensor("ident", [128, 128], F32, kind="ExternalInput").ap()
    identb_d = nc.dram_tensor("identb", [128, 128], BF16, kind="ExternalInput").ap()
    onesf_d = nc.dram_tensor("onesf", [1, 128], F32, kind="ExternalInput").ap()
    h_o = nc.dram_tensor("H", [NF, 128, S], F32, kind="ExternalOutput").ap()
    mx_o = nc.dram_tensor("mx", [1, 1], F32, kind="ExternalOutput").ap()

    es = contextlib.ExitStack()
    with es:
        sx = es.enter_context(nc.sbuf_tensor("sx_s", [128, NT * DM], F32))
        stt = es.enter_context(nc.sbuf_tensor("stt", [128, 2 * NT], F32))
        sgm = es.enter_context(nc.sbuf_tensor("sgm", [1, 8], F32))
        srow = es.enter_context(nc.sbuf_tensor("srow", [1, 4], F32))
        sb1 = es.enter_context(nc.sbuf_tensor("sb1", [32, 128], F32))
        b1T = es.enter_context(nc.sbuf_tensor("b1T", [128, 32], F32))
        ident = es.enter_context(nc.sbuf_tensor("identS", [128, 128], F32))
        identb = es.enter_context(nc.sbuf_tensor("identbS", [128, 128], BF16))
        onesf = es.enter_context(nc.sbuf_tensor("onesf_s", [1, 128], F32))
        invs_b = es.enter_context(nc.sbuf_tensor("invs_b", [128, 1], F32))
        s_b = es.enter_context(nc.sbuf_tensor("s_b", [128, 1], F32))
        scl = es.enter_context(nc.sbuf_tensor("scl", [128, NT], F32))
        bia = es.enter_context(nc.sbuf_tensor("bia", [128, NT], F32))
        qsc = es.enter_context(nc.sbuf_tensor("qsc", [128, DM], F32))
        xq = es.enter_context(nc.sbuf_tensor("xq", [128, NT * DM], BF16))
        xqT = es.enter_context(nc.sbuf_tensor("xqT", [128, ND * S], BF16))
        wbuf = [es.enter_context(nc.sbuf_tensor(f"wbuf{k}", [128, DM], F32)) for k in range(2)]
        wsc = [es.enter_context(nc.sbuf_tensor(f"wsc{k}", [128, DM], F32)) for k in range(2)]
        wq = [es.enter_context(nc.sbuf_tensor(f"wq{k}", [128, DM], BF16)) for k in range(2)]
        wqT = [es.enter_context(nc.sbuf_tensor(f"wqT{k}", [128, ND * 128], BF16)) for k in range(2)]
        amx = [es.enter_context(nc.sbuf_tensor(f"amx{k}", [128, 1], F32)) for k in range(2)]
        invw = [es.enter_context(nc.sbuf_tensor(f"invw{k}", [128, 1], F32)) for k in range(2)]
        deq = [es.enter_context(nc.sbuf_tensor(f"deq{k}", [128, 1], F32)) for k in range(2)]
        b1q = [es.enter_context(nc.sbuf_tensor(f"b1q{k}", [128, 1], F32)) for k in range(2)]
        tb = [es.enter_context(nc.sbuf_tensor(f"tb{k}", [128, 1], F32)) for k in range(2)]
        deq_all = es.enter_context(nc.sbuf_tensor("deq_all", [128, NF], F32))
        b1q_all = es.enter_context(nc.sbuf_tensor("b1q_all", [128, NF], F32))
        rawmax = es.enter_context(nc.sbuf_tensor("rawmax", [128, NF], F32))
        pm4 = es.enter_context(nc.sbuf_tensor("pm4", [128, 4], F32))
        hst = [es.enter_context(nc.sbuf_tensor(f"hst{k}", [128, S], F32)) for k in range(2)]
        rmax = es.enter_context(nc.sbuf_tensor("rmax", [128, 1], F32))
        tmax = [es.enter_context(nc.sbuf_tensor(f"tmax{k}", [128, 1], F32)) for k in range(2)]
        rowm = es.enter_context(nc.sbuf_tensor("rowm", [1, 128], F32))
        gmx = es.enter_context(nc.sbuf_tensor("gmx", [1, 1], F32))
        pful = [es.enter_context(nc.psum_tensor(f"pful{j}", [128, 2048], F32)) for j in range(2)]
        pmm = [pful[b // 4][:, (b % 4) * 512:(b % 4 + 1) * 512] for b in range(8)]

        p = Prog(es, nc)
        block = p.block
        ld = p.ctr("ld")
        cv = p.ctr("cv")
        ca = p.ctr("ca")
        cp = p.ctr("cp")
        cxb = p.ctr("cxb")
        cpm = p.ctr("cpm")
        cwt = p.ctr("cwt")
        cwte = p.ctr("cwte")
        cwd = [p.ctr(f"cwd{k}") for k in range(2)]
        cwv = [p.ctr(f"cwv{k}") for k in range(2)]
        cwx = [p.ctr(f"cwx{k}") for k in range(2)]
        cmm = [p.ctr(f"cmm{k}") for k in range(8)]
        cev = [p.ctr(f"cev{k}") for k in range(8)]
        chd = [p.ctr(f"chd{k}") for k in range(2)]
        cmx = [p.ctr(f"cmx{k}") for k in range(2)]
        so = p.ctr("so")

        @block.sync
        def _(sync):
            ld.dma(sync.dma_start(out=sx[:], in_=x.rearrange("(t p) d -> p t d", p=128)))
            ld.dma(sync.dma_start(out=stt[:], in_=st_i))
            ld.dma(sync.dma_start(out=sgm[:], in_=gmax))
            ld.dma(sync.dma_start(out=sb1[:], in_=b1))
            ld.dma(sync.dma_start(out=ident[:], in_=ident_d))
            ld.dma(sync.dma_start(out=identb[:], in_=identb_d))
            ld.dma(sync.dma_start(out=onesf[:], in_=onesf_d))

        @block.vector
        def _(vector):
            ld.wait(vector)

        _quant_scalars(p, sgm, srow, onesf, invs_b, s_b, pmm[0], cv, cp)

        @block.tensor
        def _(tensor):
            cv.wait(tensor)
            ld.wait(tensor)
            cp.inc(tensor.transpose(pmm[1][0:128, 0:32], sb1[:], ident[0:32, 0:32]))

        @block.vector
        def _(vector):
            cp.wait(vector)
            ser(vector, cv, lambda: vector.tensor_copy(b1T[:], pmm[1][0:128, 0:32]))
            ser(vector, cv, lambda: vector.tensor_scalar(scl[:], stt[:, NT:2 * NT], invs_b[:, 0:1], None, ALU.mult))
            ser(vector, cv, lambda: vector.tensor_tensor(bia[:], stt[:, 0:NT], scl[:], ALU.mult))
            ser(vector, cv, lambda: vector.tensor_scalar(bia[:], bia[:], -1.0, None, ALU.mult))
            ser(vector, cv, lambda: vector.memset(rmax[:], 0.0))
            cev[0].inc(vector.engine_nop())
            cev[1].inc(vector.engine_nop())

        p.barrier()

        for i in range(NT):
            @block.scalar
            def _(scalar, i=i):
                cv.wait(scalar)
                ser(scalar, ca, lambda: scalar.activation(
                    qsc.bitcast(I32)[:], sx[:, i * DM:(i + 1) * DM], AF.Relu,
                    bias=bia[:, i:i + 1], scale=scl[:, i:i + 1]))

            @block.vector
            def _(vector, i=i):
                ca.wait(vector)
                ser(vector, cv, lambda: vector.tensor_copy(
                    xq[:, i * DM:(i + 1) * DM], qsc.bitcast(I32)[:]))

            @block.sync
            def _(sync, i=i):
                cv.wait(sync)
                for c in range(0, ND, 2):
                    cxb.dma(sync.dma_start_transpose(
                        xqT[:, c * S + i * 128:c * S + (i + 1) * 128],
                        xq[:, i * DM + c * 128:i * DM + (c + 1) * 128]))

            @block.scalar
            def _(scalar, i=i):
                cv.wait(scalar)
                for c in range(1, ND, 2):
                    cxb.dma(scalar.dma_start_transpose(
                        xqT[:, c * S + i * 128:c * S + (i + 1) * 128],
                        xq[:, i * DM + c * 128:i * DM + (c + 1) * 128]))

        p.barrier()

        XQT_ALL = cxb.n
        CEV_BASE = [1, 1, 0, 0, 0, 0, 0, 0]
        def emit_lqx(fc):
            j = fc % 2
            use = fc // 2
            pg = j * 4

            @block.sync
            def _(sync, fc=fc, j=j, use=use):
                if use > 0:
                    cwv[j].wait(sync, at=use)
                cwd[j].dma(sync.dma_start(out=wbuf[j][:], in_=w1[fc * 128:(fc + 1) * 128, :]))

            @block.vector
            def _(vector, fc=fc, j=j, use=use):
                cwd[j].wait(vector, at=16 * (use + 1))
                if use > 0:
                    cwt.wait(vector, at=fc - 1)
                ser(vector, cv, lambda: vector.reduce_max(amx[j][:], wbuf[j][:], axis=AX.X, apply_absolute_value=True))
                ser(vector, cv, lambda: vector.tensor_scalar(amx[j][:], amx[j][:], EPS_Q, None, ALU.max))
                ser(vector, cv, lambda: vector.reciprocal(invw[j][:], amx[j][:]))
                ser(vector, cv, lambda: vector.tensor_scalar(invw[j][:], invw[j][:], 127.0, None, ALU.mult))
                ser(vector, cv, lambda: vector.tensor_scalar(deq[j][:], amx[j][:], INV127, s_b[:, 0:1], ALU.mult, ALU.mult))
                ser(vector, cv, lambda: vector.tensor_scalar(tb[j][:], b1T[:, fc:fc + 1], invs_b[:, 0:1], invw[j][:, 0:1], ALU.mult, ALU.mult))
                ser(vector, cv, lambda: vector.tensor_copy(tb[j].bitcast(I32)[:], tb[j][:]))
                ser(vector, cv, lambda: vector.tensor_copy(b1q[j][:], tb[j].bitcast(I32)[:]))
                ser(vector, cv, lambda: vector.tensor_tensor(b1q[j][:], b1q[j][:], deq[j][:], ALU.mult))
                ser(vector, cv, lambda: vector.tensor_copy(deq_all[:, fc:fc + 1], deq[j][:]))
                ser(vector, cv, lambda: vector.tensor_copy(b1q_all[:, fc:fc + 1], b1q[j][:]))
                ser(vector, cv, lambda: vector.tensor_scalar(wsc[j].bitcast(I32)[:], wbuf[j][:], invw[j][:, 0:1], None, ALU.mult))
                ser(vector, cv, lambda: vector.tensor_copy(wq[j][:], wsc[j].bitcast(I32)[:]))
                cwv[j].inc(vector.engine_nop())

            CVQS[fc] = cv.n

        def emit_post(fc):
            j = fc % 2
            use = fc // 2
            pg = j * 4

            @block.vector
            def _(vector, fc=fc, j=j, use=use, pg=pg):
                for tq in range(4):
                    cmm[pg + tq].wait(vector, at=use + 1)
                cv.wait(vector)
                cpm.inc(vector.reduce_max(rawmax[:, fc:fc + 1], pful[j][:], axis=AX.X))

            @block.scalar
            def _(scalar, fc=fc, j=j, use=use, pg=pg):
                cpm.wait(scalar, at=fc + 1)
                if use > 0:
                    chd[j].wait(scalar, at=16 * use)
                for tq in range(4):
                    bank = pg + tq
                    cmm[bank].wait(scalar, at=use + 1)
                    cev[bank].inc(scalar.activation(hst[j][:, tq * 512:(tq + 1) * 512],
                                                    pmm[bank][:], AF.Relu,
                                                    bias=b1q_all[:, fc:fc + 1],
                                                    scale=deq_all[:, fc:fc + 1]))

            @block.sync
            def _(sync, fc=fc, j=j, use=use, pg=pg):
                for tq in range(4):
                    cev[pg + tq].wait(sync, at=CEV_BASE[pg + tq] + use + 1)
                chd[j].dma(sync.dma_start(out=h_o[fc], in_=hst[j][:]))

        def emit_wt(fc):
            j = fc % 2
            use = fc // 2
            pg = j * 4

            @block.tensor
            def _(tensor, fc=fc, j=j, use=use, pg=pg):
                cv.wait(tensor, at=CVQS[fc])
                cev[pg].wait(tensor, at=CEV_BASE[pg] + use)
                pview = pmm[pg].bitcast(BF16)
                for c in range(ND):
                    ins = tensor.transpose(pview[:, c * 128:(c + 1) * 128],
                                           wq[j][:, c * 128:(c + 1) * 128], identb[:])
                    if c == ND - 1:
                        cwt.inc(ins)

            @block.vector
            def _(vector, fc=fc, j=j, pg=pg):
                cwt.wait(vector, at=fc + 1)
                cv.wait(vector)
                cwte.inc(vector.tensor_copy(wqT[j][:], pmm[pg].bitcast(BF16)[:, 0:ND * 128]))

        def emit_mms(fc):
            j = fc % 2
            use = fc // 2
            pg = j * 4

            @block.tensor
            def _(tensor, fc=fc, j=j, use=use, pg=pg):
                cwte.wait(tensor, at=fc + 1)
                cxb.wait(tensor, at=XQT_ALL)
                for tq in range(4):
                    bank = pg + tq
                    cev[bank].wait(tensor, at=CEV_BASE[bank] + use)
                    for d in range(ND):
                        ins = tensor.matmul(pmm[bank][:],
                                            wqT[j][:, d * 128:(d + 1) * 128],
                                            xqT[:, d * S + tq * 512:d * S + (tq + 1) * 512],
                                            start=(d == 0), stop=(d == ND - 1))
                        if d == ND - 1:
                            cmm[bank].inc(ins)

        CVQS = {}
        emit_lqx(0)
        emit_lqx(1)
        emit_wt(0)
        emit_wt(1)
        for k2 in range(0, NF, 2):
            if k2 + 2 < NF:
                emit_lqx(k2 + 2)
                emit_lqx(k2 + 3)
            emit_mms(k2)
            emit_mms(k2 + 1)
            emit_post(k2)
            emit_post(k2 + 1)
            if k2 + 2 < NF:
                emit_wt(k2 + 2)
                emit_wt(k2 + 3)

        @block.vector
        def _(vector):
            ser(vector, cv, lambda: vector.tensor_tensor(rawmax[:], rawmax[:], deq_all[:], ALU.mult))
            ser(vector, cv, lambda: vector.tensor_tensor(rawmax[:], rawmax[:], b1q_all[:], ALU.add))
            ser(vector, cv, lambda: vector.tensor_scalar(rawmax[:], rawmax[:], 0.0, None, ALU.max))
            ser(vector, cv, lambda: vector.reduce_max(rmax[:], rawmax[:], axis=AX.X))

        p.barrier()
        _fold_max(p, ident, pmm[0], rmax, rowm, gmx, cp, cv)

        @block.sync
        def _(sync):
            cv.wait(sync)
            so.dma(sync.dma_start(out=mx_o, in_=gmx[:]))
            chd[0].wait(sync)
            chd[1].wait(sync)
            so.wait(sync)
    return nc



def _emit_m2(nc, es, p, ctx, *, h_in, w2_in, x_in, post_tile):
    block = p.block
    cv, ca, cp = ctx["cv"], ctx["ca"], ctx["cp"]
    ident, onesf = ctx["ident"], ctx["onesf"]
    invs_b, s_b = ctx["invs_b"], ctx["s_b"]
    sb2 = ctx["sb2"]
    rows_d = ctx["rows_d"]
    pmm = ctx["pmm"]

    wbuf = es.enter_context(nc.sbuf_tensor("m2wbuf", [128, FF], F32))
    wsc = es.enter_context(nc.sbuf_tensor("m2wsc", [128, FF], F32))
    wqb = es.enter_context(nc.sbuf_tensor("m2wqb", [128, FF], BF16))
    w2q = es.enter_context(nc.sbuf_tensor("w2q_s", [128, NF * DM], BF16))
    hbuf = [es.enter_context(nc.sbuf_tensor(f"m2hb{k}", [128, FF], F32)) for k in range(2)]
    hqb = [es.enter_context(nc.sbuf_tensor(f"m2hq{k}", [128, FF], BF16)) for k in range(2)]
    xtb = [es.enter_context(nc.sbuf_tensor(f"m2xt{k}", [128, DM], F32)) for k in range(2)]
    b2T = es.enter_context(nc.sbuf_tensor("b2T", [128, 8], F32))
    sw2a = es.enter_context(nc.sbuf_tensor("sw2a", [128, 8], F32))
    w2inv = es.enter_context(nc.sbuf_tensor("w2inv", [128, 8], F32))
    w2deq = es.enter_context(nc.sbuf_tensor("w2deq", [128, 8], F32))
    b2i = es.enter_context(nc.sbuf_tensor("b2i", [128, 8], F32))
    r8a = es.enter_context(nc.sbuf_tensor("r8a", [8, 128], F32))
    r8b = es.enter_context(nc.sbuf_tensor("r8b", [8, 128], F32))
    rowln = es.enter_context(nc.sbuf_tensor("rowln", [1, DM], F32))
    rowln2 = es.enter_context(nc.sbuf_tensor("rowln2", [1, DM], F32))
    scale_b = es.enter_context(nc.sbuf_tensor("scale_b", [128, DM], F32))
    bias_b = es.enter_context(nc.sbuf_tensor("bias_b", [128, DM], F32))
    x2s = [es.enter_context(nc.sbuf_tensor(f"x2s{k}", [128, DM], F32)) for k in range(2)]
    wsg = [x2s[0].bitcast(BF16), x2s[1].bitcast(BF16)]

    cwd = p.ctr("m2cwd")
    cwxg = [p.ctr(f"m2cwx{g}") for g in range(2)]
    crw = p.ctr("m2crw")
    chg = [p.ctr(f"m2chg{k}") for k in range(2)]
    cxg = [p.ctr(f"m2cxg{k}") for k in range(2)]
    cmm = p.ctr("m2cmm")

    @block.tensor
    def _(tensor):
        cv.wait(tensor)
        ctx["ld"].wait(tensor)
        cp.inc(tensor.transpose(pmm[1][0:128, 0:8], sb2[:], ident[0:8, 0:8]))

    @block.vector
    def _(vector):
        cp.wait(vector)
        ser(vector, cv, lambda: vector.tensor_copy(b2T[:], pmm[1][0:128, 0:8]))

    p.barrier()

    identb = ctx["identb"]
    cpt = p.ctr("m2cpt")
    CVQ = {}
    CVB = {}
    for dc in range(8):
        @block.sync
        def _(sync, dc=dc):
            if dc > 0:
                cv.wait(sync, at=CVQ[dc - 1])
            cwd.dma(sync.dma_start(out=wbuf[:], in_=w2_in[dc * 128:(dc + 1) * 128, :]))

        @block.vector
        def _(vector, dc=dc):
            cwd.wait(vector)
            if dc > 0:
                cpt.wait(vector, at=dc)
            ser(vector, cv, lambda: vector.reduce_max(sw2a[:, dc:dc + 1], wbuf[:], axis=AX.X, apply_absolute_value=True))
            ser(vector, cv, lambda: vector.tensor_scalar(sw2a[:, dc:dc + 1], sw2a[:, dc:dc + 1], EPS_Q, None, ALU.max))
            ser(vector, cv, lambda: vector.reciprocal(w2inv[:, dc:dc + 1], sw2a[:, dc:dc + 1]))
            ser(vector, cv, lambda: vector.tensor_scalar(w2inv[:, dc:dc + 1], w2inv[:, dc:dc + 1], 127.0, None, ALU.mult))
            ser(vector, cv, lambda: vector.tensor_scalar(wsc.bitcast(I32)[:], wbuf[:], w2inv[:, dc:dc + 1], None, ALU.mult))
            ser(vector, cv, lambda: vector.tensor_copy(wqb[:], wsc.bitcast(I32)[:]))
            CVQ[dc] = cv.n

        @block.tensor
        def _(tensor, dc=dc):
            cv.wait(tensor, at=CVQ[dc])
            if dc > 0:
                cv.wait(tensor, at=CVB[dc - 1])
            for b in range(4):
                pv = pmm[b].bitcast(BF16)
                for c in range(8):
                    fc = b * 8 + c
                    ins = tensor.transpose(pv[:, c * 128:(c + 1) * 128],
                                           wqb[:, fc * 128:(fc + 1) * 128], identb[:])
            cpt.inc(ins)

        @block.vector
        def _(vector, dc=dc):
            cpt.wait(vector, at=dc + 1)
            if dc > 0:
                if dc % 4 == 0:
                    cwxg[dc // 4 - 1].wait(vector, at=2048)
                else:
                    cwxg[dc // 4].wait(vector, at=512 * (dc % 4))
            for b in range(4):
                ser(vector, cv, lambda b=b: vector.tensor_copy(
                    wsg[b // 2][:, (b % 2) * 1024:(b % 2 + 1) * 1024],
                    pmm[b].bitcast(BF16)[:, 0:1024]))
            CVB[dc] = cv.n

        @block.sync
        def _(sync, dc=dc):
            cv.wait(sync, at=CVB[dc])
            for fc in range(0, NF, 2):
                cwxg[dc // 4].dma(sync.dma_start(
                    out=w2q[:, fc * DM + dc * 128:fc * DM + (dc + 1) * 128],
                    in_=wsg[fc // 16][:, (fc % 16) * 128:(fc % 16 + 1) * 128]))

        @block.scalar
        def _(scalar, dc=dc):
            cv.wait(scalar, at=CVB[dc])
            for fc in range(1, NF, 2):
                cwxg[dc // 4].dma(scalar.dma_start(
                    out=w2q[:, fc * DM + dc * 128:fc * DM + (dc + 1) * 128],
                    in_=wsg[fc // 16][:, (fc % 16) * 128:(fc % 16 + 1) * 128]))

    p.barrier()

    @block.vector
    def _(vector):
        ser(vector, cv, lambda: vector.tensor_scalar(w2deq[:], sw2a[:], INV127, None, ALU.mult))
        ser(vector, cv, lambda: vector.tensor_scalar(w2deq[:], w2deq[:], s_b[:, 0:1], None, ALU.mult))
        ser(vector, cv, lambda: vector.tensor_scalar(w2deq[:], w2deq[:], 0.5, None, ALU.mult))
        ser(vector, cv, lambda: vector.tensor_scalar(b2i[:], b2T[:], invs_b[:, 0:1], None, ALU.mult))
        ser(vector, cv, lambda: vector.tensor_tensor(b2i[:], b2i[:], w2inv[:], ALU.mult))
        ser(vector, cv, lambda: vector.tensor_copy(b2i.bitcast(I32)[:], b2i[:]))
        ser(vector, cv, lambda: vector.tensor_copy(b2i[:], b2i.bitcast(I32)[:]))
        ser(vector, cv, lambda: vector.tensor_tensor(b2i[:], b2i[:], w2deq[:], ALU.mult))

    @block.tensor
    def _(tensor):
        cv.wait(tensor)
        tensor.transpose(pmm[0][0:8, 0:128], w2deq[:], ident[:])
        cp.inc(tensor.transpose(pmm[1][0:8, 0:128], b2i[:], ident[:]))

    @block.vector
    def _(vector):
        cp.wait(vector)
        ser(vector, cv, lambda: vector.tensor_copy(r8a[:], pmm[0][0:8, 0:128]))
        ser(vector, cv, lambda: vector.tensor_copy(r8b[:], pmm[1][0:8, 0:128]))

    @block.sync
    def _(sync):
        cv.wait(sync)
        crw.dma(sync.dma_start(out=rows_d[0], in_=r8a[:]))
        crw.dma(sync.dma_start(out=rows_d[1], in_=r8b[:]))
        crw.wait(sync)
        crw.dma(sync.dma_start(out=rowln[:], in_=rows_d.rearrange("a b c -> a (b c)")[0:1, :]))
        crw.dma(sync.dma_start(out=rowln2[:], in_=rows_d.rearrange("a b c -> a (b c)")[1:2, :]))

    @block.tensor
    def _(tensor):
        crw.wait(tensor)
        tensor.matmul(pmm[0][:, 0:512], onesf[:], rowln[:, 0:512], start=True, stop=True)
        tensor.matmul(pmm[1][:, 0:512], onesf[:], rowln[:, 512:1024], start=True, stop=True)
        tensor.matmul(pmm[2][:, 0:512], onesf[:], rowln2[:, 0:512], start=True, stop=True)
        cp.inc(tensor.matmul(pmm[3][:, 0:512], onesf[:], rowln2[:, 512:1024], start=True, stop=True))

    @block.vector
    def _(vector):
        cp.wait(vector)
        ser(vector, cv, lambda: vector.tensor_copy(scale_b[:, 0:512], pmm[0][:, 0:512]))
        ser(vector, cv, lambda: vector.tensor_copy(scale_b[:, 512:1024], pmm[1][:, 0:512]))
        ser(vector, cv, lambda: vector.tensor_copy(bias_b[:, 0:512], pmm[2][:, 0:512]))
        ser(vector, cv, lambda: vector.tensor_copy(bias_b[:, 512:1024], pmm[3][:, 0:512]))

    CVROWS = cv.n

    CAH, CVQ2, CVE, PT = {}, {}, {}, {}

    def emit_load(i):
        j = i % 2

        @block.gpsimd
        def _(gpsimd, i=i, j=j):
            if i >= 2:
                cv.wait(gpsimd, at=CVE[i - 2])
            chg[j].dma(gpsimd.dma_start(
                out=hbuf[j][:],
                in_=h_in.rearrange("c p t -> p c t")[:, :, i * 128:(i + 1) * 128]))
            cxg[j].dma(gpsimd.dma_start(
                out=xtb[j][:], in_=x_in.rearrange("(t p) d -> t p d", p=128)[i]))

    def emit_quant(i):
        j = i % 2

        @block.scalar
        def _(scalar, i=i, j=j):
            chg[j].wait(scalar, at=16 * (i // 2 + 1))
            if i >= 2:
                cv.wait(scalar, at=CVQ2[i - 2])
            ser(scalar, ca, lambda: scalar.activation(
                hbuf[j].bitcast(I32)[:], hbuf[j][:], AF.Copy, scale=invs_b[:, 0:1]))
        CAH[i] = ca.n

        @block.vector
        def _(vector, i=i, j=j):
            ca.wait(vector, at=CAH[i])
            if i >= 2:
                cmm.wait(vector, at=2 * (i - 1))
            ser(vector, cv, lambda: vector.tensor_copy(hqb[j][:], hbuf[j].bitcast(I32)[:]))
        CVQ2[i] = cv.n

    def emit_mm(i):
        j = i % 2

        @block.tensor
        def _(tensor, i=i, j=j):
            cv.wait(tensor, at=CVQ2[i])
            cv.wait(tensor, at=CVE[i - 2] if i >= 2 else CVROWS)
            for nb in range(2):
                cwxg[nb].wait(tensor)
                bank = 2 * j + nb
                for fc in range(NF):
                    ins = tensor.matmul(pmm[bank][:],
                                        hqb[j][:, fc * 128:(fc + 1) * 128],
                                        w2q[:, fc * DM + nb * 512:fc * DM + (nb + 1) * 512],
                                        start=(fc == 0), stop=(fc == NF - 1))
                    if fc == NF - 1:
                        cmm.inc(ins)

    def emit_epi(i):
        j = i % 2

        @block.vector
        def _(vector, i=i, j=j):
            cmm.wait(vector, at=2 * (i + 1))
            cxg[j].wait(vector, at=16 * (i // 2 + 1))
            for ctr_, at_ in PT.get(i - 2, []):
                ctr_.wait(vector, at=at_)
            ser(vector, cv, lambda: vector.scalar_tensor_tensor(
                x2s[j][:, 0:512], pmm[2 * j][:], 1.0, scale_b[:, 0:512], ALU.mult, ALU.mult))
            ser(vector, cv, lambda: vector.scalar_tensor_tensor(
                x2s[j][:, 512:1024], pmm[2 * j + 1][:], 1.0, scale_b[:, 512:1024], ALU.mult, ALU.mult))
            ser(vector, cv, lambda: vector.tensor_tensor(x2s[j][:], x2s[j][:], bias_b[:], ALU.add))
            ser(vector, cv, lambda: vector.tensor_tensor(x2s[j][:], x2s[j][:], xtb[j][:], ALU.add))
        CVE[i] = cv.n
        PT[i] = post_tile(i, x2s[j]) or []

    emit_load(0)
    emit_load(1)
    emit_quant(0)
    emit_quant(1)
    for i in range(NT):
        emit_mm(i)
        emit_epi(i)
        if i + 2 < NT:
            emit_load(i + 2)
            emit_quant(i + 2)



def build_m2f(nc):
    x = nc.dram_tensor("x", [S, DM], F32, kind="ExternalInput").ap()
    h_in = nc.dram_tensor("H", [NF, 128, S], F32, kind="ExternalInput").ap()
    gmax = nc.dram_tensor("gmax", [1, 8], F32, kind="ExternalInput").ap()
    w2 = nc.dram_tensor("w2", [DM, FF], F32, kind="ExternalInput").ap()
    b2 = nc.dram_tensor("b2", [8, 128], F32, kind="ExternalInput").ap()
    ident_d = nc.dram_tensor("ident", [128, 128], F32, kind="ExternalInput").ap()
    identb_d = nc.dram_tensor("identb", [128, 128], BF16, kind="ExternalInput").ap()
    onesf_d = nc.dram_tensor("onesf", [1, 128], F32, kind="ExternalInput").ap()
    y_o = nc.dram_tensor("y", [S, DM], F32, kind="ExternalOutput").ap()
    rows_d = nc.dram_tensor("rows_scratch", [2, 8, 128], F32).ap()

    es = contextlib.ExitStack()
    with es:
        sgm = es.enter_context(nc.sbuf_tensor("sgm", [1, 8], F32))
        srow = es.enter_context(nc.sbuf_tensor("srow", [1, 4], F32))
        sb2 = es.enter_context(nc.sbuf_tensor("sb2", [8, 128], F32))
        ident = es.enter_context(nc.sbuf_tensor("identS", [128, 128], F32))
        identb = es.enter_context(nc.sbuf_tensor("identbS", [128, 128], BF16))
        onesf = es.enter_context(nc.sbuf_tensor("onesf_s", [1, 128], F32))
        invs_b = es.enter_context(nc.sbuf_tensor("invs_b", [128, 1], F32))
        s_b = es.enter_context(nc.sbuf_tensor("s_b", [128, 1], F32))
        pmm = [es.enter_context(nc.psum_tensor(f"pmm{j}", [128, 512], F32)) for j in range(4)]

        p = Prog(es, nc)
        block = p.block
        ld = p.ctr("ld")
        cv = p.ctr("cv")
        ca = p.ctr("ca")
        cp = p.ctr("cp")
        so = [p.ctr(f"so{k}") for k in range(2)]

        @block.sync
        def _(sync):
            ld.dma(sync.dma_start(out=sgm[:], in_=gmax))
            ld.dma(sync.dma_start(out=sb2[:], in_=b2))
            ld.dma(sync.dma_start(out=ident[:], in_=ident_d))
            ld.dma(sync.dma_start(out=identb[:], in_=identb_d))
            ld.dma(sync.dma_start(out=onesf[:], in_=onesf_d))

        @block.vector
        def _(vector):
            ld.wait(vector)

        _quant_scalars(p, sgm, srow, onesf, invs_b, s_b, pmm[0], cv, cp)

        ctx = dict(cv=cv, ca=ca, cp=cp, ld=ld, ident=ident, identb=identb,
                   onesf=onesf, invs_b=invs_b, s_b=s_b, pmm=pmm,
                   rows_d=rows_d, sb2=sb2, srow=srow)

        def post_tile(i, x2_ap):
            j = i % 2

            @block.sync
            def _(sync, i=i, j=j):
                cv.wait(sync)
                so[j].dma(sync.dma_start(
                    out=y_o.rearrange("(t p) d -> t p d", p=128)[i], in_=x2_ap[:]))
            return [(so[j], 16 * (i // 2 + 1))]

        _emit_m2(nc, es, p, ctx, h_in=h_in, w2_in=w2, x_in=x, post_tile=post_tile)

        @block.sync
        def _(sync):
            so[0].wait(sync)
            so[1].wait(sync)
    return nc



def build_mid(nc):
    x = nc.dram_tensor("x", [S, DM], F32, kind="ExternalInput").ap()
    h_in = nc.dram_tensor("H", [NF, 128, S], F32, kind="ExternalInput").ap()
    gmax = nc.dram_tensor("gmax", [1, 8], F32, kind="ExternalInput").ap()
    w2 = nc.dram_tensor("w2", [DM, FF], F32, kind="ExternalInput").ap()
    b2 = nc.dram_tensor("b2", [8, 128], F32, kind="ExternalInput").ap()
    wqkv = nc.dram_tensor("wqkv", [DM, 3 * DM], F32, kind="ExternalInput").ap()
    bqkv = nc.dram_tensor("bqkv", [24, 128], F32, kind="ExternalInput").ap()
    bqkv_r = nc.dram_tensor("bqkv_r", [1, 3 * DM], F32, kind="ExternalInput").ap()
    wproj = nc.dram_tensor("wproj", [DM, DM], F32, kind="ExternalInput").ap()
    bproj_r = nc.dram_tensor("bproj_r", [1, DM], F32, kind="ExternalInput").ap()
    g_r = nc.dram_tensor("g_r", [1, DM], F32, kind="ExternalInput").ap()
    cosT = nc.dram_tensor("cosT", [512, S], BF16, kind="ExternalInput").ap()
    sinT = nc.dram_tensor("sinT", [512, S], BF16, kind="ExternalInput").ap()
    ident_d = nc.dram_tensor("ident", [128, 128], F32, kind="ExternalInput").ap()
    identb_d = nc.dram_tensor("identb", [128, 128], BF16, kind="ExternalInput").ap()
    onesf_d = nc.dram_tensor("onesf", [1, 128], F32, kind="ExternalInput").ap()
    onesb_d = nc.dram_tensor("onesb", [1, 128], BF16, kind="ExternalInput").ap()
    x2_o = nc.dram_tensor("x2", [S, DM], F32, kind="ExternalOutput").ap()
    st_o = nc.dram_tensor("st3", [128, 2 * NT], F32, kind="ExternalOutput").ap()
    mx_o = nc.dram_tensor("mx", [1, 1], F32, kind="ExternalOutput").ap()
    rows_d = nc.dram_tensor("rows_scratch", [2, 8, 128], F32).ap()
    x1s = nc.dram_tensor("x1_scratch", [S, DM], F32).ap()

    es = contextlib.ExitStack()
    with es:
        sgm = es.enter_context(nc.sbuf_tensor("sgm", [1, 8], F32))
        srow = es.enter_context(nc.sbuf_tensor("srow", [1, 4], F32))
        sb2 = es.enter_context(nc.sbuf_tensor("sb2", [8, 128], F32))
        ident = es.enter_context(nc.sbuf_tensor("identS", [128, 128], F32))
        identb = es.enter_context(nc.sbuf_tensor("identbS", [128, 128], BF16))
        onesf = es.enter_context(nc.sbuf_tensor("onesf_s", [1, 128], F32))
        onesb = es.enter_context(nc.sbuf_tensor("onesb_s", [1, 128], BF16))
        invs_b = es.enter_context(nc.sbuf_tensor("invs_b", [128, 1], F32))
        s_b = es.enter_context(nc.sbuf_tensor("s_b", [128, 1], F32))
        st2 = es.enter_context(nc.sbuf_tensor("st2_s", [128, 2 * NT], F32))
        st3 = es.enter_context(nc.sbuf_tensor("st3_s", [128, 2 * NT], F32))
        sum2 = es.enter_context(nc.sbuf_tensor("sum2", [128, NT], F32))
        sq2 = es.enter_context(nc.sbuf_tensor("sq2", [128, NT], F32))
        sum3 = es.enter_context(nc.sbuf_tensor("sum3", [128, NT], F32))
        sq3 = es.enter_context(nc.sbuf_tensor("sq3", [128, NT], F32))
        mx3t = es.enter_context(nc.sbuf_tensor("mx3t", [128, NT], F32))
        tmpn = es.enter_context(nc.sbuf_tensor("tmpn", [128, NT], F32))
        gbc = es.enter_context(nc.sbuf_tensor("gbc", [128, DM], F32))
        sqd = es.enter_context(nc.sbuf_tensor("sqd", [128, DM], BF16))
        rmax = es.enter_context(nc.sbuf_tensor("rmax2", [128, 1], F32))
        rowm = es.enter_context(nc.sbuf_tensor("rowm", [1, 128], F32))
        gmx = es.enter_context(nc.sbuf_tensor("gmx", [1, 1], F32))
        epst = es.enter_context(nc.sbuf_tensor("epst", [128, 2], F32))

        p = Prog(es, nc)
        block = p.block
        ld = p.ctr("ld")
        cv = p.ctr("cv")
        ca = p.ctr("ca")
        cp = p.ctr("cp")
        cpl = p.ctr("cpl")
        so = p.ctr("so")
        cx1 = p.ctr("cx1")
        cxs = [p.ctr(f"cxs{k}") for k in range(2)]

        @block.sync
        def _(sync):
            ld.dma(sync.dma_start(out=sgm[:], in_=gmax))
            ld.dma(sync.dma_start(out=sb2[:], in_=b2))
            ld.dma(sync.dma_start(out=ident[:], in_=ident_d))
            ld.dma(sync.dma_start(out=identb[:], in_=identb_d))
            ld.dma(sync.dma_start(out=onesf[:], in_=onesf_d))
            ld.dma(sync.dma_start(out=onesb[:], in_=onesb_d))

        @block.vector
        def _(vector):
            ld.wait(vector)
            ser(vector, cv, lambda: vector.memset(epst[:, 0:1], EPS_LN))
            ser(vector, cv, lambda: vector.memset(epst[:, 1:2], EPS_RMS))

        esA = contextlib.ExitStack()
        pmm = [esA.enter_context(nc.psum_tensor(f"pmm{j}", [128, 512], F32)) for j in range(4)]

        _quant_scalars(p, sgm, srow, onesf, invs_b, s_b, pmm[0], cv, cp)

        ctx = dict(cv=cv, ca=ca, cp=cp, ld=ld, ident=ident, identb=identb,
                   onesf=onesf, invs_b=invs_b, s_b=s_b, pmm=pmm,
                   rows_d=rows_d, sb2=sb2, srow=srow)

        def post_tile(i, x1_ap):
            j = i % 2

            @block.sync
            def _(sync, i=i, j=j):
                cv.wait(sync)
                cxs[j].dma(sync.dma_start(
                    out=x1s.rearrange("(t p) d -> t p d", p=128)[i], in_=x1_ap[:]))

            @block.vector
            def _(vector, i=i):
                ser(vector, cv, lambda: vector.reduce_sum(sum2[:, i:i + 1], x1_ap[:], axis=AX.X))

            @block.scalar
            def _(scalar, i=i):
                cv.wait(scalar)
                ser(scalar, ca, lambda: scalar.activation(
                    sqd[:], x1_ap[:], AF.Square, accum_out=sq2[:, i:i + 1]))
            return [(cxs[j], 16 * (i // 2 + 1)), (ca, ca.n)]

        _emit_m2(nc, esA, p, ctx, h_in=h_in, w2_in=w2, x_in=x, post_tile=post_tile)
        p.barrier()

        @block.vector
        def _(vector):
            ser(vector, cv, lambda: vector.tensor_scalar(st2[:, 0:NT], sum2[:], INVD, None, ALU.mult))
            ser(vector, cv, lambda: vector.tensor_tensor(tmpn[:], st2[:, 0:NT], st2[:, 0:NT], ALU.mult))
            ser(vector, cv, lambda: vector.tensor_scalar(sq2[:], sq2[:], INVD, None, ALU.mult))
            ser(vector, cv, lambda: vector.tensor_tensor(tmpn[:], sq2[:], tmpn[:], ALU.subtract))

        _ln_rstd(block, ca, cv, tmpn[:], st2[:, NT:2 * NT], epst[:, 0:1])

        @block.vector
        def _(vector):
            ca.wait(vector)
            ser(vector, cv, lambda: vector.tensor_tensor(tmpn[:], st2[:, 0:NT], st2[:, NT:2 * NT], ALU.mult))
            ser(vector, cv, lambda: vector.tensor_scalar(tmpn[:], tmpn[:], -1.0, None, ALU.mult))

        p.barrier()
        esA.close()

        esQ = contextlib.ExitStack()
        qrT = esQ.enter_context(nc.sbuf_tensor("qrT", [128, ND * S], BF16))
        krT = esQ.enter_context(nc.sbuf_tensor("krT", [128, ND * S], BF16))
        vtk = esQ.enter_context(nc.sbuf_tensor("vtk", [128, NT * DM], BF16))
        esB = contextlib.ExitStack()
        wqkv_bf = esB.enter_context(nc.sbuf_tensor("wqkv_bf", [128, ND * 3 * DM], BF16))
        sbqkv = esB.enter_context(nc.sbuf_tensor("sbqkv", [24, 128], F32))
        bqkvT = esB.enter_context(nc.sbuf_tensor("bqkvT", [128, 24], F32))
        bqkv_rb = esB.enter_context(nc.sbuf_tensor("bqkv_rb", [1, 3 * DM], BF16))
        sbq_row = esB.enter_context(nc.sbuf_tensor("sbq_row", [1, 3 * DM], F32))
        g_row = esB.enter_context(nc.sbuf_tensor("g_row", [1, DM], F32))
        x1t = esB.enter_context(nc.sbuf_tensor("x1t", [128, DM], F32))
        u2t = esB.enter_context(nc.sbuf_tensor("u2t", [128, DM], F32))
        msr = esB.enter_context(nc.sbuf_tensor("msr", [128, NT], F32))
        hbt = esB.enter_context(nc.sbuf_tensor("hbt", [128, DM], BF16))
        hTw = esB.enter_context(nc.sbuf_tensor("hTw", [128, ND * 512], BF16))
        cosw = esB.enter_context(nc.sbuf_tensor("cosw", [128, 4 * 512], BF16))
        sinw = esB.enter_context(nc.sbuf_tensor("sinw", [128, 4 * 512], BF16))
        rt1 = esB.enter_context(nc.sbuf_tensor("rt1", [128, 512], F32))
        rt2 = esB.enter_context(nc.sbuf_tensor("rt2", [128, 512], F32))
        pqk = [esB.enter_context(nc.psum_tensor(f"pqk{j}", [128, 512], F32)) for j in range(4)]

        cqkd = p.ctr("cqkd")
        cbq = p.ctr("cbq")
        cxw = p.ctr("cxw")
        cht = p.ctr("cht")
        ctw = p.ctr("ctw")
        cqm = p.ctr("cqm")
        CVH, CVHE, CVROPE, CQMW = {}, {}, {}, {}

        @block.gpsimd
        def _(gpsimd):
            ld.wait(gpsimd)
            cqkd.dma(gpsimd.dma_start(
                out=wqkv_bf[:], in_=wqkv.rearrange("(c p) m -> p c m", p=128)))

        @block.sync
        def _(sync):
            ld.wait(sync)
            cbq.dma(sync.dma_start(out=sbqkv[:], in_=bqkv))
            cbq.dma(sync.dma_start(out=sbq_row[:], in_=bqkv_r))
            cbq.dma(sync.dma_start(out=g_row[:], in_=g_r))

        @block.tensor
        def _(tensor):
            cbq.wait(tensor)
            cv.wait(tensor)
            tensor.transpose(pqk[0][0:128, 0:24], sbqkv[:], ident[0:24, 0:24])
            tensor.matmul(pqk[1][:, 0:512], onesf[:], g_row[:, 0:512], start=True, stop=True)
            cp.inc(tensor.matmul(pqk[2][:, 0:512], onesf[:], g_row[:, 512:1024], start=True, stop=True))

        @block.vector
        def _(vector):
            cp.wait(vector)
            ser(vector, cv, lambda: vector.tensor_copy(bqkvT[:], pqk[0][0:128, 0:24]))
            ser(vector, cv, lambda: vector.tensor_copy(gbc[:, 0:512], pqk[1][:, 0:512]))
            ser(vector, cv, lambda: vector.tensor_copy(gbc[:, 512:1024], pqk[2][:, 0:512]))
            ser(vector, cv, lambda: vector.tensor_copy(bqkv_rb[:], sbq_row[:]))

        p.barrier()

        for tw in range(4):
            @block.sync
            def _(sync, tw=tw):
                cv.wait(sync)
                for c in range(4):
                    ctw.dma(sync.dma_start(out=cosw[:, c * 512:(c + 1) * 512],
                                           in_=cosT[c * 128:(c + 1) * 128, tw * 512:(tw + 1) * 512]))
                    ctw.dma(sync.dma_start(out=sinw[:, c * 512:(c + 1) * 512],
                                           in_=sinT[c * 128:(c + 1) * 128, tw * 512:(tw + 1) * 512]))

            for ti in range(4):
                i = tw * 4 + ti

                @block.sync
                def _(sync, i=i):
                    cv.wait(sync)
                    ca.wait(sync)
                    cxw.dma(sync.dma_start(
                        out=x1t[:], in_=x1s.rearrange("(t p) d -> t p d", p=128)[i]))

                @block.scalar
                def _(scalar, i=i):
                    cxw.wait(scalar)
                    ser(scalar, ca, lambda: scalar.activation(
                        u2t[:], x1t[:], AF.Identity,
                        bias=tmpn[:, i:i + 1], scale=st2[:, NT + i:NT + i + 1]))
                    ser(scalar, ca, lambda: scalar.activation(
                        sqd[:], u2t[:], AF.Square, accum_out=msr[:, i:i + 1]))

                @block.vector
                def _(vector, i=i):
                    ca.wait(vector)
                    ser(vector, cv, lambda: vector.tensor_scalar(
                        msr[:, i:i + 1], msr[:, i:i + 1], INVD, None, ALU.mult))

                @block.scalar
                def _(scalar, i=i):
                    cv.wait(scalar)
                    ser(scalar, ca, lambda: scalar.activation(
                        msr[:, i:i + 1], msr[:, i:i + 1], AF.Ln, bias=epst[:, 1:2]))
                    ser(scalar, ca, lambda: scalar.activation(
                        msr[:, i:i + 1], msr[:, i:i + 1], AF.Exp, scale=-0.5))
                    ser(scalar, ca, lambda: scalar.activation(
                        u2t[:], u2t[:], AF.Copy, scale=msr[:, i:i + 1]))

                @block.vector
                def _(vector, i=i):
                    ca.wait(vector)
                    cht.wait(vector, at=i)
                    ser(vector, cv, lambda: vector.tensor_tensor(hbt[:], u2t[:], gbc[:], ALU.mult))
                CVH[i] = cv.n

                @block.tensor
                def _(tensor, tw=tw, ti=ti, i=i):
                    cv.wait(tensor, at=CVH[i])
                    if tw > 0:
                        cv.wait(tensor, at=CVROPE[tw - 1])
                    if i >= 2:
                        cv.wait(tensor, at=CVHE[i - 2])
                    pv = pqk[ti % 2].bitcast(BF16)
                    for c in range(ND):
                        ins = tensor.transpose(pv[:, c * 128:(c + 1) * 128],
                                               hbt[:, c * 128:(c + 1) * 128], identb[:])
                    cht.inc(ins)

                @block.vector
                def _(vector, tw=tw, ti=ti, i=i):
                    cht.wait(vector, at=i + 1)
                    if tw > 0:
                        cqm.wait(vector, at=CQMW[tw - 1])
                    pv = pqk[ti % 2].bitcast(BF16)
                    for c in range(ND):
                        ser(vector, cv, lambda c=c: vector.tensor_copy(
                            hTw[:, c * 512 + ti * 128:c * 512 + (ti + 1) * 128],
                            pv[:, c * 128:(c + 1) * 128]))
                    CVHE[i] = cv.n

            for qk in range(2):
                dst = qrT if qk == 0 else krT
                mbase = qk * 8
                for c in range(4):
                    mA, mB = mbase + c, mbase + c + 4

                    @block.tensor
                    def _(tensor, tw=tw, mA=mA, mB=mB):
                        cv.wait(tensor, at=CVHE[tw * 4 + 3])
                        cqkd.wait(tensor)
                        ctw.wait(tensor)
                        cv.wait(tensor)
                        for m, bank in ((mA, 0), (mB, 1)):
                            for d in range(ND):
                                ins = tensor.matmul(
                                    pqk[bank][:],
                                    wqkv_bf[:, d * 3 * DM + m * 128:d * 3 * DM + (m + 1) * 128],
                                    hTw[:, d * 512:(d + 1) * 512],
                                    start=(d == 0), stop=(d == ND - 1))
                                if d == ND - 1:
                                    cqm.inc(ins)

                    @block.vector
                    def _(vector, tw=tw, c=c, mA=mA, mB=mB, dst=dst):
                        cqm.wait(vector)
                        colA = c * S + tw * 512
                        colB = (c + 4) * S + tw * 512
                        cw_ = cosw[:, c * 512:(c + 1) * 512]
                        sw_ = sinw[:, c * 512:(c + 1) * 512]
                        ser(vector, cv, lambda: vector.scalar_tensor_tensor(
                            rt1[:], pqk[0][:], bqkvT[:, mA:mA + 1], cw_, ALU.add, ALU.mult))
                        ser(vector, cv, lambda: vector.scalar_tensor_tensor(
                            rt2[:], pqk[1][:], bqkvT[:, mB:mB + 1], sw_, ALU.add, ALU.mult))
                        ser(vector, cv, lambda: vector.tensor_tensor(
                            dst[:, colA:colA + 512], rt1[:], rt2[:], ALU.subtract))
                        ser(vector, cv, lambda: vector.scalar_tensor_tensor(
                            rt1[:], pqk[1][:], bqkvT[:, mB:mB + 1], cw_, ALU.add, ALU.mult))
                        ser(vector, cv, lambda: vector.scalar_tensor_tensor(
                            rt2[:], pqk[0][:], bqkvT[:, mA:mA + 1], sw_, ALU.add, ALU.mult))
                        ser(vector, cv, lambda: vector.tensor_tensor(
                            dst[:, colB:colB + 512], rt1[:], rt2[:], ALU.add))

            for ti in range(4):
                i = tw * 4 + ti

                @block.tensor
                def _(tensor, tw=tw, ti=ti, i=i):
                    ca.wait(tensor)
                    for e in range(2):
                        bank = 2 + e
                        for d in range(ND):
                            tensor.matmul(
                                pqk[bank][:],
                                hTw[:, d * 512 + ti * 128:d * 512 + (ti + 1) * 128],
                                wqkv_bf[:, d * 3 * DM + 2 * DM + e * 512:d * 3 * DM + 2 * DM + (e + 1) * 512],
                                start=(d == 0), stop=False)
                        ins = tensor.matmul(
                            pqk[bank][:], onesb[:],
                            bqkv_rb[:, 2 * DM + e * 512:2 * DM + (e + 1) * 512],
                            start=False, stop=True)
                        cqm.inc(ins)

                @block.scalar
                def _(scalar, i=i):
                    cqm.wait(scalar)
                    for e in range(2):
                        ser(scalar, ca, lambda e=e: scalar.activation(
                            vtk[:, i * DM + e * 512:i * DM + (e + 1) * 512],
                            pqk[2 + e][:], AF.Copy))

            CVROPE[tw] = cv.n
            CQMW[tw] = cqm.n

        p.barrier()
        esB.close()

        esC = contextlib.ExitStack()
        scps = esC.enter_context(nc.psum_tensor("scps", [128, S], F32))
        avp = esC.enter_context(nc.psum_tensor("avp", [128, DM], F32))
        pjp = esC.enter_context(nc.psum_tensor("pjp", [128, DM], F32))
        wproj_bf = esC.enter_context(nc.sbuf_tensor("wproj_bf", [128, ND * DM], BF16))
        bproj_rb = esC.enter_context(nc.sbuf_tensor("bproj_rb", [1, DM], BF16))
        sbp_row = esC.enter_context(nc.sbuf_tensor("sbp_row", [1, DM], F32))
        abuf = esC.enter_context(nc.sbuf_tensor("abuf", [128, S], BF16))
        atb = esC.enter_context(nc.sbuf_tensor("atb", [128, S], BF16))
        obf = esC.enter_context(nc.sbuf_tensor("obf", [128, DM], BF16))
        otb = esC.enter_context(nc.sbuf_tensor("otb", [128, DM], BF16))
        mxr = esC.enter_context(nc.sbuf_tensor("mxr", [128, 1], F32))
        mbr = esC.enter_context(nc.sbuf_tensor("mbr", [128, 1], F32))
        rsum = esC.enter_context(nc.sbuf_tensor("rsum", [128, 1], F32))
        rs2 = esC.enter_context(nc.sbuf_tensor("rs2", [128, 1], F32))
        x1c = esC.enter_context(nc.sbuf_tensor("x1c", [128, DM], F32))
        u2rc = esC.enter_context(nc.sbuf_tensor("u2rc", [128, DM], F32))
        x2c = esC.enter_context(nc.sbuf_tensor("x2c", [128, DM], F32))

        cpj = p.ctr("cpj")
        ctp = p.ctr("ctp")
        cto = p.ctr("cto")
        cwp = p.ctr("cwp")
        cbp = p.ctr("cbp")
        CAB, CVAT, CAO, CVOT = {}, {}, {}, {}

        @block.gpsimd
        def _(gpsimd):
            ld.wait(gpsimd)
            cwp.dma(gpsimd.dma_start(
                out=wproj_bf[:], in_=wproj.rearrange("(c p) m -> p c m", p=128)))

        @block.sync
        def _(sync):
            ld.wait(sync)
            cbp.dma(sync.dma_start(out=sbp_row[:], in_=bproj_r))

        @block.vector
        def _(vector):
            cwp.wait(vector)
            cbp.wait(vector)
            ser(vector, cv, lambda: vector.tensor_copy(bproj_rb[:], sbp_row[:]))
            ser(vector, cv, lambda: vector.memset(rmax[:], -1e30))

        p.barrier()

        for i in range(NT):
            @block.tensor
            def _(tensor, i=i):
                p.wait_all(tensor)
                for n in range(4):
                    for c in range(ND):
                        ins = tensor.matmul(
                            scps[:, n * 512:(n + 1) * 512],
                            qrT[:, c * S + i * 128:c * S + (i + 1) * 128],
                            krT[:, c * S + n * 512:c * S + (n + 1) * 512],
                            start=(c == 0), stop=(c == ND - 1))
                        if c == ND - 1:
                            cpj.inc(ins)

            @block.vector
            def _(vector, i=i):
                cpj.wait(vector)
                ser(vector, cv, lambda: vector.reduce_max(mxr[:], scps[:], axis=AX.X))
                ser(vector, cv, lambda: vector.tensor_scalar(mbr[:], mxr[:], -SCALE_SM, None, ALU.mult))

            @block.scalar
            def _(scalar, i=i):
                cv.wait(scalar)
                ser(scalar, ca, lambda: scalar.activation(
                    abuf[:], scps[:], AF.Exp, bias=mbr[:, 0:1], scale=SCALE_SM,
                    accum_out=rsum[:]))
            CAB[i] = ca.n

            @block.vector
            def _(vector, i=i):
                ca.wait(vector)
                ser(vector, cv, lambda: vector.reciprocal(rs2[:], rsum[:]))

            @block.tensor
            def _(tensor, i=i):
                ca.wait(tensor, at=CAB[i])
                pv = scps.bitcast(BF16)
                for c in range(NT):
                    ins = tensor.transpose(pv[:, c * 128:(c + 1) * 128],
                                           abuf[:, c * 128:(c + 1) * 128], identb[:])
                ctp.inc(ins)

            @block.vector
            def _(vector, i=i):
                ctp.wait(vector, at=i + 1)
                pv = scps.bitcast(BF16)
                ser(vector, cv, lambda: vector.tensor_copy(atb[:, 0:1024], pv[:, 0:1024]))
                ser(vector, cv, lambda: vector.tensor_copy(atb[:, 1024:2048], pv[:, 1024:2048]))
                CVAT[i] = cv.n

            @block.tensor
            def _(tensor, i=i):
                cv.wait(tensor, at=CVAT[i])
                ca.wait(tensor)
                for e in range(2):
                    for c in range(NT):
                        ins = tensor.matmul(
                            avp[:, e * 512:(e + 1) * 512],
                            atb[:, c * 128:(c + 1) * 128],
                            vtk[:, c * DM + e * 512:c * DM + (e + 1) * 512],
                            start=(c == 0), stop=(c == NT - 1))
                        if c == NT - 1:
                            cpj.inc(ins)

            @block.scalar
            def _(scalar, i=i):
                cpj.wait(scalar)
                cv.wait(scalar)
                ser(scalar, ca, lambda: scalar.activation(
                    obf[:], avp[:], AF.Copy, scale=rs2[:, 0:1]))
            CAO[i] = ca.n

            @block.tensor
            def _(tensor, i=i):
                ca.wait(tensor, at=CAO[i])
                pv = avp.bitcast(BF16)
                for e in range(ND):
                    ins = tensor.transpose(pv[:, e * 128:(e + 1) * 128],
                                           obf[:, e * 128:(e + 1) * 128], identb[:])
                cto.inc(ins)

            @block.vector
            def _(vector, i=i):
                cto.wait(vector, at=i + 1)
                ser(vector, cv, lambda: vector.tensor_copy(otb[:], avp.bitcast(BF16)[:, 0:1024]))
                CVOT[i] = cv.n

            @block.tensor
            def _(tensor, i=i):
                cv.wait(tensor, at=CVOT[i])
                cwp.wait(tensor)
                cv.wait(tensor)
                for n in range(2):
                    for e in range(ND):
                        tensor.matmul(
                            pjp[:, n * 512:(n + 1) * 512],
                            otb[:, e * 128:(e + 1) * 128],
                            wproj_bf[:, e * DM + n * 512:e * DM + (n + 1) * 512],
                            start=(e == 0), stop=False)
                    ins = tensor.matmul(
                        pjp[:, n * 512:(n + 1) * 512], onesb[:],
                        bproj_rb[:, n * 512:(n + 1) * 512], start=False, stop=True)
                    cpj.inc(ins)

            @block.sync
            def _(sync, i=i):
                cv.wait(sync)
                ca.wait(sync)
                cx1.dma(sync.dma_start(
                    out=x1c[:], in_=x1s.rearrange("(t p) d -> t p d", p=128)[i]))

            @block.scalar
            def _(scalar, i=i):
                cx1.wait(scalar)
                ser(scalar, ca, lambda: scalar.activation(
                    u2rc[:], x1c[:], AF.Identity,
                    bias=tmpn[:, i:i + 1], scale=st2[:, NT + i:NT + i + 1]))

            @block.vector
            def _(vector, i=i):
                p.wait_all(vector)
                ser(vector, cv, lambda: vector.tensor_tensor(x2c[:], pjp[:], u2rc[:], ALU.add))
                ser(vector, cv, lambda: vector.tensor_tensor(x2c[:], x2c[:], x1c[:], ALU.add))
                ser(vector, cv, lambda: vector.reduce_sum(sum3[:, i:i + 1], x2c[:], axis=AX.X))
                ser(vector, cv, lambda: vector.reduce_max(mx3t[:, i:i + 1], x2c[:], axis=AX.X))

            @block.scalar
            def _(scalar, i=i):
                cv.wait(scalar)
                ser(scalar, ca, lambda: scalar.activation(
                    sqd[:], x2c[:], AF.Square, accum_out=sq3[:, i:i + 1]))

            @block.sync
            def _(sync, i=i):
                cv.wait(sync)
                so.dma(sync.dma_start(
                    out=x2_o.rearrange("(t p) d -> t p d", p=128)[i], in_=x2c[:]))

        p.barrier()
        esC.close()

        @block.vector
        def _(vector):
            ser(vector, cv, lambda: vector.tensor_scalar(st3[:, 0:NT], sum3[:], INVD, None, ALU.mult))
            ser(vector, cv, lambda: vector.tensor_tensor(tmpn[:], st3[:, 0:NT], st3[:, 0:NT], ALU.mult))
            ser(vector, cv, lambda: vector.tensor_scalar(sq3[:], sq3[:], INVD, None, ALU.mult))
            ser(vector, cv, lambda: vector.tensor_tensor(tmpn[:], sq3[:], tmpn[:], ALU.subtract))

        _ln_rstd(block, ca, cv, tmpn[:], st3[:, NT:2 * NT], epst[:, 0:1])

        esD = contextlib.ExitStack()
        pfold = esD.enter_context(nc.psum_tensor("pfold", [128, 512], F32))

        @block.vector
        def _(vector):
            ca.wait(vector)
            ser(vector, cv, lambda: vector.tensor_tensor(mx3t[:], mx3t[:], st3[:, 0:NT], ALU.subtract))
            ser(vector, cv, lambda: vector.tensor_tensor(mx3t[:], mx3t[:], st3[:, NT:2 * NT], ALU.mult))
            ser(vector, cv, lambda: vector.reduce_max(rmax[:], mx3t[:], axis=AX.X))

        _fold_max(p, ident, pfold, rmax, rowm, gmx, cp, cv)

        @block.sync
        def _(sync):
            cv.wait(sync)
            so.dma(sync.dma_start(out=st_o, in_=st3[:]))
            so.dma(sync.dma_start(out=mx_o, in_=gmx[:]))
            so.wait(sync)
        esD.close()
        esQ.close()
    return nc



def _rope_tables():
    half = 512
    inv_freq = 10000.0 ** (-np.arange(half, dtype=np.float64) / half)
    ang = np.arange(S, dtype=np.float64)[:, None] * inv_freq[None, :]
    cos = np.rint(np.cos(ang)).astype(ml_dtypes.bfloat16)
    sin = np.rint(np.sin(ang)).astype(ml_dtypes.bfloat16)
    return np.ascontiguousarray(cos.T), np.ascontiguousarray(sin.T)


_NC_CACHE = {}


def _get_nc(name, builder):
    if name not in _NC_CACHE:
        nc = bass.Bass("TRN2", target_bir_lowering=False)
        _NC_CACHE[name] = builder(nc)
    return _NC_CACHE[name]


def _run(name, builder, in_maps):
    nc = _get_nc(name, builder)
    res = run_bass_kernel_spmd(nc, in_maps, core_ids=list(range(8)))
    return res.results


def kernel(L0_attn_x, L0_attn_g_norm, L0_attn_w_qkv, L0_attn_b_qkv,
           L0_attn_w_proj, L0_attn_b_proj, L0_ffn_w1, L0_ffn_b1,
           L0_ffn_w2, L0_ffn_b2):
    xs = [np.ascontiguousarray(np.asarray(L0_attn_x)[c], dtype=np.float32) for c in range(8)]
    w1 = np.ascontiguousarray(np.asarray(L0_ffn_w1, dtype=np.float32))
    b1 = np.ascontiguousarray(np.asarray(L0_ffn_b1, dtype=np.float32).reshape(32, 128))
    w2 = np.ascontiguousarray(np.asarray(L0_ffn_w2, dtype=np.float32))
    b2 = np.ascontiguousarray(np.asarray(L0_ffn_b2, dtype=np.float32).reshape(8, 128))
    wqkv = np.ascontiguousarray(np.asarray(L0_attn_w_qkv, dtype=np.float32))
    bqkv = np.ascontiguousarray(np.asarray(L0_attn_b_qkv, dtype=np.float32).reshape(24, 128))
    bqkv_r = np.ascontiguousarray(np.asarray(L0_attn_b_qkv, dtype=np.float32).reshape(1, 3 * DM))
    wproj = np.ascontiguousarray(np.asarray(L0_attn_w_proj, dtype=np.float32))
    bproj_r = np.ascontiguousarray(np.asarray(L0_attn_b_proj, dtype=np.float32).reshape(1, DM))
    g_r = np.ascontiguousarray(np.asarray(L0_attn_g_norm, dtype=np.float32).reshape(1, DM))
    cosT, sinT = _rope_tables()
    ident = np.eye(128, dtype=np.float32)
    identb = np.eye(128, dtype=ml_dtypes.bfloat16)
    onesf = np.ones((1, 128), dtype=np.float32)
    onesb = np.ones((1, 128), dtype=ml_dtypes.bfloat16)

    r0 = _run("stats", build_stats, [
        {"x": xs[c], "ident": ident} for c in range(8)])
    gmax1 = np.concatenate([r0[c]["mx"] for c in range(8)], axis=1)

    r1 = _run("m1", build_m1, [
        {"x": xs[c], "st": r0[c]["st"], "gmax": gmax1, "w1": w1, "b1": b1,
         "ident": ident, "identb": identb, "onesf": onesf} for c in range(8)])
    gmax2 = np.concatenate([r1[c]["mx"] for c in range(8)], axis=1)

    r2 = _run("mid", build_mid, [
        {"x": xs[c], "H": r1[c]["H"], "gmax": gmax2, "w2": w2, "b2": b2,
         "wqkv": wqkv, "bqkv": bqkv, "bqkv_r": bqkv_r, "wproj": wproj,
         "bproj_r": bproj_r, "g_r": g_r, "cosT": cosT, "sinT": sinT,
         "ident": ident, "identb": identb, "onesf": onesf, "onesb": onesb} for c in range(8)])
    gmax3 = np.concatenate([r2[c]["mx"] for c in range(8)], axis=1)

    r3 = _run("m1", build_m1, [
        {"x": r2[c]["x2"], "st": r2[c]["st3"], "gmax": gmax3, "w1": w1, "b1": b1,
         "ident": ident, "identb": identb, "onesf": onesf} for c in range(8)])
    gmax4 = np.concatenate([r3[c]["mx"] for c in range(8)], axis=1)

    r4 = _run("m2f", build_m2f, [
        {"x": r2[c]["x2"], "H": r3[c]["H"], "gmax": gmax4, "w2": w2, "b2": b2,
         "ident": ident, "identb": identb, "onesf": onesf} for c in range(8)])

    out = np.stack([r4[c]["y"] for c in range(8)], axis=0)
    return out.astype(np.float32)


# version 10
# speedup vs baseline: 1.5274x; 1.0327x over previous
import contextlib

import numpy as np
import ml_dtypes

import concourse.bass as bass
import concourse.mybir as mybir
from concourse.bass_utils import run_bass_kernel_spmd

F32 = mybir.dt.float32
BF16 = mybir.dt.bfloat16
I32 = mybir.dt.int32
AF = mybir.ActivationFunctionType
ALU = mybir.AluOpType
AX = mybir.AxisListType

S = 2048
DM = 1024
FF = 4096
NT = S // 128
ND = DM // 128
NF = FF // 128
EPS_LN = 1e-5
EPS_RMS = 1e-6
EPS_Q = 1e-8
LN3 = 1.0986122886681098
INV255 = float(np.float32(1.0) / np.float32(255.0))
INV127 = float(np.float32(1.0) / np.float32(127.0))
SCALE_SM = float(np.float32(LN3) / np.float32(32.0))
INVD = 1.0 / DM


class Ctr:
    def __init__(self, es, nc, name):
        self.h = es.enter_context(nc.semaphore(name))
        self.n = 0

    def inc(self, instr, k=1):
        instr.then_inc(self.h, k)
        self.n += k
        return instr

    def dma(self, instr):
        return self.inc(instr, 16)

    def wait(self, eng, at=None):
        eng.wait_ge(self.h, self.n if at is None else at)


class Prog:

    def __init__(self, es, nc):
        self.nc = nc
        self.es = es
        self.block = es.enter_context(nc.Block())
        self.ctrs = []

    def ctr(self, name):
        c = Ctr(self.es, self.nc, name)
        self.ctrs.append(c)
        return c

    def wait_all(self, eng):
        for c in self.ctrs:
            if c.n > 0:
                c.wait(eng)

    def barrier(self, engines=("sync", "scalar", "vector", "tensor", "gpsimd")):
        b = self.block
        if "sync" in engines:
            @b.sync
            def _(sync):
                self.wait_all(sync)
        if "scalar" in engines:
            @b.scalar
            def _(scalar):
                self.wait_all(scalar)
        if "vector" in engines:
            @b.vector
            def _(vector):
                self.wait_all(vector)
        if "tensor" in engines:
            @b.tensor
            def _(tensor):
                self.wait_all(tensor)
        if "gpsimd" in engines:
            @b.gpsimd
            def _(gpsimd):
                self.wait_all(gpsimd)


def ser(eng, ctr, emit):
    ctr.wait(eng)
    return ctr.inc(emit())


def _round_to_bf16(vector, cv, scratch_f32, out_bf, lo=None, hi=None):
    if lo is not None:
        ser(vector, cv, lambda: vector.tensor_scalar(
            scratch_f32[:], scratch_f32[:], float(lo), float(hi), ALU.max, ALU.min))
    ser(vector, cv, lambda: vector.tensor_copy(scratch_f32.bitcast(I32)[:], scratch_f32[:]))
    ser(vector, cv, lambda: vector.tensor_copy(out_bf, scratch_f32.bitcast(I32)[:]))


def _ln_rstd(block, ca, cv, var_ap, out_ap, eps_ap):
    @block.scalar
    def _(scalar):
        cv.wait(scalar)
        ser(scalar, ca, lambda: scalar.activation(var_ap, var_ap, AF.Ln, bias=eps_ap))
        ser(scalar, ca, lambda: scalar.activation(out_ap, var_ap, AF.Exp, scale=-0.5))


def _fold_max(p, ident, pbank, rmax, rowm, gmx, cp, cv):
    block = p.block

    @block.tensor
    def _(tensor):
        cv.wait(tensor)
        cp.inc(tensor.transpose(pbank[0:1, 0:128], rmax[:], ident[:]))

    @block.vector
    def _(vector):
        cp.wait(vector)
        ser(vector, cv, lambda: vector.tensor_copy(rowm[:], pbank[0:1, 0:128]))
        ser(vector, cv, lambda: vector.reduce_max(gmx[:], rowm[:], axis=AX.X))


def _quant_scalars(p, sgm, srow, onesf, invs_b, s_b, pbank, cv, cp):
    block = p.block

    @block.vector
    def _(vector):
        ser(vector, cv, lambda: vector.reduce_max(srow[:, 0:1], sgm[:], axis=AX.X))
        ser(vector, cv, lambda: vector.tensor_scalar(srow[:, 0:1], srow[:, 0:1], EPS_Q, None, ALU.max))
        ser(vector, cv, lambda: vector.reciprocal(srow[:, 1:2], srow[:, 0:1]))
        ser(vector, cv, lambda: vector.tensor_scalar(srow[:, 1:2], srow[:, 1:2], 255.0, None, ALU.mult))
        ser(vector, cv, lambda: vector.tensor_scalar(srow[:, 2:3], srow[:, 0:1], INV255, None, ALU.mult))

    @block.tensor
    def _(tensor):
        cv.wait(tensor)
        tensor.matmul(pbank[:, 0:1], onesf[:], srow[:, 1:2], start=True, stop=True)
        cp.inc(tensor.matmul(pbank[:, 1:2], onesf[:], srow[:, 2:3], start=True, stop=True))

    @block.vector
    def _(vector):
        cp.wait(vector)
        ser(vector, cv, lambda: vector.tensor_copy(invs_b[:], pbank[:, 0:1]))
        ser(vector, cv, lambda: vector.tensor_copy(s_b[:], pbank[:, 1:2]))



def build_stats(nc):
    x = nc.dram_tensor("x", [S, DM], F32, kind="ExternalInput").ap()
    ident_d = nc.dram_tensor("ident", [128, 128], F32, kind="ExternalInput").ap()
    st_o = nc.dram_tensor("st", [128, 2 * NT], F32, kind="ExternalOutput").ap()
    mx_o = nc.dram_tensor("mx", [1, 1], F32, kind="ExternalOutput").ap()

    es = contextlib.ExitStack()
    with es:
        sx = es.enter_context(nc.sbuf_tensor("sx_s", [128, NT * DM], F32))
        ident = es.enter_context(nc.sbuf_tensor("identS", [128, 128], F32))
        sq = es.enter_context(nc.sbuf_tensor("sq", [128, DM], BF16))
        sums = es.enter_context(nc.sbuf_tensor("sums", [128, NT], F32))
        sqs = es.enter_context(nc.sbuf_tensor("sqs", [128, NT], F32))
        mxs = es.enter_context(nc.sbuf_tensor("mxs", [128, NT], F32))
        stt = es.enter_context(nc.sbuf_tensor("stt", [128, 2 * NT], F32))
        tmp = es.enter_context(nc.sbuf_tensor("tmp", [128, NT], F32))
        tmax = es.enter_context(nc.sbuf_tensor("tmax", [128, NT], F32))
        rmax = es.enter_context(nc.sbuf_tensor("rmax", [128, 1], F32))
        rowm = es.enter_context(nc.sbuf_tensor("rowm", [1, 128], F32))
        gmx = es.enter_context(nc.sbuf_tensor("gmx", [1, 1], F32))
        epst = es.enter_context(nc.sbuf_tensor("epst", [128, 1], F32))
        ptr = es.enter_context(nc.psum_tensor("ptr", [128, 512], F32))

        p = Prog(es, nc)
        block = p.block
        ld = p.ctr("ld")
        ca = p.ctr("ca")
        cv = p.ctr("cv")
        cp = p.ctr("cp")
        so = p.ctr("so")

        @block.sync
        def _(sync):
            ld.dma(sync.dma_start(out=sx[:], in_=x.rearrange("(t p) d -> p t d", p=128)))
            ld.dma(sync.dma_start(out=ident[:], in_=ident_d))

        @block.scalar
        def _(scalar):
            ld.wait(scalar)
            for i in range(NT):
                ser(scalar, ca, lambda i=i: scalar.activation(
                    sq[:], sx[:, i * DM:(i + 1) * DM], AF.Square, accum_out=sqs[:, i:i + 1]))

        @block.vector
        def _(vector):
            ld.wait(vector)
            ser(vector, cv, lambda: vector.memset(epst[:], EPS_LN))
            for i in range(NT):
                ser(vector, cv, lambda i=i: vector.reduce_sum(
                    sums[:, i:i + 1], sx[:, i * DM:(i + 1) * DM], axis=AX.X))
                ser(vector, cv, lambda i=i: vector.reduce_max(
                    mxs[:, i:i + 1], sx[:, i * DM:(i + 1) * DM], axis=AX.X))
            ca.wait(vector)
            ser(vector, cv, lambda: vector.tensor_scalar(stt[:, 0:NT], sums[:], INVD, None, ALU.mult))
            ser(vector, cv, lambda: vector.tensor_tensor(tmp[:], stt[:, 0:NT], stt[:, 0:NT], ALU.mult))
            ser(vector, cv, lambda: vector.tensor_scalar(sqs[:], sqs[:], INVD, None, ALU.mult))
            ser(vector, cv, lambda: vector.tensor_tensor(tmp[:], sqs[:], tmp[:], ALU.subtract))

        _ln_rstd(block, ca, cv, tmp[:], stt[:, NT:2 * NT], epst[:, 0:1])

        @block.vector
        def _(vector):
            ca.wait(vector)
            ser(vector, cv, lambda: vector.tensor_tensor(tmax[:], mxs[:], stt[:, 0:NT], ALU.subtract))
            ser(vector, cv, lambda: vector.tensor_tensor(tmax[:], tmax[:], stt[:, NT:2 * NT], ALU.mult))
            ser(vector, cv, lambda: vector.reduce_max(rmax[:], tmax[:], axis=AX.X))

        _fold_max(p, ident, ptr, rmax, rowm, gmx, cp, cv)

        @block.sync
        def _(sync):
            cv.wait(sync)
            so.dma(sync.dma_start(out=st_o, in_=stt[:]))
            so.dma(sync.dma_start(out=mx_o, in_=gmx[:]))
            so.wait(sync)
    return nc



def build_m1(nc):
    x = nc.dram_tensor("x", [S, DM], F32, kind="ExternalInput").ap()
    st_i = nc.dram_tensor("st", [128, 2 * NT], F32, kind="ExternalInput").ap()
    gmax = nc.dram_tensor("gmax", [1, 8], F32, kind="ExternalInput").ap()
    w1 = nc.dram_tensor("w1", [FF, DM], F32, kind="ExternalInput").ap()
    b1 = nc.dram_tensor("b1", [32, 128], F32, kind="ExternalInput").ap()
    ident_d = nc.dram_tensor("ident", [128, 128], F32, kind="ExternalInput").ap()
    identb_d = nc.dram_tensor("identb", [128, 128], BF16, kind="ExternalInput").ap()
    onesf_d = nc.dram_tensor("onesf", [1, 128], F32, kind="ExternalInput").ap()
    h_o = nc.dram_tensor("H", [NF, 128, S], F32, kind="ExternalOutput").ap()
    mx_o = nc.dram_tensor("mx", [1, 1], F32, kind="ExternalOutput").ap()

    es = contextlib.ExitStack()
    with es:
        sx = es.enter_context(nc.sbuf_tensor("sx_s", [128, NT * DM], F32))
        stt = es.enter_context(nc.sbuf_tensor("stt", [128, 2 * NT], F32))
        sgm = es.enter_context(nc.sbuf_tensor("sgm", [1, 8], F32))
        srow = es.enter_context(nc.sbuf_tensor("srow", [1, 4], F32))
        sb1 = es.enter_context(nc.sbuf_tensor("sb1", [32, 128], F32))
        b1T = es.enter_context(nc.sbuf_tensor("b1T", [128, 32], F32))
        ident = es.enter_context(nc.sbuf_tensor("identS", [128, 128], F32))
        identb = es.enter_context(nc.sbuf_tensor("identbS", [128, 128], BF16))
        onesf = es.enter_context(nc.sbuf_tensor("onesf_s", [1, 128], F32))
        invs_b = es.enter_context(nc.sbuf_tensor("invs_b", [128, 1], F32))
        s_b = es.enter_context(nc.sbuf_tensor("s_b", [128, 1], F32))
        scl = es.enter_context(nc.sbuf_tensor("scl", [128, NT], F32))
        bia = es.enter_context(nc.sbuf_tensor("bia", [128, NT], F32))
        qsc = es.enter_context(nc.sbuf_tensor("qsc", [128, DM], F32))
        xq = es.enter_context(nc.sbuf_tensor("xq", [128, NT * DM], BF16))
        xqT = es.enter_context(nc.sbuf_tensor("xqT", [128, ND * S], BF16))
        wbuf = [es.enter_context(nc.sbuf_tensor(f"wbuf{k}", [128, DM], F32)) for k in range(2)]
        wsc = [es.enter_context(nc.sbuf_tensor(f"wsc{k}", [128, DM], F32)) for k in range(2)]
        wq = [es.enter_context(nc.sbuf_tensor(f"wq{k}", [128, DM], BF16)) for k in range(2)]
        wqT = [es.enter_context(nc.sbuf_tensor(f"wqT{k}", [128, ND * 128], BF16)) for k in range(2)]
        amx = [es.enter_context(nc.sbuf_tensor(f"amx{k}", [128, 1], F32)) for k in range(2)]
        invw = [es.enter_context(nc.sbuf_tensor(f"invw{k}", [128, 1], F32)) for k in range(2)]
        deq = [es.enter_context(nc.sbuf_tensor(f"deq{k}", [128, 1], F32)) for k in range(2)]
        b1q = [es.enter_context(nc.sbuf_tensor(f"b1q{k}", [128, 1], F32)) for k in range(2)]
        tb = [es.enter_context(nc.sbuf_tensor(f"tb{k}", [128, 1], F32)) for k in range(2)]
        deq_all = es.enter_context(nc.sbuf_tensor("deq_all", [128, NF], F32))
        b1q_all = es.enter_context(nc.sbuf_tensor("b1q_all", [128, NF], F32))
        rawmax = es.enter_context(nc.sbuf_tensor("rawmax", [128, NF], F32))
        pm4 = es.enter_context(nc.sbuf_tensor("pm4", [128, 4], F32))
        hst = [es.enter_context(nc.sbuf_tensor(f"hst{k}", [128, S], F32)) for k in range(2)]
        rmax = es.enter_context(nc.sbuf_tensor("rmax", [128, 1], F32))
        tmax = [es.enter_context(nc.sbuf_tensor(f"tmax{k}", [128, 1], F32)) for k in range(2)]
        rowm = es.enter_context(nc.sbuf_tensor("rowm", [1, 128], F32))
        gmx = es.enter_context(nc.sbuf_tensor("gmx", [1, 1], F32))
        pful = [es.enter_context(nc.psum_tensor(f"pful{j}", [128, 2048], F32)) for j in range(2)]
        pmm = [pful[b // 4][:, (b % 4) * 512:(b % 4 + 1) * 512] for b in range(8)]

        p = Prog(es, nc)
        block = p.block
        ld = p.ctr("ld")
        cv = p.ctr("cv")
        ca = p.ctr("ca")
        cp = p.ctr("cp")
        cxb = p.ctr("cxb")
        cpm = p.ctr("cpm")
        cwt = p.ctr("cwt")
        cwte = p.ctr("cwte")
        cwd = [p.ctr(f"cwd{k}") for k in range(2)]
        cwv = [p.ctr(f"cwv{k}") for k in range(2)]
        cwx = [p.ctr(f"cwx{k}") for k in range(2)]
        cmm = [p.ctr(f"cmm{k}") for k in range(8)]
        cev = [p.ctr(f"cev{k}") for k in range(8)]
        chd = [p.ctr(f"chd{k}") for k in range(2)]
        cmx = [p.ctr(f"cmx{k}") for k in range(2)]
        so = p.ctr("so")

        @block.sync
        def _(sync):
            ld.dma(sync.dma_start(out=sx[:], in_=x.rearrange("(t p) d -> p t d", p=128)))
            ld.dma(sync.dma_start(out=stt[:], in_=st_i))
            ld.dma(sync.dma_start(out=sgm[:], in_=gmax))
            ld.dma(sync.dma_start(out=sb1[:], in_=b1))
            ld.dma(sync.dma_start(out=ident[:], in_=ident_d))
            ld.dma(sync.dma_start(out=identb[:], in_=identb_d))
            ld.dma(sync.dma_start(out=onesf[:], in_=onesf_d))

        @block.vector
        def _(vector):
            ld.wait(vector)

        _quant_scalars(p, sgm, srow, onesf, invs_b, s_b, pmm[0], cv, cp)

        @block.tensor
        def _(tensor):
            cv.wait(tensor)
            ld.wait(tensor)
            cp.inc(tensor.transpose(pmm[1][0:128, 0:32], sb1[:], ident[0:32, 0:32]))

        @block.vector
        def _(vector):
            cp.wait(vector)
            ser(vector, cv, lambda: vector.tensor_copy(b1T[:], pmm[1][0:128, 0:32]))
            ser(vector, cv, lambda: vector.tensor_scalar(scl[:], stt[:, NT:2 * NT], invs_b[:, 0:1], None, ALU.mult))
            ser(vector, cv, lambda: vector.tensor_tensor(bia[:], stt[:, 0:NT], scl[:], ALU.mult))
            ser(vector, cv, lambda: vector.tensor_scalar(bia[:], bia[:], -1.0, None, ALU.mult))
            ser(vector, cv, lambda: vector.memset(rmax[:], 0.0))
            cev[0].inc(vector.engine_nop())
            cev[1].inc(vector.engine_nop())

        p.barrier()

        for i in range(NT):
            @block.scalar
            def _(scalar, i=i):
                cv.wait(scalar)
                ser(scalar, ca, lambda: scalar.activation(
                    qsc.bitcast(I32)[:], sx[:, i * DM:(i + 1) * DM], AF.Relu,
                    bias=bia[:, i:i + 1], scale=scl[:, i:i + 1]))

            @block.vector
            def _(vector, i=i):
                ca.wait(vector)
                ser(vector, cv, lambda: vector.tensor_copy(
                    xq[:, i * DM:(i + 1) * DM], qsc.bitcast(I32)[:]))

            @block.sync
            def _(sync, i=i):
                cv.wait(sync)
                for c in range(0, ND, 2):
                    cxb.dma(sync.dma_start_transpose(
                        xqT[:, c * S + i * 128:c * S + (i + 1) * 128],
                        xq[:, i * DM + c * 128:i * DM + (c + 1) * 128]))

            @block.scalar
            def _(scalar, i=i):
                cv.wait(scalar)
                for c in range(1, ND, 2):
                    cxb.dma(scalar.dma_start_transpose(
                        xqT[:, c * S + i * 128:c * S + (i + 1) * 128],
                        xq[:, i * DM + c * 128:i * DM + (c + 1) * 128]))

        p.barrier()

        XQT_ALL = cxb.n
        CEV_BASE = [1, 1, 0, 0, 0, 0, 0, 0]
        def emit_lqx(fc):
            j = fc % 2
            use = fc // 2
            pg = j * 4

            @block.sync
            def _(sync, fc=fc, j=j, use=use):
                if use > 0:
                    cwv[j].wait(sync, at=use)
                cwd[j].dma(sync.dma_start(out=wbuf[j][:], in_=w1[fc * 128:(fc + 1) * 128, :]))

            @block.vector
            def _(vector, fc=fc, j=j, use=use):
                cwd[j].wait(vector, at=16 * (use + 1))
                if use > 0:
                    cwt.wait(vector, at=fc - 1)
                ser(vector, cv, lambda: vector.reduce_max(amx[j][:], wbuf[j][:], axis=AX.X, apply_absolute_value=True))
                ser(vector, cv, lambda: vector.tensor_scalar(amx[j][:], amx[j][:], EPS_Q, None, ALU.max))
                ser(vector, cv, lambda: vector.reciprocal(invw[j][:], amx[j][:]))
                ser(vector, cv, lambda: vector.tensor_scalar(invw[j][:], invw[j][:], 127.0, None, ALU.mult))
                ser(vector, cv, lambda: vector.tensor_scalar(deq[j][:], amx[j][:], INV127, s_b[:, 0:1], ALU.mult, ALU.mult))
                ser(vector, cv, lambda: vector.tensor_scalar(tb[j][:], b1T[:, fc:fc + 1], invs_b[:, 0:1], invw[j][:, 0:1], ALU.mult, ALU.mult))
                ser(vector, cv, lambda: vector.tensor_copy(tb[j].bitcast(I32)[:], tb[j][:]))
                ser(vector, cv, lambda: vector.tensor_copy(b1q[j][:], tb[j].bitcast(I32)[:]))
                ser(vector, cv, lambda: vector.tensor_tensor(b1q[j][:], b1q[j][:], deq[j][:], ALU.mult))
                ser(vector, cv, lambda: vector.tensor_copy(deq_all[:, fc:fc + 1], deq[j][:]))
                ser(vector, cv, lambda: vector.tensor_copy(b1q_all[:, fc:fc + 1], b1q[j][:]))
                ser(vector, cv, lambda: vector.tensor_scalar(wsc[j].bitcast(I32)[:], wbuf[j][:], invw[j][:, 0:1], None, ALU.mult))
                ser(vector, cv, lambda: vector.tensor_copy(wq[j][:], wsc[j].bitcast(I32)[:]))
                cwv[j].inc(vector.engine_nop())

            CVQS[fc] = cv.n

        def emit_post(fc):
            j = fc % 2
            use = fc // 2
            pg = j * 4

            @block.vector
            def _(vector, fc=fc, j=j, use=use, pg=pg):
                for tq in range(4):
                    cmm[pg + tq].wait(vector, at=use + 1)
                cv.wait(vector)
                cpm.inc(vector.reduce_max(rawmax[:, fc:fc + 1], pful[j][:], axis=AX.X))

            @block.scalar
            def _(scalar, fc=fc, j=j, use=use, pg=pg):
                cpm.wait(scalar, at=fc + 1)
                if use > 0:
                    chd[j].wait(scalar, at=16 * use)
                for tq in range(4):
                    bank = pg + tq
                    cmm[bank].wait(scalar, at=use + 1)
                    cev[bank].inc(scalar.activation(hst[j][:, tq * 512:(tq + 1) * 512],
                                                    pmm[bank][:], AF.Relu,
                                                    bias=b1q_all[:, fc:fc + 1],
                                                    scale=deq_all[:, fc:fc + 1]))

            @block.sync
            def _(sync, fc=fc, j=j, use=use, pg=pg):
                for tq in range(4):
                    cev[pg + tq].wait(sync, at=CEV_BASE[pg + tq] + use + 1)
                chd[j].dma(sync.dma_start(out=h_o[fc], in_=hst[j][:]))

        def emit_wt(fc):
            j = fc % 2
            use = fc // 2
            pg = j * 4

            @block.tensor
            def _(tensor, fc=fc, j=j, use=use, pg=pg):
                cv.wait(tensor, at=CVQS[fc])
                cev[pg].wait(tensor, at=CEV_BASE[pg] + use)
                pview = pmm[pg].bitcast(BF16)
                for c in range(ND):
                    ins = tensor.transpose(pview[:, c * 128:(c + 1) * 128],
                                           wq[j][:, c * 128:(c + 1) * 128], identb[:])
                    if c == ND - 1:
                        cwt.inc(ins)

            @block.vector
            def _(vector, fc=fc, j=j, pg=pg):
                cwt.wait(vector, at=fc + 1)
                cv.wait(vector)
                cwte.inc(vector.tensor_copy(wqT[j][:], pmm[pg].bitcast(BF16)[:, 0:ND * 128]))

        def emit_mms(fc):
            j = fc % 2
            use = fc // 2
            pg = j * 4

            @block.tensor
            def _(tensor, fc=fc, j=j, use=use, pg=pg):
                cwte.wait(tensor, at=fc + 1)
                cxb.wait(tensor, at=XQT_ALL)
                for tq in range(4):
                    bank = pg + tq
                    cev[bank].wait(tensor, at=CEV_BASE[bank] + use)
                    for d in range(ND):
                        ins = tensor.matmul(pmm[bank][:],
                                            wqT[j][:, d * 128:(d + 1) * 128],
                                            xqT[:, d * S + tq * 512:d * S + (tq + 1) * 512],
                                            start=(d == 0), stop=(d == ND - 1))
                        if d == ND - 1:
                            cmm[bank].inc(ins)

        CVQS = {}
        emit_lqx(0)
        emit_lqx(1)
        emit_wt(0)
        emit_wt(1)
        for k2 in range(0, NF, 2):
            if k2 + 2 < NF:
                emit_lqx(k2 + 2)
                emit_lqx(k2 + 3)
            emit_mms(k2)
            emit_mms(k2 + 1)
            emit_post(k2)
            emit_post(k2 + 1)
            if k2 + 2 < NF:
                emit_wt(k2 + 2)
                emit_wt(k2 + 3)

        @block.vector
        def _(vector):
            ser(vector, cv, lambda: vector.tensor_tensor(rawmax[:], rawmax[:], deq_all[:], ALU.mult))
            ser(vector, cv, lambda: vector.tensor_tensor(rawmax[:], rawmax[:], b1q_all[:], ALU.add))
            ser(vector, cv, lambda: vector.tensor_scalar(rawmax[:], rawmax[:], 0.0, None, ALU.max))
            ser(vector, cv, lambda: vector.reduce_max(rmax[:], rawmax[:], axis=AX.X))

        p.barrier()
        _fold_max(p, ident, pmm[0], rmax, rowm, gmx, cp, cv)

        @block.sync
        def _(sync):
            cv.wait(sync)
            so.dma(sync.dma_start(out=mx_o, in_=gmx[:]))
            chd[0].wait(sync)
            chd[1].wait(sync)
            so.wait(sync)
    return nc



def _emit_m2(nc, es, p, ctx, *, h_in, w2_in, x_in, post_tile):
    block = p.block
    cv, ca, cp = ctx["cv"], ctx["ca"], ctx["cp"]
    ident, onesf = ctx["ident"], ctx["onesf"]
    invs_b, s_b = ctx["invs_b"], ctx["s_b"]
    sb2 = ctx["sb2"]
    rows_d = ctx["rows_d"]
    pmm = ctx["pmm"]

    wbuf = es.enter_context(nc.sbuf_tensor("m2wbuf", [128, FF], F32))
    wsc = es.enter_context(nc.sbuf_tensor("m2wsc", [128, FF], F32))
    wqb = es.enter_context(nc.sbuf_tensor("m2wqb", [128, FF], BF16))
    w2q = es.enter_context(nc.sbuf_tensor("w2q_s", [128, NF * DM], BF16))
    hbuf = [es.enter_context(nc.sbuf_tensor(f"m2hb{k}", [128, FF], F32)) for k in range(2)]
    hqb = [es.enter_context(nc.sbuf_tensor(f"m2hq{k}", [128, FF], BF16)) for k in range(2)]
    xtb = [es.enter_context(nc.sbuf_tensor(f"m2xt{k}", [128, DM], F32)) for k in range(2)]
    b2T = es.enter_context(nc.sbuf_tensor("b2T", [128, 8], F32))
    sw2a = es.enter_context(nc.sbuf_tensor("sw2a", [128, 8], F32))
    w2inv = es.enter_context(nc.sbuf_tensor("w2inv", [128, 8], F32))
    w2deq = es.enter_context(nc.sbuf_tensor("w2deq", [128, 8], F32))
    b2i = es.enter_context(nc.sbuf_tensor("b2i", [128, 8], F32))
    r8a = es.enter_context(nc.sbuf_tensor("r8a", [8, 128], F32))
    r8b = es.enter_context(nc.sbuf_tensor("r8b", [8, 128], F32))
    rowln = es.enter_context(nc.sbuf_tensor("rowln", [1, DM], F32))
    rowln2 = es.enter_context(nc.sbuf_tensor("rowln2", [1, DM], F32))
    scale_b = es.enter_context(nc.sbuf_tensor("scale_b", [128, DM], F32))
    bias_b = es.enter_context(nc.sbuf_tensor("bias_b", [128, DM], F32))
    x2s = [es.enter_context(nc.sbuf_tensor(f"x2s{k}", [128, DM], F32)) for k in range(2)]
    wsg = [x2s[0].bitcast(BF16), x2s[1].bitcast(BF16)]

    cwd = p.ctr("m2cwd")
    cwxg = [p.ctr(f"m2cwx{g}") for g in range(2)]
    crw = p.ctr("m2crw")
    chg = [p.ctr(f"m2chg{k}") for k in range(2)]
    cxg = [p.ctr(f"m2cxg{k}") for k in range(2)]
    cmm = p.ctr("m2cmm")

    @block.tensor
    def _(tensor):
        cv.wait(tensor)
        ctx["ld"].wait(tensor)
        cp.inc(tensor.transpose(pmm[1][0:128, 0:8], sb2[:], ident[0:8, 0:8]))

    @block.vector
    def _(vector):
        cp.wait(vector)
        ser(vector, cv, lambda: vector.tensor_copy(b2T[:], pmm[1][0:128, 0:8]))

    p.barrier()

    identb = ctx["identb"]
    cpt = p.ctr("m2cpt")
    CVQ = {}
    CVB = {}
    for dc in range(8):
        @block.sync
        def _(sync, dc=dc):
            if dc > 0:
                cv.wait(sync, at=CVQ[dc - 1])
            cwd.dma(sync.dma_start(out=wbuf[:], in_=w2_in[dc * 128:(dc + 1) * 128, :]))

        @block.vector
        def _(vector, dc=dc):
            cwd.wait(vector)
            if dc > 0:
                cpt.wait(vector, at=dc)
            ser(vector, cv, lambda: vector.reduce_max(sw2a[:, dc:dc + 1], wbuf[:], axis=AX.X, apply_absolute_value=True))
            ser(vector, cv, lambda: vector.tensor_scalar(sw2a[:, dc:dc + 1], sw2a[:, dc:dc + 1], EPS_Q, None, ALU.max))
            ser(vector, cv, lambda: vector.reciprocal(w2inv[:, dc:dc + 1], sw2a[:, dc:dc + 1]))
            ser(vector, cv, lambda: vector.tensor_scalar(w2inv[:, dc:dc + 1], w2inv[:, dc:dc + 1], 127.0, None, ALU.mult))
            ser(vector, cv, lambda: vector.tensor_scalar(wsc.bitcast(I32)[:], wbuf[:], w2inv[:, dc:dc + 1], None, ALU.mult))
            ser(vector, cv, lambda: vector.tensor_copy(wqb[:], wsc.bitcast(I32)[:]))
            CVQ[dc] = cv.n

        @block.tensor
        def _(tensor, dc=dc):
            cv.wait(tensor, at=CVQ[dc])
            if dc > 0:
                cv.wait(tensor, at=CVB[dc - 1])
            for b in range(4):
                pv = pmm[b].bitcast(BF16)
                for c in range(8):
                    fc = b * 8 + c
                    ins = tensor.transpose(pv[:, c * 128:(c + 1) * 128],
                                           wqb[:, fc * 128:(fc + 1) * 128], identb[:])
            cpt.inc(ins)

        @block.vector
        def _(vector, dc=dc):
            cpt.wait(vector, at=dc + 1)
            if dc > 0:
                if dc % 4 == 0:
                    cwxg[dc // 4 - 1].wait(vector, at=2048)
                else:
                    cwxg[dc // 4].wait(vector, at=512 * (dc % 4))
            for b in range(4):
                ser(vector, cv, lambda b=b: vector.tensor_copy(
                    wsg[b // 2][:, (b % 2) * 1024:(b % 2 + 1) * 1024],
                    pmm[b].bitcast(BF16)[:, 0:1024]))
            CVB[dc] = cv.n

        @block.sync
        def _(sync, dc=dc):
            cv.wait(sync, at=CVB[dc])
            for fc in range(0, NF, 2):
                cwxg[dc // 4].dma(sync.dma_start(
                    out=w2q[:, fc * DM + dc * 128:fc * DM + (dc + 1) * 128],
                    in_=wsg[fc // 16][:, (fc % 16) * 128:(fc % 16 + 1) * 128]))

        @block.scalar
        def _(scalar, dc=dc):
            cv.wait(scalar, at=CVB[dc])
            for fc in range(1, NF, 2):
                cwxg[dc // 4].dma(scalar.dma_start(
                    out=w2q[:, fc * DM + dc * 128:fc * DM + (dc + 1) * 128],
                    in_=wsg[fc // 16][:, (fc % 16) * 128:(fc % 16 + 1) * 128]))

    p.barrier()

    @block.vector
    def _(vector):
        ser(vector, cv, lambda: vector.tensor_scalar(w2deq[:], sw2a[:], INV127, None, ALU.mult))
        ser(vector, cv, lambda: vector.tensor_scalar(w2deq[:], w2deq[:], s_b[:, 0:1], None, ALU.mult))
        ser(vector, cv, lambda: vector.tensor_scalar(w2deq[:], w2deq[:], 0.5, None, ALU.mult))
        ser(vector, cv, lambda: vector.tensor_scalar(b2i[:], b2T[:], invs_b[:, 0:1], None, ALU.mult))
        ser(vector, cv, lambda: vector.tensor_tensor(b2i[:], b2i[:], w2inv[:], ALU.mult))
        ser(vector, cv, lambda: vector.tensor_copy(b2i.bitcast(I32)[:], b2i[:]))
        ser(vector, cv, lambda: vector.tensor_copy(b2i[:], b2i.bitcast(I32)[:]))
        ser(vector, cv, lambda: vector.tensor_tensor(b2i[:], b2i[:], w2deq[:], ALU.mult))

    @block.tensor
    def _(tensor):
        cv.wait(tensor)
        tensor.transpose(pmm[0][0:8, 0:128], w2deq[:], ident[:])
        cp.inc(tensor.transpose(pmm[1][0:8, 0:128], b2i[:], ident[:]))

    @block.vector
    def _(vector):
        cp.wait(vector)
        ser(vector, cv, lambda: vector.tensor_copy(r8a[:], pmm[0][0:8, 0:128]))
        ser(vector, cv, lambda: vector.tensor_copy(r8b[:], pmm[1][0:8, 0:128]))

    @block.sync
    def _(sync):
        cv.wait(sync)
        crw.dma(sync.dma_start(out=rows_d[0], in_=r8a[:]))
        crw.dma(sync.dma_start(out=rows_d[1], in_=r8b[:]))
        crw.wait(sync)
        crw.dma(sync.dma_start(out=rowln[:], in_=rows_d.rearrange("a b c -> a (b c)")[0:1, :]))
        crw.dma(sync.dma_start(out=rowln2[:], in_=rows_d.rearrange("a b c -> a (b c)")[1:2, :]))

    @block.tensor
    def _(tensor):
        crw.wait(tensor)
        tensor.matmul(pmm[0][:, 0:512], onesf[:], rowln[:, 0:512], start=True, stop=True)
        tensor.matmul(pmm[1][:, 0:512], onesf[:], rowln[:, 512:1024], start=True, stop=True)
        tensor.matmul(pmm[2][:, 0:512], onesf[:], rowln2[:, 0:512], start=True, stop=True)
        cp.inc(tensor.matmul(pmm[3][:, 0:512], onesf[:], rowln2[:, 512:1024], start=True, stop=True))

    @block.vector
    def _(vector):
        cp.wait(vector)
        ser(vector, cv, lambda: vector.tensor_copy(scale_b[:, 0:512], pmm[0][:, 0:512]))
        ser(vector, cv, lambda: vector.tensor_copy(scale_b[:, 512:1024], pmm[1][:, 0:512]))
        ser(vector, cv, lambda: vector.tensor_copy(bias_b[:, 0:512], pmm[2][:, 0:512]))
        ser(vector, cv, lambda: vector.tensor_copy(bias_b[:, 512:1024], pmm[3][:, 0:512]))

    CVROWS = cv.n

    CAH, CVQ2, CVE, PT = {}, {}, {}, {}

    def emit_load(i):
        j = i % 2

        @block.gpsimd
        def _(gpsimd, i=i, j=j):
            if i >= 2:
                cv.wait(gpsimd, at=CVE[i - 2])
            chg[j].dma(gpsimd.dma_start(
                out=hbuf[j][:],
                in_=h_in.rearrange("c p t -> p c t")[:, :, i * 128:(i + 1) * 128]))
            cxg[j].dma(gpsimd.dma_start(
                out=xtb[j][:], in_=x_in.rearrange("(t p) d -> t p d", p=128)[i]))

    def emit_quant(i):
        j = i % 2

        @block.scalar
        def _(scalar, i=i, j=j):
            chg[j].wait(scalar, at=16 * (i // 2 + 1))
            if i >= 2:
                cv.wait(scalar, at=CVQ2[i - 2])
            ser(scalar, ca, lambda: scalar.activation(
                hbuf[j].bitcast(I32)[:], hbuf[j][:], AF.Copy, scale=invs_b[:, 0:1]))
        CAH[i] = ca.n

        @block.vector
        def _(vector, i=i, j=j):
            ca.wait(vector, at=CAH[i])
            if i >= 2:
                cmm.wait(vector, at=2 * (i - 1))
            ser(vector, cv, lambda: vector.tensor_copy(hqb[j][:], hbuf[j].bitcast(I32)[:]))
        CVQ2[i] = cv.n

    def emit_mm(i):
        j = i % 2

        @block.tensor
        def _(tensor, i=i, j=j):
            cv.wait(tensor, at=CVQ2[i])
            cv.wait(tensor, at=CVE[i - 2] if i >= 2 else CVROWS)
            for nb in range(2):
                cwxg[nb].wait(tensor)
                bank = 2 * j + nb
                for fc in range(NF):
                    ins = tensor.matmul(pmm[bank][:],
                                        hqb[j][:, fc * 128:(fc + 1) * 128],
                                        w2q[:, fc * DM + nb * 512:fc * DM + (nb + 1) * 512],
                                        start=(fc == 0), stop=(fc == NF - 1))
                    if fc == NF - 1:
                        cmm.inc(ins)

    def emit_epi(i):
        j = i % 2

        @block.vector
        def _(vector, i=i, j=j):
            cmm.wait(vector, at=2 * (i + 1))
            cxg[j].wait(vector, at=16 * (i // 2 + 1))
            for ctr_, at_ in PT.get(i - 2, []):
                ctr_.wait(vector, at=at_)
            ser(vector, cv, lambda: vector.scalar_tensor_tensor(
                x2s[j][:, 0:512], pmm[2 * j][:], 1.0, scale_b[:, 0:512], ALU.mult, ALU.mult))
            ser(vector, cv, lambda: vector.scalar_tensor_tensor(
                x2s[j][:, 512:1024], pmm[2 * j + 1][:], 1.0, scale_b[:, 512:1024], ALU.mult, ALU.mult))
            ser(vector, cv, lambda: vector.tensor_tensor(x2s[j][:], x2s[j][:], bias_b[:], ALU.add))
            ser(vector, cv, lambda: vector.tensor_tensor(x2s[j][:], x2s[j][:], xtb[j][:], ALU.add))
        CVE[i] = cv.n
        PT[i] = post_tile(i, x2s[j]) or []

    emit_load(0)
    emit_load(1)
    emit_quant(0)
    emit_quant(1)
    for i in range(NT):
        emit_mm(i)
        emit_epi(i)
        if i + 2 < NT:
            emit_load(i + 2)
            emit_quant(i + 2)



def build_m2f(nc):
    x = nc.dram_tensor("x", [S, DM], F32, kind="ExternalInput").ap()
    h_in = nc.dram_tensor("H", [NF, 128, S], F32, kind="ExternalInput").ap()
    gmax = nc.dram_tensor("gmax", [1, 8], F32, kind="ExternalInput").ap()
    w2 = nc.dram_tensor("w2", [DM, FF], F32, kind="ExternalInput").ap()
    b2 = nc.dram_tensor("b2", [8, 128], F32, kind="ExternalInput").ap()
    ident_d = nc.dram_tensor("ident", [128, 128], F32, kind="ExternalInput").ap()
    identb_d = nc.dram_tensor("identb", [128, 128], BF16, kind="ExternalInput").ap()
    onesf_d = nc.dram_tensor("onesf", [1, 128], F32, kind="ExternalInput").ap()
    y_o = nc.dram_tensor("y", [S, DM], F32, kind="ExternalOutput").ap()
    rows_d = nc.dram_tensor("rows_scratch", [2, 8, 128], F32).ap()

    es = contextlib.ExitStack()
    with es:
        sgm = es.enter_context(nc.sbuf_tensor("sgm", [1, 8], F32))
        srow = es.enter_context(nc.sbuf_tensor("srow", [1, 4], F32))
        sb2 = es.enter_context(nc.sbuf_tensor("sb2", [8, 128], F32))
        ident = es.enter_context(nc.sbuf_tensor("identS", [128, 128], F32))
        identb = es.enter_context(nc.sbuf_tensor("identbS", [128, 128], BF16))
        onesf = es.enter_context(nc.sbuf_tensor("onesf_s", [1, 128], F32))
        invs_b = es.enter_context(nc.sbuf_tensor("invs_b", [128, 1], F32))
        s_b = es.enter_context(nc.sbuf_tensor("s_b", [128, 1], F32))
        pmm = [es.enter_context(nc.psum_tensor(f"pmm{j}", [128, 512], F32)) for j in range(4)]

        p = Prog(es, nc)
        block = p.block
        ld = p.ctr("ld")
        cv = p.ctr("cv")
        ca = p.ctr("ca")
        cp = p.ctr("cp")
        so = [p.ctr(f"so{k}") for k in range(2)]

        @block.sync
        def _(sync):
            ld.dma(sync.dma_start(out=sgm[:], in_=gmax))
            ld.dma(sync.dma_start(out=sb2[:], in_=b2))
            ld.dma(sync.dma_start(out=ident[:], in_=ident_d))
            ld.dma(sync.dma_start(out=identb[:], in_=identb_d))
            ld.dma(sync.dma_start(out=onesf[:], in_=onesf_d))

        @block.vector
        def _(vector):
            ld.wait(vector)

        _quant_scalars(p, sgm, srow, onesf, invs_b, s_b, pmm[0], cv, cp)

        ctx = dict(cv=cv, ca=ca, cp=cp, ld=ld, ident=ident, identb=identb,
                   onesf=onesf, invs_b=invs_b, s_b=s_b, pmm=pmm,
                   rows_d=rows_d, sb2=sb2, srow=srow)

        def post_tile(i, x2_ap):
            j = i % 2

            @block.sync
            def _(sync, i=i, j=j):
                cv.wait(sync)
                so[j].dma(sync.dma_start(
                    out=y_o.rearrange("(t p) d -> t p d", p=128)[i], in_=x2_ap[:]))
            return [(so[j], 16 * (i // 2 + 1))]

        _emit_m2(nc, es, p, ctx, h_in=h_in, w2_in=w2, x_in=x, post_tile=post_tile)

        @block.sync
        def _(sync):
            so[0].wait(sync)
            so[1].wait(sync)
    return nc



def build_mid(nc):
    x = nc.dram_tensor("x", [S, DM], F32, kind="ExternalInput").ap()
    h_in = nc.dram_tensor("H", [NF, 128, S], F32, kind="ExternalInput").ap()
    gmax = nc.dram_tensor("gmax", [1, 8], F32, kind="ExternalInput").ap()
    w2 = nc.dram_tensor("w2", [DM, FF], F32, kind="ExternalInput").ap()
    b2 = nc.dram_tensor("b2", [8, 128], F32, kind="ExternalInput").ap()
    wqkv = nc.dram_tensor("wqkv", [DM, 3 * DM], F32, kind="ExternalInput").ap()
    bqkv = nc.dram_tensor("bqkv", [24, 128], F32, kind="ExternalInput").ap()
    bqkv_r = nc.dram_tensor("bqkv_r", [1, 3 * DM], F32, kind="ExternalInput").ap()
    wproj = nc.dram_tensor("wproj", [DM, DM], F32, kind="ExternalInput").ap()
    bproj_r = nc.dram_tensor("bproj_r", [1, DM], F32, kind="ExternalInput").ap()
    g_r = nc.dram_tensor("g_r", [1, DM], F32, kind="ExternalInput").ap()
    cosT = nc.dram_tensor("cosT", [512, S], BF16, kind="ExternalInput").ap()
    sinT = nc.dram_tensor("sinT", [512, S], BF16, kind="ExternalInput").ap()
    ident_d = nc.dram_tensor("ident", [128, 128], F32, kind="ExternalInput").ap()
    identb_d = nc.dram_tensor("identb", [128, 128], BF16, kind="ExternalInput").ap()
    onesf_d = nc.dram_tensor("onesf", [1, 128], F32, kind="ExternalInput").ap()
    onesb_d = nc.dram_tensor("onesb", [1, 128], BF16, kind="ExternalInput").ap()
    x2_o = nc.dram_tensor("x2", [S, DM], F32, kind="ExternalOutput").ap()
    st_o = nc.dram_tensor("st3", [128, 2 * NT], F32, kind="ExternalOutput").ap()
    mx_o = nc.dram_tensor("mx", [1, 1], F32, kind="ExternalOutput").ap()
    rows_d = nc.dram_tensor("rows_scratch", [2, 8, 128], F32).ap()
    x1s = nc.dram_tensor("x1_scratch", [S, DM], F32).ap()

    es = contextlib.ExitStack()
    with es:
        sgm = es.enter_context(nc.sbuf_tensor("sgm", [1, 8], F32))
        srow = es.enter_context(nc.sbuf_tensor("srow", [1, 4], F32))
        sb2 = es.enter_context(nc.sbuf_tensor("sb2", [8, 128], F32))
        ident = es.enter_context(nc.sbuf_tensor("identS", [128, 128], F32))
        identb = es.enter_context(nc.sbuf_tensor("identbS", [128, 128], BF16))
        onesf = es.enter_context(nc.sbuf_tensor("onesf_s", [1, 128], F32))
        onesb = es.enter_context(nc.sbuf_tensor("onesb_s", [1, 128], BF16))
        invs_b = es.enter_context(nc.sbuf_tensor("invs_b", [128, 1], F32))
        s_b = es.enter_context(nc.sbuf_tensor("s_b", [128, 1], F32))
        st2 = es.enter_context(nc.sbuf_tensor("st2_s", [128, 2 * NT], F32))
        st3 = es.enter_context(nc.sbuf_tensor("st3_s", [128, 2 * NT], F32))
        sum2 = es.enter_context(nc.sbuf_tensor("sum2", [128, NT], F32))
        sq2 = es.enter_context(nc.sbuf_tensor("sq2", [128, NT], F32))
        sum3 = es.enter_context(nc.sbuf_tensor("sum3", [128, NT], F32))
        sq3 = es.enter_context(nc.sbuf_tensor("sq3", [128, NT], F32))
        mx3t = es.enter_context(nc.sbuf_tensor("mx3t", [128, NT], F32))
        tmpn = es.enter_context(nc.sbuf_tensor("tmpn", [128, NT], F32))
        gbc = es.enter_context(nc.sbuf_tensor("gbc", [128, DM], F32))
        sqd = es.enter_context(nc.sbuf_tensor("sqd", [128, DM], BF16))
        rmax = es.enter_context(nc.sbuf_tensor("rmax2", [128, 1], F32))
        rowm = es.enter_context(nc.sbuf_tensor("rowm", [1, 128], F32))
        gmx = es.enter_context(nc.sbuf_tensor("gmx", [1, 1], F32))
        epst = es.enter_context(nc.sbuf_tensor("epst", [128, 2], F32))

        p = Prog(es, nc)
        block = p.block
        ld = p.ctr("ld")
        cv = p.ctr("cv")
        ca = p.ctr("ca")
        cp = p.ctr("cp")
        cpl = p.ctr("cpl")
        so = p.ctr("so")
        cx1 = p.ctr("cx1")
        cxs = [p.ctr(f"cxs{k}") for k in range(2)]

        @block.sync
        def _(sync):
            ld.dma(sync.dma_start(out=sgm[:], in_=gmax))
            ld.dma(sync.dma_start(out=sb2[:], in_=b2))
            ld.dma(sync.dma_start(out=ident[:], in_=ident_d))
            ld.dma(sync.dma_start(out=identb[:], in_=identb_d))
            ld.dma(sync.dma_start(out=onesf[:], in_=onesf_d))
            ld.dma(sync.dma_start(out=onesb[:], in_=onesb_d))

        @block.vector
        def _(vector):
            ld.wait(vector)
            ser(vector, cv, lambda: vector.memset(epst[:, 0:1], EPS_LN))
            ser(vector, cv, lambda: vector.memset(epst[:, 1:2], EPS_RMS))

        esA = contextlib.ExitStack()
        pmm = [esA.enter_context(nc.psum_tensor(f"pmm{j}", [128, 512], F32)) for j in range(4)]

        _quant_scalars(p, sgm, srow, onesf, invs_b, s_b, pmm[0], cv, cp)

        ctx = dict(cv=cv, ca=ca, cp=cp, ld=ld, ident=ident, identb=identb,
                   onesf=onesf, invs_b=invs_b, s_b=s_b, pmm=pmm,
                   rows_d=rows_d, sb2=sb2, srow=srow)

        def post_tile(i, x1_ap):
            j = i % 2

            @block.sync
            def _(sync, i=i, j=j):
                cv.wait(sync)
                cxs[j].dma(sync.dma_start(
                    out=x1s.rearrange("(t p) d -> t p d", p=128)[i], in_=x1_ap[:]))

            @block.vector
            def _(vector, i=i):
                ser(vector, cv, lambda: vector.reduce_sum(sum2[:, i:i + 1], x1_ap[:], axis=AX.X))

            @block.scalar
            def _(scalar, i=i):
                cv.wait(scalar)
                ser(scalar, ca, lambda: scalar.activation(
                    sqd[:], x1_ap[:], AF.Square, accum_out=sq2[:, i:i + 1]))
            return [(cxs[j], 16 * (i // 2 + 1)), (ca, ca.n)]

        _emit_m2(nc, esA, p, ctx, h_in=h_in, w2_in=w2, x_in=x, post_tile=post_tile)
        p.barrier()

        @block.vector
        def _(vector):
            ser(vector, cv, lambda: vector.tensor_scalar(st2[:, 0:NT], sum2[:], INVD, None, ALU.mult))
            ser(vector, cv, lambda: vector.tensor_tensor(tmpn[:], st2[:, 0:NT], st2[:, 0:NT], ALU.mult))
            ser(vector, cv, lambda: vector.tensor_scalar(sq2[:], sq2[:], INVD, None, ALU.mult))
            ser(vector, cv, lambda: vector.tensor_tensor(tmpn[:], sq2[:], tmpn[:], ALU.subtract))

        _ln_rstd(block, ca, cv, tmpn[:], st2[:, NT:2 * NT], epst[:, 0:1])

        @block.vector
        def _(vector):
            ca.wait(vector)
            ser(vector, cv, lambda: vector.tensor_tensor(tmpn[:], st2[:, 0:NT], st2[:, NT:2 * NT], ALU.mult))
            ser(vector, cv, lambda: vector.tensor_scalar(tmpn[:], tmpn[:], -1.0, None, ALU.mult))

        p.barrier()
        esA.close()

        esQ = contextlib.ExitStack()
        qrT = esQ.enter_context(nc.sbuf_tensor("qrT", [128, ND * S], BF16))
        krT = esQ.enter_context(nc.sbuf_tensor("krT", [128, ND * S], BF16))
        vtk = esQ.enter_context(nc.sbuf_tensor("vtk", [128, NT * DM], BF16))
        esB = contextlib.ExitStack()
        wqkv_bf = esB.enter_context(nc.sbuf_tensor("wqkv_bf", [128, ND * 3 * DM], BF16))
        sbqkv = esB.enter_context(nc.sbuf_tensor("sbqkv", [24, 128], F32))
        bqkvT = esB.enter_context(nc.sbuf_tensor("bqkvT", [128, 24], F32))
        bqkv_rb = esB.enter_context(nc.sbuf_tensor("bqkv_rb", [1, 3 * DM], BF16))
        sbq_row = esB.enter_context(nc.sbuf_tensor("sbq_row", [1, 3 * DM], F32))
        g_row = esB.enter_context(nc.sbuf_tensor("g_row", [1, DM], F32))
        x1t = esB.enter_context(nc.sbuf_tensor("x1t", [128, DM], F32))
        u2t = esB.enter_context(nc.sbuf_tensor("u2t", [128, DM], F32))
        msr = esB.enter_context(nc.sbuf_tensor("msr", [128, NT], F32))
        hbt = esB.enter_context(nc.sbuf_tensor("hbt", [128, DM], BF16))
        hTw = esB.enter_context(nc.sbuf_tensor("hTw", [128, ND * 512], BF16))
        cosw = esB.enter_context(nc.sbuf_tensor("cosw", [128, 4 * 512], BF16))
        sinw = esB.enter_context(nc.sbuf_tensor("sinw", [128, 4 * 512], BF16))
        rt1 = esB.enter_context(nc.sbuf_tensor("rt1", [128, 512], F32))
        rt2 = esB.enter_context(nc.sbuf_tensor("rt2", [128, 512], F32))
        pqk = [esB.enter_context(nc.psum_tensor(f"pqk{j}", [128, 512], F32)) for j in range(4)]

        cqkd = p.ctr("cqkd")
        cbq = p.ctr("cbq")
        cxw = p.ctr("cxw")
        cht = p.ctr("cht")
        ctw = p.ctr("ctw")
        cqm = p.ctr("cqm")
        CVH, CVHE, CVROPE, CQMW = {}, {}, {}, {}

        @block.gpsimd
        def _(gpsimd):
            ld.wait(gpsimd)
            cqkd.dma(gpsimd.dma_start(
                out=wqkv_bf[:], in_=wqkv.rearrange("(c p) m -> p c m", p=128)))

        @block.sync
        def _(sync):
            ld.wait(sync)
            cbq.dma(sync.dma_start(out=sbqkv[:], in_=bqkv))
            cbq.dma(sync.dma_start(out=sbq_row[:], in_=bqkv_r))
            cbq.dma(sync.dma_start(out=g_row[:], in_=g_r))

        @block.tensor
        def _(tensor):
            cbq.wait(tensor)
            cv.wait(tensor)
            tensor.transpose(pqk[0][0:128, 0:24], sbqkv[:], ident[0:24, 0:24])
            tensor.matmul(pqk[1][:, 0:512], onesf[:], g_row[:, 0:512], start=True, stop=True)
            cp.inc(tensor.matmul(pqk[2][:, 0:512], onesf[:], g_row[:, 512:1024], start=True, stop=True))

        @block.vector
        def _(vector):
            cp.wait(vector)
            ser(vector, cv, lambda: vector.tensor_copy(bqkvT[:], pqk[0][0:128, 0:24]))
            ser(vector, cv, lambda: vector.tensor_copy(gbc[:, 0:512], pqk[1][:, 0:512]))
            ser(vector, cv, lambda: vector.tensor_copy(gbc[:, 512:1024], pqk[2][:, 0:512]))
            ser(vector, cv, lambda: vector.tensor_copy(bqkv_rb[:], sbq_row[:]))

        p.barrier()

        for tw in range(4):
            @block.sync
            def _(sync, tw=tw):
                cv.wait(sync)
                for c in range(4):
                    ctw.dma(sync.dma_start(out=cosw[:, c * 512:(c + 1) * 512],
                                           in_=cosT[c * 128:(c + 1) * 128, tw * 512:(tw + 1) * 512]))
                    ctw.dma(sync.dma_start(out=sinw[:, c * 512:(c + 1) * 512],
                                           in_=sinT[c * 128:(c + 1) * 128, tw * 512:(tw + 1) * 512]))

            for ti in range(4):
                i = tw * 4 + ti

                @block.sync
                def _(sync, i=i):
                    cv.wait(sync)
                    ca.wait(sync)
                    cxw.dma(sync.dma_start(
                        out=x1t[:], in_=x1s.rearrange("(t p) d -> t p d", p=128)[i]))

                @block.scalar
                def _(scalar, i=i):
                    cxw.wait(scalar)
                    ser(scalar, ca, lambda: scalar.activation(
                        u2t[:], x1t[:], AF.Identity,
                        bias=tmpn[:, i:i + 1], scale=st2[:, NT + i:NT + i + 1]))
                    ser(scalar, ca, lambda: scalar.activation(
                        sqd[:], u2t[:], AF.Square, accum_out=msr[:, i:i + 1]))

                @block.vector
                def _(vector, i=i):
                    ca.wait(vector)
                    ser(vector, cv, lambda: vector.tensor_scalar(
                        msr[:, i:i + 1], msr[:, i:i + 1], INVD, None, ALU.mult))

                @block.scalar
                def _(scalar, i=i):
                    cv.wait(scalar)
                    ser(scalar, ca, lambda: scalar.activation(
                        msr[:, i:i + 1], msr[:, i:i + 1], AF.Ln, bias=epst[:, 1:2]))
                    ser(scalar, ca, lambda: scalar.activation(
                        msr[:, i:i + 1], msr[:, i:i + 1], AF.Exp, scale=-0.5))
                    ser(scalar, ca, lambda: scalar.activation(
                        u2t[:], u2t[:], AF.Copy, scale=msr[:, i:i + 1]))

                @block.vector
                def _(vector, i=i):
                    ca.wait(vector)
                    cht.wait(vector, at=i)
                    ser(vector, cv, lambda: vector.tensor_tensor(hbt[:], u2t[:], gbc[:], ALU.mult))
                CVH[i] = cv.n

                @block.tensor
                def _(tensor, tw=tw, ti=ti, i=i):
                    cv.wait(tensor, at=CVH[i])
                    if tw > 0:
                        cv.wait(tensor, at=CVROPE[tw - 1])
                    if i >= 2:
                        cv.wait(tensor, at=CVHE[i - 2])
                    pv = pqk[ti % 2].bitcast(BF16)
                    for c in range(ND):
                        ins = tensor.transpose(pv[:, c * 128:(c + 1) * 128],
                                               hbt[:, c * 128:(c + 1) * 128], identb[:])
                    cht.inc(ins)

                @block.vector
                def _(vector, tw=tw, ti=ti, i=i):
                    cht.wait(vector, at=i + 1)
                    if tw > 0:
                        cqm.wait(vector, at=CQMW[tw - 1])
                    pv = pqk[ti % 2].bitcast(BF16)
                    for c in range(ND):
                        ser(vector, cv, lambda c=c: vector.tensor_copy(
                            hTw[:, c * 512 + ti * 128:c * 512 + (ti + 1) * 128],
                            pv[:, c * 128:(c + 1) * 128]))
                    CVHE[i] = cv.n

            for qk in range(2):
                dst = qrT if qk == 0 else krT
                mbase = qk * 8
                for c in range(4):
                    mA, mB = mbase + c, mbase + c + 4

                    @block.tensor
                    def _(tensor, tw=tw, mA=mA, mB=mB):
                        cv.wait(tensor, at=CVHE[tw * 4 + 3])
                        cqkd.wait(tensor)
                        ctw.wait(tensor)
                        cv.wait(tensor)
                        for m, bank in ((mA, 0), (mB, 1)):
                            for d in range(ND):
                                ins = tensor.matmul(
                                    pqk[bank][:],
                                    wqkv_bf[:, d * 3 * DM + m * 128:d * 3 * DM + (m + 1) * 128],
                                    hTw[:, d * 512:(d + 1) * 512],
                                    start=(d == 0), stop=(d == ND - 1))
                                if d == ND - 1:
                                    cqm.inc(ins)

                    @block.vector
                    def _(vector, tw=tw, c=c, mA=mA, mB=mB, dst=dst):
                        cqm.wait(vector)
                        colA = c * S + tw * 512
                        colB = (c + 4) * S + tw * 512
                        cw_ = cosw[:, c * 512:(c + 1) * 512]
                        sw_ = sinw[:, c * 512:(c + 1) * 512]
                        ser(vector, cv, lambda: vector.scalar_tensor_tensor(
                            rt1[:], pqk[0][:], bqkvT[:, mA:mA + 1], cw_, ALU.add, ALU.mult))
                        ser(vector, cv, lambda: vector.scalar_tensor_tensor(
                            rt2[:], pqk[1][:], bqkvT[:, mB:mB + 1], sw_, ALU.add, ALU.mult))
                        ser(vector, cv, lambda: vector.tensor_tensor(
                            dst[:, colA:colA + 512], rt1[:], rt2[:], ALU.subtract))
                        ser(vector, cv, lambda: vector.scalar_tensor_tensor(
                            rt1[:], pqk[1][:], bqkvT[:, mB:mB + 1], cw_, ALU.add, ALU.mult))
                        ser(vector, cv, lambda: vector.scalar_tensor_tensor(
                            rt2[:], pqk[0][:], bqkvT[:, mA:mA + 1], sw_, ALU.add, ALU.mult))
                        ser(vector, cv, lambda: vector.tensor_tensor(
                            dst[:, colB:colB + 512], rt1[:], rt2[:], ALU.add))

            for ti in range(4):
                i = tw * 4 + ti

                @block.tensor
                def _(tensor, tw=tw, ti=ti, i=i):
                    ca.wait(tensor)
                    for e in range(2):
                        bank = 2 + e
                        for d in range(ND):
                            tensor.matmul(
                                pqk[bank][:],
                                hTw[:, d * 512 + ti * 128:d * 512 + (ti + 1) * 128],
                                wqkv_bf[:, d * 3 * DM + 2 * DM + e * 512:d * 3 * DM + 2 * DM + (e + 1) * 512],
                                start=(d == 0), stop=False)
                        ins = tensor.matmul(
                            pqk[bank][:], onesb[:],
                            bqkv_rb[:, 2 * DM + e * 512:2 * DM + (e + 1) * 512],
                            start=False, stop=True)
                        cqm.inc(ins)

                @block.scalar
                def _(scalar, i=i):
                    cqm.wait(scalar)
                    for e in range(2):
                        ser(scalar, ca, lambda e=e: scalar.activation(
                            vtk[:, i * DM + e * 512:i * DM + (e + 1) * 512],
                            pqk[2 + e][:], AF.Copy))

            CVROPE[tw] = cv.n
            CQMW[tw] = cqm.n

        p.barrier()
        esB.close()

        esC = contextlib.ExitStack()
        scps = esC.enter_context(nc.psum_tensor("scps", [128, S], F32))
        avp = esC.enter_context(nc.psum_tensor("avp", [128, DM], F32))
        pjp = esC.enter_context(nc.psum_tensor("pjp", [128, DM], F32))
        wproj_bf = esC.enter_context(nc.sbuf_tensor("wproj_bf", [128, ND * DM], BF16))
        bproj_rb = esC.enter_context(nc.sbuf_tensor("bproj_rb", [1, DM], BF16))
        sbp_row = esC.enter_context(nc.sbuf_tensor("sbp_row", [1, DM], F32))
        abuf = esC.enter_context(nc.sbuf_tensor("abuf", [128, S], BF16))
        atb = esC.enter_context(nc.sbuf_tensor("atb", [128, S], BF16))
        obf = esC.enter_context(nc.sbuf_tensor("obf", [128, DM], BF16))
        otb = esC.enter_context(nc.sbuf_tensor("otb", [128, DM], BF16))
        mxr = esC.enter_context(nc.sbuf_tensor("mxr", [128, 1], F32))
        mbr = esC.enter_context(nc.sbuf_tensor("mbr", [128, 1], F32))
        rsum = esC.enter_context(nc.sbuf_tensor("rsum", [128, 1], F32))
        rs2 = [esC.enter_context(nc.sbuf_tensor(f"rs2{k}", [128, 1], F32)) for k in range(2)]
        x1c = esC.enter_context(nc.sbuf_tensor("x1c", [128, DM], F32))
        u2rc = esC.enter_context(nc.sbuf_tensor("u2rc", [128, DM], F32))
        x2c = esC.enter_context(nc.sbuf_tensor("x2c", [128, DM], F32))

        cpj = p.ctr("cpj")
        ctp = p.ctr("ctp")
        cto = p.ctr("cto")
        cwp = p.ctr("cwp")
        cbp = p.ctr("cbp")
        CAB, CVAT, CAO, CVOT = {}, {}, {}, {}

        @block.gpsimd
        def _(gpsimd):
            ld.wait(gpsimd)
            cwp.dma(gpsimd.dma_start(
                out=wproj_bf[:], in_=wproj.rearrange("(c p) m -> p c m", p=128)))

        @block.sync
        def _(sync):
            ld.wait(sync)
            cbp.dma(sync.dma_start(out=sbp_row[:], in_=bproj_r))

        @block.vector
        def _(vector):
            cwp.wait(vector)
            cbp.wait(vector)
            ser(vector, cv, lambda: vector.tensor_copy(bproj_rb[:], sbp_row[:]))
            ser(vector, cv, lambda: vector.memset(rmax[:], -1e30))

        p.barrier()

        CVMB, CPJ_S = {}, {}

        def emit_scores(i):
            @block.tensor
            def _(tensor, i=i):
                if i == 0:
                    p.wait_all(tensor)
                else:
                    ca.wait(tensor, at=CAB[i - 1])
                    cv.wait(tensor, at=CVAT[i - 1])
                for n in range(4):
                    for c in range(ND):
                        ins = tensor.matmul(
                            scps[:, n * 512:(n + 1) * 512],
                            qrT[:, c * S + i * 128:c * S + (i + 1) * 128],
                            krT[:, c * S + n * 512:c * S + (n + 1) * 512],
                            start=(c == 0), stop=(c == ND - 1))
                        if c == ND - 1:
                            cpj.inc(ins)
            CPJ_S[i] = cpj.n

        def emit_smax(i):
            @block.vector
            def _(vector, i=i):
                cpj.wait(vector, at=CPJ_S[i])
                if i > 0:
                    ca.wait(vector, at=CAB[i - 1])
                ser(vector, cv, lambda: vector.reduce_max(mxr[:], scps[:], axis=AX.X))
                ser(vector, cv, lambda: vector.tensor_scalar(mbr[:], mxr[:], -SCALE_SM, None, ALU.mult))
            CVMB[i] = cv.n

        def emit_exp(i):
            @block.scalar
            def _(scalar, i=i):
                cv.wait(scalar, at=CVMB[i])
                ctp.wait(scalar, at=i)
                ser(scalar, ca, lambda: scalar.activation(
                    abuf[:], scps[:], AF.Exp, bias=mbr[:, 0:1], scale=SCALE_SM,
                    accum_out=rsum[:]))
            CAB[i] = ca.n

            @block.vector
            def _(vector, i=i):
                ca.wait(vector, at=CAB[i])
                ser(vector, cv, lambda: vector.reciprocal(rs2[i % 2][:], rsum[:]))

        def emit_tA(i):
            @block.tensor
            def _(tensor, i=i):
                ca.wait(tensor, at=CAB[i])
                pv = scps.bitcast(BF16)
                for c in range(NT):
                    ins = tensor.transpose(pv[:, c * 128:(c + 1) * 128],
                                           abuf[:, c * 128:(c + 1) * 128], identb[:])
                ctp.inc(ins)

            @block.vector
            def _(vector, i=i):
                ctp.wait(vector, at=i + 1)
                pv = scps.bitcast(BF16)
                ser(vector, cv, lambda: vector.tensor_copy(atb[:, 0:1024], pv[:, 0:1024]))
                ser(vector, cv, lambda: vector.tensor_copy(atb[:, 1024:2048], pv[:, 1024:2048]))
                CVAT[i] = cv.n

        def emit_av(i):
            @block.tensor
            def _(tensor, i=i):
                cv.wait(tensor, at=CVAT[i])
                for e in range(2):
                    for c in range(NT):
                        ins = tensor.matmul(
                            avp[:, e * 512:(e + 1) * 512],
                            atb[:, c * 128:(c + 1) * 128],
                            vtk[:, c * DM + e * 512:c * DM + (e + 1) * 512],
                            start=(c == 0), stop=(c == NT - 1))
                        if c == NT - 1:
                            cpj.inc(ins)

        def emit_obf(i):
            @block.scalar
            def _(scalar, i=i):
                cpj.wait(scalar)
                cv.wait(scalar)
                ser(scalar, ca, lambda: scalar.activation(
                    obf[:], avp[:], AF.Copy, scale=rs2[i % 2][:, 0:1]))
            CAO[i] = ca.n

        def emit_tO(i):
            @block.tensor
            def _(tensor, i=i):
                ca.wait(tensor, at=CAO[i])
                pv = avp.bitcast(BF16)
                for e in range(ND):
                    ins = tensor.transpose(pv[:, e * 128:(e + 1) * 128],
                                           obf[:, e * 128:(e + 1) * 128], identb[:])
                cto.inc(ins)

            @block.vector
            def _(vector, i=i):
                cto.wait(vector, at=i + 1)
                ser(vector, cv, lambda: vector.tensor_copy(otb[:], avp.bitcast(BF16)[:, 0:1024]))
                CVOT[i] = cv.n

        def emit_proj(i):
            @block.tensor
            def _(tensor, i=i):
                cv.wait(tensor, at=CVOT[i])
                cwp.wait(tensor)
                cv.wait(tensor)
                for n in range(2):
                    for e in range(ND):
                        tensor.matmul(
                            pjp[:, n * 512:(n + 1) * 512],
                            otb[:, e * 128:(e + 1) * 128],
                            wproj_bf[:, e * DM + n * 512:e * DM + (n + 1) * 512],
                            start=(e == 0), stop=False)
                    ins = tensor.matmul(
                        pjp[:, n * 512:(n + 1) * 512], onesb[:],
                        bproj_rb[:, n * 512:(n + 1) * 512], start=False, stop=True)
                    cpj.inc(ins)

        def emit_post(i):
            @block.sync
            def _(sync, i=i):
                cv.wait(sync)
                ca.wait(sync)
                cx1.dma(sync.dma_start(
                    out=x1c[:], in_=x1s.rearrange("(t p) d -> t p d", p=128)[i]))

            @block.scalar
            def _(scalar, i=i):
                cx1.wait(scalar)
                ser(scalar, ca, lambda: scalar.activation(
                    u2rc[:], x1c[:], AF.Identity,
                    bias=tmpn[:, i:i + 1], scale=st2[:, NT + i:NT + i + 1]))

            @block.vector
            def _(vector, i=i):
                p.wait_all(vector)
                ser(vector, cv, lambda: vector.tensor_tensor(x2c[:], pjp[:], u2rc[:], ALU.add))
                ser(vector, cv, lambda: vector.tensor_tensor(x2c[:], x2c[:], x1c[:], ALU.add))
                ser(vector, cv, lambda: vector.reduce_sum(sum3[:, i:i + 1], x2c[:], axis=AX.X))
                ser(vector, cv, lambda: vector.reduce_max(mx3t[:, i:i + 1], x2c[:], axis=AX.X))

            @block.scalar
            def _(scalar, i=i):
                cv.wait(scalar)
                ser(scalar, ca, lambda: scalar.activation(
                    sqd[:], x2c[:], AF.Square, accum_out=sq3[:, i:i + 1]))

            @block.sync
            def _(sync, i=i):
                cv.wait(sync)
                so.dma(sync.dma_start(
                    out=x2_o.rearrange("(t p) d -> t p d", p=128)[i], in_=x2c[:]))

        emit_scores(0)
        emit_smax(0)
        emit_exp(0)
        emit_tA(0)
        for i in range(NT):
            emit_av(i)
            if i + 1 < NT:
                emit_scores(i + 1)
                emit_smax(i + 1)
                emit_exp(i + 1)
            emit_obf(i)
            emit_tO(i)
            emit_proj(i)
            emit_post(i)
            if i + 1 < NT:
                emit_tA(i + 1)

        p.barrier()
        esC.close()

        @block.vector
        def _(vector):
            ser(vector, cv, lambda: vector.tensor_scalar(st3[:, 0:NT], sum3[:], INVD, None, ALU.mult))
            ser(vector, cv, lambda: vector.tensor_tensor(tmpn[:], st3[:, 0:NT], st3[:, 0:NT], ALU.mult))
            ser(vector, cv, lambda: vector.tensor_scalar(sq3[:], sq3[:], INVD, None, ALU.mult))
            ser(vector, cv, lambda: vector.tensor_tensor(tmpn[:], sq3[:], tmpn[:], ALU.subtract))

        _ln_rstd(block, ca, cv, tmpn[:], st3[:, NT:2 * NT], epst[:, 0:1])

        esD = contextlib.ExitStack()
        pfold = esD.enter_context(nc.psum_tensor("pfold", [128, 512], F32))

        @block.vector
        def _(vector):
            ca.wait(vector)
            ser(vector, cv, lambda: vector.tensor_tensor(mx3t[:], mx3t[:], st3[:, 0:NT], ALU.subtract))
            ser(vector, cv, lambda: vector.tensor_tensor(mx3t[:], mx3t[:], st3[:, NT:2 * NT], ALU.mult))
            ser(vector, cv, lambda: vector.reduce_max(rmax[:], mx3t[:], axis=AX.X))

        _fold_max(p, ident, pfold, rmax, rowm, gmx, cp, cv)

        @block.sync
        def _(sync):
            cv.wait(sync)
            so.dma(sync.dma_start(out=st_o, in_=st3[:]))
            so.dma(sync.dma_start(out=mx_o, in_=gmx[:]))
            so.wait(sync)
        esD.close()
        esQ.close()
    return nc



def _rope_tables():
    half = 512
    inv_freq = 10000.0 ** (-np.arange(half, dtype=np.float64) / half)
    ang = np.arange(S, dtype=np.float64)[:, None] * inv_freq[None, :]
    cos = np.rint(np.cos(ang)).astype(ml_dtypes.bfloat16)
    sin = np.rint(np.sin(ang)).astype(ml_dtypes.bfloat16)
    return np.ascontiguousarray(cos.T), np.ascontiguousarray(sin.T)


_NC_CACHE = {}


def _get_nc(name, builder):
    if name not in _NC_CACHE:
        nc = bass.Bass("TRN2", target_bir_lowering=False)
        _NC_CACHE[name] = builder(nc)
    return _NC_CACHE[name]


def _run(name, builder, in_maps):
    nc = _get_nc(name, builder)
    res = run_bass_kernel_spmd(nc, in_maps, core_ids=list(range(8)))
    return res.results


def kernel(L0_attn_x, L0_attn_g_norm, L0_attn_w_qkv, L0_attn_b_qkv,
           L0_attn_w_proj, L0_attn_b_proj, L0_ffn_w1, L0_ffn_b1,
           L0_ffn_w2, L0_ffn_b2):
    xs = [np.ascontiguousarray(np.asarray(L0_attn_x)[c], dtype=np.float32) for c in range(8)]
    w1 = np.ascontiguousarray(np.asarray(L0_ffn_w1, dtype=np.float32))
    b1 = np.ascontiguousarray(np.asarray(L0_ffn_b1, dtype=np.float32).reshape(32, 128))
    w2 = np.ascontiguousarray(np.asarray(L0_ffn_w2, dtype=np.float32))
    b2 = np.ascontiguousarray(np.asarray(L0_ffn_b2, dtype=np.float32).reshape(8, 128))
    wqkv = np.ascontiguousarray(np.asarray(L0_attn_w_qkv, dtype=np.float32))
    bqkv = np.ascontiguousarray(np.asarray(L0_attn_b_qkv, dtype=np.float32).reshape(24, 128))
    bqkv_r = np.ascontiguousarray(np.asarray(L0_attn_b_qkv, dtype=np.float32).reshape(1, 3 * DM))
    wproj = np.ascontiguousarray(np.asarray(L0_attn_w_proj, dtype=np.float32))
    bproj_r = np.ascontiguousarray(np.asarray(L0_attn_b_proj, dtype=np.float32).reshape(1, DM))
    g_r = np.ascontiguousarray(np.asarray(L0_attn_g_norm, dtype=np.float32).reshape(1, DM))
    cosT, sinT = _rope_tables()
    ident = np.eye(128, dtype=np.float32)
    identb = np.eye(128, dtype=ml_dtypes.bfloat16)
    onesf = np.ones((1, 128), dtype=np.float32)
    onesb = np.ones((1, 128), dtype=ml_dtypes.bfloat16)

    r0 = _run("stats", build_stats, [
        {"x": xs[c], "ident": ident} for c in range(8)])
    gmax1 = np.concatenate([r0[c]["mx"] for c in range(8)], axis=1)

    r1 = _run("m1", build_m1, [
        {"x": xs[c], "st": r0[c]["st"], "gmax": gmax1, "w1": w1, "b1": b1,
         "ident": ident, "identb": identb, "onesf": onesf} for c in range(8)])
    gmax2 = np.concatenate([r1[c]["mx"] for c in range(8)], axis=1)

    r2 = _run("mid", build_mid, [
        {"x": xs[c], "H": r1[c]["H"], "gmax": gmax2, "w2": w2, "b2": b2,
         "wqkv": wqkv, "bqkv": bqkv, "bqkv_r": bqkv_r, "wproj": wproj,
         "bproj_r": bproj_r, "g_r": g_r, "cosT": cosT, "sinT": sinT,
         "ident": ident, "identb": identb, "onesf": onesf, "onesb": onesb} for c in range(8)])
    gmax3 = np.concatenate([r2[c]["mx"] for c in range(8)], axis=1)

    r3 = _run("m1", build_m1, [
        {"x": r2[c]["x2"], "st": r2[c]["st3"], "gmax": gmax3, "w1": w1, "b1": b1,
         "ident": ident, "identb": identb, "onesf": onesf} for c in range(8)])
    gmax4 = np.concatenate([r3[c]["mx"] for c in range(8)], axis=1)

    r4 = _run("m2f", build_m2f, [
        {"x": r2[c]["x2"], "H": r3[c]["H"], "gmax": gmax4, "w2": w2, "b2": b2,
         "ident": ident, "identb": identb, "onesf": onesf} for c in range(8)])

    out = np.stack([r4[c]["y"] for c in range(8)], axis=0)
    return out.astype(np.float32)


# version 11 (final)
# speedup vs baseline: 1.5304x; 1.0020x over previous
import contextlib

import numpy as np
import ml_dtypes

import concourse.bass as bass
import concourse.mybir as mybir
from concourse.bass_utils import run_bass_kernel_spmd

F32 = mybir.dt.float32
BF16 = mybir.dt.bfloat16
I32 = mybir.dt.int32
AF = mybir.ActivationFunctionType
ALU = mybir.AluOpType
AX = mybir.AxisListType

S = 2048
DM = 1024
FF = 4096
NT = S // 128
ND = DM // 128
NF = FF // 128
EPS_LN = 1e-5
EPS_RMS = 1e-6
EPS_Q = 1e-8
LN3 = 1.0986122886681098
INV255 = float(np.float32(1.0) / np.float32(255.0))
INV127 = float(np.float32(1.0) / np.float32(127.0))
SCALE_SM = float(np.float32(LN3) / np.float32(32.0))
INVD = 1.0 / DM


class Ctr:
    def __init__(self, es, nc, name):
        self.h = es.enter_context(nc.semaphore(name))
        self.n = 0

    def inc(self, instr, k=1):
        instr.then_inc(self.h, k)
        self.n += k
        return instr

    def dma(self, instr):
        return self.inc(instr, 16)

    def wait(self, eng, at=None):
        eng.wait_ge(self.h, self.n if at is None else at)


class Prog:

    def __init__(self, es, nc):
        self.nc = nc
        self.es = es
        self.block = es.enter_context(nc.Block())
        self.ctrs = []

    def ctr(self, name):
        c = Ctr(self.es, self.nc, name)
        self.ctrs.append(c)
        return c

    def wait_all(self, eng):
        for c in self.ctrs:
            if c.n > 0:
                c.wait(eng)

    def barrier(self, engines=("sync", "scalar", "vector", "tensor", "gpsimd")):
        b = self.block
        if "sync" in engines:
            @b.sync
            def _(sync):
                self.wait_all(sync)
        if "scalar" in engines:
            @b.scalar
            def _(scalar):
                self.wait_all(scalar)
        if "vector" in engines:
            @b.vector
            def _(vector):
                self.wait_all(vector)
        if "tensor" in engines:
            @b.tensor
            def _(tensor):
                self.wait_all(tensor)
        if "gpsimd" in engines:
            @b.gpsimd
            def _(gpsimd):
                self.wait_all(gpsimd)


def ser(eng, ctr, emit):
    ctr.wait(eng)
    return ctr.inc(emit())


def _round_to_bf16(vector, cv, scratch_f32, out_bf, lo=None, hi=None):
    if lo is not None:
        ser(vector, cv, lambda: vector.tensor_scalar(
            scratch_f32[:], scratch_f32[:], float(lo), float(hi), ALU.max, ALU.min))
    ser(vector, cv, lambda: vector.tensor_copy(scratch_f32.bitcast(I32)[:], scratch_f32[:]))
    ser(vector, cv, lambda: vector.tensor_copy(out_bf, scratch_f32.bitcast(I32)[:]))


def _ln_rstd(block, ca, cv, var_ap, out_ap, eps_ap):
    @block.scalar
    def _(scalar):
        cv.wait(scalar)
        ser(scalar, ca, lambda: scalar.activation(var_ap, var_ap, AF.Ln, bias=eps_ap))
        ser(scalar, ca, lambda: scalar.activation(out_ap, var_ap, AF.Exp, scale=-0.5))


def _fold_max(p, ident, pbank, rmax, rowm, gmx, cp, cv):
    block = p.block

    @block.tensor
    def _(tensor):
        cv.wait(tensor)
        cp.inc(tensor.transpose(pbank[0:1, 0:128], rmax[:], ident[:]))

    @block.vector
    def _(vector):
        cp.wait(vector)
        ser(vector, cv, lambda: vector.tensor_copy(rowm[:], pbank[0:1, 0:128]))
        ser(vector, cv, lambda: vector.reduce_max(gmx[:], rowm[:], axis=AX.X))


def _quant_scalars(p, sgm, srow, onesf, invs_b, s_b, pbank, cv, cp):
    block = p.block

    @block.vector
    def _(vector):
        ser(vector, cv, lambda: vector.reduce_max(srow[:, 0:1], sgm[:], axis=AX.X))
        ser(vector, cv, lambda: vector.tensor_scalar(srow[:, 0:1], srow[:, 0:1], EPS_Q, None, ALU.max))
        ser(vector, cv, lambda: vector.reciprocal(srow[:, 1:2], srow[:, 0:1]))
        ser(vector, cv, lambda: vector.tensor_scalar(srow[:, 1:2], srow[:, 1:2], 255.0, None, ALU.mult))
        ser(vector, cv, lambda: vector.tensor_scalar(srow[:, 2:3], srow[:, 0:1], INV255, None, ALU.mult))

    @block.tensor
    def _(tensor):
        cv.wait(tensor)
        tensor.matmul(pbank[:, 0:1], onesf[:], srow[:, 1:2], start=True, stop=True)
        cp.inc(tensor.matmul(pbank[:, 1:2], onesf[:], srow[:, 2:3], start=True, stop=True))

    @block.vector
    def _(vector):
        cp.wait(vector)
        ser(vector, cv, lambda: vector.tensor_copy(invs_b[:], pbank[:, 0:1]))
        ser(vector, cv, lambda: vector.tensor_copy(s_b[:], pbank[:, 1:2]))



def build_stats(nc):
    x = nc.dram_tensor("x", [S, DM], F32, kind="ExternalInput").ap()
    ident_d = nc.dram_tensor("ident", [128, 128], F32, kind="ExternalInput").ap()
    st_o = nc.dram_tensor("st", [128, 2 * NT], F32, kind="ExternalOutput").ap()
    mx_o = nc.dram_tensor("mx", [1, 1], F32, kind="ExternalOutput").ap()

    es = contextlib.ExitStack()
    with es:
        sx = es.enter_context(nc.sbuf_tensor("sx_s", [128, NT * DM], F32))
        ident = es.enter_context(nc.sbuf_tensor("identS", [128, 128], F32))
        sq = es.enter_context(nc.sbuf_tensor("sq", [128, DM], BF16))
        sums = es.enter_context(nc.sbuf_tensor("sums", [128, NT], F32))
        sqs = es.enter_context(nc.sbuf_tensor("sqs", [128, NT], F32))
        mxs = es.enter_context(nc.sbuf_tensor("mxs", [128, NT], F32))
        stt = es.enter_context(nc.sbuf_tensor("stt", [128, 2 * NT], F32))
        tmp = es.enter_context(nc.sbuf_tensor("tmp", [128, NT], F32))
        tmax = es.enter_context(nc.sbuf_tensor("tmax", [128, NT], F32))
        rmax = es.enter_context(nc.sbuf_tensor("rmax", [128, 1], F32))
        rowm = es.enter_context(nc.sbuf_tensor("rowm", [1, 128], F32))
        gmx = es.enter_context(nc.sbuf_tensor("gmx", [1, 1], F32))
        epst = es.enter_context(nc.sbuf_tensor("epst", [128, 1], F32))
        ptr = es.enter_context(nc.psum_tensor("ptr", [128, 512], F32))

        p = Prog(es, nc)
        block = p.block
        ld = p.ctr("ld")
        ca = p.ctr("ca")
        cv = p.ctr("cv")
        cp = p.ctr("cp")
        so = p.ctr("so")

        @block.sync
        def _(sync):
            ld.dma(sync.dma_start(out=sx[:], in_=x.rearrange("(t p) d -> p t d", p=128)))
            ld.dma(sync.dma_start(out=ident[:], in_=ident_d))

        @block.scalar
        def _(scalar):
            ld.wait(scalar)
            for i in range(NT):
                ser(scalar, ca, lambda i=i: scalar.activation(
                    sq[:], sx[:, i * DM:(i + 1) * DM], AF.Square, accum_out=sqs[:, i:i + 1]))

        @block.vector
        def _(vector):
            ld.wait(vector)
            ser(vector, cv, lambda: vector.memset(epst[:], EPS_LN))
            for i in range(NT):
                ser(vector, cv, lambda i=i: vector.reduce_sum(
                    sums[:, i:i + 1], sx[:, i * DM:(i + 1) * DM], axis=AX.X))
                ser(vector, cv, lambda i=i: vector.reduce_max(
                    mxs[:, i:i + 1], sx[:, i * DM:(i + 1) * DM], axis=AX.X))
            ca.wait(vector)
            ser(vector, cv, lambda: vector.tensor_scalar(stt[:, 0:NT], sums[:], INVD, None, ALU.mult))
            ser(vector, cv, lambda: vector.tensor_tensor(tmp[:], stt[:, 0:NT], stt[:, 0:NT], ALU.mult))
            ser(vector, cv, lambda: vector.tensor_scalar(sqs[:], sqs[:], INVD, None, ALU.mult))
            ser(vector, cv, lambda: vector.tensor_tensor(tmp[:], sqs[:], tmp[:], ALU.subtract))

        _ln_rstd(block, ca, cv, tmp[:], stt[:, NT:2 * NT], epst[:, 0:1])

        @block.vector
        def _(vector):
            ca.wait(vector)
            ser(vector, cv, lambda: vector.tensor_tensor(tmax[:], mxs[:], stt[:, 0:NT], ALU.subtract))
            ser(vector, cv, lambda: vector.tensor_tensor(tmax[:], tmax[:], stt[:, NT:2 * NT], ALU.mult))
            ser(vector, cv, lambda: vector.reduce_max(rmax[:], tmax[:], axis=AX.X))

        _fold_max(p, ident, ptr, rmax, rowm, gmx, cp, cv)

        @block.sync
        def _(sync):
            cv.wait(sync)
            so.dma(sync.dma_start(out=st_o, in_=stt[:]))
            so.dma(sync.dma_start(out=mx_o, in_=gmx[:]))
            so.wait(sync)
    return nc



def build_m1(nc):
    x = nc.dram_tensor("x", [S, DM], F32, kind="ExternalInput").ap()
    st_i = nc.dram_tensor("st", [128, 2 * NT], F32, kind="ExternalInput").ap()
    gmax = nc.dram_tensor("gmax", [1, 8], F32, kind="ExternalInput").ap()
    w1 = nc.dram_tensor("w1", [FF, DM], F32, kind="ExternalInput").ap()
    b1 = nc.dram_tensor("b1", [32, 128], F32, kind="ExternalInput").ap()
    ident_d = nc.dram_tensor("ident", [128, 128], F32, kind="ExternalInput").ap()
    identb_d = nc.dram_tensor("identb", [128, 128], BF16, kind="ExternalInput").ap()
    onesf_d = nc.dram_tensor("onesf", [1, 128], F32, kind="ExternalInput").ap()
    h_o = nc.dram_tensor("H", [NF, 128, S], F32, kind="ExternalOutput").ap()
    mx_o = nc.dram_tensor("mx", [1, 1], F32, kind="ExternalOutput").ap()

    es = contextlib.ExitStack()
    with es:
        sx = es.enter_context(nc.sbuf_tensor("sx_s", [128, NT * DM], F32))
        stt = es.enter_context(nc.sbuf_tensor("stt", [128, 2 * NT], F32))
        sgm = es.enter_context(nc.sbuf_tensor("sgm", [1, 8], F32))
        srow = es.enter_context(nc.sbuf_tensor("srow", [1, 4], F32))
        sb1 = es.enter_context(nc.sbuf_tensor("sb1", [32, 128], F32))
        b1T = es.enter_context(nc.sbuf_tensor("b1T", [128, 32], F32))
        ident = es.enter_context(nc.sbuf_tensor("identS", [128, 128], F32))
        identb = es.enter_context(nc.sbuf_tensor("identbS", [128, 128], BF16))
        onesf = es.enter_context(nc.sbuf_tensor("onesf_s", [1, 128], F32))
        invs_b = es.enter_context(nc.sbuf_tensor("invs_b", [128, 1], F32))
        s_b = es.enter_context(nc.sbuf_tensor("s_b", [128, 1], F32))
        scl = es.enter_context(nc.sbuf_tensor("scl", [128, NT], F32))
        bia = es.enter_context(nc.sbuf_tensor("bia", [128, NT], F32))
        qsc = es.enter_context(nc.sbuf_tensor("qsc", [128, DM], F32))
        xq = es.enter_context(nc.sbuf_tensor("xq", [128, NT * DM], BF16))
        xqT = es.enter_context(nc.sbuf_tensor("xqT", [128, ND * S], BF16))
        wbuf = [es.enter_context(nc.sbuf_tensor(f"wbuf{k}", [128, DM], F32)) for k in range(2)]
        wsc = [es.enter_context(nc.sbuf_tensor(f"wsc{k}", [128, DM], F32)) for k in range(2)]
        wq = [es.enter_context(nc.sbuf_tensor(f"wq{k}", [128, DM], BF16)) for k in range(2)]
        wqT = [es.enter_context(nc.sbuf_tensor(f"wqT{k}", [128, ND * 128], BF16)) for k in range(2)]
        amx = [es.enter_context(nc.sbuf_tensor(f"amx{k}", [128, 1], F32)) for k in range(2)]
        invw = [es.enter_context(nc.sbuf_tensor(f"invw{k}", [128, 1], F32)) for k in range(2)]
        deq = [es.enter_context(nc.sbuf_tensor(f"deq{k}", [128, 1], F32)) for k in range(2)]
        b1q = [es.enter_context(nc.sbuf_tensor(f"b1q{k}", [128, 1], F32)) for k in range(2)]
        tb = [es.enter_context(nc.sbuf_tensor(f"tb{k}", [128, 1], F32)) for k in range(2)]
        deq_all = es.enter_context(nc.sbuf_tensor("deq_all", [128, NF], F32))
        b1q_all = es.enter_context(nc.sbuf_tensor("b1q_all", [128, NF], F32))
        rawmax = es.enter_context(nc.sbuf_tensor("rawmax", [128, NF], F32))
        pm4 = es.enter_context(nc.sbuf_tensor("pm4", [128, 4], F32))
        hst = [es.enter_context(nc.sbuf_tensor(f"hst{k}", [128, S], F32)) for k in range(2)]
        rmax = es.enter_context(nc.sbuf_tensor("rmax", [128, 1], F32))
        tmax = [es.enter_context(nc.sbuf_tensor(f"tmax{k}", [128, 1], F32)) for k in range(2)]
        rowm = es.enter_context(nc.sbuf_tensor("rowm", [1, 128], F32))
        gmx = es.enter_context(nc.sbuf_tensor("gmx", [1, 1], F32))
        pful = [es.enter_context(nc.psum_tensor(f"pful{j}", [128, 2048], F32)) for j in range(2)]
        pmm = [pful[b // 4][:, (b % 4) * 512:(b % 4 + 1) * 512] for b in range(8)]

        p = Prog(es, nc)
        block = p.block
        ld = p.ctr("ld")
        cv = p.ctr("cv")
        ca = p.ctr("ca")
        cp = p.ctr("cp")
        cxb4 = [p.ctr(f"cxb{g}") for g in range(4)]
        cpm = p.ctr("cpm")
        cwt = p.ctr("cwt")
        cwte = p.ctr("cwte")
        cwd = [p.ctr(f"cwd{k}") for k in range(2)]
        cwv = [p.ctr(f"cwv{k}") for k in range(2)]
        cwx = [p.ctr(f"cwx{k}") for k in range(2)]
        cmm = [p.ctr(f"cmm{k}") for k in range(8)]
        cev = [p.ctr(f"cev{k}") for k in range(8)]
        chd = [p.ctr(f"chd{k}") for k in range(2)]
        cmx = [p.ctr(f"cmx{k}") for k in range(2)]
        so = p.ctr("so")

        @block.sync
        def _(sync):
            ld.dma(sync.dma_start(out=sx[:], in_=x.rearrange("(t p) d -> p t d", p=128)))
            ld.dma(sync.dma_start(out=stt[:], in_=st_i))
            ld.dma(sync.dma_start(out=sgm[:], in_=gmax))
            ld.dma(sync.dma_start(out=sb1[:], in_=b1))
            ld.dma(sync.dma_start(out=ident[:], in_=ident_d))
            ld.dma(sync.dma_start(out=identb[:], in_=identb_d))
            ld.dma(sync.dma_start(out=onesf[:], in_=onesf_d))

        @block.vector
        def _(vector):
            ld.wait(vector)

        _quant_scalars(p, sgm, srow, onesf, invs_b, s_b, pmm[0], cv, cp)

        @block.tensor
        def _(tensor):
            cv.wait(tensor)
            ld.wait(tensor)
            cp.inc(tensor.transpose(pmm[1][0:128, 0:32], sb1[:], ident[0:32, 0:32]))

        @block.vector
        def _(vector):
            cp.wait(vector)
            ser(vector, cv, lambda: vector.tensor_copy(b1T[:], pmm[1][0:128, 0:32]))
            ser(vector, cv, lambda: vector.tensor_scalar(scl[:], stt[:, NT:2 * NT], invs_b[:, 0:1], None, ALU.mult))
            ser(vector, cv, lambda: vector.tensor_tensor(bia[:], stt[:, 0:NT], scl[:], ALU.mult))
            ser(vector, cv, lambda: vector.tensor_scalar(bia[:], bia[:], -1.0, None, ALU.mult))
            ser(vector, cv, lambda: vector.memset(rmax[:], 0.0))
            cev[0].inc(vector.engine_nop())
            cev[1].inc(vector.engine_nop())

        p.barrier()

        for i in range(NT):
            @block.scalar
            def _(scalar, i=i):
                cv.wait(scalar)
                ser(scalar, ca, lambda: scalar.activation(
                    qsc.bitcast(I32)[:], sx[:, i * DM:(i + 1) * DM], AF.Relu,
                    bias=bia[:, i:i + 1], scale=scl[:, i:i + 1]))

            @block.vector
            def _(vector, i=i):
                ca.wait(vector)
                ser(vector, cv, lambda: vector.tensor_copy(
                    xq[:, i * DM:(i + 1) * DM], qsc.bitcast(I32)[:]))

            @block.sync
            def _(sync, i=i):
                cv.wait(sync)
                for c in range(0, ND, 2):
                    cxb4[i // 4].dma(sync.dma_start_transpose(
                        xqT[:, c * S + i * 128:c * S + (i + 1) * 128],
                        xq[:, i * DM + c * 128:i * DM + (c + 1) * 128]))

            @block.scalar
            def _(scalar, i=i):
                cv.wait(scalar)
                for c in range(1, ND, 2):
                    cxb4[i // 4].dma(scalar.dma_start_transpose(
                        xqT[:, c * S + i * 128:c * S + (i + 1) * 128],
                        xq[:, i * DM + c * 128:i * DM + (c + 1) * 128]))
        CEV_BASE = [1, 1, 0, 0, 0, 0, 0, 0]
        def emit_lqx(fc):
            j = fc % 2
            use = fc // 2
            pg = j * 4

            @block.sync
            def _(sync, fc=fc, j=j, use=use):
                if use > 0:
                    cwv[j].wait(sync, at=use)
                cwd[j].dma(sync.dma_start(out=wbuf[j][:], in_=w1[fc * 128:(fc + 1) * 128, :]))

            @block.vector
            def _(vector, fc=fc, j=j, use=use):
                cwd[j].wait(vector, at=16 * (use + 1))
                if use > 0:
                    cwt.wait(vector, at=fc - 1)
                ser(vector, cv, lambda: vector.reduce_max(amx[j][:], wbuf[j][:], axis=AX.X, apply_absolute_value=True))
                ser(vector, cv, lambda: vector.tensor_scalar(amx[j][:], amx[j][:], EPS_Q, None, ALU.max))
                ser(vector, cv, lambda: vector.reciprocal(invw[j][:], amx[j][:]))
                ser(vector, cv, lambda: vector.tensor_scalar(invw[j][:], invw[j][:], 127.0, None, ALU.mult))
                ser(vector, cv, lambda: vector.tensor_scalar(deq[j][:], amx[j][:], INV127, s_b[:, 0:1], ALU.mult, ALU.mult))
                ser(vector, cv, lambda: vector.tensor_scalar(tb[j][:], b1T[:, fc:fc + 1], invs_b[:, 0:1], invw[j][:, 0:1], ALU.mult, ALU.mult))
                ser(vector, cv, lambda: vector.tensor_copy(tb[j].bitcast(I32)[:], tb[j][:]))
                ser(vector, cv, lambda: vector.tensor_copy(b1q[j][:], tb[j].bitcast(I32)[:]))
                ser(vector, cv, lambda: vector.tensor_tensor(b1q[j][:], b1q[j][:], deq[j][:], ALU.mult))
                ser(vector, cv, lambda: vector.tensor_copy(deq_all[:, fc:fc + 1], deq[j][:]))
                ser(vector, cv, lambda: vector.tensor_copy(b1q_all[:, fc:fc + 1], b1q[j][:]))
                ser(vector, cv, lambda: vector.tensor_scalar(wsc[j].bitcast(I32)[:], wbuf[j][:], invw[j][:, 0:1], None, ALU.mult))
                ser(vector, cv, lambda: vector.tensor_copy(wq[j][:], wsc[j].bitcast(I32)[:]))
                cwv[j].inc(vector.engine_nop())

            CVQS[fc] = cv.n

        def emit_post(fc):
            j = fc % 2
            use = fc // 2
            pg = j * 4

            @block.vector
            def _(vector, fc=fc, j=j, use=use, pg=pg):
                for tq in range(4):
                    cmm[pg + tq].wait(vector, at=use + 1)
                cv.wait(vector)
                cpm.inc(vector.reduce_max(rawmax[:, fc:fc + 1], pful[j][:], axis=AX.X))

            @block.scalar
            def _(scalar, fc=fc, j=j, use=use, pg=pg):
                cpm.wait(scalar, at=fc + 1)
                if use > 0:
                    chd[j].wait(scalar, at=16 * use)
                for tq in range(4):
                    bank = pg + tq
                    cmm[bank].wait(scalar, at=use + 1)
                    cev[bank].inc(scalar.activation(hst[j][:, tq * 512:(tq + 1) * 512],
                                                    pmm[bank][:], AF.Relu,
                                                    bias=b1q_all[:, fc:fc + 1],
                                                    scale=deq_all[:, fc:fc + 1]))

            @block.sync
            def _(sync, fc=fc, j=j, use=use, pg=pg):
                for tq in range(4):
                    cev[pg + tq].wait(sync, at=CEV_BASE[pg + tq] + use + 1)
                chd[j].dma(sync.dma_start(out=h_o[fc], in_=hst[j][:]))

        def emit_wt(fc):
            j = fc % 2
            use = fc // 2
            pg = j * 4

            @block.tensor
            def _(tensor, fc=fc, j=j, use=use, pg=pg):
                cv.wait(tensor, at=CVQS[fc])
                cev[pg].wait(tensor, at=CEV_BASE[pg] + use)
                pview = pmm[pg].bitcast(BF16)
                for c in range(ND):
                    ins = tensor.transpose(pview[:, c * 128:(c + 1) * 128],
                                           wq[j][:, c * 128:(c + 1) * 128], identb[:])
                    if c == ND - 1:
                        cwt.inc(ins)

            @block.vector
            def _(vector, fc=fc, j=j, pg=pg):
                cwt.wait(vector, at=fc + 1)
                cv.wait(vector)
                cwte.inc(vector.tensor_copy(wqT[j][:], pmm[pg].bitcast(BF16)[:, 0:ND * 128]))

        def emit_mms(fc):
            j = fc % 2
            use = fc // 2
            pg = j * 4

            @block.tensor
            def _(tensor, fc=fc, j=j, use=use, pg=pg):
                cwte.wait(tensor, at=fc + 1)
                for tq in range(4):
                    cxb4[tq].wait(tensor)
                    bank = pg + tq
                    cev[bank].wait(tensor, at=CEV_BASE[bank] + use)
                    for d in range(ND):
                        ins = tensor.matmul(pmm[bank][:],
                                            wqT[j][:, d * 128:(d + 1) * 128],
                                            xqT[:, d * S + tq * 512:d * S + (tq + 1) * 512],
                                            start=(d == 0), stop=(d == ND - 1))
                        if d == ND - 1:
                            cmm[bank].inc(ins)

        p.barrier()

        CVQS = {}
        emit_lqx(0)
        emit_lqx(1)
        emit_wt(0)
        emit_wt(1)
        for k2 in range(0, NF, 2):
            if k2 + 2 < NF:
                emit_lqx(k2 + 2)
                emit_lqx(k2 + 3)
            emit_mms(k2)
            emit_mms(k2 + 1)
            emit_post(k2)
            emit_post(k2 + 1)
            if k2 + 2 < NF:
                emit_wt(k2 + 2)
                emit_wt(k2 + 3)

        @block.vector
        def _(vector):
            cpm.wait(vector, at=NF)
            ser(vector, cv, lambda: vector.tensor_tensor(rawmax[:], rawmax[:], deq_all[:], ALU.mult))
            ser(vector, cv, lambda: vector.tensor_tensor(rawmax[:], rawmax[:], b1q_all[:], ALU.add))
            ser(vector, cv, lambda: vector.tensor_scalar(rawmax[:], rawmax[:], 0.0, None, ALU.max))
            ser(vector, cv, lambda: vector.reduce_max(rmax[:], rawmax[:], axis=AX.X))

        p.barrier()
        _fold_max(p, ident, pmm[0], rmax, rowm, gmx, cp, cv)

        @block.sync
        def _(sync):
            cv.wait(sync)
            so.dma(sync.dma_start(out=mx_o, in_=gmx[:]))
            chd[0].wait(sync)
            chd[1].wait(sync)
            so.wait(sync)
    return nc



def _emit_m2(nc, es, p, ctx, *, h_in, w2_in, x_in, post_tile):
    block = p.block
    cv, ca, cp = ctx["cv"], ctx["ca"], ctx["cp"]
    ident, onesf = ctx["ident"], ctx["onesf"]
    invs_b, s_b = ctx["invs_b"], ctx["s_b"]
    sb2 = ctx["sb2"]
    rows_d = ctx["rows_d"]
    pmm = ctx["pmm"]

    wbuf = es.enter_context(nc.sbuf_tensor("m2wbuf", [128, FF], F32))
    wsc = es.enter_context(nc.sbuf_tensor("m2wsc", [128, FF], F32))
    wqb = es.enter_context(nc.sbuf_tensor("m2wqb", [128, FF], BF16))
    w2q = es.enter_context(nc.sbuf_tensor("w2q_s", [128, NF * DM], BF16))
    hbuf = [es.enter_context(nc.sbuf_tensor(f"m2hb{k}", [128, FF], F32)) for k in range(2)]
    hqb = [es.enter_context(nc.sbuf_tensor(f"m2hq{k}", [128, FF], BF16)) for k in range(2)]
    xtb = [es.enter_context(nc.sbuf_tensor(f"m2xt{k}", [128, DM], F32)) for k in range(2)]
    b2T = es.enter_context(nc.sbuf_tensor("b2T", [128, 8], F32))
    sw2a = es.enter_context(nc.sbuf_tensor("sw2a", [128, 8], F32))
    w2inv = es.enter_context(nc.sbuf_tensor("w2inv", [128, 8], F32))
    w2deq = es.enter_context(nc.sbuf_tensor("w2deq", [128, 8], F32))
    b2i = es.enter_context(nc.sbuf_tensor("b2i", [128, 8], F32))
    r8a = es.enter_context(nc.sbuf_tensor("r8a", [8, 128], F32))
    r8b = es.enter_context(nc.sbuf_tensor("r8b", [8, 128], F32))
    rowln = es.enter_context(nc.sbuf_tensor("rowln", [1, DM], F32))
    rowln2 = es.enter_context(nc.sbuf_tensor("rowln2", [1, DM], F32))
    scale_b = es.enter_context(nc.sbuf_tensor("scale_b", [128, DM], F32))
    bias_b = es.enter_context(nc.sbuf_tensor("bias_b", [128, DM], F32))
    x2s = [es.enter_context(nc.sbuf_tensor(f"x2s{k}", [128, DM], F32)) for k in range(2)]
    wsg = [x2s[0].bitcast(BF16), x2s[1].bitcast(BF16)]

    cwd = p.ctr("m2cwd")
    cwxg = [p.ctr(f"m2cwx{g}") for g in range(2)]
    crw = p.ctr("m2crw")
    chg = [p.ctr(f"m2chg{k}") for k in range(2)]
    cxg = [p.ctr(f"m2cxg{k}") for k in range(2)]
    cmm = p.ctr("m2cmm")

    @block.tensor
    def _(tensor):
        cv.wait(tensor)
        ctx["ld"].wait(tensor)
        cp.inc(tensor.transpose(pmm[1][0:128, 0:8], sb2[:], ident[0:8, 0:8]))

    @block.vector
    def _(vector):
        cp.wait(vector)
        ser(vector, cv, lambda: vector.tensor_copy(b2T[:], pmm[1][0:128, 0:8]))

    p.barrier()

    identb = ctx["identb"]
    cpt = p.ctr("m2cpt")
    CVQ = {}
    CVB = {}
    for dc in range(8):
        @block.sync
        def _(sync, dc=dc):
            if dc > 0:
                cv.wait(sync, at=CVQ[dc - 1])
            cwd.dma(sync.dma_start(out=wbuf[:], in_=w2_in[dc * 128:(dc + 1) * 128, :]))

        @block.vector
        def _(vector, dc=dc):
            cwd.wait(vector)
            if dc > 0:
                cpt.wait(vector, at=dc)
            ser(vector, cv, lambda: vector.reduce_max(sw2a[:, dc:dc + 1], wbuf[:], axis=AX.X, apply_absolute_value=True))
            ser(vector, cv, lambda: vector.tensor_scalar(sw2a[:, dc:dc + 1], sw2a[:, dc:dc + 1], EPS_Q, None, ALU.max))
            ser(vector, cv, lambda: vector.reciprocal(w2inv[:, dc:dc + 1], sw2a[:, dc:dc + 1]))
            ser(vector, cv, lambda: vector.tensor_scalar(w2inv[:, dc:dc + 1], w2inv[:, dc:dc + 1], 127.0, None, ALU.mult))
            ser(vector, cv, lambda: vector.tensor_scalar(wsc.bitcast(I32)[:], wbuf[:], w2inv[:, dc:dc + 1], None, ALU.mult))
            ser(vector, cv, lambda: vector.tensor_copy(wqb[:], wsc.bitcast(I32)[:]))
            CVQ[dc] = cv.n

        @block.tensor
        def _(tensor, dc=dc):
            cv.wait(tensor, at=CVQ[dc])
            if dc > 0:
                cv.wait(tensor, at=CVB[dc - 1])
            for b in range(4):
                pv = pmm[b].bitcast(BF16)
                for c in range(8):
                    fc = b * 8 + c
                    ins = tensor.transpose(pv[:, c * 128:(c + 1) * 128],
                                           wqb[:, fc * 128:(fc + 1) * 128], identb[:])
            cpt.inc(ins)

        @block.vector
        def _(vector, dc=dc):
            cpt.wait(vector, at=dc + 1)
            if dc > 0:
                if dc % 4 == 0:
                    cwxg[dc // 4 - 1].wait(vector, at=2048)
                else:
                    cwxg[dc // 4].wait(vector, at=512 * (dc % 4))
            for b in range(4):
                ser(vector, cv, lambda b=b: vector.tensor_copy(
                    wsg[b // 2][:, (b % 2) * 1024:(b % 2 + 1) * 1024],
                    pmm[b].bitcast(BF16)[:, 0:1024]))
            CVB[dc] = cv.n

        @block.sync
        def _(sync, dc=dc):
            cv.wait(sync, at=CVB[dc])
            for fc in range(0, NF, 2):
                cwxg[dc // 4].dma(sync.dma_start(
                    out=w2q[:, fc * DM + dc * 128:fc * DM + (dc + 1) * 128],
                    in_=wsg[fc // 16][:, (fc % 16) * 128:(fc % 16 + 1) * 128]))

        @block.scalar
        def _(scalar, dc=dc):
            cv.wait(scalar, at=CVB[dc])
            for fc in range(1, NF, 2):
                cwxg[dc // 4].dma(scalar.dma_start(
                    out=w2q[:, fc * DM + dc * 128:fc * DM + (dc + 1) * 128],
                    in_=wsg[fc // 16][:, (fc % 16) * 128:(fc % 16 + 1) * 128]))

    p.barrier()

    @block.vector
    def _(vector):
        ser(vector, cv, lambda: vector.tensor_scalar(w2deq[:], sw2a[:], INV127, None, ALU.mult))
        ser(vector, cv, lambda: vector.tensor_scalar(w2deq[:], w2deq[:], s_b[:, 0:1], None, ALU.mult))
        ser(vector, cv, lambda: vector.tensor_scalar(w2deq[:], w2deq[:], 0.5, None, ALU.mult))
        ser(vector, cv, lambda: vector.tensor_scalar(b2i[:], b2T[:], invs_b[:, 0:1], None, ALU.mult))
        ser(vector, cv, lambda: vector.tensor_tensor(b2i[:], b2i[:], w2inv[:], ALU.mult))
        ser(vector, cv, lambda: vector.tensor_copy(b2i.bitcast(I32)[:], b2i[:]))
        ser(vector, cv, lambda: vector.tensor_copy(b2i[:], b2i.bitcast(I32)[:]))
        ser(vector, cv, lambda: vector.tensor_tensor(b2i[:], b2i[:], w2deq[:], ALU.mult))

    @block.tensor
    def _(tensor):
        cv.wait(tensor)
        tensor.transpose(pmm[0][0:8, 0:128], w2deq[:], ident[:])
        cp.inc(tensor.transpose(pmm[1][0:8, 0:128], b2i[:], ident[:]))

    @block.vector
    def _(vector):
        cp.wait(vector)
        ser(vector, cv, lambda: vector.tensor_copy(r8a[:], pmm[0][0:8, 0:128]))
        ser(vector, cv, lambda: vector.tensor_copy(r8b[:], pmm[1][0:8, 0:128]))

    @block.sync
    def _(sync):
        cv.wait(sync)
        crw.dma(sync.dma_start(out=rows_d[0], in_=r8a[:]))
        crw.dma(sync.dma_start(out=rows_d[1], in_=r8b[:]))
        crw.wait(sync)
        crw.dma(sync.dma_start(out=rowln[:], in_=rows_d.rearrange("a b c -> a (b c)")[0:1, :]))
        crw.dma(sync.dma_start(out=rowln2[:], in_=rows_d.rearrange("a b c -> a (b c)")[1:2, :]))

    @block.tensor
    def _(tensor):
        crw.wait(tensor)
        tensor.matmul(pmm[0][:, 0:512], onesf[:], rowln[:, 0:512], start=True, stop=True)
        tensor.matmul(pmm[1][:, 0:512], onesf[:], rowln[:, 512:1024], start=True, stop=True)
        tensor.matmul(pmm[2][:, 0:512], onesf[:], rowln2[:, 0:512], start=True, stop=True)
        cp.inc(tensor.matmul(pmm[3][:, 0:512], onesf[:], rowln2[:, 512:1024], start=True, stop=True))

    @block.vector
    def _(vector):
        cp.wait(vector)
        ser(vector, cv, lambda: vector.tensor_copy(scale_b[:, 0:512], pmm[0][:, 0:512]))
        ser(vector, cv, lambda: vector.tensor_copy(scale_b[:, 512:1024], pmm[1][:, 0:512]))
        ser(vector, cv, lambda: vector.tensor_copy(bias_b[:, 0:512], pmm[2][:, 0:512]))
        ser(vector, cv, lambda: vector.tensor_copy(bias_b[:, 512:1024], pmm[3][:, 0:512]))

    CVROWS = cv.n

    CAH, CVQ2, CVE, PT = {}, {}, {}, {}

    def emit_load(i):
        j = i % 2

        @block.gpsimd
        def _(gpsimd, i=i, j=j):
            if i >= 2:
                cv.wait(gpsimd, at=CVE[i - 2])
            chg[j].dma(gpsimd.dma_start(
                out=hbuf[j][:],
                in_=h_in.rearrange("c p t -> p c t")[:, :, i * 128:(i + 1) * 128]))
            cxg[j].dma(gpsimd.dma_start(
                out=xtb[j][:], in_=x_in.rearrange("(t p) d -> t p d", p=128)[i]))

    def emit_quant(i):
        j = i % 2

        @block.scalar
        def _(scalar, i=i, j=j):
            chg[j].wait(scalar, at=16 * (i // 2 + 1))
            if i >= 2:
                cv.wait(scalar, at=CVQ2[i - 2])
            ser(scalar, ca, lambda: scalar.activation(
                hbuf[j].bitcast(I32)[:], hbuf[j][:], AF.Copy, scale=invs_b[:, 0:1]))
        CAH[i] = ca.n

        @block.vector
        def _(vector, i=i, j=j):
            ca.wait(vector, at=CAH[i])
            if i >= 2:
                cmm.wait(vector, at=2 * (i - 1))
            ser(vector, cv, lambda: vector.tensor_copy(hqb[j][:], hbuf[j].bitcast(I32)[:]))
        CVQ2[i] = cv.n

    def emit_mm(i):
        j = i % 2

        @block.tensor
        def _(tensor, i=i, j=j):
            cv.wait(tensor, at=CVQ2[i])
            cv.wait(tensor, at=CVE[i - 2] if i >= 2 else CVROWS)
            for nb in range(2):
                cwxg[nb].wait(tensor)
                bank = 2 * j + nb
                for fc in range(NF):
                    ins = tensor.matmul(pmm[bank][:],
                                        hqb[j][:, fc * 128:(fc + 1) * 128],
                                        w2q[:, fc * DM + nb * 512:fc * DM + (nb + 1) * 512],
                                        start=(fc == 0), stop=(fc == NF - 1))
                    if fc == NF - 1:
                        cmm.inc(ins)

    def emit_epi(i):
        j = i % 2

        @block.vector
        def _(vector, i=i, j=j):
            cmm.wait(vector, at=2 * (i + 1))
            cxg[j].wait(vector, at=16 * (i // 2 + 1))
            for ctr_, at_ in PT.get(i - 2, []):
                ctr_.wait(vector, at=at_)
            ser(vector, cv, lambda: vector.scalar_tensor_tensor(
                x2s[j][:, 0:512], pmm[2 * j][:], 1.0, scale_b[:, 0:512], ALU.mult, ALU.mult))
            ser(vector, cv, lambda: vector.scalar_tensor_tensor(
                x2s[j][:, 512:1024], pmm[2 * j + 1][:], 1.0, scale_b[:, 512:1024], ALU.mult, ALU.mult))
            ser(vector, cv, lambda: vector.tensor_tensor(x2s[j][:], x2s[j][:], bias_b[:], ALU.add))
            ser(vector, cv, lambda: vector.tensor_tensor(x2s[j][:], x2s[j][:], xtb[j][:], ALU.add))
        CVE[i] = cv.n
        PT[i] = post_tile(i, x2s[j]) or []

    emit_load(0)
    emit_load(1)
    emit_quant(0)
    emit_quant(1)
    for i in range(NT):
        emit_mm(i)
        emit_epi(i)
        if i + 2 < NT:
            emit_load(i + 2)
            emit_quant(i + 2)



def build_m2f(nc):
    x = nc.dram_tensor("x", [S, DM], F32, kind="ExternalInput").ap()
    h_in = nc.dram_tensor("H", [NF, 128, S], F32, kind="ExternalInput").ap()
    gmax = nc.dram_tensor("gmax", [1, 8], F32, kind="ExternalInput").ap()
    w2 = nc.dram_tensor("w2", [DM, FF], F32, kind="ExternalInput").ap()
    b2 = nc.dram_tensor("b2", [8, 128], F32, kind="ExternalInput").ap()
    ident_d = nc.dram_tensor("ident", [128, 128], F32, kind="ExternalInput").ap()
    identb_d = nc.dram_tensor("identb", [128, 128], BF16, kind="ExternalInput").ap()
    onesf_d = nc.dram_tensor("onesf", [1, 128], F32, kind="ExternalInput").ap()
    y_o = nc.dram_tensor("y", [S, DM], F32, kind="ExternalOutput").ap()
    rows_d = nc.dram_tensor("rows_scratch", [2, 8, 128], F32).ap()

    es = contextlib.ExitStack()
    with es:
        sgm = es.enter_context(nc.sbuf_tensor("sgm", [1, 8], F32))
        srow = es.enter_context(nc.sbuf_tensor("srow", [1, 4], F32))
        sb2 = es.enter_context(nc.sbuf_tensor("sb2", [8, 128], F32))
        ident = es.enter_context(nc.sbuf_tensor("identS", [128, 128], F32))
        identb = es.enter_context(nc.sbuf_tensor("identbS", [128, 128], BF16))
        onesf = es.enter_context(nc.sbuf_tensor("onesf_s", [1, 128], F32))
        invs_b = es.enter_context(nc.sbuf_tensor("invs_b", [128, 1], F32))
        s_b = es.enter_context(nc.sbuf_tensor("s_b", [128, 1], F32))
        pmm = [es.enter_context(nc.psum_tensor(f"pmm{j}", [128, 512], F32)) for j in range(4)]

        p = Prog(es, nc)
        block = p.block
        ld = p.ctr("ld")
        cv = p.ctr("cv")
        ca = p.ctr("ca")
        cp = p.ctr("cp")
        so = [p.ctr(f"so{k}") for k in range(2)]

        @block.sync
        def _(sync):
            ld.dma(sync.dma_start(out=sgm[:], in_=gmax))
            ld.dma(sync.dma_start(out=sb2[:], in_=b2))
            ld.dma(sync.dma_start(out=ident[:], in_=ident_d))
            ld.dma(sync.dma_start(out=identb[:], in_=identb_d))
            ld.dma(sync.dma_start(out=onesf[:], in_=onesf_d))

        @block.vector
        def _(vector):
            ld.wait(vector)

        _quant_scalars(p, sgm, srow, onesf, invs_b, s_b, pmm[0], cv, cp)

        ctx = dict(cv=cv, ca=ca, cp=cp, ld=ld, ident=ident, identb=identb,
                   onesf=onesf, invs_b=invs_b, s_b=s_b, pmm=pmm,
                   rows_d=rows_d, sb2=sb2, srow=srow)

        def post_tile(i, x2_ap):
            j = i % 2

            @block.sync
            def _(sync, i=i, j=j):
                cv.wait(sync)
                so[j].dma(sync.dma_start(
                    out=y_o.rearrange("(t p) d -> t p d", p=128)[i], in_=x2_ap[:]))
            return [(so[j], 16 * (i // 2 + 1))]

        _emit_m2(nc, es, p, ctx, h_in=h_in, w2_in=w2, x_in=x, post_tile=post_tile)

        @block.sync
        def _(sync):
            so[0].wait(sync)
            so[1].wait(sync)
    return nc



def build_mid(nc):
    x = nc.dram_tensor("x", [S, DM], F32, kind="ExternalInput").ap()
    h_in = nc.dram_tensor("H", [NF, 128, S], F32, kind="ExternalInput").ap()
    gmax = nc.dram_tensor("gmax", [1, 8], F32, kind="ExternalInput").ap()
    w2 = nc.dram_tensor("w2", [DM, FF], F32, kind="ExternalInput").ap()
    b2 = nc.dram_tensor("b2", [8, 128], F32, kind="ExternalInput").ap()
    wqkv = nc.dram_tensor("wqkv", [DM, 3 * DM], F32, kind="ExternalInput").ap()
    bqkv = nc.dram_tensor("bqkv", [24, 128], F32, kind="ExternalInput").ap()
    bqkv_r = nc.dram_tensor("bqkv_r", [1, 3 * DM], F32, kind="ExternalInput").ap()
    wproj = nc.dram_tensor("wproj", [DM, DM], F32, kind="ExternalInput").ap()
    bproj_r = nc.dram_tensor("bproj_r", [1, DM], F32, kind="ExternalInput").ap()
    g_r = nc.dram_tensor("g_r", [1, DM], F32, kind="ExternalInput").ap()
    cosT = nc.dram_tensor("cosT", [512, S], BF16, kind="ExternalInput").ap()
    sinT = nc.dram_tensor("sinT", [512, S], BF16, kind="ExternalInput").ap()
    ident_d = nc.dram_tensor("ident", [128, 128], F32, kind="ExternalInput").ap()
    identb_d = nc.dram_tensor("identb", [128, 128], BF16, kind="ExternalInput").ap()
    onesf_d = nc.dram_tensor("onesf", [1, 128], F32, kind="ExternalInput").ap()
    onesb_d = nc.dram_tensor("onesb", [1, 128], BF16, kind="ExternalInput").ap()
    x2_o = nc.dram_tensor("x2", [S, DM], F32, kind="ExternalOutput").ap()
    st_o = nc.dram_tensor("st3", [128, 2 * NT], F32, kind="ExternalOutput").ap()
    mx_o = nc.dram_tensor("mx", [1, 1], F32, kind="ExternalOutput").ap()
    rows_d = nc.dram_tensor("rows_scratch", [2, 8, 128], F32).ap()
    x1s = nc.dram_tensor("x1_scratch", [S, DM], F32).ap()

    es = contextlib.ExitStack()
    with es:
        sgm = es.enter_context(nc.sbuf_tensor("sgm", [1, 8], F32))
        srow = es.enter_context(nc.sbuf_tensor("srow", [1, 4], F32))
        sb2 = es.enter_context(nc.sbuf_tensor("sb2", [8, 128], F32))
        ident = es.enter_context(nc.sbuf_tensor("identS", [128, 128], F32))
        identb = es.enter_context(nc.sbuf_tensor("identbS", [128, 128], BF16))
        onesf = es.enter_context(nc.sbuf_tensor("onesf_s", [1, 128], F32))
        onesb = es.enter_context(nc.sbuf_tensor("onesb_s", [1, 128], BF16))
        invs_b = es.enter_context(nc.sbuf_tensor("invs_b", [128, 1], F32))
        s_b = es.enter_context(nc.sbuf_tensor("s_b", [128, 1], F32))
        st2 = es.enter_context(nc.sbuf_tensor("st2_s", [128, 2 * NT], F32))
        st3 = es.enter_context(nc.sbuf_tensor("st3_s", [128, 2 * NT], F32))
        sum2 = es.enter_context(nc.sbuf_tensor("sum2", [128, NT], F32))
        sq2 = es.enter_context(nc.sbuf_tensor("sq2", [128, NT], F32))
        sum3 = es.enter_context(nc.sbuf_tensor("sum3", [128, NT], F32))
        sq3 = es.enter_context(nc.sbuf_tensor("sq3", [128, NT], F32))
        mx3t = es.enter_context(nc.sbuf_tensor("mx3t", [128, NT], F32))
        tmpn = es.enter_context(nc.sbuf_tensor("tmpn", [128, NT], F32))
        gbc = es.enter_context(nc.sbuf_tensor("gbc", [128, DM], F32))
        sqd = es.enter_context(nc.sbuf_tensor("sqd", [128, DM], BF16))
        rmax = es.enter_context(nc.sbuf_tensor("rmax2", [128, 1], F32))
        rowm = es.enter_context(nc.sbuf_tensor("rowm", [1, 128], F32))
        gmx = es.enter_context(nc.sbuf_tensor("gmx", [1, 1], F32))
        epst = es.enter_context(nc.sbuf_tensor("epst", [128, 2], F32))

        p = Prog(es, nc)
        block = p.block
        ld = p.ctr("ld")
        cv = p.ctr("cv")
        ca = p.ctr("ca")
        cp = p.ctr("cp")
        cpl = p.ctr("cpl")
        so = p.ctr("so")
        cx1 = p.ctr("cx1")
        cxs = [p.ctr(f"cxs{k}") for k in range(2)]

        @block.sync
        def _(sync):
            ld.dma(sync.dma_start(out=sgm[:], in_=gmax))
            ld.dma(sync.dma_start(out=sb2[:], in_=b2))
            ld.dma(sync.dma_start(out=ident[:], in_=ident_d))
            ld.dma(sync.dma_start(out=identb[:], in_=identb_d))
            ld.dma(sync.dma_start(out=onesf[:], in_=onesf_d))
            ld.dma(sync.dma_start(out=onesb[:], in_=onesb_d))

        @block.vector
        def _(vector):
            ld.wait(vector)
            ser(vector, cv, lambda: vector.memset(epst[:, 0:1], EPS_LN))
            ser(vector, cv, lambda: vector.memset(epst[:, 1:2], EPS_RMS))

        esA = contextlib.ExitStack()
        pmm = [esA.enter_context(nc.psum_tensor(f"pmm{j}", [128, 512], F32)) for j in range(4)]

        _quant_scalars(p, sgm, srow, onesf, invs_b, s_b, pmm[0], cv, cp)

        ctx = dict(cv=cv, ca=ca, cp=cp, ld=ld, ident=ident, identb=identb,
                   onesf=onesf, invs_b=invs_b, s_b=s_b, pmm=pmm,
                   rows_d=rows_d, sb2=sb2, srow=srow)

        def post_tile(i, x1_ap):
            j = i % 2

            @block.sync
            def _(sync, i=i, j=j):
                cv.wait(sync)
                cxs[j].dma(sync.dma_start(
                    out=x1s.rearrange("(t p) d -> t p d", p=128)[i], in_=x1_ap[:]))

            @block.vector
            def _(vector, i=i):
                ser(vector, cv, lambda: vector.reduce_sum(sum2[:, i:i + 1], x1_ap[:], axis=AX.X))

            @block.scalar
            def _(scalar, i=i):
                cv.wait(scalar)
                ser(scalar, ca, lambda: scalar.activation(
                    sqd[:], x1_ap[:], AF.Square, accum_out=sq2[:, i:i + 1]))
            return [(cxs[j], 16 * (i // 2 + 1)), (ca, ca.n)]

        _emit_m2(nc, esA, p, ctx, h_in=h_in, w2_in=w2, x_in=x, post_tile=post_tile)
        p.barrier()

        @block.vector
        def _(vector):
            ser(vector, cv, lambda: vector.tensor_scalar(st2[:, 0:NT], sum2[:], INVD, None, ALU.mult))
            ser(vector, cv, lambda: vector.tensor_tensor(tmpn[:], st2[:, 0:NT], st2[:, 0:NT], ALU.mult))
            ser(vector, cv, lambda: vector.tensor_scalar(sq2[:], sq2[:], INVD, None, ALU.mult))
            ser(vector, cv, lambda: vector.tensor_tensor(tmpn[:], sq2[:], tmpn[:], ALU.subtract))

        _ln_rstd(block, ca, cv, tmpn[:], st2[:, NT:2 * NT], epst[:, 0:1])

        @block.vector
        def _(vector):
            ca.wait(vector)
            ser(vector, cv, lambda: vector.tensor_tensor(tmpn[:], st2[:, 0:NT], st2[:, NT:2 * NT], ALU.mult))
            ser(vector, cv, lambda: vector.tensor_scalar(tmpn[:], tmpn[:], -1.0, None, ALU.mult))

        p.barrier()
        esA.close()

        esQ = contextlib.ExitStack()
        qrT = esQ.enter_context(nc.sbuf_tensor("qrT", [128, ND * S], BF16))
        krT = esQ.enter_context(nc.sbuf_tensor("krT", [128, ND * S], BF16))
        vtk = esQ.enter_context(nc.sbuf_tensor("vtk", [128, NT * DM], BF16))
        esB = contextlib.ExitStack()
        wqkv_bf = esB.enter_context(nc.sbuf_tensor("wqkv_bf", [128, ND * 3 * DM], BF16))
        sbqkv = esB.enter_context(nc.sbuf_tensor("sbqkv", [24, 128], F32))
        bqkvT = esB.enter_context(nc.sbuf_tensor("bqkvT", [128, 24], F32))
        bqkv_rb = esB.enter_context(nc.sbuf_tensor("bqkv_rb", [1, 3 * DM], BF16))
        sbq_row = esB.enter_context(nc.sbuf_tensor("sbq_row", [1, 3 * DM], F32))
        g_row = esB.enter_context(nc.sbuf_tensor("g_row", [1, DM], F32))
        x1t = esB.enter_context(nc.sbuf_tensor("x1t", [128, DM], F32))
        u2t = esB.enter_context(nc.sbuf_tensor("u2t", [128, DM], F32))
        msr = esB.enter_context(nc.sbuf_tensor("msr", [128, NT], F32))
        hbt = esB.enter_context(nc.sbuf_tensor("hbt", [128, DM], BF16))
        hTw = esB.enter_context(nc.sbuf_tensor("hTw", [128, ND * 512], BF16))
        cosw = esB.enter_context(nc.sbuf_tensor("cosw", [128, 4 * 512], BF16))
        sinw = esB.enter_context(nc.sbuf_tensor("sinw", [128, 4 * 512], BF16))
        rt1 = esB.enter_context(nc.sbuf_tensor("rt1", [128, 512], F32))
        rt2 = esB.enter_context(nc.sbuf_tensor("rt2", [128, 512], F32))
        pqk = [esB.enter_context(nc.psum_tensor(f"pqk{j}", [128, 512], F32)) for j in range(4)]

        cqkd = p.ctr("cqkd")
        cbq = p.ctr("cbq")
        cxw = p.ctr("cxw")
        cht = p.ctr("cht")
        ctw = p.ctr("ctw")
        cqm = p.ctr("cqm")
        CVH, CVHE, CVROPE, CQMW = {}, {}, {}, {}

        @block.gpsimd
        def _(gpsimd):
            ld.wait(gpsimd)
            cqkd.dma(gpsimd.dma_start(
                out=wqkv_bf[:], in_=wqkv.rearrange("(c p) m -> p c m", p=128)))

        @block.sync
        def _(sync):
            ld.wait(sync)
            cbq.dma(sync.dma_start(out=sbqkv[:], in_=bqkv))
            cbq.dma(sync.dma_start(out=sbq_row[:], in_=bqkv_r))
            cbq.dma(sync.dma_start(out=g_row[:], in_=g_r))

        @block.tensor
        def _(tensor):
            cbq.wait(tensor)
            cv.wait(tensor)
            tensor.transpose(pqk[0][0:128, 0:24], sbqkv[:], ident[0:24, 0:24])
            tensor.matmul(pqk[1][:, 0:512], onesf[:], g_row[:, 0:512], start=True, stop=True)
            cp.inc(tensor.matmul(pqk[2][:, 0:512], onesf[:], g_row[:, 512:1024], start=True, stop=True))

        @block.vector
        def _(vector):
            cp.wait(vector)
            ser(vector, cv, lambda: vector.tensor_copy(bqkvT[:], pqk[0][0:128, 0:24]))
            ser(vector, cv, lambda: vector.tensor_copy(gbc[:, 0:512], pqk[1][:, 0:512]))
            ser(vector, cv, lambda: vector.tensor_copy(gbc[:, 512:1024], pqk[2][:, 0:512]))
            ser(vector, cv, lambda: vector.tensor_copy(bqkv_rb[:], sbq_row[:]))

        p.barrier()

        for tw in range(4):
            @block.sync
            def _(sync, tw=tw):
                cv.wait(sync)
                for c in range(4):
                    ctw.dma(sync.dma_start(out=cosw[:, c * 512:(c + 1) * 512],
                                           in_=cosT[c * 128:(c + 1) * 128, tw * 512:(tw + 1) * 512]))
                    ctw.dma(sync.dma_start(out=sinw[:, c * 512:(c + 1) * 512],
                                           in_=sinT[c * 128:(c + 1) * 128, tw * 512:(tw + 1) * 512]))

            for ti in range(4):
                i = tw * 4 + ti

                @block.sync
                def _(sync, i=i):
                    cv.wait(sync)
                    ca.wait(sync)
                    cxw.dma(sync.dma_start(
                        out=x1t[:], in_=x1s.rearrange("(t p) d -> t p d", p=128)[i]))

                @block.scalar
                def _(scalar, i=i):
                    cxw.wait(scalar)
                    ser(scalar, ca, lambda: scalar.activation(
                        u2t[:], x1t[:], AF.Identity,
                        bias=tmpn[:, i:i + 1], scale=st2[:, NT + i:NT + i + 1]))
                    ser(scalar, ca, lambda: scalar.activation(
                        sqd[:], u2t[:], AF.Square, accum_out=msr[:, i:i + 1]))

                @block.vector
                def _(vector, i=i):
                    ca.wait(vector)
                    ser(vector, cv, lambda: vector.tensor_scalar(
                        msr[:, i:i + 1], msr[:, i:i + 1], INVD, None, ALU.mult))

                @block.scalar
                def _(scalar, i=i):
                    cv.wait(scalar)
                    ser(scalar, ca, lambda: scalar.activation(
                        msr[:, i:i + 1], msr[:, i:i + 1], AF.Ln, bias=epst[:, 1:2]))
                    ser(scalar, ca, lambda: scalar.activation(
                        msr[:, i:i + 1], msr[:, i:i + 1], AF.Exp, scale=-0.5))
                    ser(scalar, ca, lambda: scalar.activation(
                        u2t[:], u2t[:], AF.Copy, scale=msr[:, i:i + 1]))

                @block.vector
                def _(vector, i=i):
                    ca.wait(vector)
                    cht.wait(vector, at=i)
                    ser(vector, cv, lambda: vector.tensor_tensor(hbt[:], u2t[:], gbc[:], ALU.mult))
                CVH[i] = cv.n

                @block.tensor
                def _(tensor, tw=tw, ti=ti, i=i):
                    cv.wait(tensor, at=CVH[i])
                    if tw > 0:
                        cv.wait(tensor, at=CVROPE[tw - 1])
                    if i >= 2:
                        cv.wait(tensor, at=CVHE[i - 2])
                    pv = pqk[ti % 2].bitcast(BF16)
                    for c in range(ND):
                        ins = tensor.transpose(pv[:, c * 128:(c + 1) * 128],
                                               hbt[:, c * 128:(c + 1) * 128], identb[:])
                    cht.inc(ins)

                @block.vector
                def _(vector, tw=tw, ti=ti, i=i):
                    cht.wait(vector, at=i + 1)
                    if tw > 0:
                        cqm.wait(vector, at=CQMW[tw - 1])
                    pv = pqk[ti % 2].bitcast(BF16)
                    for c in range(ND):
                        ser(vector, cv, lambda c=c: vector.tensor_copy(
                            hTw[:, c * 512 + ti * 128:c * 512 + (ti + 1) * 128],
                            pv[:, c * 128:(c + 1) * 128]))
                    CVHE[i] = cv.n

            for qk in range(2):
                dst = qrT if qk == 0 else krT
                mbase = qk * 8
                for c in range(4):
                    mA, mB = mbase + c, mbase + c + 4

                    @block.tensor
                    def _(tensor, tw=tw, mA=mA, mB=mB):
                        cv.wait(tensor, at=CVHE[tw * 4 + 3])
                        cqkd.wait(tensor)
                        ctw.wait(tensor)
                        cv.wait(tensor)
                        for m, bank in ((mA, 0), (mB, 1)):
                            for d in range(ND):
                                ins = tensor.matmul(
                                    pqk[bank][:],
                                    wqkv_bf[:, d * 3 * DM + m * 128:d * 3 * DM + (m + 1) * 128],
                                    hTw[:, d * 512:(d + 1) * 512],
                                    start=(d == 0), stop=(d == ND - 1))
                                if d == ND - 1:
                                    cqm.inc(ins)

                    @block.vector
                    def _(vector, tw=tw, c=c, mA=mA, mB=mB, dst=dst):
                        cqm.wait(vector)
                        colA = c * S + tw * 512
                        colB = (c + 4) * S + tw * 512
                        cw_ = cosw[:, c * 512:(c + 1) * 512]
                        sw_ = sinw[:, c * 512:(c + 1) * 512]
                        ser(vector, cv, lambda: vector.scalar_tensor_tensor(
                            rt1[:], pqk[0][:], bqkvT[:, mA:mA + 1], cw_, ALU.add, ALU.mult))
                        ser(vector, cv, lambda: vector.scalar_tensor_tensor(
                            rt2[:], pqk[1][:], bqkvT[:, mB:mB + 1], sw_, ALU.add, ALU.mult))
                        ser(vector, cv, lambda: vector.tensor_tensor(
                            dst[:, colA:colA + 512], rt1[:], rt2[:], ALU.subtract))
                        ser(vector, cv, lambda: vector.scalar_tensor_tensor(
                            rt1[:], pqk[1][:], bqkvT[:, mB:mB + 1], cw_, ALU.add, ALU.mult))
                        ser(vector, cv, lambda: vector.scalar_tensor_tensor(
                            rt2[:], pqk[0][:], bqkvT[:, mA:mA + 1], sw_, ALU.add, ALU.mult))
                        ser(vector, cv, lambda: vector.tensor_tensor(
                            dst[:, colB:colB + 512], rt1[:], rt2[:], ALU.add))

            for ti in range(4):
                i = tw * 4 + ti

                @block.tensor
                def _(tensor, tw=tw, ti=ti, i=i):
                    ca.wait(tensor)
                    for e in range(2):
                        bank = 2 + e
                        for d in range(ND):
                            tensor.matmul(
                                pqk[bank][:],
                                hTw[:, d * 512 + ti * 128:d * 512 + (ti + 1) * 128],
                                wqkv_bf[:, d * 3 * DM + 2 * DM + e * 512:d * 3 * DM + 2 * DM + (e + 1) * 512],
                                start=(d == 0), stop=False)
                        ins = tensor.matmul(
                            pqk[bank][:], onesb[:],
                            bqkv_rb[:, 2 * DM + e * 512:2 * DM + (e + 1) * 512],
                            start=False, stop=True)
                        cqm.inc(ins)

                @block.scalar
                def _(scalar, i=i):
                    cqm.wait(scalar)
                    for e in range(2):
                        ser(scalar, ca, lambda e=e: scalar.activation(
                            vtk[:, i * DM + e * 512:i * DM + (e + 1) * 512],
                            pqk[2 + e][:], AF.Copy))

            CVROPE[tw] = cv.n
            CQMW[tw] = cqm.n

        p.barrier()
        esB.close()

        esC = contextlib.ExitStack()
        scps = esC.enter_context(nc.psum_tensor("scps", [128, S], F32))
        avp = esC.enter_context(nc.psum_tensor("avp", [128, DM], F32))
        pjp = esC.enter_context(nc.psum_tensor("pjp", [128, DM], F32))
        wproj_bf = esC.enter_context(nc.sbuf_tensor("wproj_bf", [128, ND * DM], BF16))
        bproj_rb = esC.enter_context(nc.sbuf_tensor("bproj_rb", [1, DM], BF16))
        sbp_row = esC.enter_context(nc.sbuf_tensor("sbp_row", [1, DM], F32))
        abuf = esC.enter_context(nc.sbuf_tensor("abuf", [128, S], BF16))
        atb = esC.enter_context(nc.sbuf_tensor("atb", [128, S], BF16))
        obf = esC.enter_context(nc.sbuf_tensor("obf", [128, DM], BF16))
        otb = esC.enter_context(nc.sbuf_tensor("otb", [128, DM], BF16))
        mxr = esC.enter_context(nc.sbuf_tensor("mxr", [128, 1], F32))
        mbr = esC.enter_context(nc.sbuf_tensor("mbr", [128, 1], F32))
        rsum = esC.enter_context(nc.sbuf_tensor("rsum", [128, 1], F32))
        rs2 = [esC.enter_context(nc.sbuf_tensor(f"rs2{k}", [128, 1], F32)) for k in range(2)]
        x1c = esC.enter_context(nc.sbuf_tensor("x1c", [128, DM], F32))
        u2rc = esC.enter_context(nc.sbuf_tensor("u2rc", [128, DM], F32))
        x2c = esC.enter_context(nc.sbuf_tensor("x2c", [128, DM], F32))

        cpj = p.ctr("cpj")
        ctp = p.ctr("ctp")
        cto = p.ctr("cto")
        cwp = p.ctr("cwp")
        cbp = p.ctr("cbp")
        CAB, CVAT, CAO, CVOT = {}, {}, {}, {}

        @block.gpsimd
        def _(gpsimd):
            ld.wait(gpsimd)
            cwp.dma(gpsimd.dma_start(
                out=wproj_bf[:], in_=wproj.rearrange("(c p) m -> p c m", p=128)))

        @block.sync
        def _(sync):
            ld.wait(sync)
            cbp.dma(sync.dma_start(out=sbp_row[:], in_=bproj_r))

        @block.vector
        def _(vector):
            cwp.wait(vector)
            cbp.wait(vector)
            ser(vector, cv, lambda: vector.tensor_copy(bproj_rb[:], sbp_row[:]))
            ser(vector, cv, lambda: vector.memset(rmax[:], -1e30))

        p.barrier()

        CVMB, CPJ_S = {}, {}

        def emit_scores(i):
            @block.tensor
            def _(tensor, i=i):
                if i == 0:
                    p.wait_all(tensor)
                else:
                    ca.wait(tensor, at=CAB[i - 1])
                    cv.wait(tensor, at=CVAT[i - 1])
                for n in range(4):
                    for c in range(ND):
                        ins = tensor.matmul(
                            scps[:, n * 512:(n + 1) * 512],
                            qrT[:, c * S + i * 128:c * S + (i + 1) * 128],
                            krT[:, c * S + n * 512:c * S + (n + 1) * 512],
                            start=(c == 0), stop=(c == ND - 1))
                        if c == ND - 1:
                            cpj.inc(ins)
            CPJ_S[i] = cpj.n

        def emit_smax(i):
            @block.vector
            def _(vector, i=i):
                cpj.wait(vector, at=CPJ_S[i])
                if i > 0:
                    ca.wait(vector, at=CAB[i - 1])
                ser(vector, cv, lambda: vector.reduce_max(mxr[:], scps[:], axis=AX.X))
                ser(vector, cv, lambda: vector.tensor_scalar(mbr[:], mxr[:], -SCALE_SM, None, ALU.mult))
            CVMB[i] = cv.n

        def emit_exp(i):
            @block.scalar
            def _(scalar, i=i):
                cv.wait(scalar, at=CVMB[i])
                ctp.wait(scalar, at=i)
                ser(scalar, ca, lambda: scalar.activation(
                    abuf[:], scps[:], AF.Exp, bias=mbr[:, 0:1], scale=SCALE_SM,
                    accum_out=rsum[:]))
            CAB[i] = ca.n

            @block.vector
            def _(vector, i=i):
                ca.wait(vector, at=CAB[i])
                ser(vector, cv, lambda: vector.reciprocal(rs2[i % 2][:], rsum[:]))

        def emit_tA(i):
            @block.tensor
            def _(tensor, i=i):
                ca.wait(tensor, at=CAB[i])
                pv = scps.bitcast(BF16)
                for c in range(NT):
                    ins = tensor.transpose(pv[:, c * 128:(c + 1) * 128],
                                           abuf[:, c * 128:(c + 1) * 128], identb[:])
                ctp.inc(ins)

            @block.vector
            def _(vector, i=i):
                ctp.wait(vector, at=i + 1)
                pv = scps.bitcast(BF16)
                ser(vector, cv, lambda: vector.tensor_copy(atb[:, 0:1024], pv[:, 0:1024]))
                ser(vector, cv, lambda: vector.tensor_copy(atb[:, 1024:2048], pv[:, 1024:2048]))
                CVAT[i] = cv.n

        def emit_av(i):
            @block.tensor
            def _(tensor, i=i):
                cv.wait(tensor, at=CVAT[i])
                for e in range(2):
                    for c in range(NT):
                        ins = tensor.matmul(
                            avp[:, e * 512:(e + 1) * 512],
                            atb[:, c * 128:(c + 1) * 128],
                            vtk[:, c * DM + e * 512:c * DM + (e + 1) * 512],
                            start=(c == 0), stop=(c == NT - 1))
                        if c == NT - 1:
                            cpj.inc(ins)

        def emit_obf(i):
            @block.scalar
            def _(scalar, i=i):
                cpj.wait(scalar)
                cv.wait(scalar)
                ser(scalar, ca, lambda: scalar.activation(
                    obf[:], avp[:], AF.Copy, scale=rs2[i % 2][:, 0:1]))
            CAO[i] = ca.n

        def emit_tO(i):
            @block.tensor
            def _(tensor, i=i):
                ca.wait(tensor, at=CAO[i])
                pv = avp.bitcast(BF16)
                for e in range(ND):
                    ins = tensor.transpose(pv[:, e * 128:(e + 1) * 128],
                                           obf[:, e * 128:(e + 1) * 128], identb[:])
                cto.inc(ins)

            @block.vector
            def _(vector, i=i):
                cto.wait(vector, at=i + 1)
                ser(vector, cv, lambda: vector.tensor_copy(otb[:], avp.bitcast(BF16)[:, 0:1024]))
                CVOT[i] = cv.n

        def emit_proj(i):
            @block.tensor
            def _(tensor, i=i):
                cv.wait(tensor, at=CVOT[i])
                cwp.wait(tensor)
                cv.wait(tensor)
                for n in range(2):
                    for e in range(ND):
                        tensor.matmul(
                            pjp[:, n * 512:(n + 1) * 512],
                            otb[:, e * 128:(e + 1) * 128],
                            wproj_bf[:, e * DM + n * 512:e * DM + (n + 1) * 512],
                            start=(e == 0), stop=False)
                    ins = tensor.matmul(
                        pjp[:, n * 512:(n + 1) * 512], onesb[:],
                        bproj_rb[:, n * 512:(n + 1) * 512], start=False, stop=True)
                    cpj.inc(ins)

        def emit_post(i):
            @block.sync
            def _(sync, i=i):
                cv.wait(sync)
                ca.wait(sync)
                cx1.dma(sync.dma_start(
                    out=x1c[:], in_=x1s.rearrange("(t p) d -> t p d", p=128)[i]))

            @block.scalar
            def _(scalar, i=i):
                cx1.wait(scalar)
                ser(scalar, ca, lambda: scalar.activation(
                    u2rc[:], x1c[:], AF.Identity,
                    bias=tmpn[:, i:i + 1], scale=st2[:, NT + i:NT + i + 1]))

            @block.vector
            def _(vector, i=i):
                p.wait_all(vector)
                ser(vector, cv, lambda: vector.tensor_tensor(x2c[:], pjp[:], u2rc[:], ALU.add))
                ser(vector, cv, lambda: vector.tensor_tensor(x2c[:], x2c[:], x1c[:], ALU.add))
                ser(vector, cv, lambda: vector.reduce_sum(sum3[:, i:i + 1], x2c[:], axis=AX.X))
                ser(vector, cv, lambda: vector.reduce_max(mx3t[:, i:i + 1], x2c[:], axis=AX.X))

            @block.scalar
            def _(scalar, i=i):
                cv.wait(scalar)
                ser(scalar, ca, lambda: scalar.activation(
                    sqd[:], x2c[:], AF.Square, accum_out=sq3[:, i:i + 1]))

            @block.sync
            def _(sync, i=i):
                cv.wait(sync)
                so.dma(sync.dma_start(
                    out=x2_o.rearrange("(t p) d -> t p d", p=128)[i], in_=x2c[:]))

        emit_scores(0)
        emit_smax(0)
        emit_exp(0)
        emit_tA(0)
        for i in range(NT):
            emit_av(i)
            if i + 1 < NT:
                emit_scores(i + 1)
                emit_smax(i + 1)
                emit_exp(i + 1)
            emit_obf(i)
            emit_tO(i)
            emit_proj(i)
            emit_post(i)
            if i + 1 < NT:
                emit_tA(i + 1)

        p.barrier()
        esC.close()

        @block.vector
        def _(vector):
            ser(vector, cv, lambda: vector.tensor_scalar(st3[:, 0:NT], sum3[:], INVD, None, ALU.mult))
            ser(vector, cv, lambda: vector.tensor_tensor(tmpn[:], st3[:, 0:NT], st3[:, 0:NT], ALU.mult))
            ser(vector, cv, lambda: vector.tensor_scalar(sq3[:], sq3[:], INVD, None, ALU.mult))
            ser(vector, cv, lambda: vector.tensor_tensor(tmpn[:], sq3[:], tmpn[:], ALU.subtract))

        _ln_rstd(block, ca, cv, tmpn[:], st3[:, NT:2 * NT], epst[:, 0:1])

        esD = contextlib.ExitStack()
        pfold = esD.enter_context(nc.psum_tensor("pfold", [128, 512], F32))

        @block.vector
        def _(vector):
            ca.wait(vector)
            ser(vector, cv, lambda: vector.tensor_tensor(mx3t[:], mx3t[:], st3[:, 0:NT], ALU.subtract))
            ser(vector, cv, lambda: vector.tensor_tensor(mx3t[:], mx3t[:], st3[:, NT:2 * NT], ALU.mult))
            ser(vector, cv, lambda: vector.reduce_max(rmax[:], mx3t[:], axis=AX.X))

        _fold_max(p, ident, pfold, rmax, rowm, gmx, cp, cv)

        @block.sync
        def _(sync):
            cv.wait(sync)
            so.dma(sync.dma_start(out=st_o, in_=st3[:]))
            so.dma(sync.dma_start(out=mx_o, in_=gmx[:]))
            so.wait(sync)
        esD.close()
        esQ.close()
    return nc



def _rope_tables():
    half = 512
    inv_freq = 10000.0 ** (-np.arange(half, dtype=np.float64) / half)
    ang = np.arange(S, dtype=np.float64)[:, None] * inv_freq[None, :]
    cos = np.rint(np.cos(ang)).astype(ml_dtypes.bfloat16)
    sin = np.rint(np.sin(ang)).astype(ml_dtypes.bfloat16)
    return np.ascontiguousarray(cos.T), np.ascontiguousarray(sin.T)


_NC_CACHE = {}


def _get_nc(name, builder):
    if name not in _NC_CACHE:
        nc = bass.Bass("TRN2", target_bir_lowering=False)
        _NC_CACHE[name] = builder(nc)
    return _NC_CACHE[name]


def _run(name, builder, in_maps):
    nc = _get_nc(name, builder)
    res = run_bass_kernel_spmd(nc, in_maps, core_ids=list(range(8)))
    return res.results


def kernel(L0_attn_x, L0_attn_g_norm, L0_attn_w_qkv, L0_attn_b_qkv,
           L0_attn_w_proj, L0_attn_b_proj, L0_ffn_w1, L0_ffn_b1,
           L0_ffn_w2, L0_ffn_b2):
    xs = [np.ascontiguousarray(np.asarray(L0_attn_x)[c], dtype=np.float32) for c in range(8)]
    w1 = np.ascontiguousarray(np.asarray(L0_ffn_w1, dtype=np.float32))
    b1 = np.ascontiguousarray(np.asarray(L0_ffn_b1, dtype=np.float32).reshape(32, 128))
    w2 = np.ascontiguousarray(np.asarray(L0_ffn_w2, dtype=np.float32))
    b2 = np.ascontiguousarray(np.asarray(L0_ffn_b2, dtype=np.float32).reshape(8, 128))
    wqkv = np.ascontiguousarray(np.asarray(L0_attn_w_qkv, dtype=np.float32))
    bqkv = np.ascontiguousarray(np.asarray(L0_attn_b_qkv, dtype=np.float32).reshape(24, 128))
    bqkv_r = np.ascontiguousarray(np.asarray(L0_attn_b_qkv, dtype=np.float32).reshape(1, 3 * DM))
    wproj = np.ascontiguousarray(np.asarray(L0_attn_w_proj, dtype=np.float32))
    bproj_r = np.ascontiguousarray(np.asarray(L0_attn_b_proj, dtype=np.float32).reshape(1, DM))
    g_r = np.ascontiguousarray(np.asarray(L0_attn_g_norm, dtype=np.float32).reshape(1, DM))
    cosT, sinT = _rope_tables()
    ident = np.eye(128, dtype=np.float32)
    identb = np.eye(128, dtype=ml_dtypes.bfloat16)
    onesf = np.ones((1, 128), dtype=np.float32)
    onesb = np.ones((1, 128), dtype=ml_dtypes.bfloat16)

    r0 = _run("stats", build_stats, [
        {"x": xs[c], "ident": ident} for c in range(8)])
    gmax1 = np.concatenate([r0[c]["mx"] for c in range(8)], axis=1)

    r1 = _run("m1", build_m1, [
        {"x": xs[c], "st": r0[c]["st"], "gmax": gmax1, "w1": w1, "b1": b1,
         "ident": ident, "identb": identb, "onesf": onesf} for c in range(8)])
    gmax2 = np.concatenate([r1[c]["mx"] for c in range(8)], axis=1)

    r2 = _run("mid", build_mid, [
        {"x": xs[c], "H": r1[c]["H"], "gmax": gmax2, "w2": w2, "b2": b2,
         "wqkv": wqkv, "bqkv": bqkv, "bqkv_r": bqkv_r, "wproj": wproj,
         "bproj_r": bproj_r, "g_r": g_r, "cosT": cosT, "sinT": sinT,
         "ident": ident, "identb": identb, "onesf": onesf, "onesb": onesb} for c in range(8)])
    gmax3 = np.concatenate([r2[c]["mx"] for c in range(8)], axis=1)

    r3 = _run("m1", build_m1, [
        {"x": r2[c]["x2"], "st": r2[c]["st3"], "gmax": gmax3, "w1": w1, "b1": b1,
         "ident": ident, "identb": identb, "onesf": onesf} for c in range(8)])
    gmax4 = np.concatenate([r3[c]["mx"] for c in range(8)], axis=1)

    r4 = _run("m2f", build_m2f, [
        {"x": r2[c]["x2"], "H": r3[c]["H"], "gmax": gmax4, "w2": w2, "b2": b2,
         "ident": ident, "identb": identb, "onesf": onesf} for c in range(8)])

    out = np.stack([r4[c]["y"] for c in range(8)], axis=0)
    return out.astype(np.float32)
